# version 02
# baseline (speedup 1.0000x reference)
import contextlib
import numpy as np
import ml_dtypes
_bf16 = ml_dtypes.bfloat16
import concourse.bass as bass
import concourse.mybir as mybir
from concourse.bass_utils import run_bass_kernel_spmd

F32 = mybir.dt.float32
BF16 = mybir.dt.bfloat16
I32 = mybir.dt.int32
AF = mybir.ActivationFunctionType
ALU = mybir.AluOpType
AX = mybir.AxisListType

RACE_CHECK = False
L1DBG = False
NCORES = 8
B = 16
S = 2048
D = 1024
BL = B // NCORES
NT = BL * S
DT = D // 128
TCH = 512
NCH = NT // TCH
EPS = 1e-5


class Sem:
    __slots__ = ("h", "n")
    def __init__(self, h):
        self.h = h
        self.n = 0
    def bump(self, k=1):
        self.n += k
        return self.n


class Prog:
    ENGS = ("sync", "tensor", "vector", "scalar", "gpsimd")

    def __init__(self, name="prog"):
        self.nc = bass.Bass("TRN2", target_bir_lowering=False, debug=False,
                            detect_race_conditions=RACE_CHECK)
        self.es = contextlib.ExitStack()
        self.streams = {k: [] for k in self.ENGS}
        self._i = 0
        self.sd = self.sem("sd")
        self.sg = self.sem("sg")
        self.st = self.sem("st")
        self.sv = self.sem("sv")
        self.ss = self.sem("ss")

    def dram(self, name, shape, dtype, kind):
        return self.nc.dram_tensor(name, list(shape), dtype, kind=kind).ap()

    def sbuf(self, name, shape, dtype):
        return self.es.enter_context(self.nc.sbuf_tensor(name, list(shape), dtype))

    def psum(self, name, shape, dtype=F32):
        return self.es.enter_context(self.nc.psum_tensor(name, list(shape), dtype))

    def sem(self, name=None):
        self._i += 1
        return Sem(self.es.enter_context(self.nc.semaphore(name or f"s{self._i}")))

    def on(self, eng):
        def deco(fn):
            self.streams[eng].append(fn)
            return None
        return deco

    def load(self, out, in_, wait=()):
        w = [(s, s.n if t is None else t) for s, t in wait]
        tgt = self.sd.bump(16)
        @self.on("sync")
        def _(e):
            for s, t in w:
                e.wait_ge(s.h, t)
            e.dma_start(out=out, in_=in_).then_inc(self.sd.h, 16)
        return tgt

    def store(self, out, in_, wait=()):
        w = [(s, s.n if t is None else t) for s, t in wait]
        tgt = self.sg.bump(16)
        @self.on("gpsimd")
        def _(e):
            for s, t in w:
                e.wait_ge(s.h, t)
            e.dma_start(out=out, in_=in_).then_inc(self.sg.h, 16)
        return tgt

    def barrier_engine(self, eng, waits):
        w = [(s, s.n if t is None else t) for s, t in waits]
        @self.on(eng)
        def _(e):
            for s, t in w:
                e.wait_ge(s.h, t)

    def finalize(self):
        sgf = self.sg
        @self.on("gpsimd")
        def _(e):
            e.wait_ge(sgf.h, sgf.n)
        with self.nc.Block() as blk:
            for eng in self.ENGS:
                fns = self.streams[eng]
                if not fns:
                    continue
                def runner(e, fns=fns):
                    for f in fns:
                        f(e)
                getattr(blk, eng)(runner)
        self.es.close()
        return self.nc



def emit_ln_chunk(P, xt, xnb, sq, rows, ones_f, ones1_f, PST, PBC, n, gb=None, keep_f32=False):
    st, sv, ss = P.st, P.sv, P.ss
    r_mean, r_var, r_tmp = rows
    t_sq = None
    for k in range(DT):
        t_sq = ss.bump(1)
        @P.on("scalar")
        def _(e, k=k):
            e.activation(sq[k][:, 0:n], xt[k][:, 0:n], AF.Square).then_inc(ss.h, 1)
    t_mm = st.bump(1)
    @P.on("tensor")
    def _(e, t_sq=t_sq):
        for k in range(DT):
            e.matmul(PST[0:1, 0:n], ones_f[:, 0:1], xt[k][:, 0:n], start=(k == 0), stop=(k == DT - 1))
        e.wait_ge(P.ss.h, t_sq)
        for k in range(DT):
            r = e.matmul(PBC[0:1, 0:n], ones_f[:, 0:1], sq[k][:, 0:n], start=(k == 0), stop=(k == DT - 1))
        r.then_inc(st.h, 1)
    t_stats = sv.bump(1)
    @P.on("vector")
    def _(e, t_mm=t_mm):
        e.wait_ge(P.st.h, t_mm)
        e.tensor_scalar_mul(r_mean[0:1, 0:n], PST[0:1, 0:n], 1.0 / D)
        e.tensor_scalar_mul(r_var[0:1, 0:n], PBC[0:1, 0:n], 1.0 / D)
        e.tensor_tensor(r_tmp[0:1, 0:n], r_mean[0:1, 0:n], r_mean[0:1, 0:n], ALU.mult)
        e.tensor_tensor(r_var[0:1, 0:n], r_var[0:1, 0:n], r_tmp[0:1, 0:n], ALU.subtract).then_inc(sv.h, 1)
    t_rstd = ss.bump(1)
    @P.on("scalar")
    def _(e, t_stats=t_stats):
        e.wait_ge(P.sv.h, t_stats)
        e.activation(r_var[0:1, 0:n], r_var[0:1, 0:n], AF.Ln, bias=P.c_eps[0:1, 0:1])
        e.activation(r_var[0:1, 0:n], r_var[0:1, 0:n], AF.Exp, scale=-0.5).then_inc(ss.h, 1)
    t_bc = st.bump(1)
    @P.on("tensor")
    def _(e, t_rstd=t_rstd):
        e.wait_ge(P.ss.h, t_rstd)
        e.matmul(PBC[:, 0:n], ones1_f[0:1, :], r_mean[0:1, 0:n], start=True, stop=True)
        e.matmul(PST[:, 0:n], ones1_f[0:1, :], r_var[0:1, 0:n], start=True, stop=True).then_inc(st.h, 1)
    t_out = None
    for k in range(DT):
        t_out = sv.bump(1)
        @P.on("vector")
        def _(e, k=k, t_bc=t_bc):
            if k == 0:
                e.wait_ge(P.st.h, t_bc)
            e.tensor_tensor(sq[k][:, 0:n], xt[k][:, 0:n], PBC[:, 0:n], ALU.subtract)
            if gb is None:
                if keep_f32:
                    e.tensor_tensor(sq[k][:, 0:n], sq[k][:, 0:n], PST[:, 0:n], ALU.mult)
                    e.tensor_copy(xnb[k][:, 0:n], sq[k][:, 0:n]).then_inc(sv.h, 1)
                else:
                    e.tensor_tensor(xnb[k][:, 0:n], sq[k][:, 0:n], PST[:, 0:n], ALU.mult).then_inc(sv.h, 1)
            else:
                e.tensor_tensor(sq[k][:, 0:n], sq[k][:, 0:n], PST[:, 0:n], ALU.mult)
                e.tensor_scalar(xnb[k][:, 0:n], sq[k][:, 0:n], gb[0][:, k:k + 1], gb[1][:, k:k + 1], ALU.mult, ALU.add).then_inc(sv.h, 1)
    return t_out


def emit_proj(P, nk, n_out_tiles, lhsT_fn, rhs_fn, evac_fn, PP, n):
    st = P.st
    bank_free = [None, None]
    for ot in range(n_out_tiles):
        pb = PP[ot % 2]
        bf = bank_free[ot % 2]
        tgt = st.bump(1)
        @P.on("tensor")
        def _(e, ot=ot, pb=pb, bf=bf):
            if bf is not None:
                e.wait_ge(bf[0].h, bf[1])
            for k in range(nk):
                r = e.matmul(pb[:, 0:n], lhsT_fn(k, ot), rhs_fn(k, ot), start=(k == 0), stop=(k == nk - 1))
            r.then_inc(st.h, 1)
        bank_free[ot % 2] = evac_fn(ot, pb, tgt)



def build_l2a():
    P = Prog("l2a")
    x_in = P.dram("x_in", [DT, 128, NT], F32, "ExternalInput")
    wvrT = P.dram("l2_wvrT", [DT, 128, 2048], F32, "ExternalInput")
    woT = P.dram("l2_woT", [DT, 128, D], F32, "ExternalInput")
    mixc = P.dram("l2_mixc", [128, 3 * DT], F32, "ExternalInput")
    ones = P.dram("c_ones", [128, 1], F32, "ExternalInput")
    ones1 = P.dram("c_ones1", [1, 128], F32, "ExternalInput")
    xn_s = P.dram("xn_s", [DT, 128, NT], BF16, "ExternalOutput")
    s_s = P.dram("s_s", [DT, 128, NT], F32, "ExternalOutput")

    ones_f = P.sbuf("ones_f", [128, 1], F32)
    ones1_f = P.sbuf("ones1_f", [1, 128], F32)
    misc = P.dram("c_misc", [1, 8], F32, "ExternalInput")
    P.c_eps = P.sbuf("c_eps", [1, 8], F32)
    P.load(ones_f[:, :], ones[:, :])
    P.load(ones1_f[:, :], ones1[:, :])
    P.load(P.c_eps[0:1, :], misc[0:1, :])

    mixs = P.sbuf("mixs", [128, 3 * DT], F32)
    omix = P.sbuf("omix", [128, 2 * DT], F32)
    dcol = P.sbuf("dcol", [128, DT], BF16)
    tmx = P.load(mixs[:, :], mixc[:, :])
    t_omix = P.sv.bump(1)
    @P.on("vector")
    def _(e, tmx=tmx):
        e.wait_ge(P.sd.h, tmx)
        e.tensor_scalar(omix[:, :], mixs[:, 0:2 * DT], -1.0, 1.0, ALU.mult, ALU.add).then_inc(P.sv.h, 1)
    t_dcol = P.ss.bump(1)
    @P.on("scalar")
    def _(e, tmx=tmx):
        e.wait_ge(P.sd.h, tmx)
        e.activation(mixs[:, 2 * DT:3 * DT], mixs[:, 2 * DT:3 * DT], AF.Exp)
        e.activation(dcol[:, :], mixs[:, 2 * DT:3 * DT], AF.Exp, scale=-1.0).then_inc(P.ss.h, 1)
    t_zc = P.sv.bump(1)
    @P.on("vector")
    def _(e):
        e.memset(zcol[:, :], 0.0).then_inc(P.sv.h, 1)

    wscr = P.sbuf("wscr", [128, 2048], F32)
    wp = [P.sbuf(f"wp{k}", [128, 2048], BF16) for k in range(2 * DT)]
    woTb = [P.sbuf(f"woTb{k}", [128, D], BF16) for k in range(DT)]
    for k in range(DT):
        tw = P.load(wscr[:, :], wvrT[k], wait=[(P.sv, None)])
        tvw = P.sv.bump(1)
        @P.on("vector")
        def _(e, k=k, tw=tw, t_omix=t_omix):
            e.wait_ge(P.sd.h, tw)
            if k == 0:
                e.wait_ge(P.sv.h, t_omix)
            e.tensor_single_scalar(wp[k][:, 0:1024], wscr[:, 0:1024], mixs[:, k:k + 1], ALU.mult)
            e.tensor_single_scalar(wp[k][:, 1024:2048], wscr[:, 1024:2048], mixs[:, DT + k:DT + k + 1], ALU.mult)
            e.tensor_single_scalar(wp[DT + k][:, 0:1024], wscr[:, 0:1024], omix[:, k:k + 1], ALU.mult)
            e.tensor_single_scalar(wp[DT + k][:, 1024:2048], wscr[:, 1024:2048], omix[:, DT + k:DT + k + 1], ALU.mult).then_inc(P.sv.h, 1)
    for k in range(DT):
        tw = P.load(wscr[:, 0:D], woT[k], wait=[(P.sv, None)])
        tvw = P.sv.bump(1)
        @P.on("vector")
        def _(e, k=k, tw=tw):
            e.wait_ge(P.sd.h, tw)
            e.tensor_copy(woTb[k][:, :], wscr[:, 0:D]).then_inc(P.sv.h, 1)

    xt = [P.sbuf(f"xt{k}", [128, TCH], F32) for k in range(DT)]
    sq = [P.sbuf(f"sq{k}", [128, TCH], F32) for k in range(DT)]
    xnb = [P.sbuf(f"xnb{k}", [128, TCH], BF16) for k in range(DT)]
    prevb = [P.sbuf(f"prevb{k}", [128, TCH], BF16) for k in range(DT)]
    so = [P.sbuf(f"so{k}", [128, TCH], F32) for k in range(DT)]
    carry = [P.sbuf(f"carry{k}", [128, 1], F32) for k in range(DT)]
    zcol = P.sbuf("zcol", [128, 1], F32)
    vb = [P.sbuf(f"vbt{k}", [128, TCH], BF16) for k in range(DT)]
    rsb = [P.sbuf(f"rsb{k}", [128, TCH], BF16) for k in range(DT)]
    sot = [P.sbuf(f"sot{k}", [128, TCH], F32) for k in range(DT)]
    rows = (P.sbuf("r_mean", [1, TCH], F32), P.sbuf("r_var", [1, TCH], F32), P.sbuf("r_tmp", [1, TCH], F32))
    PST = P.psum("PST", [128, TCH])
    PBC = P.psum("PBC", [128, TCH])
    PM0 = P.psum("PM0", [128, TCH])
    PM1 = P.psum("PM1", [128, TCH])

    for ci in range(NCH):
        off = ci * TCH
        tld = None
        for k in range(DT):
            tld = P.load(xt[k][:, :], x_in[k, :, off:off + TCH],
                         wait=[(P.st, None), (P.sv, None)] if k == 0 else ())
        P.barrier_engine("tensor", [(P.sd, tld), (P.sv, None)])
        P.barrier_engine("scalar", [(P.sd, tld)])
        P.barrier_engine("vector", [(P.sd, tld), (P.sg, None), (P.st, None), (P.ss, t_dcol)])
        t_xn = emit_ln_chunk(P, xt, xnb, sq, rows, ones_f, ones1_f, PST, PBC, TCH)
        for k in range(DT):
            P.store(xn_s[k, :, off:off + TCH], xnb[k][:, :], wait=[(P.sv, t_xn)] if k == 0 else ())
        t_scan = None
        for k in range(DT):
            t_scan = P.sv.bump(1)
            @P.on("vector")
            def _(e, k=k, ci=ci):
                if ci % (NCH // BL) == 0:
                    e.tensor_tensor_scan(so[k][:, :], dcol[:, k:k + 1].broadcast_to([128, TCH]), xnb[k][:, :], 0.0, ALU.mult, ALU.add)
                    e.memset(prevb[k][:, 0:1], 0.0)
                else:
                    e.tensor_tensor_scan(so[k][:, :], dcol[:, k:k + 1].broadcast_to([128, TCH]), xnb[k][:, :], carry[k][:, 0:1], ALU.mult, ALU.add)
                    e.tensor_copy(prevb[k][:, 0:1], carry[k][:, :])
                e.tensor_copy(prevb[k][:, 1:TCH], so[k][:, 0:TCH - 1])
                e.tensor_copy(carry[k][:, :], so[k][:, TCH - 1:TCH]).then_inc(P.sv.h, 1)
        def lhsT_vr(k, ot):
            return wp[k][:, ot * 128:(ot + 1) * 128]
        def rhs_vr(k, ot):
            return (xnb[k] if k < DT else prevb[k - DT])[:, 0:TCH]
        def evac_vr(ot, pb, tgt):
            te = P.ss.bump(1)
            @P.on("scalar")
            def _(e, ot=ot, pb=pb, tgt=tgt):
                e.wait_ge(P.st.h, tgt)
                if ot < DT:
                    e.activation(vb[ot][:, :], pb[:, 0:TCH], AF.Copy).then_inc(P.ss.h, 1)
                else:
                    e.activation(rsb[ot - DT][:, :], pb[:, 0:TCH], AF.Sigmoid).then_inc(P.ss.h, 1)
            return (P.ss, te)
        P.barrier_engine("tensor", [(P.sv, t_scan)])
        emit_proj(P, 2 * DT, 2 * DT, lhsT_vr, rhs_vr, evac_vr, [PM0, PM1], TCH)
        def lhsT_wo(k, ot):
            return woTb[k][:, ot * 128:(ot + 1) * 128]
        def rhs_wo(k, ot):
            return vb[k][:, 0:TCH]
        def evac_wo(ot, pb, tgt):
            te = P.sv.bump(1)
            @P.on("vector")
            def _(e, ot=ot, pb=pb, tgt=tgt):
                e.wait_ge(P.st.h, tgt)
                e.tensor_tensor(sot[ot][:, :], pb[:, 0:TCH], rsb[ot][:, :], ALU.mult).then_inc(P.sv.h, 1)
            P.store(s_s[ot, :, off:off + TCH], sot[ot][:, :], wait=[(P.sv, te)])
            return (P.sv, te)
        P.barrier_engine("tensor", [(P.ss, None)])
        emit_proj(P, DT, DT, lhsT_wo, rhs_wo, evac_wo, [PM0, PM1], TCH)
    return P.finalize()


def build_l2b():
    P = Prog("l2b")
    x_in = P.dram("x_in", [DT, 128, NT], F32, "ExternalInput")
    xn_s = P.dram("xn_s", [DT, 128, NT], BF16, "ExternalInput")
    s_s = P.dram("s_s", [DT, 128, NT], F32, "ExternalInput")
    wg = P.dram("l2_wg", [DT, 128, 3072], I32, "ExternalInput")
    wu = P.dram("l2_wu", [DT, 128, 3072], I32, "ExternalInput")
    wd = P.dram("l2_wd", [24, 128, D], I32, "ExternalInput")
    x_out = P.dram("x_out", [DT, 128, NT], F32, "ExternalOutput")

    QS = 1.0 / 127.0
    wscr = P.sbuf("wscr", [128, 3072], I32)
    wgb = [P.sbuf(f"wgb{k}", [128, 3072], BF16) for k in range(DT)]
    wub = [P.sbuf(f"wub{k}", [128, 3072], BF16) for k in range(DT)]
    wdb = [P.sbuf(f"wdb{k}", [128, D], BF16) for k in range(24)]
    def wload(dst, src_ap, ncol):
        tw = P.load(wscr[:, 0:ncol], src_ap, wait=[(P.sv, None)])
        tv = P.sv.bump(1)
        @P.on("vector")
        def _(e, dst=dst, tw=tw, ncol=ncol):
            e.wait_ge(P.sd.h, tw)
            e.tensor_copy(dst[:, :], wscr[:, 0:ncol]).then_inc(P.sv.h, 1)
    for k in range(DT):
        wload(wgb[k], wg[k], 3072)
        wload(wub[k], wu[k], 3072)
    for k in range(24):
        wload(wdb[k], wd[k], D)

    xnb = [P.sbuf(f"xnb{k}", [128, TCH], BF16) for k in range(DT)]
    hb = [P.sbuf(f"hb{k}", [128, TCH], BF16) for k in range(24)]
    ub = P.sbuf("ub", [128, TCH], F32)
    xts = P.sbuf("xts", [128, TCH], F32)
    sts = P.sbuf("sts", [128, TCH], F32)
    xo = P.sbuf("xo", [128, TCH], F32)
    PM = [P.psum(f"PM{i}", [128, TCH]) for i in range(4)]

    for ci in range(NCH):
        off = ci * TCH
        tld = None
        for k in range(DT):
            tld = P.load(xnb[k][:, :], xn_s[k, :, off:off + TCH],
                         wait=[(P.st, None), (P.sv, None)] if k == 0 else ())
        P.barrier_engine("tensor", [(P.sd, tld), (P.sv, None)])
        P.barrier_engine("scalar", [(P.st, None)])
        gfree = [None, None]
        ufree = [None, None]
        for ot in range(24):
            pg = PM[ot % 2]
            pu = PM[2 + ot % 2]
            gf, uf = gfree[ot % 2], ufree[ot % 2]
            tgt = P.st.bump(1)
            @P.on("tensor")
            def _(e, ot=ot, pg=pg, pu=pu, gf=gf, uf=uf):
                if gf is not None:
                    e.wait_ge(gf[0].h, gf[1])
                if uf is not None:
                    e.wait_ge(uf[0].h, uf[1])
                for k in range(DT):
                    e.matmul(pg[:, 0:TCH], wgb[k][:, ot * 128:(ot + 1) * 128], xnb[k][:, 0:TCH], start=(k == 0), stop=(k == DT - 1))
                for k in range(DT):
                    r = e.matmul(pu[:, 0:TCH], wub[k][:, ot * 128:(ot + 1) * 128], xnb[k][:, 0:TCH], start=(k == 0), stop=(k == DT - 1))
                r.then_inc(P.st.h, 1)
            ts_ = P.ss.bump(1)
            @P.on("scalar")
            def _(e, ot=ot, pg=pg, tgt=tgt):
                e.wait_ge(P.st.h, tgt)
                e.activation(hb[ot][:, :], pg[:, 0:TCH], AF.Silu, scale=QS).then_inc(P.ss.h, 1)
            tv = P.sv.bump(1)
            @P.on("vector")
            def _(e, ot=ot, pu=pu, tgt=tgt, ts_=ts_):
                e.wait_ge(P.st.h, tgt)
                e.tensor_scalar_mul(ub[:, :], pu[:, 0:TCH], QS)
                e.wait_ge(P.ss.h, ts_)
                e.tensor_tensor(hb[ot][:, :], hb[ot][:, :], ub[:, :], ALU.mult).then_inc(P.sv.h, 1)
            gfree[ot % 2] = (P.ss, ts_)
            ufree[ot % 2] = (P.sv, tv)
        P.barrier_engine("tensor", [(P.sv, None)])
        dfree = [None, None]
        for ot in range(DT):
            pb = PM[ot % 2]
            df = dfree[ot % 2]
            tgt = P.st.bump(1)
            @P.on("tensor")
            def _(e, ot=ot, pb=pb, df=df):
                if df is not None:
                    e.wait_ge(df[0].h, df[1])
                for k in range(24):
                    r = e.matmul(pb[:, 0:TCH], wdb[k][:, ot * 128:(ot + 1) * 128], hb[k][:, 0:TCH], start=(k == 0), stop=(k == 23))
                r.then_inc(P.st.h, 1)
            tx = P.load(xts[:, :], x_in[ot, :, off:off + TCH], wait=[(P.sv, None)])
            ts2 = P.load(sts[:, :], s_s[ot, :, off:off + TCH])
            tsi = P.ss.bump(1)
            sgn = P.sg.n
            @P.on("scalar")
            def _(e, ot=ot, pb=pb, tgt=tgt, sgn=sgn):
                e.wait_ge(P.st.h, tgt)
                e.wait_ge(P.sg.h, sgn)
                e.activation(xo[:, :], pb[:, 0:TCH], AF.Silu, scale=QS).then_inc(P.ss.h, 1)
            tv = P.sv.bump(1)
            @P.on("vector")
            def _(e, ot=ot, tsi=tsi, ts2=ts2):
                e.wait_ge(P.ss.h, tsi)
                e.wait_ge(P.sd.h, ts2)
                e.tensor_tensor(xo[:, :], xo[:, :], sts[:, :], ALU.mult)
                e.tensor_tensor(xo[:, :], xo[:, :], xts[:, :], ALU.add).then_inc(P.sv.h, 1)
            P.store(x_out[ot, :, off:off + TCH], xo[:, :], wait=[(P.sv, tv)])
            dfree[ot % 2] = (P.ss, tsi)
    return P.finalize()



def _fm(x):
    xt = np.ascontiguousarray(x.transpose(2, 0, 1).reshape(D, NT))
    return np.ascontiguousarray(xt.reshape(DT, 128, NT))


def _unfm(xf):
    xd = np.asarray(xf).reshape(D, BL, S)
    return np.ascontiguousarray(xd.transpose(1, 2, 0))


def _wT_tiles(w):
    wt = np.ascontiguousarray(np.transpose(np.asarray(w)))
    return np.ascontiguousarray(wt.reshape(wt.shape[0] // 128, 128, wt.shape[1]))


def _colsT(*vs):
    mats = [np.ascontiguousarray(np.asarray(v).reshape(DT, 128).T) for v in vs]
    return np.ascontiguousarray(np.concatenate(mats, axis=1))


_CACHE = {}


def _run(build_fn, name, in_maps):
    if name not in _CACHE:
        _CACHE[name] = build_fn()
    nc = _CACHE[name]
    try:
        res = run_bass_kernel_spmd(nc, in_maps, core_ids=list(range(NCORES)))
        _ = [np.asarray(next(iter(r.values()))) for r in res.results]
    except Exception:
        res = run_bass_kernel_spmd(nc, in_maps, core_ids=list(range(NCORES)))
    return res.results


CONSTS = {
    "c_ones": np.ones((128, 1), np.float32),
    "c_ones1": np.ones((1, 128), np.float32),
    "c_misc": np.array([[EPS, 0, 0, 0, 0, 0, 0, 0]], np.float32),
}


def prep_l2(ins):
    l2w = {
        "l2_wvrT": np.ascontiguousarray(
            np.concatenate([np.transpose(np.asarray(ins["L2_attn_Wv"])),
                            np.transpose(np.asarray(ins["L2_attn_Wr"]))], axis=1
                           ).reshape(DT, 128, 2048)),
        "l2_woT": _wT_tiles(ins["L2_attn_Wo"]),
        "l2_mixc": _colsT(ins["L2_attn_time_mix_v"], ins["L2_attn_time_mix_r"], ins["L2_attn_time_decay"]),
    }
    l2f = {
        "l2_wg": _wT_tiles(ins["L2_ffn_w_gate"]),
        "l2_wu": _wT_tiles(ins["L2_ffn_w_up"]),
        "l2_wd": _wT_tiles(ins["L2_ffn_w_down"]),
    }
    return l2w, l2f


def run_l2(xcore, l2w, l2f):
    in_maps = [{"x_in": xcore[c], **l2w, **CONSTS} for c in range(NCORES)]
    ra = _run(build_l2a, "l2a", in_maps)
    in_maps = [{"x_in": xcore[c], "xn_s": ra[c]["xn_s"], "s_s": ra[c]["s_s"], **l2f} for c in range(NCORES)]
    rb = _run(build_l2b, "l2b", in_maps)
    return [rb[c]["x_out"] for c in range(NCORES)]


def kernel(**inputs):
    ins = {k: np.asarray(v) for k, v in inputs.items()}
    x0 = ins["L0_attn_x"]
    xcore = [_fm(np.ascontiguousarray(x0[c * BL:(c + 1) * BL])) for c in range(NCORES)]
    l0a, l0f = prep_l0(ins)
    xcore = run_l0(xcore, l0a, l0f)
    l1a, l1s, l1f = prep_l1(ins)
    xcore = run_l1(xcore, l1a, l1s, l1f)
    l2w, l2f = prep_l2(ins)
    xcore = run_l2(xcore, l2w, l2f)
    l3a, l3b1, l3b2 = prep_l3(ins)
    xcore = run_l3(xcore, l3a, l3b1, l3b2)
    out = np.empty((B, S, D), np.float32)
    for c in range(NCORES):
        out[c * BL:(c + 1) * BL] = _unfm(xcore[c])
    return out



def build_l3a1():
    P = Prog("l3a1")
    x_in = P.dram("x_in", [DT, 128, NT], F32, "ExternalInput")
    waT = P.dram("l3_waT", [DT, 128, 3072], F32, "ExternalInput")
    ba = P.dram("l3_ba", [1, 3072], F32, "ExternalInput")
    rope = P.dram("c_rope", [128, S], F32, "ExternalInput")
    ones = P.dram("c_ones", [128, 1], F32, "ExternalInput")
    ones1 = P.dram("c_ones1", [1, 128], F32, "ExternalInput")
    onesr = P.dram("c_onesr", [1, 512], F32, "ExternalInput")
    misc = P.dram("c_misc", [1, 8], F32, "ExternalInput")
    xn_s = P.dram("xn_s", [DT, 128, NT], BF16, "ExternalOutput")
    K_s = P.dram("K_s", [DT, 128, NT], BF16, "ExternalOutput")
    Q_s = P.dram("Q_s", [DT, 128, NT], BF16, "ExternalOutput")
    VT_s = P.dram("VT_s", [BL, 16, 128, 16 * 65], BF16, "ExternalOutput")

    ones_f = P.sbuf("ones_f", [128, 1], F32)
    ones1_f = P.sbuf("ones1_f", [1, 128], F32)
    onesr_f = P.sbuf("onesr_f", [1, 512], F32)
    P.c_eps = P.sbuf("c_eps", [1, 8], F32)
    ropeb = P.sbuf("ropeb", [128, S], BF16)
    bar = P.sbuf("bar", [1, 3072], F32)
    P.load(ones_f[:, :], ones[:, :])
    P.load(ones1_f[:, :], ones1[:, :])
    P.load(onesr_f[:, :], onesr[:, :])
    P.load(P.c_eps[0:1, :], misc[0:1, :])
    P.load(bar[0:1, :], ba[0:1, :])
    wscr = P.sbuf("wscr", [128, 3072], F32)
    waTb = [P.sbuf(f"waTb{k}", [128, 3072], BF16) for k in range(DT)]
    def wload(dst, src_ap, ncol):
        tw = P.load(wscr[:, 0:ncol], src_ap, wait=[(P.sv, None)])
        tvv = P.sv.bump(1)
        @P.on("vector")
        def _(e, dst=dst, tw=tw, ncol=ncol):
            e.wait_ge(P.sd.h, tw)
            e.tensor_copy(dst[:, 0:ncol], wscr[:, 0:ncol]).then_inc(P.sv.h, 1)
    wload(ropeb, rope[:, :], 2048)
    for k in range(DT):
        wload(waTb[k], waT[k], 3072)

    xt = [P.sbuf(f"xt{k}", [128, TCH], F32) for k in range(DT)]
    sq = [P.sbuf(f"sq{k}", [128, TCH], F32) for k in range(DT)]
    xnb = [P.sbuf(f"xnb{k}", [128, TCH], BF16) for k in range(DT)]
    rows = (P.sbuf("r_mean", [1, TCH], F32), P.sbuf("r_var", [1, TCH], F32), P.sbuf("r_tmp", [1, TCH], F32))
    qkt = [P.sbuf(f"qkt{k}", [128, TCH], BF16) for k in range(2)]
    VTe = [P.sbuf(f"VTec{i}", [128, 16 * 65], BF16) for i in range(4)]
    PST = P.psum("PST", [128, TCH])
    PBC = P.psum("PBC", [128, TCH])
    PM0 = P.psum("PM0", [128, TCH])
    PM1 = P.psum("PM1", [128, TCH])

    for b in range(BL):
        for ciq in range(S // TCH):
            off = b * S + ciq * TCH
            qoff = ciq * TCH
            tld = None
            for k in range(DT):
                tld = P.load(xt[k][:, :], x_in[k, :, off:off + TCH],
                             wait=[(P.st, None), (P.sv, None)] if k == 0 else ())
            P.barrier_engine("tensor", [(P.sd, tld), (P.sv, None)])
            P.barrier_engine("scalar", [(P.sd, tld)])
            P.barrier_engine("vector", [(P.sd, tld), (P.sg, None), (P.st, None)])
            t_xn = emit_ln_chunk(P, xt, xnb, sq, rows, ones_f, ones1_f, PST, PBC, TCH)
            for k in range(DT):
                P.store(xn_s[k, :, off:off + TCH], xnb[k][:, :], wait=[(P.sv, t_xn)] if k == 0 else ())
            P.barrier_engine("tensor", [(P.sv, t_xn)])
            for (dstD, fbase) in ((Q_s, 0), (K_s, 1024)):
                for ot in range(DT):
                    pb = [PM0, PM1][ot % 2]
                    buf = qkt[ot % 2]
                    svn = P.sv.n
                    sgn = P.sg.n
                    tgt = P.st.bump(1)
                    @P.on("tensor")
                    def _(e, pb=pb, ot=ot, fbase=fbase, svn=svn):
                        e.wait_ge(P.sv.h, svn)
                        e.matmul(pb[:, 0:TCH], bar[0:1, fbase + ot * 128:fbase + (ot + 1) * 128], onesr_f[0:1, 0:TCH], start=True, stop=False)
                        for k in range(DT):
                            r = e.matmul(pb[:, 0:TCH], waTb[k][:, fbase + ot * 128:fbase + (ot + 1) * 128], xnb[k][:, 0:TCH], start=False, stop=(k == DT - 1))
                        r.then_inc(P.st.h, 1)
                    tvx = P.sv.bump(1)
                    @P.on("vector")
                    def _(e, pb=pb, buf=buf, qoff=qoff, tgt=tgt, sgn=sgn):
                        e.wait_ge(P.st.h, tgt)
                        e.wait_ge(P.sg.h, sgn)
                        e.tensor_copy(buf[:, :], pb[:, 0:TCH])
                        e.tensor_tensor(buf[:, :], buf[:, :], ropeb[:, qoff:qoff + TCH], ALU.mult).then_inc(P.sv.h, 1)
                    P.store(dstD[ot, :, off:off + TCH], buf[:, :], wait=[(P.sv, tvx)])
            for sub in range(TCH // 128):
                kt = (qoff + sub * 128) // 128
                vbuf = VTe[sub]
                sgn = P.sg.n
                tvm = P.sv.bump(1)
                @P.on("vector")
                def _(e, vbuf=vbuf, sgn=sgn):
                    e.wait_ge(P.sg.h, sgn)
                    e.memset(vbuf.rearrange("p (h c) -> p h c", c=65)[:, :, 64:65], 1.0).then_inc(P.sv.h, 1)
                for fc in range(2):
                    pb = [PM0, PM1][fc % 2]
                    svn = P.sv.n
                    tgt = P.st.bump(1)
                    @P.on("tensor")
                    def _(e, pb=pb, sub=sub, fc=fc, svn=svn):
                        e.wait_ge(P.sv.h, svn)
                        e.matmul(pb[:, 0:TCH], ones1_f[0:1, :], bar[0:1, 2048 + fc * 512:2048 + (fc + 1) * 512], start=True, stop=False)
                        for k in range(DT):
                            r = e.matmul(pb[:, 0:TCH], xnb[k][:, sub * 128:(sub + 1) * 128], waTb[k][:, 2048 + fc * 512:2048 + (fc + 1) * 512], start=False, stop=(k == DT - 1))
                        r.then_inc(P.st.h, 1)
                    tvv = P.sv.bump(1)
                    @P.on("vector")
                    def _(e, pb=pb, vbuf=vbuf, fc=fc, tgt=tgt):
                        e.wait_ge(P.st.h, tgt)
                        ve = vbuf.rearrange("p (h c) -> p h c", c=65)
                        e.tensor_copy(ve[:, fc * 8:(fc + 1) * 8, 0:64], pb[:, 0:TCH].rearrange("p (h c) -> p h c", c=64)).then_inc(P.sv.h, 1)
                P.store(VT_s[b, kt, :, :], vbuf[:, :], wait=[(P.sv, None)])
    return P.finalize()


def build_l3a2():
    P = Prog("l3a2")
    K_sD = P.dram("K_s", [DT, 128, NT], BF16, "ExternalInput")
    Q_sD = P.dram("Q_s", [DT, 128, NT], BF16, "ExternalInput")
    VT_sD = P.dram("VT_s", [BL, 16, 128, 16 * 65], BF16, "ExternalInput")
    wpT = P.dram("l3_wpTh", [16, 64, D], F32, "ExternalInput")
    bp = P.dram("l3_bp", [1, D], F32, "ExternalInput")
    maskt = P.dram("c_mask", [128, 1536], F32, "ExternalInput")
    ones1 = P.dram("c_ones1", [1, 128], F32, "ExternalInput")
    onesr = P.dram("c_onesr", [1, 512], F32, "ExternalInput")
    ccols = P.dram("c_cols", [128, 4], F32, "ExternalInput")
    s_s = P.dram("s_s", [DT, 128, NT], F32, "ExternalOutput")

    ones1_f = P.sbuf("ones1_f", [1, 128], F32)
    onesr_f = P.sbuf("onesr_f", [1, 512], F32)
    ccol_f = P.sbuf("ccol_f", [128, 4], F32)
    maskb = P.sbuf("maskb", [128, 1536], BF16)
    bpr = P.sbuf("bpr", [1, D], F32)
    P.load(ones1_f[:, :], ones1[:, :])
    P.load(onesr_f[:, :], onesr[:, :])
    P.load(ccol_f[:, :], ccols[:, :])
    P.load(bpr[0:1, :], bp[0:1, :])
    wscr = P.sbuf("wscr", [128, 1536], F32)
    wpTb = [P.sbuf(f"wpTb{k}", [64, D], BF16) for k in range(16)]
    def wload(dst, src_ap, ncol, npart=128):
        tw = P.load(wscr[0:npart, 0:ncol], src_ap, wait=[(P.sv, None)])
        tvv = P.sv.bump(1)
        @P.on("vector")
        def _(e, dst=dst, tw=tw, ncol=ncol, npart=npart):
            e.wait_ge(P.sd.h, tw)
            e.tensor_copy(dst[0:npart, 0:ncol], wscr[0:npart, 0:ncol]).then_inc(P.sv.h, 1)
    wload(maskb, maskt[:, :], 1536)
    for k in range(16):
        wload(wpTb[k], wpT[k], D, npart=64)

    Kb = [P.sbuf(f"Kb{k}", [128, S], BF16) for k in range(DT)]
    Qb = [P.sbuf(f"Qb{k}", [128, S], BF16) for k in range(DT)]
    VTe = [P.sbuf(f"VTe{k}", [128, 16 * 65], BF16) for k in range(16)]
    eS = [P.sbuf(f"eS{i}", [128, TCH], BF16) for i in range(2)]
    Yh = [P.sbuf(f"Yh{h}", [64, TCH], BF16) for h in range(16)]
    rrow = P.sbuf("rrow", [1, TCH], F32)
    sot = P.sbuf("sot", [128, TCH], F32)
    PBC = P.psum("PBC", [128, TCH])
    PM0 = P.psum("PM0", [128, TCH])
    PM1 = P.psum("PM1", [128, TCH])
    PAV = P.psum("PAV", [128, TCH])

    for b in range(BL):
        tld = None
        stn = P.st.n
        for k in range(DT):
            tld = P.load(Kb[k][:, :], K_sD[k, :, b * S:(b + 1) * S], wait=[(P.st, stn), (P.sv, None)] if k == 0 else ())
            tld = P.load(Qb[k][:, :], Q_sD[k, :, b * S:(b + 1) * S])
        for kt in range(16):
            tld = P.load(VTe[kt][:, :], VT_sD[b, kt, :, :])
        P.barrier_engine("tensor", [(P.sd, tld)])
        for ciq in range(S // TCH):
            qoff = ciq * TCH
            off = b * S + qoff
            nkt = qoff // 128 + 4
            for h in range(16):
                hh = h % 2
                ht = h // 2
                esfree = [None, None]
                pingfree = [None, None]
                for kt in range(nkt):
                    pb = [PM0, PM1][kt % 2]
                    ebuf = eS[kt % 2]
                    pf = pingfree[kt % 2]
                    ef = esfree[kt % 2]
                    tgt = P.st.bump(1)
                    @P.on("tensor")
                    def _(e, pb=pb, kt=kt, hh=hh, ht=ht, qoff=qoff, pf=pf):
                        if pf is not None:
                            e.wait_ge(pf[0].h, pf[1])
                        e.matmul(pb[:, 0:TCH], Kb[ht][hh * 64:hh * 64 + 64, kt * 128:(kt + 1) * 128], Qb[ht][hh * 64:hh * 64 + 64, qoff:qoff + TCH], start=True, stop=True).then_inc(P.st.h, 1)
                    tse = P.ss.bump(1)
                    @P.on("scalar")
                    def _(e, pb=pb, ebuf=ebuf, tgt=tgt, ef=ef):
                        e.wait_ge(P.st.h, tgt)
                        if ef is not None:
                            e.wait_ge(ef[0].h, ef[1])
                        e.activation(ebuf[:, :], pb[:, 0:TCH], AF.Exp, scale=0.125, bias=ccol_f[:, 0:1]).then_inc(P.ss.h, 1)
                    pingfree[kt % 2] = (P.ss, tse)
                    t_ready = (P.ss, tse)
                    offd = qoff - kt * 128
                    if offd < 512:
                        tvm = P.sv.bump(1)
                        @P.on("vector")
                        def _(e, ebuf=ebuf, offd=offd, tse=tse):
                            e.wait_ge(P.ss.h, tse)
                            e.tensor_tensor(ebuf[:, :], ebuf[:, :], maskb[:, 512 + offd:1024 + offd], ALU.mult).then_inc(P.sv.h, 1)
                        t_ready = (P.sv, tvm)
                    tav = P.st.bump(1)
                    @P.on("tensor")
                    def _(e, kt=kt, h=h, ebuf=ebuf, t_ready=t_ready, nkt=nkt):
                        e.wait_ge(t_ready[0].h, t_ready[1])
                        e.matmul(PAV[0:65, 0:TCH], VTe[kt][:, 65 * h:65 * h + 65], ebuf[:, 0:TCH], start=(kt == 0), stop=(kt == nkt - 1)).then_inc(P.st.h, 1)
                    esfree[kt % 2] = (P.st, tav)
                t_fin = P.st.n
                tvr = P.sv.bump(1)
                @P.on("vector")
                def _(e, t_fin=t_fin):
                    e.wait_ge(P.st.h, t_fin)
                    e.reciprocal(rrow[0:1, :], PAV[64:65, 0:TCH]).then_inc(P.sv.h, 1)
                tbc = P.st.bump(1)
                @P.on("tensor")
                def _(e, tvr=tvr):
                    e.wait_ge(P.sv.h, tvr)
                    e.matmul(PBC[:, 0:TCH], ones1_f[0:1, :], rrow[0:1, 0:TCH], start=True, stop=True).then_inc(P.st.h, 1)
                tsy = P.ss.bump(1)
                @P.on("scalar")
                def _(e, h=h, tbc=tbc):
                    e.wait_ge(P.st.h, tbc)
                    e.activation(Yh[h][:, :], PAV[0:64, 0:TCH], AF.Copy).then_inc(P.ss.h, 1)
                tvy = P.sv.bump(1)
                @P.on("vector")
                def _(e, h=h, tsy=tsy):
                    e.wait_ge(P.ss.h, tsy)
                    e.tensor_tensor(Yh[h][:, :], Yh[h][:, :], PBC[0:64, 0:TCH], ALU.mult).then_inc(P.sv.h, 1)
                P.barrier_engine("tensor", [(P.ss, tsy), (P.sv, tvy)])
            dfree = [None, None]
            P.barrier_engine("tensor", [(P.sv, None)])
            for ot in range(DT):
                pb = [PM0, PM1][ot % 2]
                df = dfree[ot % 2]
                tgt = P.st.bump(1)
                @P.on("tensor")
                def _(e, pb=pb, ot=ot, df=df):
                    if df is not None:
                        e.wait_ge(df[0].h, df[1])
                    e.matmul(pb[:, 0:TCH], bpr[0:1, ot * 128:(ot + 1) * 128], onesr_f[0:1, 0:TCH], start=True, stop=False)
                    for h in range(16):
                        r = e.matmul(pb[:, 0:TCH], wpTb[h][:, ot * 128:(ot + 1) * 128], Yh[h][:, 0:TCH], start=False, stop=(h == 15))
                    r.then_inc(P.st.h, 1)
                tvs = P.sv.bump(1)
                @P.on("vector")
                def _(e, pb=pb, tgt=tgt):
                    e.wait_ge(P.st.h, tgt)
                    e.tensor_copy(sot[:, :], pb[:, 0:TCH]).then_inc(P.sv.h, 1)
                tst = P.store(s_s[ot, :, off:off + TCH], sot[:, :], wait=[(P.sv, tvs)])
                dfree[ot % 2] = (P.sv, tvs)
                P.barrier_engine("vector", [(P.sg, tst)])
    return P.finalize()


def build_l3b1():
    P = Prog("l3b1")
    xn_s = P.dram("xn_s", [DT, 128, NT], BF16, "ExternalInput")
    rwT = P.dram("l3_rwT", [DT, 128, 8], F32, "ExternalInput")
    egD = P.dram("l3_eg", [8, DT, 128, 2048], F32, "ExternalInput")
    euD = P.dram("l3_eu", [8, DT, 128, 2048], F32, "ExternalInput")
    edD = P.dram("l3_ed", [8, 16, 128, 1024], F32, "ExternalInput")
    ones1 = P.dram("c_ones1", [1, 128], F32, "ExternalInput")
    iden = P.dram("c_iden", [128, 128], F32, "ExternalInput")
    moe_s = P.dram("moe_s", [DT, 128, NT], BF16, "ExternalOutput")
    xs_s = P.dram("xs_s", [DT, 128, NT], BF16, "Internal")
    rows_s = P.dram("rows_s", [9, NT], F32, "Internal")

    ones1_f = P.sbuf("ones1_f", [1, 128], F32)
    iden_f = P.sbuf("iden_f", [128, 128], F32)
    P.load(ones1_f[:, :], ones1[:, :])
    P.load(iden_f[:, :], iden[:, :])
    rwb = P.sbuf("rwb", [128, DT * 8], BF16)
    rww = P.sbuf("rww", [128, DT * 8], F32)
    t_rw = None
    for k in range(DT):
        t_rw = P.load(rww[:, 8 * k:8 * k + 8], rwT[k])
    tvr = P.sv.bump(1)
    @P.on("vector")
    def _(e, t_rw=t_rw):
        e.wait_ge(P.sd.h, t_rw)
        e.tensor_copy(rwb[:, :], rww[:, :]).then_inc(P.sv.h, 1)

    xnb = [P.sbuf(f"xnb{k}", [128, TCH], BF16) for k in range(DT)]
    rowstage = P.sbuf("rowstage", [1, 9 * TCH], F32)
    rsl = P.sbuf("rsl", [1, TCH], F32)
    selsc = P.sbuf("selsc", [128, 8], F32)
    topc = P.sbuf("topc", [128, 1], F32)
    scc = P.sbuf("scc", [128, 1], F32)
    mbb = P.sbuf("mbb", [128, TCH], BF16)
    mbi = P.sbuf("mbi", [128, TCH], mybir.dt.uint8)
    sg = P.sbuf("sg_", [128, TCH], BF16)
    h1 = P.sbuf("h1_", [128, TCH], BF16)
    hb = [P.sbuf(f"hb{o}", [128, TCH], BF16) for o in range(16)]
    moet = [P.sbuf(f"moet{k}", [128, TCH], BF16) for k in range(DT)]
    wscr = P.sbuf("wscr", [128, 1024], F32)
    egb = [P.sbuf(f"egb{k}", [128, 2048], BF16) for k in range(DT)]
    eub = [P.sbuf(f"eub{k}", [128, 2048], BF16) for k in range(DT)]
    edb = [P.sbuf(f"edb{k}", [128, 1024], BF16) for k in range(16)]
    PRT = P.psum("PRT", [128, TCH])
    PSC = P.psum("PSC", [128, TCH])
    PG = [P.psum(f"PG{i}", [128, TCH]) for i in range(2)]
    PU = [P.psum(f"PU{i}", [128, TCH]) for i in range(2)]
    PD = [P.psum(f"PD{i}", [128, TCH]) for i in range(2)]

    tvz = P.sv.bump(1)
    @P.on("vector")
    def _(e):
        for k in range(DT):
            r = e.memset(moet[k][:, :], 0.0)
        r.then_inc(P.sv.h, 1)
    for ci in range(NCH):
        for k in range(DT):
            P.store(moe_s[k, :, ci * TCH:(ci + 1) * TCH], moet[k][:, :],
                    wait=[(P.sv, tvz)] if (ci == 0 and k == 0) else ())

    for ci in range(NCH):
        off = ci * TCH
        tld = None
        for k in range(DT):
            tld = P.load(xnb[k][:, :], xn_s[k, :, off:off + TCH],
                         wait=[(P.st, None)] if k == 0 else ())
        P.barrier_engine("tensor", [(P.sd, tld), (P.sv, None)])
        for sub in range(TCH // 128):
            tgt = P.st.bump(1)
            @P.on("tensor")
            def _(e, sub=sub):
                for k in range(DT):
                    r = e.matmul(PRT[:, 0:8], xnb[k][:, sub * 128:(sub + 1) * 128], rwb[:, 8 * k:8 * k + 8], start=(k == 0), stop=(k == DT - 1))
                r.then_inc(P.st.h, 1)
            tsel = P.sv.bump(1)
            @P.on("vector")
            def _(e, tgt=tgt):
                e.wait_ge(P.st.h, tgt)
                e.tensor_reduce(topc[:, 0:1], PRT[:, 0:8], AX.X, ALU.max)
                e.tensor_tensor(selsc[:, :], PRT[:, 0:8], topc[:, 0:1].broadcast_to([128, 8]), ALU.is_ge).then_inc(P.sv.h, 1)
            tss = P.ss.bump(1)
            @P.on("scalar")
            def _(e, tsel=tsel):
                e.wait_ge(P.sv.h, tsel)
                e.activation(scc[:, 0:1], topc[:, 0:1], AF.Sigmoid).then_inc(P.ss.h, 1)
            tsc = P.sv.bump(1)
            @P.on("vector")
            def _(e, tss=tss):
                e.wait_ge(P.ss.h, tss)
                e.tensor_single_scalar(selsc[:, :], selsc[:, :], scc[:, 0:1], ALU.mult).then_inc(P.sv.h, 1)
            for j in range(9):
                src = selsc[:, j:j + 1] if j < 8 else scc[:, 0:1]
                pcol = 128 + (j % 2) * 128
                svn2 = P.sv.n
                ttr = P.st.bump(1)
                @P.on("tensor")
                def _(e, src=src, pcol=pcol, svn2=svn2):
                    e.wait_ge(P.sv.h, svn2)
                    e.transpose(PRT[0:1, pcol:pcol + 128], src, iden_f[:, :]).then_inc(P.st.h, 1)
                tcp = P.sv.bump(1)
                @P.on("vector")
                def _(e, j=j, sub=sub, pcol=pcol, ttr=ttr):
                    e.wait_ge(P.st.h, ttr)
                    e.tensor_copy(rowstage[0:1, j * TCH + sub * 128:j * TCH + sub * 128 + 128], PRT[0:1, pcol:pcol + 128]).then_inc(P.sv.h, 1)
        svn = P.sv.n
        for j in range(9):
            P.store(rows_s[j:j + 1, off:off + TCH], rowstage[0:1, j * TCH:(j + 1) * TCH],
                    wait=[(P.sv, svn)] if j == 0 else ())
        tbc = P.st.bump(1)
        @P.on("tensor")
        def _(e, svn=svn):
            e.wait_ge(P.sv.h, svn)
            e.matmul(PSC[:, 0:TCH], ones1_f[0:1, :], rowstage[0:1, 8 * TCH:9 * TCH], start=True, stop=True).then_inc(P.st.h, 1)
        t_xs = None
        for k in range(DT):
            t_xs = P.sv.bump(1)
            @P.on("vector")
            def _(e, k=k, tbc=tbc):
                if k == 0:
                    e.wait_ge(P.st.h, tbc)
                e.tensor_tensor(xnb[k][:, :], xnb[k][:, :], PSC[:, 0:TCH], ALU.mult).then_inc(P.sv.h, 1)
            P.store(xs_s[k, :, off:off + TCH], xnb[k][:, :], wait=[(P.sv, t_xs)])

    for ex in range(8):
        def wload2(dst, src_ap, base, ncol):
            tw = P.load(wscr[:, 0:ncol], src_ap, wait=[(P.sv, None), (P.st, None)])
            tvv = P.sv.bump(1)
            @P.on("vector")
            def _(e, dst=dst, tw=tw, base=base, ncol=ncol):
                e.wait_ge(P.sd.h, tw)
                e.tensor_copy(dst[:, base:base + ncol], wscr[:, 0:ncol]).then_inc(P.sv.h, 1)
        for k in range(DT):
            wload2(egb[k], egD[ex, k, :, 0:1024], 0, 1024)
            wload2(egb[k], egD[ex, k, :, 1024:2048], 1024, 1024)
            wload2(eub[k], euD[ex, k, :, 0:1024], 0, 1024)
            wload2(eub[k], euD[ex, k, :, 1024:2048], 1024, 1024)
        for k in range(16):
            wload2(edb[k], edD[ex, k], 0, 1024)
        for ci in range(NCH):
            off = ci * TCH
            tld = None
            for k in range(DT):
                tld = P.load(xnb[k][:, :], xs_s[k, :, off:off + TCH],
                             wait=[(P.st, None), (P.sv, None)] if k == 0 else ())
            trl = P.load(rsl[0:1, :], rows_s[ex:ex + 1, off:off + TCH])
            tbc = P.st.bump(1)
            svn = P.sv.n
            @P.on("tensor")
            def _(e, svn=svn, trl=trl, tld=tld):
                e.wait_ge(P.sv.h, svn)
                e.wait_ge(P.sd.h, trl)
                e.matmul(PSC[:, 0:TCH], ones1_f[0:1, :], rsl[0:1, 0:TCH], start=True, stop=True).then_inc(P.st.h, 1)
            tmb = P.ss.bump(1)
            @P.on("scalar")
            def _(e, tbc=tbc):
                e.wait_ge(P.st.h, tbc)
                e.activation(mbb[:, :], PSC[:, 0:TCH], AF.Copy).then_inc(P.ss.h, 1)
            tmi = P.sv.bump(1)
            @P.on("vector")
            def _(e, tmb=tmb):
                e.wait_ge(P.ss.h, tmb)
                e.tensor_single_scalar(mbi[:, :], mbb[:, :], 0.0, ALU.is_gt).then_inc(P.sv.h, 1)
            gfree = [None, None]
            ufree = [None, None]
            for ot in range(16):
                pg = PG[ot % 2]
                pu = PU[ot % 2]
                gf, uf = gfree[ot % 2], ufree[ot % 2]
                tgt = P.st.bump(1)
                @P.on("tensor")
                def _(e, ot=ot, pg=pg, pu=pu, gf=gf, uf=uf):
                    if gf is not None:
                        e.wait_ge(gf[0].h, gf[1])
                    if uf is not None:
                        e.wait_ge(uf[0].h, uf[1])
                    for k in range(DT):
                        e.matmul(pg[:, 0:TCH], egb[k][:, ot * 128:(ot + 1) * 128], xnb[k][:, 0:TCH], start=(k == 0), stop=(k == DT - 1))
                    for k in range(DT):
                        r = e.matmul(pu[:, 0:TCH], eub[k][:, ot * 128:(ot + 1) * 128], xnb[k][:, 0:TCH], start=(k == 0), stop=(k == DT - 1))
                    r.then_inc(P.st.h, 1)
                tsg = P.ss.bump(1)
                @P.on("scalar")
                def _(e, pg=pg, tgt=tgt):
                    e.wait_ge(P.st.h, tgt)
                    e.activation(sg[:, :], pg[:, 0:TCH], AF.Silu).then_inc(P.ss.h, 1)
                tvh = P.sv.bump(1)
                @P.on("vector")
                def _(e, ot=ot, pu=pu, tsg=tsg, tgt=tgt, tmb=tmb):
                    e.wait_ge(P.ss.h, tsg)
                    e.wait_ge(P.st.h, tgt)
                    if ot == 0:
                        e.wait_ge(P.ss.h, tmb)
                    e.tensor_tensor(h1[:, :], pu[:, 0:TCH], sg[:, :], ALU.mult)
                    e.tensor_tensor(hb[ot][:, :], h1[:, :], mbb[:, :], ALU.mult).then_inc(P.sv.h, 1)
                gfree[ot % 2] = (P.ss, tsg)
                ufree[ot % 2] = (P.sv, tvh)
            P.barrier_engine("tensor", [(P.sv, None)])
            sgn_pre = P.sg.n
            tmo = None
            for k in range(DT):
                tmo = P.load(moet[k][:, :], moe_s[k, :, off:off + TCH],
                             wait=[(P.sg, sgn_pre), (P.sv, None)] if k == 0 else ())
            dfree = [None, None]
            t_sel = None
            for ot in range(DT):
                pd = PD[ot % 2]
                df = dfree[ot % 2]
                tgt = P.st.bump(1)
                @P.on("tensor")
                def _(e, ot=ot, pd=pd, df=df):
                    if df is not None:
                        e.wait_ge(df[0].h, df[1])
                    for k in range(16):
                        r = e.matmul(pd[:, 0:TCH], edb[k][:, ot * 128:(ot + 1) * 128], hb[k][:, 0:TCH], start=(k == 0), stop=(k == 15))
                    r.then_inc(P.st.h, 1)
                t_sel = P.sv.bump(1)
                @P.on("vector")
                def _(e, ot=ot, pd=pd, tgt=tgt, tmo=tmo):
                    e.wait_ge(P.st.h, tgt)
                    if ot == 0:
                        e.wait_ge(P.sd.h, tmo)
                    e.copy_predicated(moet[ot][:, :], mbi[:, :], pd[:, 0:TCH]).then_inc(P.sv.h, 1)
                dfree[ot % 2] = (P.sv, t_sel)
            for k in range(DT):
                P.store(moe_s[k, :, off:off + TCH], moet[k][:, :],
                        wait=[(P.sv, t_sel)] if k == 0 else ())
    return P.finalize()


def build_l3b2():
    P = Prog("l3b2")
    x_in = P.dram("x_in", [DT, 128, NT], F32, "ExternalInput")
    xn_s = P.dram("xn_s", [DT, 128, NT], BF16, "ExternalInput")
    s_s = P.dram("s_s", [DT, 128, NT], F32, "ExternalInput")
    moe_s = P.dram("moe_s", [DT, 128, NT], BF16, "ExternalInput")
    sgD = P.dram("l3_sg", [DT, 128, 2048], F32, "ExternalInput")
    suD = P.dram("l3_su", [DT, 128, 2048], F32, "ExternalInput")
    sdD = P.dram("l3_sd", [16, 128, 1024], F32, "ExternalInput")
    x_out = P.dram("x_out", [DT, 128, NT], F32, "ExternalOutput")

    wscr = P.sbuf("wscr", [128, 2048], F32)
    sgb = [P.sbuf(f"sgb{k}", [128, 2048], BF16) for k in range(DT)]
    sub = [P.sbuf(f"sub{k}", [128, 2048], BF16) for k in range(DT)]
    sdb = [P.sbuf(f"sdb{k}", [128, 1024], BF16) for k in range(16)]
    def wload(dst, src_ap, ncol):
        tw = P.load(wscr[:, 0:ncol], src_ap, wait=[(P.sv, None)])
        tvv = P.sv.bump(1)
        @P.on("vector")
        def _(e, dst=dst, tw=tw, ncol=ncol):
            e.wait_ge(P.sd.h, tw)
            e.tensor_copy(dst[:, 0:ncol], wscr[:, 0:ncol]).then_inc(P.sv.h, 1)
    for k in range(DT):
        wload(sgb[k], sgD[k], 2048)
        wload(sub[k], suD[k], 2048)
    for k in range(16):
        wload(sdb[k], sdD[k], 1024)

    xnb = [P.sbuf(f"xnb{k}", [128, TCH], BF16) for k in range(DT)]
    hb = [P.sbuf(f"hb{o}", [128, TCH], BF16) for o in range(16)]
    sgs = P.sbuf("sgs", [128, TCH], BF16)
    moet = P.sbuf("moet", [128, TCH], BF16)
    xts = P.sbuf("xts", [128, TCH], F32)
    sts = P.sbuf("sts", [128, TCH], F32)
    xo = P.sbuf("xo", [128, TCH], F32)
    PG = [P.psum(f"PG{i}", [128, TCH]) for i in range(2)]
    PU = [P.psum(f"PU{i}", [128, TCH]) for i in range(2)]
    PD = [P.psum(f"PD{i}", [128, TCH]) for i in range(2)]

    for ci in range(NCH):
        off = ci * TCH
        tld = None
        for k in range(DT):
            tld = P.load(xnb[k][:, :], xn_s[k, :, off:off + TCH],
                         wait=[(P.st, None), (P.sv, None)] if k == 0 else ())
        P.barrier_engine("tensor", [(P.sd, tld), (P.sv, None)])
        P.barrier_engine("scalar", [(P.st, None)])
        gfree = [None, None]
        ufree = [None, None]
        for ot in range(16):
            pg = PG[ot % 2]
            pu = PU[ot % 2]
            gf, uf = gfree[ot % 2], ufree[ot % 2]
            tgt = P.st.bump(1)
            @P.on("tensor")
            def _(e, ot=ot, pg=pg, pu=pu, gf=gf, uf=uf):
                if gf is not None:
                    e.wait_ge(gf[0].h, gf[1])
                if uf is not None:
                    e.wait_ge(uf[0].h, uf[1])
                for k in range(DT):
                    e.matmul(pg[:, 0:TCH], sgb[k][:, ot * 128:(ot + 1) * 128], xnb[k][:, 0:TCH], start=(k == 0), stop=(k == DT - 1))
                for k in range(DT):
                    r = e.matmul(pu[:, 0:TCH], sub[k][:, ot * 128:(ot + 1) * 128], xnb[k][:, 0:TCH], start=(k == 0), stop=(k == DT - 1))
                r.then_inc(P.st.h, 1)
            tsg = P.ss.bump(1)
            @P.on("scalar")
            def _(e, pg=pg, tgt=tgt):
                e.wait_ge(P.st.h, tgt)
                e.activation(sgs[:, :], pg[:, 0:TCH], AF.Silu).then_inc(P.ss.h, 1)
            tvh = P.sv.bump(1)
            @P.on("vector")
            def _(e, ot=ot, pu=pu, tsg=tsg, tgt=tgt):
                e.wait_ge(P.ss.h, tsg)
                e.wait_ge(P.st.h, tgt)
                e.tensor_tensor(hb[ot][:, :], pu[:, 0:TCH], sgs[:, :], ALU.mult).then_inc(P.sv.h, 1)
            gfree[ot % 2] = (P.ss, tsg)
            ufree[ot % 2] = (P.sv, tvh)
        P.barrier_engine("tensor", [(P.sv, None)])
        dfree = [None, None]
        for ot in range(DT):
            pd = PD[ot % 2]
            df = dfree[ot % 2]
            tgt = P.st.bump(1)
            @P.on("tensor")
            def _(e, ot=ot, pd=pd, df=df):
                if df is not None:
                    e.wait_ge(df[0].h, df[1])
                for k in range(16):
                    r = e.matmul(pd[:, 0:TCH], sdb[k][:, ot * 128:(ot + 1) * 128], hb[k][:, 0:TCH], start=(k == 0), stop=(k == 15))
                r.then_inc(P.st.h, 1)
            tmo = P.load(moet[:, :], moe_s[ot, :, off:off + TCH], wait=[(P.sv, None)])
            tx = P.load(xts[:, :], x_in[ot, :, off:off + TCH])
            ts2 = P.load(sts[:, :], s_s[ot, :, off:off + TCH])
            tvg = P.sv.bump(1)
            sgn = P.sg.n
            @P.on("vector")
            def _(e, pd=pd, tgt=tgt, ts2=ts2, sgn=sgn):
                e.wait_ge(P.st.h, tgt)
                e.wait_ge(P.sd.h, ts2)
                e.wait_ge(P.sg.h, sgn)
                e.tensor_tensor(xo[:, :], pd[:, 0:TCH], moet[:, :], ALU.add).then_inc(P.sv.h, 1)
            tsi = P.ss.bump(1)
            @P.on("scalar")
            def _(e, tvg=tvg):
                e.wait_ge(P.sv.h, tvg)
                e.activation(xo[:, :], xo[:, :], AF.Silu).then_inc(P.ss.h, 1)
            tvc = P.sv.bump(1)
            @P.on("vector")
            def _(e, tsi=tsi):
                e.wait_ge(P.ss.h, tsi)
                e.tensor_tensor(xo[:, :], xo[:, :], sts[:, :], ALU.mult)
                e.tensor_tensor(xo[:, :], xo[:, :], xts[:, :], ALU.add).then_inc(P.sv.h, 1)
            P.store(x_out[ot, :, off:off + TCH], xo[:, :], wait=[(P.sv, tvc)])
            dfree[ot % 2] = (P.sv, tvg)
    return P.finalize()


def _rope_table():
    d = 64
    half = np.arange(0, d // 2, dtype=np.float64)
    theta = 1.0 / (10000.0 ** (2.0 * half / d))
    seq = np.arange(1, S + 1, dtype=np.float64)
    ang = seq[:, None] * theta[None, :]
    ang = np.concatenate([ang, ang], axis=1)
    f = (np.cos(ang) + np.sin(ang)).astype(np.float32)
    return np.ascontiguousarray(np.tile(f.T, (2, 1)))


def _mask_table():
    p = np.arange(128)[:, None]
    i = np.arange(1536)[None, :]
    return (p <= i - 512).astype(np.float32)


CONSTS.update({
    "c_onesr": np.ones((1, 512), np.float32),
    "c_rope": _rope_table(),
    "c_mask": _mask_table(),
    "c_cols": np.concatenate([np.full((128, 1), -40.0, np.float32), np.ones((128, 1), np.float32), np.zeros((128, 2), np.float32)], axis=1),
    "c_iden": np.eye(128, dtype=np.float32),
})


def prep_l3(ins):
    l3a = {
        "l3_waT": _wT_tiles(ins["L3_attn_W_attn"]),
        "l3_ba": np.asarray(ins["L3_attn_b_attn"]).reshape(1, 3072),
        "l3_wpTh": np.ascontiguousarray(np.transpose(np.asarray(ins["L3_attn_W_proj"])).reshape(16, 64, D)),
        "l3_bp": np.asarray(ins["L3_attn_b_proj"]).reshape(1, D),
    }
    l3b1 = {
        "l3_rwT": _wT_tiles(ins["L3_ffn_router_w"]),
        "l3_eg": np.ascontiguousarray(np.asarray(ins["L3_ffn_expert_gate"]).reshape(8, DT, 128, 2048)),
        "l3_eu": np.ascontiguousarray(np.asarray(ins["L3_ffn_expert_up"]).reshape(8, DT, 128, 2048)),
        "l3_ed": np.ascontiguousarray(np.asarray(ins["L3_ffn_expert_down"]).reshape(8, 16, 128, 1024)),
    }
    l3b2 = {
        "l3_sg": np.ascontiguousarray(np.asarray(ins["L3_ffn_shared_gate"]).reshape(DT, 128, 2048)),
        "l3_su": np.ascontiguousarray(np.asarray(ins["L3_ffn_shared_up"]).reshape(DT, 128, 2048)),
        "l3_sd": np.ascontiguousarray(np.asarray(ins["L3_ffn_shared_down"]).reshape(16, 128, 1024)),
    }
    return l3a, l3b1, l3b2


def run_l3(xcore, l3a, l3b1, l3b2):
    in_maps = [{"x_in": xcore[c], **l3a, **CONSTS} for c in range(NCORES)]
    ra = _run(build_l3a1, "l3a1", in_maps)
    la2 = {k: l3a[k] for k in ("l3_wpTh", "l3_bp")}
    in_maps = [{"K_s": ra[c]["K_s"], "Q_s": ra[c]["Q_s"], "VT_s": ra[c]["VT_s"], **la2, **CONSTS} for c in range(NCORES)]
    ra2 = _run(build_l3a2, "l3a2", in_maps)
    in_maps = [{"xn_s": ra[c]["xn_s"], **l3b1, **CONSTS} for c in range(NCORES)]
    rb = _run(build_l3b1, "l3b1", in_maps)
    in_maps = [{"x_in": xcore[c], "xn_s": ra[c]["xn_s"], "s_s": ra2[c]["s_s"],
                "moe_s": rb[c]["moe_s"], **l3b2} for c in range(NCORES)]
    rc = _run(build_l3b2, "l3b2", in_maps)
    return [rc[c]["x_out"] for c in range(NCORES)]



L1CH = 128
L1SEG = 512
DCONV = 4


def build_l1a1():
    P = Prog("l1a1")
    xn_s = P.dram("xn_s", [DT, 128, NT], BF16, "ExternalInput")
    winT = P.dram("l1_winT", [DT, 128, 4384], F32, "ExternalInput")
    convw = P.dram("l1_convw", [18, 128, DCONV + 1], F32, "ExternalInput")
    iden = P.dram("c_iden", [128, 128], F32, "ExternalInput")
    z_s = P.dram("z_s", [BL, S // L1CH, 128, 2048], BF16, "ExternalOutput")
    cdt_s = P.dram("cdt_s", [BL, S // L1CH, 128, 2336], BF16, "ExternalOutput")
    xbc_s = P.dram("xbc_s", [19, 128, NT], BF16, "Internal")

    iden_f = P.sbuf("iden_f", [128, 128], F32)
    iden_b = P.sbuf("iden_b", [128, 128], BF16)
    P.load(iden_f[:, :], iden[:, :])
    wscr = P.sbuf("wscr", [128, 2192], F32)
    wb = [P.sbuf(f"wb{k}", [128, 4384], BF16) for k in range(DT)]
    cwb = P.sbuf("cwb", [128, 18 * (DCONV + 1)], F32)
    t_cw = None
    for k in range(18):
        t_cw = P.load(cwb[:, k * 5:(k + 1) * 5], convw[k])
    tvi = P.sv.bump(1)
    @P.on("vector")
    def _(e, t_cw=t_cw):
        e.wait_ge(P.sd.h, t_cw)
        e.tensor_copy(iden_b[:, :], iden_f[:, :]).then_inc(P.sv.h, 1)
    def wload(dst, base, src_ap, ncol):
        tw = P.load(wscr[:, 0:ncol], src_ap, wait=[(P.sv, None)])
        tvv = P.sv.bump(1)
        @P.on("vector")
        def _(e, dst=dst, base=base, tw=tw, ncol=ncol):
            e.wait_ge(P.sd.h, tw)
            e.tensor_copy(dst[:, base:base + ncol], wscr[:, 0:ncol]).then_inc(P.sv.h, 1)
    for k in range(DT):
        wload(wb[k], 0, winT[k][:, 0:2192], 2192)
        wload(wb[k], 2192, winT[k][:, 2192:4384], 2192)

    xnb = [P.sbuf(f"xnb{k}", [128, TCH], BF16) for k in range(DT)]
    obuf = [P.sbuf(f"obuf{i}", [128, TCH], BF16) for i in range(2)]
    PM0 = P.psum("PM0", [128, TCH])
    PM1 = P.psum("PM1", [128, TCH])

    for ci in range(NCH):
        off = ci * TCH
        b = off // S
        tld = None
        for k in range(DT):
            tld = P.load(xnb[k][:, :], xn_s[k, :, off:off + TCH],
                         wait=[(P.st, None), (P.sv, None)] if k == 0 else ())
        P.barrier_engine("tensor", [(P.sd, tld), (P.sv, None), (P.ss, None)])
        bfree = [None, None]
        for ot in range(35):
            nrow = 128 if ot < 34 else 32
            pb = [PM0, PM1][ot % 2]
            ob = obuf[ot % 2]
            bf = bfree[ot % 2]
            tgt = P.st.bump(1)
            @P.on("tensor")
            def _(e, ot=ot, pb=pb, nrow=nrow, bf=bf):
                if bf is not None:
                    e.wait_ge(bf[0].h, bf[1])
                for k in range(DT):
                    r = e.matmul(pb[0:nrow, 0:TCH], wb[k][:, ot * 128:ot * 128 + nrow], xnb[k][:, 0:TCH], start=(k == 0), stop=(k == DT - 1))
                r.then_inc(P.st.h, 1)
            sgn = P.sg.n
            tse = P.ss.bump(1)
            @P.on("scalar")
            def _(e, pb=pb, ob=ob, tgt=tgt, sgn=sgn, nrow=nrow, ot=ot):
                e.wait_ge(P.st.h, tgt)
                e.wait_ge(P.sg.h, sgn)
                if ot < 16:
                    e.activation(ob[:, :], pb[:, 0:TCH], AF.Silu).then_inc(P.ss.h, 1)
                else:
                    e.activation(ob[0:nrow, :], pb[0:nrow, 0:TCH], AF.Copy).then_inc(P.ss.h, 1)
            if ot < 16:
                for sc in range(TCH // L1CH):
                    cc = (off % S + sc * L1CH) // L1CH
                    P.store(z_s[b, cc, :, ot * 128:(ot + 1) * 128], ob[:, sc * L1CH:(sc + 1) * L1CH],
                            wait=[(P.ss, tse)] if sc == 0 else ())
            else:
                P.store(xbc_s[ot - 16, 0:nrow, off:off + TCH], ob[0:nrow, :], wait=[(P.ss, tse)])
            bfree[ot % 2] = (P.ss, tse)
    xh = P.sbuf("xh", [128, L1CH + 3], BF16)
    cva = P.sbuf("cva", [128, L1CH], F32)
    cvb = P.sbuf("cvb", [128, L1CH], BF16)
    ttb = P.sbuf("ttb", [128, L1CH], BF16)
    PMT = P.psum("PMT", [128, L1CH], BF16)
    P.barrier_engine("sync", [(P.sg, None)])
    for b in range(BL):
        for cc in range(S // L1CH):
            gcol = b * S + cc * L1CH
            for ft in range(19):
                nrow = 128 if ft < 18 else 32
                svn = P.sv.n
                ssn = P.ss.n
                stn = P.st.n
                if ft < 18:
                    if cc == 0:
                        tvh = P.sv.bump(1)
                        @P.on("vector")
                        def _(e, ssn=ssn, stn=stn):
                            e.wait_ge(P.ss.h, ssn)
                            e.wait_ge(P.st.h, stn)
                            e.memset(xh[:, 0:3], 0.0).then_inc(P.sv.h, 1)
                        tld = P.load(xh[:, 3:3 + L1CH], xbc_s[ft, :, gcol:gcol + L1CH], wait=[(P.sv, tvh)])
                    else:
                        tld = P.load(xh[:, 0:3 + L1CH], xbc_s[ft, :, gcol - 3:gcol + L1CH],
                                     wait=[(P.sv, svn), (P.ss, ssn), (P.st, stn)])
                    tcv = P.sv.bump(1)
                    @P.on("vector")
                    def _(e, ft=ft, tld=tld):
                        e.wait_ge(P.sd.h, tld)
                        e.tensor_single_scalar(cva[:, :], xh[:, 3:3 + L1CH], cwb[:, 5 * ft + 3:5 * ft + 4], ALU.mult)
                        for kk in range(3):
                            e.scalar_tensor_tensor(cva[:, :], xh[:, kk:kk + L1CH], cwb[:, 5 * ft + kk:5 * ft + kk + 1], cva[:, :], ALU.mult, ALU.add)
                        e.tensor_single_scalar(cvb[:, :], cva[:, :], cwb[:, 5 * ft + 4:5 * ft + 5], ALU.add).then_inc(P.sv.h, 1)
                    src = cvb
                else:
                    tld = P.load(xh[0:32, 3:3 + L1CH], xbc_s[18, 0:32, gcol:gcol + L1CH],
                                 wait=[(P.sv, svn), (P.ss, ssn), (P.st, stn)])
                    tcv = P.sv.bump(1)
                    @P.on("vector")
                    def _(e, tld=tld):
                        e.wait_ge(P.sd.h, tld)
                        e.tensor_copy(cvb[0:32, :], xh[0:32, 3:3 + L1CH]).then_inc(P.sv.h, 1)
                    src = cvb
                ssn3 = P.ss.n
                ttr = P.st.bump(1)
                @P.on("tensor")
                def _(e, src=src, tcv=tcv, nrow=nrow, ssn3=ssn3):
                    e.wait_ge(P.sv.h, tcv)
                    e.wait_ge(P.ss.h, ssn3)
                    e.transpose(PMT[:, 0:nrow], src[0:nrow, 0:L1CH], iden_b[0:nrow, 0:nrow]).then_inc(P.st.h, 1)
                sgn = P.sg.n
                tev = P.ss.bump(1)
                @P.on("scalar")
                def _(e, ttr=ttr, sgn=sgn, nrow=nrow):
                    e.wait_ge(P.st.h, ttr)
                    e.wait_ge(P.sg.h, sgn)
                    e.activation(ttb[:, 0:nrow], PMT[:, 0:nrow], AF.Copy).then_inc(P.ss.h, 1)
                P.store(cdt_s[b, cc, :, ft * 128:ft * 128 + nrow], ttb[:, 0:nrow], wait=[(P.ss, tev)])
    return P.finalize()


def build_l1scan():
    P = Prog("l1scan")
    wrT = P.dram("l1_wrT", [16, 128, 2336], F32, "ExternalInput")
    woutT = P.dram("l1_woutT", [16, 128, 1024], F32, "ExternalInput")
    cdtD = P.dram("cdt_c", [BL, 4, 128, 2336], BF16, "ExternalInput")
    zD = P.dram("z_c", [BL, 4, 128, 2048], BF16, "ExternalInput")
    S0i = P.dram("S0_i", [BL, 128, 2048], BF16, "ExternalInput")
    ypi = P.dram("yprev_i", [128, 32], BF16, "ExternalInput")
    hcolsD = P.dram("l1_hcols", [32, 4], F32, "ExternalInput")
    nwD = P.dram("l1_nw", [128, 16], F32, "ExternalInput")
    DexpD = P.dram("l1_Dexp", [128, 16], F32, "ExternalInput")
    iden = P.dram("c_iden", [128, 128], F32, "ExternalInput")
    inj = P.dram("c_inj", [128, 512], F32, "ExternalInput")
    eye2D = P.dram("c_eye2", [128, 2], F32, "ExternalInput")
    ones = P.dram("c_ones", [128, 1], F32, "ExternalInput")
    ones1 = P.dram("c_ones1", [1, 128], F32, "ExternalInput")
    ccols = P.dram("c_cols", [128, 4], F32, "ExternalInput")
    misc = P.dram("c_misc", [1, 8], F32, "ExternalInput")
    S0o = P.dram("S0_o", [BL, 128, 2048], BF16, "ExternalOutput")
    dbgD = P.dram("dbg_o", [128, 1024], F32, "ExternalOutput") if L1DBG else None
    ypo = P.dram("yprev_o", [128, 32], BF16, "ExternalOutput")
    o_s = P.dram("o_s", [DT, 128, BL * L1SEG], F32, "ExternalOutput")

    Wm = [P.sbuf(f"Wm{k}", [128, 2336], BF16) for k in range(16)]
    wout = [P.sbuf(f"wout{k}", [128, 1024], BF16) for k in range(16)]
    wscr = P.sbuf("wscr", [128, 2336], F32)
    iden_f = P.sbuf("iden_f", [128, 128], F32)
    iden_b = P.sbuf("iden_b", [128, 128], BF16)
    injb = P.sbuf("injb", [128, 512], BF16)
    eye2b = P.sbuf("eye2b", [128, 2], BF16)
    eye2f = P.sbuf("eye2f", [128, 2], F32)
    ones_f = P.sbuf("ones_f", [128, 1], F32)
    ones1_f = P.sbuf("ones1_f", [1, 128], F32)
    ccol_f = P.sbuf("ccol_f", [128, 4], F32)
    ceps = P.sbuf("ceps", [1, 8], F32)
    hcols = P.sbuf("hcols", [32, 8], F32)
    nw = P.sbuf("nw", [128, 16], F32)
    Dexp = P.sbuf("Dexp", [128, 16], F32)
    tmpD = P.sbuf("tmpD", [128, 32], F32)
    cdt = [P.sbuf(f"cdt{b}", [128, 2336], BF16) for b in range(BL)]
    zc = [P.sbuf(f"zc{b}", [128, 2048], BF16) for b in range(BL)]
    X_s = [P.sbuf(f"X_s{b}", [128, 2048], BF16) for b in range(BL)]
    S0 = [P.sbuf(f"S0{b}", [128, 2048], BF16) for b in range(BL)]
    XU = P.sbuf("XU", [128, 2048], BF16)
    B_m = [P.sbuf(f"B_m{b}", [128, 128], BF16) for b in range(BL)]
    BTsb = P.sbuf("BTsb", [128, 128], BF16)
    U_r = P.sbuf("U_r", [128, 64], F32)
    U_v = P.sbuf("U_v", [128, 64], F32)
    yfm = P.sbuf("yfm", [128, 32], BF16)
    srs = P.sbuf("srs", [128, 512], BF16)
    srs2 = P.sbuf("srs2", [128, 512], BF16)
    fm2 = P.sbuf("fm2", [128, 38], BF16)
    dtc = P.sbuf("dtc", [32, 2], F32)
    dAc = P.sbuf("dAc", [32, 2], F32)
    negeA = P.sbuf("negeA", [32, 1], F32)
    drow = P.sbuf("drow", [1, 64], F32)
    dtrow = P.sbuf("dtrow", [1, 64], F32)
    ur_sb = P.sbuf("ur_sb", [128, 64], BF16)
    Cp = P.sbuf("Cp", [128, 64], BF16)
    uv_sb = P.sbuf("uv_sb", [128, 64], BF16)
    ysb = P.sbuf("ysb", [128, 32], F32)
    g32 = P.sbuf("g32", [128, 32], F32)
    sqy = P.sbuf("sqy", [128, 32], F32)
    msr = P.sbuf("msr", [1, 2], F32)
    YN = P.sbuf("YN", [128, 4096], BF16)
    obuf = P.sbuf("obuf", [128, 256], F32)
    zsb = P.sbuf("zsb", [128, 128], BF16)
    dbgb = P.sbuf("dbgb", [128, 1024], F32) if L1DBG else None
    PRa = P.psum("PRa", [128, 512])
    PRb = P.psum("PRb", [128, 512])
    PT = P.psum("PT", [128, 128], BF16)
    PBC = P.psum("PBC", [128, 512])
    YP = P.psum("YP", [128, 128])
    DD = P.psum("DD", [128, 64])

    P.load(iden_f[:, :], iden[:, :])
    P.load(ones_f[:, :], ones[:, :])
    P.load(ones1_f[:, :], ones1[:, :])
    P.load(ccol_f[:, :], ccols[:, :])
    P.load(ceps[0:1, :], misc[0:1, :])
    P.load(hcols[:, 0:4], hcolsD[:, :])
    P.load(nw[:, :], nwD[:, :])
    P.load(Dexp[:, :], DexpD[:, :])
    P.load(yfm[:, :], ypi[:, :])
    for b in range(BL):
        P.load(S0[b][:, :], S0i[b])
    P.load(eye2f[:, :], eye2D[:, :])
    tw0 = P.load(wscr[:, 0:512], inj[:, :])
    t_ini = P.sv.bump(1)
    @P.on("vector")
    def _(e, tw0=tw0):
        e.wait_ge(P.sd.h, tw0)
        e.tensor_copy(iden_b[:, :], iden_f[:, :])
        e.tensor_copy(eye2b[:, :], eye2f[:, :])
        for b in range(BL):
            e.memset(B_m[b][:, :], 0.0)
            e.memset(X_s[b][:, :], 0.0)
        e.memset(PRa[:, :], 0.0)
        e.memset(PRb[:, :], 0.0)
        e.memset(zsb[:, :], 0.0)
        e.memset(PBC[:, :], 0.0)
        e.memset(YP[:, :], 0.0)
        e.memset(DD[:, :], 0.0)
        e.tensor_copy(injb[:, :], wscr[:, 0:512]).then_inc(P.sv.h, 1)
    t_iniS = P.ss.bump(1)
    @P.on("scalar")
    def _(e, tw0=tw0):
        e.wait_ge(P.sd.h, tw0)
        e.activation(negeA[:, 0:1], hcols[:, 1:2], AF.Exp)
        e.activation(negeA[:, 0:1], negeA[:, 0:1], AF.Copy, scale=-1.0).then_inc(P.ss.h, 1)
    t_zpt = P.st.bump(1)
    @P.on("tensor")
    def _(e, t_ini=t_ini):
        e.wait_ge(P.sv.h, t_ini)
        e.transpose(PT[:, 0:128], zsb[:, 0:128], iden_b[:, :]).then_inc(P.st.h, 1)
    def wload(dst, src_ap, ncol):
        tw = P.load(wscr[:, 0:ncol], src_ap, wait=[(P.sv, None)])
        tvv = P.sv.bump(1)
        @P.on("vector")
        def _(e, dst=dst, tw=tw, ncol=ncol):
            e.wait_ge(P.sd.h, tw)
            e.tensor_copy(dst[:, 0:ncol], wscr[:, 0:ncol]).then_inc(P.sv.h, 1)
    for k in range(16):
        wload(Wm[k], wrT[k], 2336)
    for k in range(16):
        wload(wout[k], woutT[k], 1024)

    NCC = L1SEG // L1CH
    t_prev = {"silu": 0, "transp": 0, "fmevac": 0, "yfm": t_ini, "xdma": 0,
              "udma": 0, "ymm": 0, "extr": 0, "udec": t_ini, "ucons": 0,
              "bcopy": 0, "dtch": t_iniS, "pbcv": 0, "pbct": 0, "dots": 0,
              "ynyn": 0, "sq": 0}
    for cc in range(NCC):
        svn = P.sv.n
        stn = P.st.n
        sgn0 = P.sg.n
        tld = None
        for b in range(BL):
            tld = P.load(cdt[b][:, :], cdtD[b, cc], wait=[(P.st, stn), (P.sv, svn)] if b == 0 else ())
            tld = P.load(zc[b][:, :], zD[b, cc])
        tum = P.sv.bump(1)
        @P.on("vector")
        def _(e, svn=svn, stn=stn, sgn0=sgn0, tld=tld):
            e.wait_ge(P.st.h, stn)
            e.wait_ge(P.sg.h, sgn0)
            e.wait_ge(P.sd.h, tld)
            e.memset(U_r[:, :], 0.0)
            e.memset(U_v[:, :], 1.0).then_inc(P.sv.h, 1)
        t_prev["udec"] = tum
        P.barrier_engine("tensor", [(P.sd, tld), (P.sv, None)])
        for t in range(L1CH):
            waits = [(P.sd, None), (P.sg, None), (P.st, None), (P.sv, None), (P.ss, None)]
            P.barrier_engine("tensor", [w for w in waits if w[0] is not P.st])
            P.barrier_engine("vector", [w for w in waits if w[0] is not P.sv])
            P.barrier_engine("scalar", [w for w in waits if w[0] is not P.ss])
            P.barrier_engine("sync", [(P.st, None), (P.sv, None), (P.ss, None)])
            P.barrier_engine("gpsimd", [(P.st, None), (P.sv, None), (P.ss, None)])
            tmv = P.st.bump(1)
            @P.on("tensor")
            def _(e, t=t, tp_yfm=t_prev["yfm"], tp_silu=t_prev["silu"]):
                e.wait_ge(P.sv.h, tp_yfm)
                e.wait_ge(P.ss.h, tp_silu)
                for j in range(5):
                    bank = PRa if j < 3 else PRb
                    base = 32 * (j % 3)
                    ncol = 512 if j < 4 else 288
                    tp = (0, base)
                    for b in range(BL):
                        e.matmul(bank[base:base + 2, 0:ncol], injb[:, 4 * t + 2 * b:4 * t + 2 * b + 2],
                                 cdt[b][:, 512 * j:512 * j + ncol], start=(b == 0), stop=False, tile_position=tp)
                    for k in range(16):
                        r = e.matmul(bank[base:base + 2, 0:ncol], yfm[:, 2 * k:2 * k + 2],
                                     Wm[k][:, 512 * j:512 * j + ncol], start=False, stop=(k == 15), tile_position=tp)
                r.then_inc(P.st.h, 1)
            tsl = P.ss.bump(1)
            @P.on("scalar")
            def _(e, tmv=tmv, tp_tr=t_prev["transp"], tp_x=t_prev["xdma"]):
                e.wait_ge(P.st.h, tmv)
                e.wait_ge(P.st.h, tp_tr)
                e.wait_ge(P.sd.h, tp_x)
                e.activation(srs[0:66, :], PRa[0:66, 0:512], AF.Silu)
                e.activation(srs2[0:66, :], PRb[0:66, 0:512], AF.Silu).then_inc(P.ss.h, 1)
            t_prev["silu"] = tsl
            txd = None
            for b in range(BL):
                for j in range(4):
                    src = srs if j < 3 else srs2
                    row = 32 * (j % 3) + b
                    txd = P.load(X_s[b][t:t + 1, 512 * j:512 * (j + 1)], src[row:row + 1, 0:512],
                                 wait=[(P.ss, tsl)] if (b == 0 and j == 0) else ())
            t_prev["xdma"] = txd
            ttr = P.st.bump(1)
            @P.on("tensor")
            def _(e, tsl=tsl, tp_fm=t_prev["fmevac"]):
                e.wait_ge(P.ss.h, tsl)
                e.wait_ge(P.sv.h, tp_fm)
                for J in range(19):
                    j = J // 4
                    co = 128 * (J % 4)
                    src = srs if j < 3 else srs2
                    base = 32 * (j % 3) if j < 3 else 32 * (j - 3)
                    ncol = 128 if J < 18 else 32
                    r = e.transpose(PT[0:ncol, 2 * J:2 * J + 2], src[base:base + 2, co:co + ncol], eye2b[base:base + 2, 0:2])
                r.then_inc(P.st.h, 1)
            t_prev["transp"] = ttr
            tfm = P.sv.bump(1)
            @P.on("vector")
            def _(e, ttr=ttr, tp_dt=t_prev["dtch"], tp_dots=t_prev["dots"]):
                e.wait_ge(P.st.h, ttr)
                e.wait_ge(P.ss.h, tp_dt)
                e.wait_ge(P.st.h, tp_dots)
                pt_v = PT[:, 0:38].rearrange("p (j two) -> p j two", two=2)
                fm_v = fm2.rearrange("p (b j) -> p j b", j=19)
                e.tensor_copy(fm_v, pt_v).then_inc(P.sv.h, 1)
            t_prev["fmevac"] = tfm
            tbc_ = P.sv.bump(1)
            @P.on("vector")
            def _(e, t=t, tfm=tfm):
                e.wait_ge(P.sv.h, tfm)
                for b in range(BL):
                    r = e.tensor_copy(B_m[b][:, t:t + 1], fm2[:, 19 * b + 16:19 * b + 17])
                r.then_inc(P.sv.h, 1)
            t_prev["bcopy"] = tbc_
            tdt1 = P.ss.bump(1)
            @P.on("scalar")
            def _(e, tfm=tfm):
                e.wait_ge(P.sv.h, tfm)
                fmdt = fm2[0:32, :].rearrange("p (b j) -> p b j", j=19)[:, :, 18]
                e.add(dtc[:, :], fmdt, hcols[:, 0:1])
                e.activation(dAc[:, :], dtc[:, :], AF.Abs)
                e.activation(dAc[:, :], dAc[:, :], AF.Exp, scale=-1.0)
                e.activation(dAc[:, :], dAc[:, :], AF.Ln, bias=ccol_f[0:32, 1:2])
                e.activation(dtc[:, :], dtc[:, :], AF.Relu).then_inc(P.ss.h, 1)
            tva = P.sv.bump(1)
            @P.on("vector")
            def _(e, tdt1=tdt1):
                e.wait_ge(P.ss.h, tdt1)
                e.tensor_tensor(dtc[:, :], dtc[:, :], dAc[:, :], ALU.add).then_inc(P.sv.h, 1)
            tdt = P.ss.bump(1)
            @P.on("scalar")
            def _(e, tva=tva):
                e.wait_ge(P.sv.h, tva)
                e.activation(dAc[:, :], dtc[:, :], AF.Exp, scale=negeA[:, 0:1]).then_inc(P.ss.h, 1)
            t_prev["dtch"] = tdt
            ttr2 = P.st.bump(1)
            @P.on("tensor")
            def _(e, tdt=tdt, tp_pv=t_prev["pbcv"]):
                e.wait_ge(P.ss.h, tdt)
                e.wait_ge(P.sv.h, tp_pv)
                e.transpose(PBC[0:1, 0:32], dAc[:, 0:1], iden_f[0:32, 0:32])
                e.transpose(PBC[0:1, 32:64], dAc[:, 1:2], iden_f[0:32, 0:32])
                e.transpose(PBC[0:1, 136:168], dtc[:, 0:1], iden_f[0:32, 0:32])
                r = e.transpose(PBC[0:1, 168:200], dtc[:, 1:2], iden_f[0:32, 0:32])
                r.then_inc(P.st.h, 1)
            tev = P.sv.bump(1)
            @P.on("vector")
            def _(e, ttr2=ttr2):
                e.wait_ge(P.st.h, ttr2)
                e.tensor_copy(drow[0:1, :], PBC[0:1, 0:64])
                e.tensor_copy(dtrow[0:1, :], PBC[0:1, 136:200]).then_inc(P.sv.h, 1)
            tbcm = P.st.bump(1)
            @P.on("tensor")
            def _(e, tev=tev):
                e.wait_ge(P.sv.h, tev)
                e.matmul(PBC[:, 64:128], ones1_f[0:1, :], drow[0:1, 0:64], start=True, stop=True).then_inc(P.st.h, 1)
            tud = P.sv.bump(1)
            @P.on("vector")
            def _(e, tbcm=tbcm, tp_udma=t_prev["udma"]):
                e.wait_ge(P.st.h, tbcm)
                e.wait_ge(P.sg.h, tp_udma)
                e.tensor_tensor(U_r[:, :], U_r[:, :], PBC[:, 64:128], ALU.mult)
                e.tensor_tensor(U_v[:, :], U_v[:, :], PBC[:, 64:128], ALU.mult).then_inc(P.sv.h, 1)
            t_prev["udec"] = tud
            tudma = P.store(U_r[t:t + 1, :], dtrow[0:1, :], wait=[(P.sv, tud)])
            t_prev["udma"] = tudma
            tcp_ = P.sv.bump(1)
            @P.on("vector")
            def _(e, tbc_=tbc_):
                e.wait_ge(P.sv.h, tbc_)
                fmc2 = fm2.rearrange("p (b j) -> p b j", j=19)[:, :, 17:18].broadcast_to([128, BL, 32])
                e.tensor_copy(Cp.rearrange("p (b h) -> p b h", h=32), fmc2).then_inc(P.sv.h, 1)
            tdot = P.st.bump(1)
            @P.on("tensor")
            def _(e, tcp_=tcp_, tp_u=t_prev["ucons"]):
                e.wait_ge(P.sv.h, tcp_)
                e.wait_ge(P.sv.h, tp_u)
                for b in range(BL):
                    r = e.matmul(DD[:, 32 * b:32 * b + 32], B_m[b][:, :],
                                 Cp[:, 32 * b:32 * b + 32], start=True, stop=True)
                r.then_inc(P.st.h, 1)
            t_prev["dots"] = tdot
            tu = P.sv.bump(1)
            @P.on("vector")
            def _(e, tdot=tdot, tud=tud, tudma=tudma):
                e.wait_ge(P.st.h, tdot)
                e.wait_ge(P.sg.h, tudma)
                e.wait_ge(P.sv.h, tud)
                e.tensor_tensor(ur_sb[:, :], DD[:, 0:64], U_r[:, :], ALU.mult)
                fmc = fm2.rearrange("p (b j) -> p b j", j=19)[:, :, 17:18].broadcast_to([128, BL, 32])
                e.tensor_tensor(uv_sb.rearrange("p (b h) -> p b h", h=32), U_v.rearrange("p (b h) -> p b h", h=32), fmc, ALU.mult).then_inc(P.sv.h, 1)
            t_prev["ucons"] = tu
            tym = P.st.bump(1)
            @P.on("tensor")
            def _(e, tu=tu, tp_ex=t_prev["extr"], txd=t_prev["xdma"]):
                e.wait_ge(P.sv.h, tu)
                e.wait_ge(P.sv.h, tp_ex)
                e.wait_ge(P.sd.h, txd)
                for b in range(BL):
                    for j in range(16):
                        pc = (b * 16 + j) * 2
                        e.matmul(YP[:, pc:pc + 2], S0[b][:, 128 * j:128 * (j + 1)], uv_sb[:, b * 32 + 2 * j:b * 32 + 2 * j + 2], start=True, stop=False)
                        r = e.matmul(YP[:, pc:pc + 2], X_s[b][:, 128 * j:128 * (j + 1)], ur_sb[:, b * 32 + 2 * j:b * 32 + 2 * j + 2], start=False, stop=True)
                r.then_inc(P.st.h, 1)
            t_prev["ymm"] = tym
            tex0 = P.sv.bump(1)
            @P.on("vector")
            def _(e, tym=tym):
                e.wait_ge(P.st.h, tym)
                yp_u = YP[:, 0:64].rearrange("p (c two) -> p c two", two=2)
                e.tensor_copy(ysb[0:64, :], yp_u[0:64, :, 0])
                e.tensor_copy(ysb[64:128, :], yp_u[64:128, :, 1])
                xv = fm2.rearrange("p (b j) -> p b j", j=19)[:, :, 0:16]
                e.tensor_tensor(tmpD.rearrange("p (b j) -> p b j", j=16), xv,
                                Dexp[:, None, :].broadcast_to([128, 2, 16]), ALU.mult).then_inc(P.sv.h, 1)
            tex = P.sv.bump(1)
            @P.on("vector")
            def _(e, tex0=tex0):
                e.wait_ge(P.sv.h, tex0)
                e.tensor_tensor(ysb[:, :], ysb[:, :], tmpD[:, :], ALU.add).then_inc(P.sv.h, 1)
            t_prev["extr"] = tex
            tg = P.sv.bump(1)
            @P.on("vector")
            def _(e, t=t, tex=tex):
                e.wait_ge(P.sv.h, tex)
                for b in range(BL):
                    r = e.tensor_tensor(g32[:, 16 * b:16 * (b + 1)], ysb[:, 16 * b:16 * (b + 1)],
                                        zc[b].rearrange("p (k t2) -> p k t2", t2=128)[:, :, t], ALU.mult)
                r.then_inc(P.sv.h, 1)
            tsq = P.ss.bump(1)
            @P.on("scalar")
            def _(e, tg=tg):
                e.wait_ge(P.sv.h, tg)
                e.activation(sqy[:, :], g32[:, :], AF.Square).then_inc(P.ss.h, 1)
            t_prev["sq"] = tsq
            trm = P.st.bump(1)
            @P.on("tensor")
            def _(e, tsq=tsq):
                e.wait_ge(P.ss.h, tsq)
                e.matmul(PBC[0:1, 328:360], ones_f[:, 0:1], sqy[:, :], start=True, stop=True).then_inc(P.st.h, 1)
            tms0 = P.sv.bump(1)
            @P.on("vector")
            def _(e, trm=trm, tg=tg):
                e.wait_ge(P.st.h, trm)
                e.wait_ge(P.sv.h, tg)
                e.tensor_reduce(msr[0:1, 0:2], PBC[0:1, 328:360].rearrange("p (b j) -> p b j", j=16), AX.X, ALU.add).then_inc(P.sv.h, 1)
            tms = P.sv.bump(1)
            @P.on("vector")
            def _(e, tms0=tms0):
                e.wait_ge(P.sv.h, tms0)
                e.tensor_scalar_mul(msr[0:1, 0:2], msr[0:1, 0:2], 1.0 / 2048.0).then_inc(P.sv.h, 1)
            trs = P.ss.bump(1)
            @P.on("scalar")
            def _(e, tms=tms):
                e.wait_ge(P.sv.h, tms)
                e.activation(msr[0:1, 0:2], msr[0:1, 0:2], AF.Ln, bias=ceps[0:1, 0:1])
                e.activation(msr[0:1, 0:2], msr[0:1, 0:2], AF.Exp, scale=-0.5).then_inc(P.ss.h, 1)
            tbc2 = P.st.bump(1)
            @P.on("tensor")
            def _(e, trs=trs):
                e.wait_ge(P.ss.h, trs)
                e.matmul(PBC[:, 360:362], ones1_f[0:1, :], msr[0:1, 0:2], start=True, stop=True).then_inc(P.st.h, 1)
            t_prev["pbct"] = tbc2
            tyn0 = P.sv.bump(1)
            @P.on("vector")
            def _(e, tbc2=tbc2):
                e.wait_ge(P.st.h, tbc2)
                g3 = g32.rearrange("p (b j) -> p b j", j=16)
                e.tensor_tensor(g3, g3, PBC[:, 360:362][:, :, None].broadcast_to([128, 2, 16]), ALU.mult).then_inc(P.sv.h, 1)
            tyn1 = P.sv.bump(1)
            @P.on("vector")
            def _(e, tyn0=tyn0):
                e.wait_ge(P.sv.h, tyn0)
                g3 = g32.rearrange("p (b j) -> p b j", j=16)
                e.tensor_tensor(yfm.rearrange("p (j b) -> p b j", b=2), g3,
                                nw[:, None, :].broadcast_to([128, 2, 16]), ALU.mult).then_inc(P.sv.h, 1)
            tyn = P.sv.bump(1)
            @P.on("vector")
            def _(e, t=t, tyn1=tyn1):
                e.wait_ge(P.sv.h, tyn1)
                yn_v = YN.rearrange("p (j b t2) -> p j b t2", b=2, t2=128)
                e.tensor_copy(yn_v[:, :, :, t], yfm.rearrange("p (j b) -> p j b", b=2)).then_inc(P.sv.h, 1)
            t_prev["yfm"] = tyn
            t_prev["pbcv"] = tyn
            if L1DBG and cc == 0 and t < 4:
                tdbg = P.sv.bump(1)
                @P.on("vector")
                def _(e, t=t, tyn=tyn):
                    e.wait_ge(P.sv.h, tyn)
                    base = t * 256
                    e.tensor_copy(dbgb[:, base:base + 38], fm2[:, :])
                    e.tensor_copy(dbgb[0:32, base + 40:base + 42], dtc[:, :])
                    e.tensor_copy(dbgb[0:32, base + 42:base + 44], dAc[:, :])
                    e.tensor_copy(dbgb[:, base + 44:base + 45], B_m[0][:, t:t + 1])
                    e.tensor_copy(dbgb[:, base + 45:base + 46], Cp[:, 0:1])
                    e.tensor_copy(dbgb[:, base + 108:base + 140], ysb[:, :])
                    e.tensor_copy(dbgb[:, base + 140:base + 172], DD[:, 0:32])
                    e.tensor_copy(dbgb[:, base + 172:base + 236], ur_sb[:, :])
                    e.tensor_copy(dbgb[0:1, base + 236:base + 238], msr[0:1, :]).then_inc(P.sv.h, 1)
        svn = P.sv.n
        tfold_v = []
        for b in range(BL):
            tvx = P.sv.bump(1)
            sgn_f = P.sg.n
            sdn_f = P.sd.n
            @P.on("vector")
            def _(e, b=b, sgn_f=sgn_f, sdn_f=sdn_f):
                e.wait_ge(P.sg.h, sgn_f)
                e.wait_ge(P.sd.h, sdn_f)
                ur_v = U_r.rearrange("p (b h) -> p b h", h=32)[:, b, :][:, :, None].broadcast_to([128, 32, 64])
                uv_e = U_v.rearrange("p (b h) -> p b h", h=32)[:, b, :][:, :, None].broadcast_to([128, 32, 64])
                e.tensor_tensor(XU.rearrange("p (h d) -> p h d", d=64), S0[b].rearrange("p (h d) -> p h d", d=64), uv_e, ALU.mult)
                e.tensor_tensor(X_s[b].rearrange("p (h d) -> p h d", d=64), X_s[b].rearrange("p (h d) -> p h d", d=64), ur_v, ALU.mult).then_inc(P.sv.h, 1)
            ttb = P.st.bump(1)
            @P.on("tensor")
            def _(e, b=b, tvx=tvx):
                e.wait_ge(P.sv.h, tvx)
                e.transpose(PT[:, 0:128], B_m[b][:, :], iden_b[:, :]).then_inc(P.st.h, 1)
            tbe = P.sv.bump(1)
            @P.on("vector")
            def _(e, ttb=ttb):
                e.wait_ge(P.st.h, ttb)
                e.tensor_copy(BTsb[:, :], PT[:, 0:128]).then_inc(P.sv.h, 1)
            tlast = tbe
            for c4 in range(4):
                pk = [PRa, PRb][c4 % 2]
                tmm = P.st.bump(1)
                @P.on("tensor")
                def _(e, b=b, c4=c4, pk=pk, tbe=tbe, tlast=tlast):
                    e.wait_ge(P.sv.h, tlast)
                    e.matmul(pk[:, 0:512], BTsb[:, :], X_s[b][:, 512 * c4:512 * (c4 + 1)], start=True, stop=True).then_inc(P.st.h, 1)
                tse = P.sv.bump(1)
                @P.on("vector")
                def _(e, b=b, c4=c4, pk=pk, tmm=tmm):
                    e.wait_ge(P.st.h, tmm)
                    e.tensor_tensor(S0[b][:, 512 * c4:512 * (c4 + 1)], XU[:, 512 * c4:512 * (c4 + 1)], pk[:, 0:512], ALU.add).then_inc(P.sv.h, 1)
                tlast = tse
            tfold_v.append(tlast)
        P.barrier_engine("tensor", [(P.sv, None)])
        ofree = [None, None]
        for ot in range(DT):
            pk = [PRa, PRb][ot % 2]
            of = ofree[ot % 2]
            tgt = P.st.bump(1)
            @P.on("tensor")
            def _(e, ot=ot, pk=pk, of=of):
                if of is not None:
                    e.wait_ge(of[0].h, of[1])
                for j in range(16):
                    r = e.matmul(pk[:, 0:256], wout[j][:, 128 * ot:128 * (ot + 1)], YN[:, 256 * j:256 * (j + 1)], start=(j == 0), stop=(j == 15))
                r.then_inc(P.st.h, 1)
            sgn = P.sg.n
            tse = P.ss.bump(1)
            @P.on("scalar")
            def _(e, pk=pk, tgt=tgt, sgn=sgn):
                e.wait_ge(P.st.h, tgt)
                e.wait_ge(P.sg.h, sgn)
                e.activation(obuf[:, :], pk[:, 0:256], AF.Copy).then_inc(P.ss.h, 1)
            for b in range(BL):
                P.store(o_s[ot, :, b * L1SEG + cc * L1CH:b * L1SEG + (cc + 1) * L1CH],
                        obuf[:, b * 128:(b + 1) * 128], wait=[(P.ss, tse)] if b == 0 else ())
            ofree[ot % 2] = (P.ss, tse)
        P.barrier_engine("vector", [(P.st, None)])
    svn = P.sv.n
    for b in range(BL):
        P.store(S0o[b], S0[b][:, :], wait=[(P.sv, svn)] if b == 0 else ())
    P.store(ypo[:, :], yfm[:, :])
    if L1DBG:
        P.store(dbgD[:, :], dbgb[:, :])
    return P.finalize()


def build_l1ln():
    P = Prog("l1ln")
    x_in = P.dram("x_in", [DT, 128, NT], F32, "ExternalInput")
    ones = P.dram("c_ones", [128, 1], F32, "ExternalInput")
    ones1 = P.dram("c_ones1", [1, 128], F32, "ExternalInput")
    misc = P.dram("c_misc", [1, 8], F32, "ExternalInput")
    iden = P.dram("c_iden", [128, 128], F32, "ExternalInput")
    xn_s = P.dram("xn_s", [DT, 128, NT], BF16, "ExternalOutput")
    xnT_s = P.dram("xnT_s", [NT // 128, 128, D], BF16, "ExternalOutput")

    ones_f = P.sbuf("ones_f", [128, 1], F32)
    ones1_f = P.sbuf("ones1_f", [1, 128], F32)
    iden_b = P.sbuf("iden_b", [128, 128], BF16)
    iden_f = P.sbuf("iden_f", [128, 128], F32)
    P.c_eps = P.sbuf("c_eps", [1, 8], F32)
    P.load(ones_f[:, :], ones[:, :])
    P.load(ones1_f[:, :], ones1[:, :])
    P.load(P.c_eps[0:1, :], misc[0:1, :])
    ti = P.load(iden_f[:, :], iden[:, :])
    tvi = P.sv.bump(1)
    @P.on("vector")
    def _(e, ti=ti):
        e.wait_ge(P.sd.h, ti)
        e.tensor_copy(iden_b[:, :], iden_f[:, :]).then_inc(P.sv.h, 1)

    xt = [P.sbuf(f"xt{k}", [128, TCH], F32) for k in range(DT)]
    sq = [P.sbuf(f"sq{k}", [128, TCH], F32) for k in range(DT)]
    xnb = [P.sbuf(f"xnb{k}", [128, TCH], BF16) for k in range(DT)]
    rows = (P.sbuf("r_mean", [1, TCH], F32), P.sbuf("r_var", [1, TCH], F32), P.sbuf("r_tmp", [1, TCH], F32))
    tby = P.sbuf("tby", [128, TCH], BF16)
    PST = P.psum("PST", [128, TCH])
    PBC = P.psum("PBC", [128, TCH])
    PMT = P.psum("PMT", [128, 128], BF16)

    for ci in range(NCH):
        off = ci * TCH
        tld = None
        for k in range(DT):
            tld = P.load(xt[k][:, :], x_in[k, :, off:off + TCH],
                         wait=[(P.st, None), (P.sv, None)] if k == 0 else ())
        P.barrier_engine("tensor", [(P.sd, tld), (P.sv, None)])
        P.barrier_engine("scalar", [(P.sd, tld)])
        P.barrier_engine("vector", [(P.sd, tld), (P.sg, None), (P.st, None)])
        t_xn = emit_ln_chunk(P, xt, xnb, sq, rows, ones_f, ones1_f, PST, PBC, TCH)
        for k in range(DT):
            P.store(xn_s[k, :, off:off + TCH], xnb[k][:, :], wait=[(P.sv, t_xn)] if k == 0 else ())
        for sub in range(TCH // 128):
            tt = off // 128 + sub
            for k in range(DT):
                svn = P.sv.n
                ttr = P.st.bump(1)
                @P.on("tensor")
                def _(e, k=k, sub=sub, t_xn=t_xn, svn=svn):
                    e.wait_ge(P.sv.h, t_xn)
                    e.wait_ge(P.sv.h, svn)
                    e.transpose(PMT[:, :], xnb[k][:, sub * 128:(sub + 1) * 128], iden_b[:, :]).then_inc(P.st.h, 1)
                sgn = P.sg.n
                tev = P.sv.bump(1)
                @P.on("vector")
                def _(e, ttr=ttr, sgn=sgn):
                    e.wait_ge(P.st.h, ttr)
                    e.wait_ge(P.sg.h, sgn)
                    e.tensor_copy(tby[:, 0:128], PMT[:, :]).then_inc(P.sv.h, 1)
                P.store(xnT_s[tt, :, k * 128:(k + 1) * 128], tby[:, 0:128], wait=[(P.sv, tev)])
    return P.finalize()


def emit_fp4q(P, e, out_ap, in_ap, scr, n, scb, rsb):
    ng = n // 16
    vin = in_ap.rearrange("p (g c) -> p g c", c=16)
    ab, c1, c2 = scr["ab"], scr["c1"], scr["c2"]
    e.tensor_reduce(scb[:, 0:ng], vin, AX.X, ALU.max, apply_absolute_value=True)
    e.tensor_scalar(scb[:, 0:ng], scb[:, 0:ng], 1.0 / 6.0, 1e-12, ALU.mult, ALU.max)
    e.reciprocal(rsb[:, 0:ng], scb[:, 0:ng])
    e.tensor_single_scalar(c1[:, 0:n], in_ap, -1.0, ALU.mult)
    e.tensor_tensor(ab[:, 0:n], in_ap, c1[:, 0:n], ALU.max)
    e.tensor_tensor(ab[:, 0:n].rearrange("p (g c) -> p g c", c=16), ab[:, 0:n].rearrange("p (g c) -> p g c", c=16),
                    rsb[:, 0:ng, None].broadcast_to([128, ng, 16]), ALU.mult)
    e.tensor_scalar(c1[:, 0:n], ab[:, 0:n], -2.0, 0.0, ALU.add, ALU.max)
    e.tensor_scalar(c2[:, 0:n], ab[:, 0:n], -4.0, 0.0, ALU.add, ALU.max)
    e.scalar_tensor_tensor(c1[:, 0:n], c2[:, 0:n], 0.5, c1[:, 0:n], ALU.mult, ALU.add)
    e.scalar_tensor_tensor(c2[:, 0:n], ab[:, 0:n], 2.0, c1[:, 0:n], ALU.mult, ALU.subtract)
    e.tensor_single_scalar(ab[:, 0:n], c2[:, 0:n], 0.5, ALU.is_gt)
    for k in range(1, 7):
        e.scalar_tensor_tensor(ab[:, 0:n], c2[:, 0:n], k + 0.5, ab[:, 0:n], ALU.is_gt, ALU.add)
    e.tensor_scalar(c2[:, 0:n], ab[:, 0:n], -4.0, 0.0, ALU.add, ALU.max)
    e.tensor_scalar(c1[:, 0:n], ab[:, 0:n], -6.0, 0.0, ALU.add, ALU.max)
    e.scalar_tensor_tensor(c2[:, 0:n], c1[:, 0:n], 3.0, c2[:, 0:n], ALU.mult, ALU.add)
    e.tensor_tensor(c2[:, 0:n], c2[:, 0:n], ab[:, 0:n], ALU.add)
    e.tensor_single_scalar(c1[:, 0:n], in_ap, 0.0, ALU.is_gt)
    e.tensor_scalar(c1[:, 0:n], c1[:, 0:n], 1.0, -0.5, ALU.mult, ALU.add)
    e.tensor_tensor(c2[:, 0:n], c2[:, 0:n], c1[:, 0:n], ALU.mult)
    r = e.tensor_tensor(out_ap.rearrange("p (g c) -> p g c", c=16), c2[:, 0:n].rearrange("p (g c) -> p g c", c=16),
                        scb[:, 0:ng, None].broadcast_to([128, ng, 16]), ALU.mult)
    return r


def build_l1f():
    P = Prog("l1f")
    x_in = P.dram("x_in", [DT, 128, NT], F32, "ExternalInput")
    xnT_s = P.dram("xnT_s", [NT // 128, 128, D], BF16, "ExternalInput")
    s_s = P.dram("s_s", [DT, 128, NT], F32, "ExternalInput")
    wgD = P.dram("l1_wg", [22, 128, 1024], F32, "ExternalInput")
    wuD = P.dram("l1_wu", [22, 128, 1024], F32, "ExternalInput")
    wdD = P.dram("l1_wd", [8, 128, 2816], F32, "ExternalInput")
    iden = P.dram("c_iden", [128, 128], F32, "ExternalInput")
    x_out = P.dram("x_out", [DT, 128, NT], F32, "ExternalOutput")

    iden_f = P.sbuf("iden_f", [128, 128], F32)
    iden_b = P.sbuf("iden_b", [128, 128], BF16)
    ti = P.load(iden_f[:, :], iden[:, :])
    tvi = P.sv.bump(1)
    @P.on("vector")
    def _(e, ti=ti):
        e.wait_ge(P.sd.h, ti)
        e.tensor_copy(iden_b[:, :], iden_f[:, :]).then_inc(P.sv.h, 1)
    wscr = P.sbuf("wscr", [128, 1408], F32)
    wstg = P.sbuf("wstg", [128, 2816], BF16)
    qscr = {k: P.sbuf(f"q_{k}", [128, 2816], BF16) for k in ("ab", "c1", "c2")}
    qsc = P.sbuf("qsc", [128, 2816 // 16], F32)
    qrs = P.sbuf("qrs", [128, 2816 // 16], F32)
    wgT = [P.sbuf(f"wgT{j}", [128, 2816], BF16) for j in range(DT)]
    wuT = [P.sbuf(f"wuT{j}", [128, 2816], BF16) for j in range(DT)]
    wdT = [P.sbuf(f"wdT{j}", [128, 1024], BF16) for j in range(22)]
    PMT = P.psum("PMT", [128, 128], BF16)
    PG = [P.psum(f"PG{i}", [128, TCH]) for i in range(2)]
    PU = [P.psum(f"PU{i}", [128, TCH]) for i in range(2)]
    PD = [P.psum(f"PD{i}", [128, TCH]) for i in range(2)]

    def quant_transpose(srcD, k, ncol, dstlist, dst_colbase):
        nh = ncol // 2
        for half in range(2):
            tw = P.load(wscr[:, 0:nh], srcD[k][:, half * nh:(half + 1) * nh], wait=[(P.sv, None), (P.st, None)])
            tq0 = P.sv.bump(1)
            @P.on("vector")
            def _(e, tw=tw, nh=nh, half=half):
                e.wait_ge(P.sd.h, tw)
                e.tensor_copy(wstg[:, half * nh:(half + 1) * nh], wscr[:, 0:nh]).then_inc(P.sv.h, 1)
        tq = P.sv.bump(1)
        @P.on("vector")
        def _(e, ncol=ncol):
            emit_fp4q(P, e, wstg[:, 0:ncol], wstg[:, 0:ncol], qscr, ncol, qsc, qrs).then_inc(P.sv.h, 1)
        for j in range(ncol // 128):
            svn3 = P.sv.n
            ttr = P.st.bump(1)
            @P.on("tensor")
            def _(e, j=j, tq=tq, svn3=svn3):
                e.wait_ge(P.sv.h, max(tq, svn3))
                e.transpose(PMT[:, :], wstg[:, j * 128:(j + 1) * 128], iden_b[:, :]).then_inc(P.st.h, 1)
            tev = P.sv.bump(1)
            @P.on("vector")
            def _(e, j=j, ttr=ttr, dst_colbase=dst_colbase):
                e.wait_ge(P.st.h, ttr)
                e.tensor_copy(dstlist[j][:, dst_colbase:dst_colbase + 128], PMT[:, :]).then_inc(P.sv.h, 1)
    for k in range(22):
        quant_transpose(wgD, k, 1024, wgT, k * 128)
        quant_transpose(wuD, k, 1024, wuT, k * 128)
    for k in range(DT):
        quant_transpose(wdD, k, 2816, wdT, k * 128)

    xqT = P.sbuf("xqT", [128, 1024], BF16)
    xqf = [P.sbuf(f"xqf{k}", [128, TCH], BF16) for k in range(DT)]
    gT = qscr["ab"]
    hT = P.sbuf("hT", [128, 2816], BF16)
    hqf = [P.sbuf(f"hqf{k}", [128, TCH], BF16) for k in range(22)]
    xts = P.sbuf("xts", [128, TCH], F32)
    sts = P.sbuf("sts", [128, TCH], F32)
    xo = P.sbuf("xo", [128, TCH], F32)

    for ci in range(NCH):
        off = ci * TCH
        for sub in range(TCH // 128):
            tt = off // 128 + sub
            svn = P.sv.n
            stn = P.st.n
            tld = P.load(xqT[:, :], xnT_s[tt], wait=[(P.sv, svn), (P.st, stn)])
            tq = P.sv.bump(1)
            @P.on("vector")
            def _(e, tld=tld):
                e.wait_ge(P.sd.h, tld)
                emit_fp4q(P, e, xqT[:, :], xqT[:, :], qscr, 1024, qsc, qrs).then_inc(P.sv.h, 1)
            for j in range(DT):
                svn3 = P.sv.n
                ttr = P.st.bump(1)
                @P.on("tensor")
                def _(e, j=j, tq=tq, svn3=svn3):
                    e.wait_ge(P.sv.h, max(tq, svn3))
                    e.transpose(PMT[:, :], xqT[:, j * 128:(j + 1) * 128], iden_b[:, :]).then_inc(P.st.h, 1)
                tev = P.sv.bump(1)
                @P.on("vector")
                def _(e, j=j, sub=sub, ttr=ttr):
                    e.wait_ge(P.st.h, ttr)
                    e.tensor_copy(xqf[j][:, sub * 128:(sub + 1) * 128], PMT[:, :]).then_inc(P.sv.h, 1)
        for sub in range(TCH // 128):
            for hc in range(6):
                ncol = 512 if hc < 5 else 256
                pg = PG[hc % 2]
                pu = PU[hc % 2]
                svn = P.sv.n
                ssn = P.ss.n
                tgt = P.st.bump(1)
                @P.on("tensor")
                def _(e, sub=sub, hc=hc, pg=pg, pu=pu, ncol=ncol, svn=svn, ssn=ssn):
                    e.wait_ge(P.sv.h, svn)
                    e.wait_ge(P.ss.h, ssn)
                    for k in range(DT):
                        e.matmul(pg[:, 0:ncol], xqf[k][:, sub * 128:(sub + 1) * 128], wgT[k][:, hc * 512:hc * 512 + ncol], start=(k == 0), stop=(k == DT - 1))
                    for k in range(DT):
                        r = e.matmul(pu[:, 0:ncol], xqf[k][:, sub * 128:(sub + 1) * 128], wuT[k][:, hc * 512:hc * 512 + ncol], start=(k == 0), stop=(k == DT - 1))
                    r.then_inc(P.st.h, 1)
                tsg = P.ss.bump(1)
                @P.on("scalar")
                def _(e, hc=hc, pg=pg, tgt=tgt, ncol=ncol):
                    e.wait_ge(P.st.h, tgt)
                    e.activation(gT[:, hc * 512:hc * 512 + ncol], pg[:, 0:ncol], AF.Silu).then_inc(P.ss.h, 1)
                tvh = P.sv.bump(1)
                @P.on("vector")
                def _(e, hc=hc, pu=pu, tsg=tsg, tgt=tgt, ncol=ncol):
                    e.wait_ge(P.ss.h, tsg)
                    e.wait_ge(P.st.h, tgt)
                    e.tensor_tensor(hT[:, hc * 512:hc * 512 + ncol], pu[:, 0:ncol], gT[:, hc * 512:hc * 512 + ncol], ALU.mult).then_inc(P.sv.h, 1)
            tqh = P.sv.bump(1)
            @P.on("vector")
            def _(e):
                emit_fp4q(P, e, hT[:, :], hT[:, :], qscr, 2816, qsc, qrs).then_inc(P.sv.h, 1)
            for j in range(22):
                svn3 = P.sv.n
                ttr = P.st.bump(1)
                @P.on("tensor")
                def _(e, j=j, tqh=tqh, svn3=svn3):
                    e.wait_ge(P.sv.h, max(tqh, svn3))
                    e.transpose(PMT[:, :], hT[:, j * 128:(j + 1) * 128], iden_b[:, :]).then_inc(P.st.h, 1)
                tev = P.sv.bump(1)
                @P.on("vector")
                def _(e, j=j, sub=sub, ttr=ttr):
                    e.wait_ge(P.st.h, ttr)
                    e.tensor_copy(hqf[j][:, sub * 128:(sub + 1) * 128], PMT[:, :]).then_inc(P.sv.h, 1)
        P.barrier_engine("tensor", [(P.sv, None)])
        dfree = [None, None]
        for ot in range(DT):
            pd = PD[ot % 2]
            df = dfree[ot % 2]
            tgt = P.st.bump(1)
            @P.on("tensor")
            def _(e, ot=ot, pd=pd, df=df):
                if df is not None:
                    e.wait_ge(df[0].h, df[1])
                for k in range(22):
                    r = e.matmul(pd[:, 0:TCH], wdT[k][:, ot * 128:(ot + 1) * 128], hqf[k][:, 0:TCH], start=(k == 0), stop=(k == 21))
                r.then_inc(P.st.h, 1)
            tx = P.load(xts[:, :], x_in[ot, :, off:off + TCH], wait=[(P.sv, None)])
            ts2 = P.load(sts[:, :], s_s[ot, :, off:off + TCH])
            sgn = P.sg.n
            tsi = P.ss.bump(1)
            @P.on("scalar")
            def _(e, pd=pd, tgt=tgt, sgn=sgn):
                e.wait_ge(P.st.h, tgt)
                e.wait_ge(P.sg.h, sgn)
                e.activation(xo[:, :], pd[:, 0:TCH], AF.Silu).then_inc(P.ss.h, 1)
            tvc = P.sv.bump(1)
            @P.on("vector")
            def _(e, tsi=tsi, ts2=ts2):
                e.wait_ge(P.ss.h, tsi)
                e.wait_ge(P.sd.h, ts2)
                e.tensor_tensor(xo[:, :], xo[:, :], sts[:, :], ALU.mult)
                e.tensor_tensor(xo[:, :], xo[:, :], xts[:, :], ALU.add).then_inc(P.sv.h, 1)
            P.store(x_out[ot, :, off:off + TCH], xo[:, :], wait=[(P.sv, tvc)])
            dfree[ot % 2] = (P.ss, tsi)
    return P.finalize()


def prep_l1(ins):
    W_in = np.asarray(ins["L1_attn_W_in"])
    conv_w = np.asarray(ins["L1_attn_conv_w"]).reshape(2304, 4)
    conv_b = np.asarray(ins["L1_attn_conv_b"]).reshape(2304, 1)
    cw = np.concatenate([conv_w, conv_b], axis=1).reshape(18, 128, 5)
    Wr = np.concatenate([np.asarray(ins["L1_attn_W_rxbc"]), np.asarray(ins["L1_attn_W_rdt"])], axis=0)
    Dp = np.asarray(ins["L1_attn_D_param"])
    p = np.arange(128)[:, None]
    j = np.arange(16)[None, :]
    Dexp = Dp[(2 * j + (p >= 64)).astype(np.int64)].astype(np.float32)
    l1a = {
        "l1_winT": _wT_tiles(W_in),
        "l1_convw": np.ascontiguousarray(cw),
    }
    l1s = {
        "l1_wrT": _wT_tiles(Wr),
        "l1_woutT": _wT_tiles(ins["L1_attn_W_out"]),
        "l1_hcols": np.ascontiguousarray(np.stack(
            [np.asarray(ins["L1_attn_dt_bias"]), np.asarray(ins["L1_attn_A_log"]),
             Dp, np.zeros(32, np.float32)], axis=1).astype(np.float32)),
        "l1_nw": np.ascontiguousarray(np.asarray(ins["L1_attn_norm_w"]).reshape(16, 128).T),
        "l1_Dexp": np.ascontiguousarray(Dexp),
    }
    l1f = {
        "l1_wg": np.ascontiguousarray(np.asarray(ins["L1_ffn_w_gate"]).reshape(22, 128, 1024)),
        "l1_wu": np.ascontiguousarray(np.asarray(ins["L1_ffn_w_up"]).reshape(22, 128, 1024)),
        "l1_wd": np.ascontiguousarray(np.asarray(ins["L1_ffn_w_down"]).reshape(8, 128, 2816)),
    }
    return l1a, l1s, l1f


def _inj_const():
    inj = np.zeros((128, 512), np.float32)
    t = np.arange(128)
    inj[t, 4 * t] = 1.0
    inj[t, 4 * t + 3] = 1.0
    return inj


CONSTS["c_inj"] = _inj_const()
_e2 = np.zeros((128, 2), np.float32)
_e2[np.arange(128), np.arange(128) % 32 % 2] = (np.arange(128) % 32 < 2).astype(np.float32)
_e2 = np.zeros((128, 2), np.float32)
for _p in range(128):
    if _p % 32 < 2:
        _e2[_p, _p % 32] = 1.0
CONSTS["c_eye2"] = _e2


def run_l1(xcore, l1a, l1s, l1f):
    in_maps = [{"x_in": xcore[c], **CONSTS} for c in range(NCORES)]
    rln = _run(build_l1ln, "l1ln", in_maps)
    in_maps = [{"xn_s": rln[c]["xn_s"], **l1a, **CONSTS} for c in range(NCORES)]
    ra = _run(build_l1a1, "l1a1", in_maps)
    S0 = [np.zeros((BL, 128, 2048), np.float32).astype(_bf16) for _ in range(NCORES)]
    yp = [np.zeros((128, 32), np.float32).astype(_bf16) for _ in range(NCORES)]
    o_parts = []
    for seg in range(4):
        in_maps = []
        for c in range(NCORES):
            in_maps.append({
                "cdt_c": np.ascontiguousarray(np.asarray(ra[c]["cdt_s"])[:, seg * 4:(seg + 1) * 4]),
                "z_c": np.ascontiguousarray(np.asarray(ra[c]["z_s"])[:, seg * 4:(seg + 1) * 4]),
                "S0_i": S0[c], "yprev_i": yp[c], **l1s, **CONSTS})
        rs = _run(build_l1scan, "l1scan", in_maps)
        S0 = [np.asarray(rs[c]["S0_o"]) for c in range(NCORES)]
        yp = [np.asarray(rs[c]["yprev_o"]) for c in range(NCORES)]
        o_parts.append([np.asarray(rs[c]["o_s"]) for c in range(NCORES)])
    s_s = []
    for c in range(NCORES):
        sc = np.zeros((DT, 128, NT), np.float32)
        for seg in range(4):
            for b in range(BL):
                sc[:, :, b * S + seg * L1SEG:(b * S + (seg + 1) * L1SEG)] = \
                    o_parts[seg][c][:, :, b * L1SEG:(b + 1) * L1SEG]
        s_s.append(sc)
    in_maps = [{"x_in": xcore[c], "xnT_s": rln[c]["xnT_s"], "s_s": s_s[c], **l1f, **CONSTS}
               for c in range(NCORES)]
    rf = _run(build_l1f, "l1f", in_maps)
    return [rf[c]["x_out"] for c in range(NCORES)]



def build_l0a1():
    P = Prog("l0a1")
    x_in = P.dram("x_in", [DT, 128, NT], F32, "ExternalInput")
    ones = P.dram("c_ones", [128, 1], F32, "ExternalInput")
    ones1 = P.dram("c_ones1", [1, 128], F32, "ExternalInput")
    misc = P.dram("c_misc", [1, 8], F32, "ExternalInput")
    xn_s = P.dram("xn_s", [DT, 128, NT], BF16, "ExternalOutput")
    xnf_s = P.dram("xnf_s", [DT, 128, NT], F32, "ExternalOutput")
    ones_f = P.sbuf("ones_f", [128, 1], F32)
    ones1_f = P.sbuf("ones1_f", [1, 128], F32)
    P.c_eps = P.sbuf("c_eps", [1, 8], F32)
    P.load(ones_f[:, :], ones[:, :])
    P.load(ones1_f[:, :], ones1[:, :])
    P.load(P.c_eps[0:1, :], misc[0:1, :])
    xt = [P.sbuf(f"xt{k}", [128, TCH], F32) for k in range(DT)]
    sq = [P.sbuf(f"sq{k}", [128, TCH], F32) for k in range(DT)]
    xnb = [P.sbuf(f"xnb{k}", [128, TCH], BF16) for k in range(DT)]
    rows = (P.sbuf("r_mean", [1, TCH], F32), P.sbuf("r_var", [1, TCH], F32), P.sbuf("r_tmp", [1, TCH], F32))
    PST = P.psum("PST", [128, TCH])
    PBC = P.psum("PBC", [128, TCH])
    for ci in range(NCH):
        off = ci * TCH
        tld = None
        for k in range(DT):
            tld = P.load(xt[k][:, :], x_in[k, :, off:off + TCH],
                         wait=[(P.st, None), (P.sv, None), (P.sg, None)] if k == 0 else ())
        P.barrier_engine("tensor", [(P.sd, tld), (P.sv, None)])
        P.barrier_engine("scalar", [(P.sd, tld)])
        P.barrier_engine("vector", [(P.sd, tld), (P.sg, None), (P.st, None)])
        t_xn = emit_ln_chunk(P, xt, xnb, sq, rows, ones_f, ones1_f, PST, PBC, TCH, keep_f32=True)
        for k in range(DT):
            P.store(xn_s[k, :, off:off + TCH], xnb[k][:, :], wait=[(P.sv, t_xn)] if k == 0 else ())
            P.store(xnf_s[k, :, off:off + TCH], sq[k][:, :])
    return P.finalize()


def build_l0a2():
    P = Prog("l0a2")
    xn_s = P.dram("xn_s", [DT, 128, NT], BF16, "ExternalInput")
    wqkvT = P.dram("l0_wqkvT", [DT, 128, 3072], F32, "ExternalInput")
    lncol = P.dram("l0_ln1", [128, 2 * DT], F32, "ExternalInput")
    ones = P.dram("c_ones", [128, 1], F32, "ExternalInput")
    ones1 = P.dram("c_ones1", [1, 128], F32, "ExternalInput")
    misc = P.dram("c_misc", [1, 8], F32, "ExternalInput")
    Q_s = P.dram("Q_s", [DT, 128, NT], BF16, "ExternalOutput")
    K_s = P.dram("K_s", [DT, 128, NT], BF16, "ExternalOutput")
    KT_s = P.dram("KT_s", [NT // 128, 128, D], BF16, "ExternalOutput")
    VT_s = P.dram("VT_s", [NT // 128, 128, D], BF16, "ExternalOutput")

    ones_f = P.sbuf("ones_f", [128, 1], F32)
    ones1_f = P.sbuf("ones1_f", [1, 128], F32)
    P.c_eps = P.sbuf("c_eps", [1, 8], F32)
    lnc = P.sbuf("lnc", [128, 2 * DT], F32)
    P.load(ones_f[:, :], ones[:, :])
    P.load(ones1_f[:, :], ones1[:, :])
    P.load(P.c_eps[0:1, :], misc[0:1, :])
    P.load(lnc[:, :], lncol[:, :])
    wscr = P.sbuf("wscr", [128, 1536], F32)
    wb = [P.sbuf(f"wb{k}", [128, 3072], BF16) for k in range(DT)]
    def wload(dst, base, src_ap, ncol):
        tw = P.load(wscr[:, 0:ncol], src_ap, wait=[(P.sv, None)])
        tvv = P.sv.bump(1)
        @P.on("vector")
        def _(e, dst=dst, base=base, tw=tw, ncol=ncol):
            e.wait_ge(P.sd.h, tw)
            e.tensor_copy(dst[:, base:base + ncol], wscr[:, 0:ncol]).then_inc(P.sv.h, 1)
    for k in range(DT):
        wload(wb[k], 0, wqkvT[k][:, 0:1536], 1536)
        wload(wb[k], 1536, wqkvT[k][:, 1536:3072], 1536)

    xt = [P.sbuf(f"xt{k}", [128, TCH], F32) for k in range(DT)]
    xld = [P.sbuf(f"xld{k}", [128, TCH], BF16) for k in range(DT)]
    sq = [P.sbuf(f"sq{k}", [128, TCH], F32) for k in range(DT)]
    xnb = [P.sbuf(f"xnb{k}", [128, TCH], BF16) for k in range(DT)]
    rows = (P.sbuf("r_mean", [1, TCH], F32), P.sbuf("r_var", [1, TCH], F32), P.sbuf("r_tmp", [1, TCH], F32))
    qkt = [P.sbuf(f"qkt{i}", [128, TCH], BF16) for i in range(2)]
    tmt = [P.sbuf(f"tmt{i}", [128, TCH], BF16) for i in range(2)]
    PST = P.psum("PST", [128, TCH])
    PBC = P.psum("PBC", [128, TCH])
    PM0 = P.psum("PM0", [128, TCH])
    PM1 = P.psum("PM1", [128, TCH])

    for ci in range(NCH):
        off = ci * TCH
        tld = None
        for k in range(DT):
            tld = P.load(xld[k][:, :], xn_s[k, :, off:off + TCH],
                         wait=[(P.st, None), (P.sv, None)] if k == 0 else ())
        P.barrier_engine("vector", [(P.sd, tld), (P.sg, None), (P.st, None), (P.ss, None)])
        tcv = P.sv.bump(1)
        @P.on("vector")
        def _(e):
            for k in range(DT):
                r = e.tensor_copy(xt[k][:, :], xld[k][:, :])
            r.then_inc(P.sv.h, 1)
        P.barrier_engine("tensor", [(P.sv, tcv)])
        P.barrier_engine("scalar", [(P.sv, tcv)])
        P.barrier_engine("vector", [(P.sv, tcv)])
        t_xn = emit_ln_chunk(P, xt, xnb, sq, rows, ones_f, ones1_f, PST, PBC, TCH,
                             gb=(lnc[:, 0:DT], lnc[:, DT:2 * DT]))
        P.barrier_engine("tensor", [(P.sv, t_xn)])
        for (dstD, fbase) in ((Q_s, 0), (K_s, 1024)):
            for ot in range(DT):
                pb = [PM0, PM1][ot % 2]
                buf = qkt[ot % 2]
                svn = P.sv.n
                sgn = P.sg.n
                tgt = P.st.bump(1)
                @P.on("tensor")
                def _(e, pb=pb, ot=ot, fbase=fbase, svn=svn):
                    e.wait_ge(P.sv.h, svn)
                    for k in range(DT):
                        r = e.matmul(pb[:, 0:TCH], wb[k][:, fbase + ot * 128:fbase + (ot + 1) * 128], xnb[k][:, 0:TCH], start=(k == 0), stop=(k == DT - 1))
                    r.then_inc(P.st.h, 1)
                tvx = P.sv.bump(1)
                @P.on("vector")
                def _(e, pb=pb, buf=buf, tgt=tgt, sgn=sgn):
                    e.wait_ge(P.st.h, tgt)
                    e.wait_ge(P.sg.h, sgn)
                    e.tensor_copy(buf[:, :], pb[:, 0:TCH]).then_inc(P.sv.h, 1)
                P.store(dstD[ot, :, off:off + TCH], buf[:, :], wait=[(P.sv, tvx)])
        for (dstD, fbase) in ((KT_s, 1024), (VT_s, 2048)):
            for sub in range(TCH // 128):
                tt = off // 128 + sub
                for fc in range(2):
                    pb = [PM0, PM1][fc % 2]
                    buf = tmt[fc % 2]
                    svn = P.sv.n
                    sgn = P.sg.n
                    tgt = P.st.bump(1)
                    @P.on("tensor")
                    def _(e, pb=pb, sub=sub, fc=fc, fbase=fbase, svn=svn):
                        e.wait_ge(P.sv.h, svn)
                        for k in range(DT):
                            r = e.matmul(pb[:, 0:TCH], xnb[k][:, sub * 128:(sub + 1) * 128], wb[k][:, fbase + fc * 512:fbase + (fc + 1) * 512], start=(k == 0), stop=(k == DT - 1))
                        r.then_inc(P.st.h, 1)
                    tvv = P.sv.bump(1)
                    @P.on("vector")
                    def _(e, pb=pb, buf=buf, tgt=tgt, sgn=sgn):
                        e.wait_ge(P.st.h, tgt)
                        e.wait_ge(P.sg.h, sgn)
                        e.tensor_copy(buf[:, 0:TCH], pb[:, 0:TCH]).then_inc(P.sv.h, 1)
                    P.store(dstD[tt, :, fc * 512:(fc + 1) * 512], buf[:, 0:TCH], wait=[(P.sv, tvv)])
    return P.finalize()


def build_l0a3():
    P = Prog("l0a3")
    Q_sD = P.dram("Q_s", [DT, 128, NT], BF16, "ExternalInput")
    K_sD = P.dram("K_s", [DT, 128, NT], BF16, "ExternalInput")
    KT_sD = P.dram("KT_s", [NT // 128, 128, D], BF16, "ExternalInput")
    VT_sD = P.dram("VT_s", [NT // 128, 128, D], BF16, "ExternalInput")
    maskt = P.dram("c_mask", [128, 1536], F32, "ExternalInput")
    y_s = P.dram("y_s", [DT, 128, NT], BF16, "ExternalOutput")

    maskb = P.sbuf("maskb", [128, 128], BF16)
    mscr = P.sbuf("mscr", [128, 128], F32)
    tmk = P.load(mscr[:, :], maskt[:, 512:640])
    tvm = P.sv.bump(1)
    @P.on("vector")
    def _(e, tmk=tmk):
        e.wait_ge(P.sd.h, tmk)
        e.tensor_copy(maskb[:, :], mscr[:, :])
        e.memset(MTs[:, :], 0.0)
        e.memset(MTf[:, :], 0.0).then_inc(P.sv.h, 1)
    Qb = [P.sbuf(f"Qb{k}", [128, S], BF16) for k in range(DT)]
    Kb = [P.sbuf(f"Kb{k}", [128, S], BF16) for k in range(DT)]
    KTb = [P.sbuf(f"KTb{c}", [128, D], BF16) for c in range(16)]
    VTb = [P.sbuf(f"VTb{c}", [128, D], BF16) for c in range(16)]
    sm = [P.sbuf(f"sm{i}", [128, 128], BF16) for i in range(2)]
    MTs = P.sbuf("MTs", [128, 64], BF16)
    MTf = P.sbuf("MTf", [128, 64], F32)
    yh = P.sbuf("yh", [64, S], BF16)
    PS = P.psum("PS", [128, 128])
    PY = P.psum("PY", [128, 128])
    PM = P.psum("PM", [128, 64])

    for b in range(BL):
        tld = None
        svn = P.sv.n
        stn = P.st.n
        for k in range(DT):
            tld = P.load(Qb[k][:, :], Q_sD[k, :, b * S:(b + 1) * S],
                         wait=[(P.st, stn), (P.sv, svn), (P.ss, None), (P.sg, None)] if k == 0 else ())
            tld = P.load(Kb[k][:, :], K_sD[k, :, b * S:(b + 1) * S])
        for c in range(16):
            tld = P.load(KTb[c][:, :], KT_sD[b * 16 + c])
            tld = P.load(VTb[c][:, :], VT_sD[b * 16 + c])
        P.barrier_engine("tensor", [(P.sd, tld), (P.sv, None)])
        for h in range(16):
            ht, hh = h // 2, (h % 2) * 64
            for c in range(16):
                smb = sm[c % 2]
                svn2 = P.sv.n
                ssn2 = P.ss.n
                tS = P.st.bump(1)
                @P.on("tensor")
                def _(e, c=c, ht=ht, hh=hh, svn2=svn2):
                    e.wait_ge(P.sv.h, svn2)
                    e.matmul(PS[:, 0:128], Kb[ht][hh:hh + 64, c * 128:(c + 1) * 128], Qb[ht][hh:hh + 64, c * 128:(c + 1) * 128], start=True, stop=True).then_inc(P.st.h, 1)
                tmsk = P.sv.bump(1)
                @P.on("vector")
                def _(e, c=c, smb=smb, tS=tS):
                    e.wait_ge(P.st.h, tS)
                    e.tensor_tensor(smb[:, :], PS[:, 0:128], maskb[:, :], ALU.mult).then_inc(P.sv.h, 1)
                svn3 = P.sv.n
                tY = P.st.bump(1)
                @P.on("tensor")
                def _(e, c=c, h=h, ht=ht, hh=hh, smb=smb, tmsk=tmsk, ssn2=ssn2, svn3=svn3):
                    e.wait_ge(P.sv.h, tmsk)
                    e.wait_ge(P.ss.h, ssn2)
                    e.wait_ge(P.sv.h, svn3)
                    e.matmul(PY[0:64, 0:128], MTs[hh:hh + 64, :], Qb[ht][hh:hh + 64, c * 128:(c + 1) * 128], start=True, stop=False)
                    e.matmul(PY[0:64, 0:128], VTb[c][:, 64 * h:64 * h + 64], smb[:, :], start=False, stop=True)
                    r = e.matmul(PM[0:64, :], KTb[c][:, 64 * h:64 * h + 64], VTb[c][:, 64 * h:64 * h + 64], start=True, stop=True)
                    r.then_inc(P.st.h, 1)
                tev = P.ss.bump(1)
                @P.on("scalar")
                def _(e, c=c, tY=tY):
                    e.wait_ge(P.st.h, tY)
                    e.activation(yh[:, c * 128:(c + 1) * 128], PY[0:64, 0:128], AF.Copy).then_inc(P.ss.h, 1)
                tvM0 = P.sv.bump(1)
                @P.on("vector")
                def _(e, hh=hh, tY=tY):
                    e.wait_ge(P.st.h, tY)
                    e.tensor_tensor(MTf[hh:hh + 64, :], MTf[hh:hh + 64, :], PM[0:64, :], ALU.add).then_inc(P.sv.h, 1)
                tvM = P.sv.bump(1)
                @P.on("vector")
                def _(e, hh=hh, tvM0=tvM0):
                    e.wait_ge(P.sv.h, tvM0)
                    e.tensor_copy(MTs[hh:hh + 64, :], MTf[hh:hh + 64, :]).then_inc(P.sv.h, 1)
            sgn = P.sg.n
            ssn3 = P.ss.n
            for kt in range(16):
                P.store(y_s[h // 2, (h % 2) * 64:(h % 2) * 64 + 64, b * S + kt * 128:b * S + (kt + 1) * 128],
                        yh[:, kt * 128:(kt + 1) * 128], wait=[(P.ss, ssn3)] if kt == 0 else ())
            tvz = P.sv.bump(1)
            @P.on("vector")
            def _(e, ssn3=ssn3):
                e.wait_ge(P.ss.h, ssn3)
                e.memset(MTs[:, :], 0.0)
                e.memset(MTf[:, :], 0.0).then_inc(P.sv.h, 1)
            P.barrier_engine("scalar", [(P.sv, tvz), (P.sg, None)])
            P.barrier_engine("tensor", [(P.sv, tvz)])
    return P.finalize()


def build_l0a4():
    P = Prog("l0a4")
    xn_s = P.dram("xnf_s", [DT, 128, NT], F32, "ExternalInput")
    y_sD = P.dram("y_s", [DT, 128, NT], BF16, "ExternalInput")
    woT = P.dram("l0_woT", [DT, 128, D], F32, "ExternalInput")
    lncol = P.dram("l0_ln2", [128, 2 * DT], F32, "ExternalInput")
    ones = P.dram("c_ones", [128, 1], F32, "ExternalInput")
    ones1 = P.dram("c_ones1", [1, 128], F32, "ExternalInput")
    misc = P.dram("c_misc", [1, 8], F32, "ExternalInput")
    s1_s = P.dram("s1_s", [DT, 128, NT], F32, "ExternalOutput")
    h_s = P.dram("h_s", [DT, 128, NT], BF16, "ExternalOutput")

    ones_f = P.sbuf("ones_f", [128, 1], F32)
    ones1_f = P.sbuf("ones1_f", [1, 128], F32)
    P.c_eps = P.sbuf("c_eps", [1, 8], F32)
    lnc = P.sbuf("lnc", [128, 2 * DT], F32)
    P.load(ones_f[:, :], ones[:, :])
    P.load(ones1_f[:, :], ones1[:, :])
    P.load(P.c_eps[0:1, :], misc[0:1, :])
    P.load(lnc[:, :], lncol[:, :])
    wscr = P.sbuf("wscr", [128, 1024], F32)
    wob = [P.sbuf(f"wob{k}", [128, D], BF16) for k in range(DT)]
    def wload(dst, src_ap, ncol):
        tw = P.load(wscr[:, 0:ncol], src_ap, wait=[(P.sv, None)])
        tvv = P.sv.bump(1)
        @P.on("vector")
        def _(e, dst=dst, tw=tw, ncol=ncol):
            e.wait_ge(P.sd.h, tw)
            e.tensor_copy(dst[:, 0:ncol], wscr[:, 0:ncol]).then_inc(P.sv.h, 1)
    for k in range(DT):
        wload(wob[k], woT[k], D)

    yb = [P.sbuf(f"yb{k}", [128, TCH], BF16) for k in range(DT)]
    xnb = [P.sbuf(f"xnl{k}", [128, TCH], F32) for k in range(DT)]
    s1t = [P.sbuf(f"s1t{k}", [128, TCH], F32) for k in range(DT)]
    sq = [P.sbuf(f"sq{k}", [128, TCH], F32) for k in range(DT)]
    hb = [P.sbuf(f"hbt{k}", [128, TCH], BF16) for k in range(DT)]
    rows = (P.sbuf("r_mean", [1, TCH], F32), P.sbuf("r_var", [1, TCH], F32), P.sbuf("r_tmp", [1, TCH], F32))
    PST = P.psum("PST", [128, TCH])
    PBC = P.psum("PBC", [128, TCH])
    PM0 = P.psum("PM0", [128, TCH])
    PM1 = P.psum("PM1", [128, TCH])

    for ci in range(NCH):
        off = ci * TCH
        tld = None
        for k in range(DT):
            tld = P.load(yb[k][:, :], y_sD[k, :, off:off + TCH],
                         wait=[(P.st, None), (P.sv, None)] if k == 0 else ())
            tld = P.load(xnb[k][:, :], xn_s[k, :, off:off + TCH])
        P.barrier_engine("tensor", [(P.sd, tld), (P.sv, None)])
        dfree = [None, None]
        t_s1 = None
        for ot in range(DT):
            pb = [PM0, PM1][ot % 2]
            df = dfree[ot % 2]
            tgt = P.st.bump(1)
            @P.on("tensor")
            def _(e, ot=ot, pb=pb, df=df):
                if df is not None:
                    e.wait_ge(df[0].h, df[1])
                for k in range(DT):
                    r = e.matmul(pb[:, 0:TCH], wob[k][:, ot * 128:(ot + 1) * 128], yb[k][:, 0:TCH], start=(k == 0), stop=(k == DT - 1))
                r.then_inc(P.st.h, 1)
            t_s1 = P.sv.bump(1)
            @P.on("vector")
            def _(e, ot=ot, pb=pb, tgt=tgt):
                e.wait_ge(P.st.h, tgt)
                e.tensor_tensor(s1t[ot][:, :], pb[:, 0:TCH], xnb[ot][:, :], ALU.add).then_inc(P.sv.h, 1)
            P.store(s1_s[ot, :, off:off + TCH], s1t[ot][:, :], wait=[(P.sv, t_s1)])
            dfree[ot % 2] = (P.sv, t_s1)
        P.barrier_engine("tensor", [(P.sv, t_s1)])
        P.barrier_engine("scalar", [(P.sv, t_s1)])
        P.barrier_engine("vector", [(P.sv, t_s1)])
        t_h = emit_ln_chunk(P, s1t, hb, sq, rows, ones_f, ones1_f, PST, PBC, TCH,
                            gb=(lnc[:, 0:DT], lnc[:, DT:2 * DT]))
        for k in range(DT):
            P.store(h_s[k, :, off:off + TCH], hb[k][:, :], wait=[(P.sv, t_h)] if k == 0 else ())
    return P.finalize()


def build_l0a5():
    P = Prog("l0a5")
    h_sD = P.dram("h_s", [DT, 128, NT], BF16, "ExternalInput")
    s1_sD = P.dram("s1_s", [DT, 128, NT], F32, "ExternalInput")
    w1D = P.dram("l0_cw1T", [3, DT, 128, 4096], F32, "ExternalInput")
    b1D = P.dram("l0_cb1", [3, 1, 4096], F32, "ExternalInput")
    w2D = P.dram("l0_cw2T", [3, 32, 128, D], F32, "ExternalInput")
    b2D = P.dram("l0_cb2", [3, 1, D], F32, "ExternalInput")
    onesr = P.dram("c_onesr", [1, 512], F32, "ExternalInput")
    sA_s = P.dram("sA_s", [DT, 128, NT], F32, "ExternalOutput")
    hx_s = P.dram("hx_s", [DT, 128, NT], BF16, "Internal")

    onesr_f = P.sbuf("onesr_f", [1, 512], F32)
    P.load(onesr_f[:, :], onesr[:, :])
    wscr = P.sbuf("wscr", [128, 2048], F32)
    w1b = [P.sbuf(f"w1b{k}", [128, 4096], BF16) for k in range(DT)]
    w2b = [P.sbuf(f"w2b{k}", [128, D], BF16) for k in range(32)]
    b1r = P.sbuf("b1r", [1, 4096], F32)
    b2r = P.sbuf("b2r", [1, D], F32)
    hin = [P.sbuf(f"hin{k}", [128, TCH], BF16) for k in range(DT)]
    hmid = [P.sbuf(f"hmid{k}", [128, TCH], BF16) for k in range(32)]
    s1t = P.sbuf("s1t", [128, TCH], F32)
    hot = P.sbuf("hot", [128, TCH], BF16)
    sot = P.sbuf("sot", [128, TCH], F32)
    PM0 = P.psum("PM0", [128, TCH])
    PM1 = P.psum("PM1", [128, TCH])
    PM2 = P.psum("PM2", [128, TCH])
    PM3 = P.psum("PM3", [128, TCH])

    def wload(dst, src_ap, ncol, base=0):
        tw = P.load(wscr[:, 0:ncol], src_ap, wait=[(P.sv, None), (P.st, None)])
        tvv = P.sv.bump(1)
        @P.on("vector")
        def _(e, dst=dst, tw=tw, ncol=ncol, base=base):
            e.wait_ge(P.sd.h, tw)
            e.tensor_copy(dst[:, base:base + ncol], wscr[:, 0:ncol]).then_inc(P.sv.h, 1)

    for lev in range(3):
        for k in range(DT):
            wload(w1b[k], w1D[lev, k][:, 0:2048], 2048, 0)
            wload(w1b[k], w1D[lev, k][:, 2048:4096], 2048, 2048)
        for k in range(32):
            wload(w2b[k], w2D[lev, k], D)
        tb = P.load(b1r[0:1, :], b1D[lev], wait=[(P.st, None)])
        tb = P.load(b2r[0:1, :], b2D[lev])
        P.barrier_engine("tensor", [(P.sd, tb), (P.sv, None)])
        inD = h_sD if lev == 0 else hx_s
        for ci in range(NCH):
            off = ci * TCH
            tld = None
            for k in range(DT):
                tld = P.load(hin[k][:, :], inD[k, :, off:off + TCH],
                             wait=[(P.st, None), (P.sv, None)] if k == 0 else ())
            P.barrier_engine("tensor", [(P.sd, tld), (P.sv, None), (P.ss, None)])
            gfree = [None, None]
            for ot in range(32):
                pb = [PM0, PM1][ot % 2]
                gf = gfree[ot % 2]
                tgt = P.st.bump(1)
                @P.on("tensor")
                def _(e, ot=ot, pb=pb, gf=gf):
                    if gf is not None:
                        e.wait_ge(gf[0].h, gf[1])
                    e.matmul(pb[:, 0:TCH], b1r[0:1, ot * 128:(ot + 1) * 128], onesr_f[0:1, 0:TCH], start=True, stop=False)
                    for k in range(DT):
                        r = e.matmul(pb[:, 0:TCH], w1b[k][:, ot * 128:(ot + 1) * 128], hin[k][:, 0:TCH], start=False, stop=(k == DT - 1))
                    r.then_inc(P.st.h, 1)
                tsg = P.ss.bump(1)
                @P.on("scalar")
                def _(e, ot=ot, pb=pb, tgt=tgt):
                    e.wait_ge(P.st.h, tgt)
                    e.activation(hmid[ot][:, :], pb[:, 0:TCH], AF.Gelu_apprx_tanh).then_inc(P.ss.h, 1)
                gfree[ot % 2] = (P.ss, tsg)
            P.barrier_engine("tensor", [(P.ss, None)])
            dfree = [None, None]
            for ot in range(DT):
                pb = [PM2, PM3][ot % 2]
                df = dfree[ot % 2]
                tgt = P.st.bump(1)
                @P.on("tensor")
                def _(e, ot=ot, pb=pb, df=df):
                    if df is not None:
                        e.wait_ge(df[0].h, df[1])
                    e.matmul(pb[:, 0:TCH], b2r[0:1, ot * 128:(ot + 1) * 128], onesr_f[0:1, 0:TCH], start=True, stop=False)
                    for k in range(32):
                        r = e.matmul(pb[:, 0:TCH], w2b[k][:, ot * 128:(ot + 1) * 128], hmid[k][:, 0:TCH], start=False, stop=(k == 31))
                    r.then_inc(P.st.h, 1)
                if lev < 2:
                    sgn = P.sg.n
                    tcv = P.sv.bump(1)
                    @P.on("vector")
                    def _(e, ot=ot, pb=pb, tgt=tgt, sgn=sgn):
                        e.wait_ge(P.st.h, tgt)
                        e.wait_ge(P.sg.h, sgn)
                        e.tensor_copy(hot[:, :], pb[:, 0:TCH]).then_inc(P.sv.h, 1)
                    P.store(hx_s[ot, :, off:off + TCH], hot[:, :], wait=[(P.sv, tcv)])
                    dfree[ot % 2] = (P.sv, tcv)
                else:
                    ts1 = P.load(s1t[:, :], s1_sD[ot, :, off:off + TCH], wait=[(P.sv, None)])
                    sgn = P.sg.n
                    tcv = P.sv.bump(1)
                    @P.on("vector")
                    def _(e, ot=ot, pb=pb, tgt=tgt, ts1=ts1, sgn=sgn):
                        e.wait_ge(P.st.h, tgt)
                        e.wait_ge(P.sd.h, ts1)
                        e.wait_ge(P.sg.h, sgn)
                        e.tensor_tensor(sot[:, :], pb[:, 0:TCH], s1t[:, :], ALU.add).then_inc(P.sv.h, 1)
                    P.store(sA_s[ot, :, off:off + TCH], sot[:, :], wait=[(P.sv, tcv)])
                    dfree[ot % 2] = (P.sv, tcv)
    return P.finalize()


def build_l0f1():
    P = Prog("l0f1")
    xc_in = P.dram("xc_in", [DT, 128, NT], F32, "ExternalInput")
    wqkvT = P.dram("f_wqkvT", [DT, 128, 1536], F32, "ExternalInput")
    qs2D = P.dram("f_qs2", [128, 2], F32, "ExternalInput")
    rTD = P.dram("c_rT", [128, 128], F32, "ExternalInput")
    xc1D = P.dram("c_xc1", [128, S], F32, "ExternalInput")
    xs1D = P.dram("c_xs1", [128, S], F32, "ExternalInput")
    xc2D = P.dram("c_xc2", [128, S], F32, "ExternalInput")
    xs2D = P.dram("c_xs2", [128, S], F32, "ExternalInput")
    ones = P.dram("c_ones", [128, 1], F32, "ExternalInput")
    ones1 = P.dram("c_ones1", [1, 128], F32, "ExternalInput")
    Q_s = P.dram("Q_s", [4, 128, NT], BF16, "ExternalOutput")
    K_s = P.dram("K_s", [4, 128, NT], BF16, "ExternalOutput")
    VT_s = P.dram("VT_s", [NT // 128, 128, 8 * 65], BF16, "ExternalOutput")

    ones_f = P.sbuf("ones_f", [128, 1], F32)
    ones1_f = P.sbuf("ones1_f", [1, 128], F32)
    P.load(ones_f[:, :], ones[:, :])
    P.load(ones1_f[:, :], ones1[:, :])
    qs2 = P.sbuf("qs2", [128, 2], F32)
    P.load(qs2[:, :], qs2D[:, :])
    wscr = P.sbuf("wscr", [128, 2048], F32)
    wb = [P.sbuf(f"wb{k}", [128, 1536], BF16) for k in range(DT)]
    rqT = P.sbuf("rqT", [128, 128], BF16)
    rkT = P.sbuf("rkT", [128, 128], BF16)
    tabs = {k: P.sbuf(f"tab{k}", [128, S], BF16) for k in ("c1", "s1", "c2", "s2")}
    def wload_f(dst, src_ap, ncol, fn=None):
        tw = P.load(wscr[:, 0:ncol], src_ap, wait=[(P.sv, None)])
        tvv = P.sv.bump(1)
        @P.on("vector")
        def _(e, dst=dst, tw=tw, ncol=ncol, fn=fn):
            e.wait_ge(P.sd.h, tw)
            if fn is None:
                e.tensor_copy(dst[:, 0:ncol], wscr[:, 0:ncol]).then_inc(P.sv.h, 1)
            else:
                fn(e).then_inc(P.sv.h, 1)
    for k in range(DT):
        wload_f(wb[k], wqkvT[k], 1536)
    wload_f(rqT, rTD[:, :], 128, lambda e: e.tensor_single_scalar(rqT[:, :], wscr[:, 0:128], qs2[:, 0:1], ALU.mult))
    wload_f(rkT, rTD[:, :], 128, lambda e: e.tensor_single_scalar(rkT[:, :], wscr[:, 0:128], qs2[:, 1:2], ALU.mult))
    for half in range(4):
        nm, src = [("c1", xc1D), ("s1", xs1D), ("c2", xc2D), ("s2", xs2D)][half]
        for hc in range(2):
            sl = slice(hc * 1024, (hc + 1) * 1024)
            if nm in ("c1",):
                wload_f(tabs[nm], src[:, sl], 1024, (lambda e, nm=nm, sl=sl: e.tensor_single_scalar(tabs[nm][:, sl], wscr[:, 0:1024], qs2[:, 0:1], ALU.mult)))
            elif nm in ("c2",):
                wload_f(tabs[nm], src[:, sl], 1024, (lambda e, nm=nm, sl=sl: e.tensor_single_scalar(tabs[nm][:, sl], wscr[:, 0:1024], qs2[:, 1:2], ALU.mult)))
            else:
                wload_f(tabs[nm], src[:, sl], 1024, (lambda e, nm=nm, sl=sl: e.tensor_copy(tabs[nm][:, sl], wscr[:, 0:1024])))

    xt = [P.sbuf(f"xt{k}", [128, TCH + 1], F32) for k in range(DT)]
    hsh = [P.sbuf(f"hsh{k}", [128, TCH], BF16) for k in range(DT)]
    qn = [P.sbuf(f"qn{i}", [128, TCH], BF16) for i in range(4)]
    kn = [P.sbuf(f"kn{i}", [128, TCH], BF16) for i in range(4)]
    sqx = P.sbuf("sqx", [128, TCH], F32)
    nr1 = P.sbuf("nr1", [1, TCH], F32)
    nr2 = P.sbuf("nr2", [1, TCH], F32)
    t1b = P.sbuf("t1b", [128, TCH], BF16)
    t2b = P.sbuf("t2b", [128, TCH], BF16)
    vb = P.sbuf("vb", [128, 528], BF16)
    PM0 = P.psum("PM0", [128, TCH])
    PM1 = P.psum("PM1", [128, TCH])
    PRR = P.psum("PRR", [128, TCH])
    PB1 = P.psum("PB1", [128, TCH])

    for ci in range(NCH):
        off = ci * TCH
        bstart = (off % S == 0)
        tld = None
        for k in range(DT):
            if k < 4:
                tld = P.load(xt[k][:, 1:1 + TCH], xc_in[k, :, off:off + TCH],
                             wait=[(P.st, None), (P.sv, None), (P.ss, None)] if k == 0 else ())
            else:
                if bstart:
                    tld = P.load(xt[k][:, 1:1 + TCH], xc_in[k, :, off:off + TCH])
                else:
                    tld = P.load(xt[k][:, 0:1 + TCH], xc_in[k, :, off - 1:off + TCH])
        tsh = P.sv.bump(1)
        @P.on("vector")
        def _(e, bstart=bstart, tld=tld):
            e.wait_ge(P.sd.h, tld)
            r = None
            for k in range(DT):
                if k < 4:
                    r = e.tensor_copy(hsh[k][:, :], xt[k][:, 1:1 + TCH])
                else:
                    if bstart:
                        e.memset(hsh[k][:, 0:1], 0.0)
                        r = e.tensor_copy(hsh[k][:, 1:TCH], xt[k][:, 1:TCH])
                    else:
                        r = e.tensor_copy(hsh[k][:, :], xt[k][:, 0:TCH])
            r.then_inc(P.sv.h, 1)
        P.barrier_engine("tensor", [(P.sv, tsh)])
        for ot in range(8):
            pb = [PM0, PM1][ot % 2]
            dst = qn[ot] if ot < 4 else kn[ot - 4]
            svn = P.sv.n
            tgt = P.st.bump(1)
            @P.on("tensor")
            def _(e, ot=ot, pb=pb, svn=svn):
                e.wait_ge(P.sv.h, svn)
                for k in range(DT):
                    r = e.matmul(pb[:, 0:TCH], wb[k][:, ot * 128:(ot + 1) * 128], hsh[k][:, 0:TCH], start=(k == 0), stop=(k == DT - 1))
                r.then_inc(P.st.h, 1)
            sgn = P.sg.n
            tvx = P.sv.bump(1)
            @P.on("vector")
            def _(e, pb=pb, dst=dst, tgt=tgt, sgn=sgn):
                e.wait_ge(P.st.h, tgt)
                e.wait_ge(P.sg.h, sgn)
                e.tensor_copy(dst[:, :], pb[:, 0:TCH]).then_inc(P.sv.h, 1)
        for (tiles, rT, ctab, stab, outD) in ((qn, rqT, "c1", "s1", Q_s), (kn, rkT, "c2", "s2", K_s)):
            for tt in range(4):
                svn = P.sv.n
                tsq = P.ss.bump(1)
                @P.on("scalar")
                def _(e, tiles=tiles, tt=tt, svn=svn):
                    e.wait_ge(P.sv.h, svn)
                    e.activation(sqx[:, :], tiles[tt][:, :], AF.Square).then_inc(P.ss.h, 1)
                tnm = P.st.bump(1)
                @P.on("tensor")
                def _(e, tsq=tsq):
                    e.wait_ge(P.ss.h, tsq)
                    e.matmul(PRR[0:1, 0:TCH], ones_f[0:64, 0:1], sqx[0:64, :], start=True, stop=True)
                    e.matmul(PB1[0:1, 0:TCH], ones_f[64:128, 0:1], sqx[64:128, :], start=True, stop=True).then_inc(P.st.h, 1)
                trw = P.ss.bump(1)
                @P.on("scalar")
                def _(e, tnm=tnm):
                    e.wait_ge(P.st.h, tnm)
                    e.activation(nr1[0:1, :], PRR[0:1, 0:TCH], AF.Sqrt)
                    e.activation(nr2[0:1, :], PB1[0:1, 0:TCH], AF.Sqrt).then_inc(P.ss.h, 1)
                trc = P.sv.bump(1)
                @P.on("vector")
                def _(e, trw=trw):
                    e.wait_ge(P.ss.h, trw)
                    e.tensor_scalar_max(nr1[0:1, :], nr1[0:1, :], 1e-12)
                    e.tensor_scalar_max(nr2[0:1, :], nr2[0:1, :], 1e-12)
                    e.reciprocal(nr1[0:1, :], nr1[0:1, :])
                    e.reciprocal(nr2[0:1, :], nr2[0:1, :]).then_inc(P.sv.h, 1)
                tbc = P.st.bump(1)
                @P.on("tensor")
                def _(e, trc=trc):
                    e.wait_ge(P.sv.h, trc)
                    e.matmul(PRR[:, 0:TCH], ones1_f[0:1, :], nr1[0:1, 0:TCH], start=True, stop=True)
                    e.matmul(PB1[:, 0:TCH], ones1_f[0:1, :], nr2[0:1, 0:TCH], start=True, stop=True).then_inc(P.st.h, 1)
                tap = P.sv.bump(1)
                @P.on("vector")
                def _(e, tiles=tiles, tt=tt, tbc=tbc):
                    e.wait_ge(P.st.h, tbc)
                    e.tensor_tensor(tiles[tt][0:64, :], tiles[tt][0:64, :], PRR[0:64, 0:TCH], ALU.mult)
                    e.tensor_tensor(tiles[tt][64:128, :], tiles[tt][64:128, :], PB1[64:128, 0:TCH], ALU.mult).then_inc(P.sv.h, 1)
                trm = P.st.bump(1)
                @P.on("tensor")
                def _(e, tiles=tiles, tt=tt, rT=rT, tap=tap):
                    e.wait_ge(P.sv.h, tap)
                    e.matmul(PM0[:, 0:TCH], rT[:, :], tiles[tt][:, 0:TCH], start=True, stop=True).then_inc(P.st.h, 1)
                tcm = P.sv.bump(1)
                @P.on("vector")
                def _(e, tiles=tiles, tt=tt, ctab=ctab, stab=stab, trm=trm, off=off):
                    e.wait_ge(P.st.h, trm)
                    toff = off % S
                    e.tensor_tensor(t1b[:, :], tiles[tt][:, :], tabs[ctab][:, toff:toff + TCH], ALU.mult)
                    e.tensor_tensor(t2b[:, :], PM0[:, 0:TCH], tabs[stab][:, toff:toff + TCH], ALU.mult).then_inc(P.sv.h, 1)
                tcm2 = P.sv.bump(1)
                @P.on("vector")
                def _(e, tiles=tiles, tt=tt, tcm=tcm):
                    e.wait_ge(P.sv.h, tcm)
                    e.tensor_tensor(tiles[tt][:, :], t1b[:, :], t2b[:, :], ALU.add).then_inc(P.sv.h, 1)
                P.store(outD[tt, :, off:off + TCH], tiles[tt][:, :], wait=[(P.sv, tcm2)])
        for sub in range(TCH // 128):
            tt2 = off // 128 + sub
            svn = P.sv.n
            sgn = P.sg.n
            tgt = P.st.bump(1)
            @P.on("tensor")
            def _(e, sub=sub, svn=svn):
                e.wait_ge(P.sv.h, svn)
                for k in range(DT):
                    r = e.matmul(PM1[:, 0:512], hsh[k][:, sub * 128:(sub + 1) * 128], wb[k][:, 1024:1536], start=(k == 0), stop=(k == DT - 1))
                r.then_inc(P.st.h, 1)
            tvv = P.sv.bump(1)
            @P.on("vector")
            def _(e, tgt=tgt, sgn=sgn):
                e.wait_ge(P.st.h, tgt)
                e.wait_ge(P.sg.h, sgn)
                vv = vb[:, 0:520].rearrange("p (h c) -> p h c", c=65)
                e.memset(vv[:, :, 64:65], 1.0)
                e.tensor_copy(vv[:, :, 0:64], PM1[:, 0:512].rearrange("p (h c) -> p h c", c=64)).then_inc(P.sv.h, 1)
            P.store(VT_s[tt2, :, :], vb[:, 0:520], wait=[(P.sv, tvv)])
    return P.finalize()


def build_l0f2():
    P = Prog("l0f2")
    xc_in = P.dram("xc_in", [DT, 128, NT], F32, "ExternalInput")
    Q_sD = P.dram("Q_s", [4, 128, NT], BF16, "ExternalInput")
    K_sD = P.dram("K_s", [4, 128, NT], BF16, "ExternalInput")
    VT_sD = P.dram("VT_s", [NT // 128, 128, 8 * 65], BF16, "ExternalInput")
    woTh = P.dram("f_woTh", [8, 64, D], F32, "ExternalInput")
    mask2 = P.dram("c_mask2", [128, 1536], F32, "ExternalInput")
    ones1 = P.dram("c_ones1", [1, 128], F32, "ExternalInput")
    ccols = P.dram("c_cols", [128, 4], F32, "ExternalInput")
    xmid_s = P.dram("xmid_s", [DT, 128, NT], F32, "ExternalOutput")

    ones1_f = P.sbuf("ones1_f", [1, 128], F32)
    ccol_f = P.sbuf("ccol_f", [128, 4], F32)
    maskb = P.sbuf("maskb", [128, 1536], BF16)
    P.load(ones1_f[:, :], ones1[:, :])
    P.load(ccol_f[:, :], ccols[:, :])
    wscr = P.sbuf("wscr", [128, 1536], F32)
    wob = [P.sbuf(f"wob{h}", [64, D], BF16) for h in range(8)]
    def wload(dst, src_ap, ncol, npart=128):
        tw = P.load(wscr[0:npart, 0:ncol], src_ap, wait=[(P.sv, None)])
        tvv = P.sv.bump(1)
        @P.on("vector")
        def _(e, dst=dst, tw=tw, ncol=ncol, npart=npart):
            e.wait_ge(P.sd.h, tw)
            e.tensor_copy(dst[0:npart, 0:ncol], wscr[0:npart, 0:ncol]).then_inc(P.sv.h, 1)
    wload(maskb, mask2[:, :], 1536)
    for h in range(8):
        wload(wob[h], woTh[h], D, npart=64)

    Qb = [P.sbuf(f"Qb{k}", [128, S], BF16) for k in range(4)]
    Kb = [P.sbuf(f"Kb{k}", [128, S], BF16) for k in range(4)]
    VTe = [P.sbuf(f"VTe{c}", [128, 8 * 65], BF16) for c in range(16)]
    eS = [P.sbuf(f"eS{i}", [128, TCH], BF16) for i in range(2)]
    Yh = [P.sbuf(f"Yh{h}", [64, TCH], BF16) for h in range(8)]
    rrow = P.sbuf("rrow", [1, TCH], F32)
    zrow = P.sbuf("zrow", [1, TCH], F32)
    xts = P.sbuf("xts", [128, TCH], F32)
    sot = P.sbuf("sot", [128, TCH], F32)
    PBC = P.psum("PBC", [128, TCH])
    PM0 = P.psum("PM0", [128, TCH])
    PM1 = P.psum("PM1", [128, TCH])
    PAV = P.psum("PAV", [128, TCH])
    tvz0 = P.sv.bump(1)
    @P.on("vector")
    def _(e):
        e.memset(zrow[0:1, :], 0.0).then_inc(P.sv.h, 1)

    for b in range(BL):
        tld = None
        stn = P.st.n
        for k in range(4):
            tld = P.load(Qb[k][:, :], Q_sD[k, :, b * S:(b + 1) * S], wait=[(P.st, stn), (P.sv, None)] if k == 0 else ())
            tld = P.load(Kb[k][:, :], K_sD[k, :, b * S:(b + 1) * S])
        for c in range(16):
            tld = P.load(VTe[c][:, :], VT_sD[b * 16 + c])
        P.barrier_engine("tensor", [(P.sd, tld)])
        for w in range(4):
            qoff = w * 512
            ktg0 = max(0, (w - 1) * 4)
            nkt = w * 4 + 4 - ktg0
            for h in range(8):
                ht, hh = h // 2, (h % 2) * 64
                esfree = [None, None]
                pingfree = [None, None]
                for ki in range(nkt):
                    ktg = ktg0 + ki
                    pb = [PM0, PM1][ki % 2]
                    ebuf = eS[ki % 2]
                    pf = pingfree[ki % 2]
                    ef = esfree[ki % 2]
                    tgt = P.st.bump(1)
                    @P.on("tensor")
                    def _(e, pb=pb, ktg=ktg, ht=ht, hh=hh, qoff=qoff, pf=pf):
                        if pf is not None:
                            e.wait_ge(pf[0].h, pf[1])
                        e.matmul(pb[:, 0:TCH], Kb[ht][hh:hh + 64, ktg * 128:(ktg + 1) * 128], Qb[ht][hh:hh + 64, qoff:qoff + TCH], start=True, stop=True).then_inc(P.st.h, 1)
                    tse = P.ss.bump(1)
                    @P.on("scalar")
                    def _(e, pb=pb, ebuf=ebuf, tgt=tgt, ef=ef):
                        e.wait_ge(P.st.h, tgt)
                        if ef is not None:
                            e.wait_ge(ef[0].h, ef[1])
                        e.activation(ebuf[:, :], pb[:, 0:TCH], AF.Exp, scale=8.0, bias=ccol_f[:, 2:3]).then_inc(P.ss.h, 1)
                    pingfree[ki % 2] = (P.ss, tse)
                    offd = qoff - ktg * 128
                    tvm = P.sv.bump(1)
                    @P.on("vector")
                    def _(e, ebuf=ebuf, offd=offd, tse=tse):
                        e.wait_ge(P.ss.h, tse)
                        e.tensor_tensor(ebuf[:, :], ebuf[:, :], maskb[:, 512 + offd:1024 + offd], ALU.mult).then_inc(P.sv.h, 1)
                    tav = P.st.bump(1)
                    @P.on("tensor")
                    def _(e, ktg=ktg, h=h, ebuf=ebuf, tvm=tvm, ki=ki, nkt=nkt, b=b):
                        e.wait_ge(P.sv.h, tvm)
                        e.matmul(PAV[0:65, 0:TCH], VTe[ktg][:, 65 * h:65 * h + 65], ebuf[:, 0:TCH], start=(ki == 0), stop=(ki == nkt - 1)).then_inc(P.st.h, 1)
                    esfree[ki % 2] = (P.st, tav)
                t_fin = P.st.n
                tvr = P.sv.bump(1)
                @P.on("vector")
                def _(e, t_fin=t_fin):
                    e.wait_ge(P.st.h, t_fin)
                    e.reciprocal(rrow[0:1, :], PAV[64:65, 0:TCH]).then_inc(P.sv.h, 1)
                tbc = P.st.bump(1)
                @P.on("tensor")
                def _(e, tvr=tvr):
                    e.wait_ge(P.sv.h, tvr)
                    e.matmul(PBC[:, 0:TCH], ones1_f[0:1, :], rrow[0:1, 0:TCH], start=True, stop=True).then_inc(P.st.h, 1)
                tsy = P.ss.bump(1)
                @P.on("scalar")
                def _(e, h=h, tbc=tbc):
                    e.wait_ge(P.st.h, tbc)
                    e.activation(Yh[h][:, :], PAV[0:64, 0:TCH], AF.Copy).then_inc(P.ss.h, 1)
                tvy = P.sv.bump(1)
                @P.on("vector")
                def _(e, h=h, tsy=tsy):
                    e.wait_ge(P.ss.h, tsy)
                    e.tensor_tensor(Yh[h][:, :], Yh[h][:, :], PBC[0:64, 0:TCH], ALU.mult).then_inc(P.sv.h, 1)
                P.barrier_engine("tensor", [(P.ss, tsy), (P.sv, tvy)])
            off = b * S + qoff
            dfree = [None, None]
            P.barrier_engine("tensor", [(P.sv, None)])
            for ot in range(DT):
                pb = [PM0, PM1][ot % 2]
                df = dfree[ot % 2]
                tgt = P.st.bump(1)
                @P.on("tensor")
                def _(e, pb=pb, ot=ot, df=df):
                    if df is not None:
                        e.wait_ge(df[0].h, df[1])
                    e.matmul(pb[:, 0:TCH], ones1_f[0:1, :], zrow[0:1, 0:TCH], start=True, stop=False)
                    for h in range(8):
                        r = e.matmul(pb[:, 0:TCH], wob[h][:, ot * 128:(ot + 1) * 128], Yh[h][:, 0:TCH], start=False, stop=(h == 7))
                    r.then_inc(P.st.h, 1)
                tx = P.load(xts[:, :], xc_in[ot, :, off:off + TCH], wait=[(P.sv, None)])
                sgn = P.sg.n
                tvs = P.sv.bump(1)
                @P.on("vector")
                def _(e, pb=pb, tgt=tgt, tx=tx, sgn=sgn):
                    e.wait_ge(P.st.h, tgt)
                    e.wait_ge(P.sd.h, tx)
                    e.wait_ge(P.sg.h, sgn)
                    e.tensor_tensor(sot[:, :], pb[:, 0:TCH], xts[:, :], ALU.add).then_inc(P.sv.h, 1)
                P.store(xmid_s[ot, :, off:off + TCH], sot[:, :], wait=[(P.sv, tvs)])
                dfree[ot % 2] = (P.sv, tvs)
    return P.finalize()


def build_l0f3a():
    P = Prog("l0f3a")
    xm_in = P.dram("xm_in", [DT, 128, NT], F32, "ExternalInput")
    w1T = P.dram("f_w1T", [DT, 128, 5460], F32, "ExternalInput")
    b1D = P.dram("f_b1", [1, 5460], F32, "ExternalInput")
    lncol = P.dram("f_lng", [128, 2 * DT], F32, "ExternalInput")
    ones = P.dram("c_ones", [128, 1], F32, "ExternalInput")
    ones1 = P.dram("c_ones1", [1, 128], F32, "ExternalInput")
    onesr = P.dram("c_onesr", [1, 512], F32, "ExternalInput")
    misc = P.dram("c_misc", [1, 8], F32, "ExternalInput")
    hbD = P.dram("hb_s", [22, 128, NT], BF16, "ExternalOutput")

    ones_f = P.sbuf("ones_f", [128, 1], F32)
    ones1_f = P.sbuf("ones1_f", [1, 128], F32)
    onesr_f = P.sbuf("onesr_f", [1, 512], F32)
    P.c_eps = P.sbuf("c_eps", [1, 8], F32)
    lnc = P.sbuf("lnc", [128, 2 * DT], F32)
    b1r = P.sbuf("b1r", [1, 5460], F32)
    P.load(ones_f[:, :], ones[:, :])
    P.load(ones1_f[:, :], ones1[:, :])
    P.load(onesr_f[:, :], onesr[:, :])
    P.load(P.c_eps[0:1, :], misc[0:1, :])
    P.load(lnc[:, :], lncol[:, :])
    P.load(b1r[0:1, :], b1D[0:1, :])
    wscr = P.sbuf("wscr", [128, 2730], F32)
    w1b = [P.sbuf(f"w1b{k}", [128, 5460], BF16) for k in range(DT)]
    def wload(dst, base, src_ap, ncol):
        tw = P.load(wscr[:, 0:ncol], src_ap, wait=[(P.sv, None)])
        tvv = P.sv.bump(1)
        @P.on("vector")
        def _(e, dst=dst, base=base, tw=tw, ncol=ncol):
            e.wait_ge(P.sd.h, tw)
            e.tensor_copy(dst[:, base:base + ncol], wscr[:, 0:ncol]).then_inc(P.sv.h, 1)
    for k in range(DT):
        wload(w1b[k], 0, w1T[k][:, 0:2730], 2730)
        wload(w1b[k], 2730, w1T[k][:, 2730:5460], 2730)
    AOFF = [128 * i for i in range(21)] + [2688]
    ASZ = [128] * 21 + [42]

    xt = [P.sbuf(f"xt{k}", [128, TCH + 1], F32) for k in range(DT)]
    hsh = [P.sbuf(f"hsh{k}", [128, TCH], F32) for k in range(DT)]
    sq = [P.sbuf(f"sq{k}", [128, TCH], F32) for k in range(DT)]
    hnb = [P.sbuf(f"hnb{k}", [128, TCH], BF16) for k in range(DT)]
    rows = (P.sbuf("r_mean", [1, TCH], F32), P.sbuf("r_var", [1, TCH], F32), P.sbuf("r_tmp", [1, TCH], F32))
    ga = [P.sbuf(f"ga{i}", [128, TCH], BF16) for i in range(2)]
    hbt = [P.sbuf(f"hbt{i}", [128, TCH], BF16) for i in range(2)]
    PST = P.psum("PST", [128, TCH])
    PBC = P.psum("PBC", [128, TCH])
    PM0 = P.psum("PM0", [128, TCH])
    PM1 = P.psum("PM1", [128, TCH])

    for ci in range(NCH):
        off = ci * TCH
        bstart = (off % S == 0)
        tld = None
        for k in range(DT):
            if k < 4:
                tld = P.load(xt[k][:, 1:1 + TCH], xm_in[k, :, off:off + TCH],
                             wait=[(P.st, None), (P.sv, None), (P.ss, None)] if k == 0 else ())
            else:
                if bstart:
                    tld = P.load(xt[k][:, 1:1 + TCH], xm_in[k, :, off:off + TCH])
                else:
                    tld = P.load(xt[k][:, 0:1 + TCH], xm_in[k, :, off - 1:off + TCH])
        tsh = P.sv.bump(1)
        @P.on("vector")
        def _(e, bstart=bstart, tld=tld):
            e.wait_ge(P.sd.h, tld)
            r = None
            for k in range(DT):
                if k < 4:
                    r = e.tensor_copy(hsh[k][:, :], xt[k][:, 1:1 + TCH])
                else:
                    if bstart:
                        e.memset(hsh[k][:, 0:1], 0.0)
                        r = e.tensor_copy(hsh[k][:, 1:TCH], xt[k][:, 1:TCH])
                    else:
                        r = e.tensor_copy(hsh[k][:, :], xt[k][:, 0:TCH])
            r.then_inc(P.sv.h, 1)
        P.barrier_engine("tensor", [(P.sv, tsh)])
        P.barrier_engine("scalar", [(P.sv, tsh)])
        P.barrier_engine("vector", [(P.sv, tsh)])
        t_h = emit_ln_chunk(P, hsh, hnb, sq, rows, ones_f, ones1_f, PST, PBC, TCH,
                            gb=(lnc[:, 0:DT], lnc[:, DT:2 * DT]))
        P.barrier_engine("tensor", [(P.sv, t_h)])
        gfree = [None, None]
        prev_thb = None
        for i in range(22):
            aoff, asz = AOFF[i], ASZ[i]
            pa = [PM0, PM1][i % 2]
            gf = gfree[i % 2]
            tgt = P.st.bump(1)
            @P.on("tensor")
            def _(e, aoff=aoff, asz=asz, pa=pa, gf=gf):
                if gf is not None:
                    e.wait_ge(gf[0].h, gf[1])
                e.matmul(pa[0:asz, 0:TCH], b1r[0:1, aoff:aoff + asz], onesr_f[0:1, 0:TCH], start=True, stop=False)
                r = None
                for k in range(DT):
                    r = e.matmul(pa[0:asz, 0:TCH], w1b[k][:, aoff:aoff + asz], hnb[k][:, 0:TCH], start=False, stop=(k == DT - 1))
                r.then_inc(P.st.h, 1)
            tga = P.ss.bump(1)
            @P.on("scalar")
            def _(e, pa=pa, asz=asz, tgt=tgt, prev_thb=prev_thb):
                e.wait_ge(P.st.h, tgt)
                if prev_thb is not None:
                    e.wait_ge(P.sv.h, prev_thb)
                e.activation(ga[0][0:asz, :], pa[0:asz, 0:TCH], AF.Copy).then_inc(P.ss.h, 1)
            tgt2 = P.st.bump(1)
            @P.on("tensor")
            def _(e, aoff=aoff, asz=asz, pa=pa, tga=tga):
                e.wait_ge(P.ss.h, tga)
                e.matmul(pa[0:asz, 0:TCH], b1r[0:1, 2730 + aoff:2730 + aoff + asz], onesr_f[0:1, 0:TCH], start=True, stop=False)
                r = None
                for k in range(DT):
                    r = e.matmul(pa[0:asz, 0:TCH], w1b[k][:, 2730 + aoff:2730 + aoff + asz], hnb[k][:, 0:TCH], start=False, stop=(k == DT - 1))
                r.then_inc(P.st.h, 1)
            tgg = P.ss.bump(1)
            @P.on("scalar")
            def _(e, pa=pa, asz=asz, tgt2=tgt2):
                e.wait_ge(P.st.h, tgt2)
                e.activation(ga[1][0:asz, :], pa[0:asz, 0:TCH], AF.Gelu).then_inc(P.ss.h, 1)
            sgn = P.sg.n
            thb = P.sv.bump(1)
            @P.on("vector")
            def _(e, i=i, asz=asz, tgg=tgg, sgn=sgn):
                e.wait_ge(P.ss.h, tgg)
                e.wait_ge(P.sg.h, sgn)
                e.tensor_tensor(hbt[i % 2][0:asz, :], ga[0][0:asz, :], ga[1][0:asz, :], ALU.mult).then_inc(P.sv.h, 1)
            P.store(hbD[i, 0:asz, off:off + TCH], hbt[i % 2][0:asz, :], wait=[(P.sv, thb)])
            gfree[i % 2] = (P.sv, thb)
            prev_thb = thb
    return P.finalize()


def build_l0f3b(final):
    P = Prog("l0f3bf" if final else "l0f3b")
    xm_in = P.dram("xm_in", [DT, 128, NT], F32, "ExternalInput")
    hbD = P.dram("hb_s", [22, 128, NT], BF16, "ExternalInput")
    w2T = P.dram("f_w2T", [22, 128, D], F32, "ExternalInput")
    b2D = P.dram("f_b2", [1, D], F32, "ExternalInput")
    ones1 = P.dram("c_ones1", [1, 128], F32, "ExternalInput")
    onesr = P.dram("c_onesr", [1, 512], F32, "ExternalInput")
    if final:
        xorig = P.dram("x_in", [DT, 128, NT], F32, "ExternalInput")
        sAD = P.dram("sA_s", [DT, 128, NT], F32, "ExternalInput")
    out_s = P.dram("x_out", [DT, 128, NT], F32, "ExternalOutput")

    ones1_f = P.sbuf("ones1_f", [1, 128], F32)
    onesr_f = P.sbuf("onesr_f", [1, 512], F32)
    b2r = P.sbuf("b2r", [1, D], F32)
    P.load(ones1_f[:, :], ones1[:, :])
    P.load(onesr_f[:, :], onesr[:, :])
    P.load(b2r[0:1, :], b2D[0:1, :])
    wscr = P.sbuf("wscr", [128, 2048], F32)
    w2b = [P.sbuf(f"w2b{k}", [128, D], BF16) for k in range(22)]
    def wload(dst, src_ap, ncol):
        tw = P.load(wscr[:, 0:ncol], src_ap, wait=[(P.sv, None)])
        tvv = P.sv.bump(1)
        @P.on("vector")
        def _(e, dst=dst, tw=tw, ncol=ncol):
            e.wait_ge(P.sd.h, tw)
            e.tensor_copy(dst[:, 0:ncol], wscr[:, 0:ncol]).then_inc(P.sv.h, 1)
    for k in range(22):
        wload(w2b[k], w2T[k], D)

    hb2 = [P.sbuf(f"hb2{i}", [128, TCH], BF16) for i in range(22)]
    xmt = P.sbuf("xmt", [128, TCH], F32)
    xts = P.sbuf("xts", [128, TCH], F32)
    sAt = P.sbuf("sAt", [128, TCH], F32)
    xo = P.sbuf("xo", [128, TCH], F32)
    PM0 = P.psum("PM0", [128, TCH])
    PM1 = P.psum("PM1", [128, TCH])
    tvz = P.sv.bump(1)
    @P.on("vector")
    def _(e):
        e.memset(hb2[21][:, :], 0.0).then_inc(P.sv.h, 1)

    for ci in range(NCH):
        off = ci * TCH
        tld = None
        for i in range(22):
            asz = 128 if i < 21 else 42
            tld = P.load(hb2[i][0:asz, :], hbD[i, 0:asz, off:off + TCH],
                         wait=[(P.st, None), (P.sv, None), (P.ss, None)] if i == 0 else ())
        P.barrier_engine("tensor", [(P.sd, tld)])
        dfree = [None, None]
        for ot in range(DT):
            pb = [PM0, PM1][ot % 2]
            df = dfree[ot % 2]
            tgt = P.st.bump(1)
            @P.on("tensor")
            def _(e, ot=ot, pb=pb, df=df):
                if df is not None:
                    e.wait_ge(df[0].h, df[1])
                e.matmul(pb[:, 0:TCH], b2r[0:1, ot * 128:(ot + 1) * 128], onesr_f[0:1, 0:TCH], start=True, stop=False)
                r = None
                for k in range(22):
                    r = e.matmul(pb[:, 0:TCH], w2b[k][:, ot * 128:(ot + 1) * 128], hb2[k][:, 0:TCH], start=False, stop=(k == 21))
                r.then_inc(P.st.h, 1)
            tx = P.load(xmt[:, :], xm_in[ot, :, off:off + TCH], wait=[(P.sv, None)])
            sgn = P.sg.n
            if not final:
                tvs = P.sv.bump(1)
                @P.on("vector")
                def _(e, pb=pb, tgt=tgt, tx=tx, sgn=sgn):
                    e.wait_ge(P.st.h, tgt)
                    e.wait_ge(P.sd.h, tx)
                    e.wait_ge(P.sg.h, sgn)
                    e.tensor_tensor(xo[:, :], pb[:, 0:TCH], xmt[:, :], ALU.add).then_inc(P.sv.h, 1)
                P.store(out_s[ot, :, off:off + TCH], xo[:, :], wait=[(P.sv, tvs)])
                dfree[ot % 2] = (P.sv, tvs)
            else:
                tx2 = P.load(xts[:, :], xorig[ot, :, off:off + TCH])
                tsa = P.load(sAt[:, :], sAD[ot, :, off:off + TCH])
                tvs = P.sv.bump(1)
                @P.on("vector")
                def _(e, pb=pb, tgt=tgt, tx=tx, sgn=sgn):
                    e.wait_ge(P.st.h, tgt)
                    e.wait_ge(P.sd.h, tx)
                    e.wait_ge(P.sg.h, sgn)
                    e.tensor_tensor(xo[:, :], pb[:, 0:TCH], xmt[:, :], ALU.add).then_inc(P.sv.h, 1)
                tsi = P.ss.bump(1)
                @P.on("scalar")
                def _(e, tvs=tvs):
                    e.wait_ge(P.sv.h, tvs)
                    e.activation(xo[:, :], xo[:, :], AF.Silu).then_inc(P.ss.h, 1)
                tvc = P.sv.bump(1)
                @P.on("vector")
                def _(e, tsi=tsi, tsa=tsa):
                    e.wait_ge(P.ss.h, tsi)
                    e.wait_ge(P.sd.h, tsa)
                    e.tensor_tensor(xo[:, :], xo[:, :], sAt[:, :], ALU.mult)
                    e.tensor_tensor(xo[:, :], xo[:, :], xts[:, :], ALU.add).then_inc(P.sv.h, 1)
                P.store(out_s[ot, :, off:off + TCH], xo[:, :], wait=[(P.sv, tvc)])
                dfree[ot % 2] = (P.sv, tvc)
    return P.finalize()



def _xpos_tables():
    half = np.arange(0, 32, dtype=np.float64)
    inv_freq = 1.0 / 10000.0 ** (2.0 * half / 64.0)
    t = np.arange(S, dtype=np.float64)
    fr = t[:, None] * inv_freq[None, :]
    base = (2.0 * half + 0.4 * 64.0) / (1.4 * 64.0)
    power = (t - S // 2) / 256.0
    sc = base[None, :] ** power[:, None]
    fr64 = np.concatenate([fr, fr], axis=1)
    sc64 = np.concatenate([sc, sc], axis=1)
    c, sn = np.cos(fr64), np.sin(fr64)
    xc1 = (c * sc64).T; xs1 = (sn * sc64).T
    xc2 = (c / sc64).T; xs2 = (sn / sc64).T
    tile2 = lambda a: np.ascontiguousarray(np.tile(a, (2, 1)).astype(np.float32))
    return tile2(xc1), tile2(xs1), tile2(xc2), tile2(xs2)


def _rot_const():
    r = np.zeros((128, 128), np.float32)
    for blk in range(2):
        o = blk * 64
        for c in range(32):
            r[o + c + 32, o + c] = -1.0
            r[o + c, o + c + 32] = 1.0
    return r


def _mask2_table():
    p = np.arange(128)[:, None]
    c = np.arange(1536)[None, :]
    return ((p <= c - 512) & (p >= c - 1024)).astype(np.float32)


_xc1, _xs1, _xc2, _xs2 = _xpos_tables()
CONSTS.update({
    "c_rT": _rot_const(),
    "c_xc1": _xc1, "c_xs1": _xs1, "c_xc2": _xc2, "c_xs2": _xs2,
    "c_mask2": _mask2_table(),
})
CONSTS["c_cols"] = np.ascontiguousarray(np.tile(
    np.array([[-40.0, 1.0, -64.0, 0.0]], np.float32), (128, 1)))


def prep_l0(ins):
    l0a = {
        "l0_wqkvT": np.ascontiguousarray(np.concatenate(
            [np.asarray(ins["L0_attn_wq"]), np.asarray(ins["L0_attn_wk"]),
             np.asarray(ins["L0_attn_wv"])], axis=1).reshape(DT, 128, 3072)),
        "l0_ln1": _colsT(ins["L0_attn_ln1_g"], ins["L0_attn_ln1_b"]),
        "l0_ln2": _colsT(ins["L0_attn_ln2_g"], ins["L0_attn_ln2_b"]),
        "l0_woT": np.ascontiguousarray(np.asarray(ins["L0_attn_wo"]).reshape(DT, 128, D)),
        "l0_cw1T": np.ascontiguousarray(np.asarray(ins["L0_attn_cms_w1"]).reshape(3, DT, 128, 4096)),
        "l0_cb1": np.ascontiguousarray(np.asarray(ins["L0_attn_cms_b1"]).reshape(3, 1, 4096)),
        "l0_cw2T": np.ascontiguousarray(np.asarray(ins["L0_attn_cms_w2"]).reshape(3, 32, 128, D)),
        "l0_cb2": np.ascontiguousarray(np.asarray(ins["L0_attn_cms_b2"]).reshape(3, 1, D)),
    }
    w2p = np.zeros((2, 2816, D), np.float32)
    w2p[:, 0:2730] = np.asarray(ins["L0_ffn_w2"])
    l0f = []
    for l in range(2):
        l0f.append({
            "f_wqkvT": np.ascontiguousarray(np.asarray(ins["L0_ffn_wqkv"])[l].reshape(DT, 128, 1536)),
            "f_qs2": np.ascontiguousarray(np.stack(
                [np.tile(np.asarray(ins["L0_ffn_q_scale"])[l], 2),
                 np.tile(np.asarray(ins["L0_ffn_k_scale"])[l], 2)], axis=1).astype(np.float32)),
            "f_woTh": np.ascontiguousarray(np.asarray(ins["L0_ffn_wo"])[l].reshape(8, 64, D)),
            "f_lng": _colsT(np.asarray(ins["L0_ffn_ln_g"])[l], np.asarray(ins["L0_ffn_ln_b"])[l]),
            "f_w1T": np.ascontiguousarray(np.asarray(ins["L0_ffn_w1"])[l].reshape(DT, 128, 5460)),
            "f_b1": np.ascontiguousarray(np.asarray(ins["L0_ffn_b1"])[l].reshape(1, 5460)),
            "f_w2T": np.ascontiguousarray(w2p[l].reshape(22, 128, D)),
            "f_b2": np.ascontiguousarray(np.asarray(ins["L0_ffn_b2"])[l].reshape(1, D)),
        })
    return l0a, l0f


def run_l0(xcore, l0a, l0f):
    cm = CONSTS
    R = range(NCORES)
    in_maps = [{"x_in": xcore[c], **cm} for c in R]
    r1 = _run(build_l0a1, "l0a1", in_maps)
    in_maps = [{"xn_s": r1[c]["xn_s"], "l0_wqkvT": l0a["l0_wqkvT"], "l0_ln1": l0a["l0_ln1"], **cm} for c in R]
    r2 = _run(build_l0a2, "l0a2", in_maps)
    in_maps = [{"Q_s": r2[c]["Q_s"], "K_s": r2[c]["K_s"], "KT_s": r2[c]["KT_s"],
                "VT_s": r2[c]["VT_s"], **cm} for c in R]
    r3 = _run(build_l0a3, "l0a3", in_maps)
    in_maps = [{"xnf_s": r1[c]["xnf_s"], "y_s": r3[c]["y_s"], "l0_woT": l0a["l0_woT"],
                "l0_ln2": l0a["l0_ln2"], **cm} for c in R]
    r4 = _run(build_l0a4, "l0a4", in_maps)
    in_maps = [{"h_s": r4[c]["h_s"], "s1_s": r4[c]["s1_s"], "l0_cw1T": l0a["l0_cw1T"],
                "l0_cb1": l0a["l0_cb1"], "l0_cw2T": l0a["l0_cw2T"], "l0_cb2": l0a["l0_cb2"], **cm} for c in R]
    r5 = _run(build_l0a5, "l0a5", in_maps)
    xcur = [r1[c]["xnf_s"] for c in R]
    for l in range(2):
        in_maps = [{"xc_in": xcur[c], **l0f[l], **cm} for c in R]
        rf1 = _run(build_l0f1, "l0f1", in_maps)
        in_maps = [{"xc_in": xcur[c], "Q_s": rf1[c]["Q_s"], "K_s": rf1[c]["K_s"],
                    "VT_s": rf1[c]["VT_s"], "f_woTh": l0f[l]["f_woTh"], **cm} for c in R]
        rf2 = _run(build_l0f2, "l0f2", in_maps)
        in_maps = [{"xm_in": rf2[c]["xmid_s"], **l0f[l], **cm} for c in R]
        rf3a = _run(build_l0f3a, "l0f3a", in_maps)
        if l == 0:
            in_maps = [{"xm_in": rf2[c]["xmid_s"], "hb_s": rf3a[c]["hb_s"],
                        **l0f[l], **cm} for c in R]
            rf3 = _run(lambda: build_l0f3b(False), "l0f3b", in_maps)
            xcur = [rf3[c]["x_out"] for c in R]
        else:
            in_maps = [{"xm_in": rf2[c]["xmid_s"], "hb_s": rf3a[c]["hb_s"],
                        "x_in": xcore[c], "sA_s": r5[c]["sA_s"], **l0f[l], **cm} for c in R]
            rf3 = _run(lambda: build_l0f3b(True), "l0f3bf", in_maps)
            xcur = [rf3[c]["x_out"] for c in R]
    return xcur


# version 03
# speedup vs baseline: 1.0060x; 1.0060x over previous
import contextlib
import numpy as np
import ml_dtypes
_bf16 = ml_dtypes.bfloat16
import concourse.bass as bass
import concourse.mybir as mybir
from concourse.bass_utils import run_bass_kernel_spmd

F32 = mybir.dt.float32
BF16 = mybir.dt.bfloat16
I32 = mybir.dt.int32
AF = mybir.ActivationFunctionType
ALU = mybir.AluOpType
AX = mybir.AxisListType

RACE_CHECK = False
L1DBG = False
NCORES = 8
B = 16
S = 2048
D = 1024
BL = B // NCORES
NT = BL * S
DT = D // 128
TCH = 512
NCH = NT // TCH
EPS = 1e-5


class Sem:
    __slots__ = ("h", "n")
    def __init__(self, h):
        self.h = h
        self.n = 0
    def bump(self, k=1):
        self.n += k
        return self.n


class Prog:
    ENGS = ("sync", "tensor", "vector", "scalar", "gpsimd")

    def __init__(self, name="prog"):
        self.nc = bass.Bass("TRN2", target_bir_lowering=False, debug=False,
                            detect_race_conditions=RACE_CHECK)
        self.es = contextlib.ExitStack()
        self.streams = {k: [] for k in self.ENGS}
        self._i = 0
        self.ns = ""
        self.bind = {}
        self.force_internal = False
        self._dram_memo = {}
        self.phase_tensors = {}
        self._phase_es = None
        self.sd = self.sem("sd")
        self.sg = self.sem("sg")
        self.st = self.sem("st")
        self.sv = self.sem("sv")
        self.ss = self.sem("ss")

    def dram(self, name, shape, dtype, kind):
        if name in self.bind:
            ap = self.bind[name]
            self.phase_tensors[name] = ap
            return ap
        full = name if name.startswith("c_") else self.ns + name
        if full in self._dram_memo:
            ap = self._dram_memo[full]
        else:
            if kind == "ExternalOutput" and self.force_internal:
                kind = "Internal"
            ap = self.nc.dram_tensor(full, list(shape), dtype, kind=kind).ap()
            self._dram_memo[full] = ap
        self.phase_tensors[name] = ap
        return ap

    def sbuf(self, name, shape, dtype):
        es = self._phase_es if self._phase_es is not None else self.es
        return es.enter_context(self.nc.sbuf_tensor(self.ns + name, list(shape), dtype))

    def psum(self, name, shape, dtype=F32):
        es = self._phase_es if self._phase_es is not None else self.es
        return es.enter_context(self.nc.psum_tensor(self.ns + name, list(shape), dtype))

    def begin_phase(self, ns, bind=None, force_internal=True):
        self.ns = ns + "."
        self.bind = dict(bind or {})
        self.force_internal = force_internal
        self.phase_tensors = {}
        self._phase_es = contextlib.ExitStack()

    def end_phase(self):
        self._phase_es.close()
        self._phase_es = None
        counts = [(self.sd, self.sd.n), (self.sg, self.sg.n), (self.st, self.st.n),
                  (self.sv, self.sv.n), (self.ss, self.ss.n)]
        for eng in self.ENGS:
            self.barrier_engine(eng, counts)
        out = self.phase_tensors
        self.ns = ""
        self.bind = {}
        self.force_internal = False
        self.phase_tensors = {}
        return out

    def sem(self, name=None):
        self._i += 1
        return Sem(self.es.enter_context(self.nc.semaphore(name or f"s{self._i}")))

    def on(self, eng):
        def deco(fn):
            self.streams[eng].append(fn)
            return None
        return deco

    def load(self, out, in_, wait=()):
        w = [(s, s.n if t is None else t) for s, t in wait]
        tgt = self.sd.bump(16)
        @self.on("sync")
        def _(e):
            for s, t in w:
                e.wait_ge(s.h, t)
            e.dma_start(out=out, in_=in_).then_inc(self.sd.h, 16)
        return tgt

    def store(self, out, in_, wait=()):
        w = [(s, s.n if t is None else t) for s, t in wait]
        tgt = self.sg.bump(16)
        @self.on("gpsimd")
        def _(e):
            for s, t in w:
                e.wait_ge(s.h, t)
            e.dma_start(out=out, in_=in_).then_inc(self.sg.h, 16)
        return tgt

    def barrier_engine(self, eng, waits):
        w = [(s, s.n if t is None else t) for s, t in waits]
        @self.on(eng)
        def _(e):
            for s, t in w:
                e.wait_ge(s.h, t)

    def finalize(self):
        sgf = self.sg
        @self.on("gpsimd")
        def _(e):
            e.wait_ge(sgf.h, sgf.n)
        with self.nc.Block() as blk:
            for eng in self.ENGS:
                fns = self.streams[eng]
                if not fns:
                    continue
                def runner(e, fns=fns):
                    for f in fns:
                        f(e)
                getattr(blk, eng)(runner)
        self.es.close()
        return self.nc



def emit_ln_chunk(P, xt, xnb, sq, rows, ones_f, ones1_f, PST, PBC, n, gb=None, keep_f32=False):
    st, sv, ss = P.st, P.sv, P.ss
    r_mean, r_var, r_tmp = rows
    t_sq = None
    for k in range(DT):
        t_sq = ss.bump(1)
        @P.on("scalar")
        def _(e, k=k):
            e.activation(sq[k][:, 0:n], xt[k][:, 0:n], AF.Square).then_inc(ss.h, 1)
    t_mm = st.bump(1)
    @P.on("tensor")
    def _(e, t_sq=t_sq):
        for k in range(DT):
            e.matmul(PST[0:1, 0:n], ones_f[:, 0:1], xt[k][:, 0:n], start=(k == 0), stop=(k == DT - 1))
        e.wait_ge(P.ss.h, t_sq)
        for k in range(DT):
            r = e.matmul(PBC[0:1, 0:n], ones_f[:, 0:1], sq[k][:, 0:n], start=(k == 0), stop=(k == DT - 1))
        r.then_inc(st.h, 1)
    t_stats = sv.bump(1)
    @P.on("vector")
    def _(e, t_mm=t_mm):
        e.wait_ge(P.st.h, t_mm)
        e.tensor_scalar_mul(r_mean[0:1, 0:n], PST[0:1, 0:n], 1.0 / D)
        e.tensor_scalar_mul(r_var[0:1, 0:n], PBC[0:1, 0:n], 1.0 / D)
        e.tensor_tensor(r_tmp[0:1, 0:n], r_mean[0:1, 0:n], r_mean[0:1, 0:n], ALU.mult)
        e.tensor_tensor(r_var[0:1, 0:n], r_var[0:1, 0:n], r_tmp[0:1, 0:n], ALU.subtract).then_inc(sv.h, 1)
    t_rstd = ss.bump(1)
    ceps = P.c_eps
    @P.on("scalar")
    def _(e, t_stats=t_stats, ceps=ceps):
        e.wait_ge(P.sv.h, t_stats)
        e.activation(r_var[0:1, 0:n], r_var[0:1, 0:n], AF.Ln, bias=ceps[0:1, 0:1])
        e.activation(r_var[0:1, 0:n], r_var[0:1, 0:n], AF.Exp, scale=-0.5).then_inc(ss.h, 1)
    t_bc = st.bump(1)
    @P.on("tensor")
    def _(e, t_rstd=t_rstd):
        e.wait_ge(P.ss.h, t_rstd)
        e.matmul(PBC[:, 0:n], ones1_f[0:1, :], r_mean[0:1, 0:n], start=True, stop=True)
        e.matmul(PST[:, 0:n], ones1_f[0:1, :], r_var[0:1, 0:n], start=True, stop=True).then_inc(st.h, 1)
    t_out = None
    for k in range(DT):
        t_out = sv.bump(1)
        @P.on("vector")
        def _(e, k=k, t_bc=t_bc):
            if k == 0:
                e.wait_ge(P.st.h, t_bc)
            e.tensor_tensor(sq[k][:, 0:n], xt[k][:, 0:n], PBC[:, 0:n], ALU.subtract)
            if gb is None:
                if keep_f32:
                    e.tensor_tensor(sq[k][:, 0:n], sq[k][:, 0:n], PST[:, 0:n], ALU.mult)
                    e.tensor_copy(xnb[k][:, 0:n], sq[k][:, 0:n]).then_inc(sv.h, 1)
                else:
                    e.tensor_tensor(xnb[k][:, 0:n], sq[k][:, 0:n], PST[:, 0:n], ALU.mult).then_inc(sv.h, 1)
            else:
                e.tensor_tensor(sq[k][:, 0:n], sq[k][:, 0:n], PST[:, 0:n], ALU.mult)
                e.tensor_scalar(xnb[k][:, 0:n], sq[k][:, 0:n], gb[0][:, k:k + 1], gb[1][:, k:k + 1], ALU.mult, ALU.add).then_inc(sv.h, 1)
    return t_out


def emit_proj(P, nk, n_out_tiles, lhsT_fn, rhs_fn, evac_fn, PP, n):
    st = P.st
    bank_free = [None, None]
    for ot in range(n_out_tiles):
        pb = PP[ot % 2]
        bf = bank_free[ot % 2]
        tgt = st.bump(1)
        @P.on("tensor")
        def _(e, ot=ot, pb=pb, bf=bf):
            if bf is not None:
                e.wait_ge(bf[0].h, bf[1])
            for k in range(nk):
                r = e.matmul(pb[:, 0:n], lhsT_fn(k, ot), rhs_fn(k, ot), start=(k == 0), stop=(k == nk - 1))
            r.then_inc(st.h, 1)
        bank_free[ot % 2] = evac_fn(ot, pb, tgt)



def build_l2a(P=None):
    own = P is None
    if own:
        P = Prog("l2a")
    x_in = P.dram("x_in", [DT, 128, NT], F32, "ExternalInput")
    wvrT = P.dram("l2_wvrT", [DT, 128, 2048], F32, "ExternalInput")
    woT = P.dram("l2_woT", [DT, 128, D], F32, "ExternalInput")
    mixc = P.dram("l2_mixc", [128, 3 * DT], F32, "ExternalInput")
    ones = P.dram("c_ones", [128, 1], F32, "ExternalInput")
    ones1 = P.dram("c_ones1", [1, 128], F32, "ExternalInput")
    xn_s = P.dram("xn_s", [DT, 128, NT], BF16, "ExternalOutput")
    s_s = P.dram("s_s", [DT, 128, NT], F32, "ExternalOutput")

    ones_f = P.sbuf("ones_f", [128, 1], F32)
    ones1_f = P.sbuf("ones1_f", [1, 128], F32)
    misc = P.dram("c_misc", [1, 8], F32, "ExternalInput")
    P.c_eps = P.sbuf("c_eps", [1, 8], F32)
    P.load(ones_f[:, :], ones[:, :])
    P.load(ones1_f[:, :], ones1[:, :])
    P.load(P.c_eps[0:1, :], misc[0:1, :])

    mixs = P.sbuf("mixs", [128, 3 * DT], F32)
    omix = P.sbuf("omix", [128, 2 * DT], F32)
    dcol = P.sbuf("dcol", [128, DT], BF16)
    tmx = P.load(mixs[:, :], mixc[:, :])
    t_omix = P.sv.bump(1)
    @P.on("vector")
    def _(e, tmx=tmx):
        e.wait_ge(P.sd.h, tmx)
        e.tensor_scalar(omix[:, :], mixs[:, 0:2 * DT], -1.0, 1.0, ALU.mult, ALU.add).then_inc(P.sv.h, 1)
    t_dcol = P.ss.bump(1)
    @P.on("scalar")
    def _(e, tmx=tmx):
        e.wait_ge(P.sd.h, tmx)
        e.activation(mixs[:, 2 * DT:3 * DT], mixs[:, 2 * DT:3 * DT], AF.Exp)
        e.activation(dcol[:, :], mixs[:, 2 * DT:3 * DT], AF.Exp, scale=-1.0).then_inc(P.ss.h, 1)
    t_zc = P.sv.bump(1)
    @P.on("vector")
    def _(e):
        e.memset(zcol[:, :], 0.0).then_inc(P.sv.h, 1)

    wscr = P.sbuf("wscr", [128, 2048], F32)
    wp = [P.sbuf(f"wp{k}", [128, 2048], BF16) for k in range(2 * DT)]
    woTb = [P.sbuf(f"woTb{k}", [128, D], BF16) for k in range(DT)]
    for k in range(DT):
        tw = P.load(wscr[:, :], wvrT[k], wait=[(P.sv, None)])
        tvw = P.sv.bump(1)
        @P.on("vector")
        def _(e, k=k, tw=tw, t_omix=t_omix):
            e.wait_ge(P.sd.h, tw)
            if k == 0:
                e.wait_ge(P.sv.h, t_omix)
            e.tensor_single_scalar(wp[k][:, 0:1024], wscr[:, 0:1024], mixs[:, k:k + 1], ALU.mult)
            e.tensor_single_scalar(wp[k][:, 1024:2048], wscr[:, 1024:2048], mixs[:, DT + k:DT + k + 1], ALU.mult)
            e.tensor_single_scalar(wp[DT + k][:, 0:1024], wscr[:, 0:1024], omix[:, k:k + 1], ALU.mult)
            e.tensor_single_scalar(wp[DT + k][:, 1024:2048], wscr[:, 1024:2048], omix[:, DT + k:DT + k + 1], ALU.mult).then_inc(P.sv.h, 1)
    for k in range(DT):
        tw = P.load(wscr[:, 0:D], woT[k], wait=[(P.sv, None)])
        tvw = P.sv.bump(1)
        @P.on("vector")
        def _(e, k=k, tw=tw):
            e.wait_ge(P.sd.h, tw)
            e.tensor_copy(woTb[k][:, :], wscr[:, 0:D]).then_inc(P.sv.h, 1)

    xt = [P.sbuf(f"xt{k}", [128, TCH], F32) for k in range(DT)]
    sq = [P.sbuf(f"sq{k}", [128, TCH], F32) for k in range(DT)]
    xnb = [P.sbuf(f"xnb{k}", [128, TCH], BF16) for k in range(DT)]
    prevb = [P.sbuf(f"prevb{k}", [128, TCH], BF16) for k in range(DT)]
    so = [P.sbuf(f"so{k}", [128, TCH], F32) for k in range(DT)]
    carry = [P.sbuf(f"carry{k}", [128, 1], F32) for k in range(DT)]
    zcol = P.sbuf("zcol", [128, 1], F32)
    vb = [P.sbuf(f"vbt{k}", [128, TCH], BF16) for k in range(DT)]
    rsb = [P.sbuf(f"rsb{k}", [128, TCH], BF16) for k in range(DT)]
    sot = [P.sbuf(f"sot{k}", [128, TCH], F32) for k in range(DT)]
    rows = (P.sbuf("r_mean", [1, TCH], F32), P.sbuf("r_var", [1, TCH], F32), P.sbuf("r_tmp", [1, TCH], F32))
    PST = P.psum("PST", [128, TCH])
    PBC = P.psum("PBC", [128, TCH])
    PM0 = P.psum("PM0", [128, TCH])
    PM1 = P.psum("PM1", [128, TCH])

    for ci in range(NCH):
        off = ci * TCH
        tld = None
        for k in range(DT):
            tld = P.load(xt[k][:, :], x_in[k, :, off:off + TCH],
                         wait=[(P.st, None), (P.sv, None)] if k == 0 else ())
        P.barrier_engine("tensor", [(P.sd, tld), (P.sv, None)])
        P.barrier_engine("scalar", [(P.sd, tld)])
        P.barrier_engine("vector", [(P.sd, tld), (P.sg, None), (P.st, None), (P.ss, t_dcol)])
        t_xn = emit_ln_chunk(P, xt, xnb, sq, rows, ones_f, ones1_f, PST, PBC, TCH)
        for k in range(DT):
            P.store(xn_s[k, :, off:off + TCH], xnb[k][:, :], wait=[(P.sv, t_xn)] if k == 0 else ())
        t_scan = None
        for k in range(DT):
            t_scan = P.sv.bump(1)
            @P.on("vector")
            def _(e, k=k, ci=ci):
                if ci % (NCH // BL) == 0:
                    e.tensor_tensor_scan(so[k][:, :], dcol[:, k:k + 1].broadcast_to([128, TCH]), xnb[k][:, :], 0.0, ALU.mult, ALU.add)
                    e.memset(prevb[k][:, 0:1], 0.0)
                else:
                    e.tensor_tensor_scan(so[k][:, :], dcol[:, k:k + 1].broadcast_to([128, TCH]), xnb[k][:, :], carry[k][:, 0:1], ALU.mult, ALU.add)
                    e.tensor_copy(prevb[k][:, 0:1], carry[k][:, :])
                e.tensor_copy(prevb[k][:, 1:TCH], so[k][:, 0:TCH - 1])
                e.tensor_copy(carry[k][:, :], so[k][:, TCH - 1:TCH]).then_inc(P.sv.h, 1)
        def lhsT_vr(k, ot):
            return wp[k][:, ot * 128:(ot + 1) * 128]
        def rhs_vr(k, ot):
            return (xnb[k] if k < DT else prevb[k - DT])[:, 0:TCH]
        def evac_vr(ot, pb, tgt):
            te = P.ss.bump(1)
            @P.on("scalar")
            def _(e, ot=ot, pb=pb, tgt=tgt):
                e.wait_ge(P.st.h, tgt)
                if ot < DT:
                    e.activation(vb[ot][:, :], pb[:, 0:TCH], AF.Copy).then_inc(P.ss.h, 1)
                else:
                    e.activation(rsb[ot - DT][:, :], pb[:, 0:TCH], AF.Sigmoid).then_inc(P.ss.h, 1)
            return (P.ss, te)
        P.barrier_engine("tensor", [(P.sv, t_scan)])
        emit_proj(P, 2 * DT, 2 * DT, lhsT_vr, rhs_vr, evac_vr, [PM0, PM1], TCH)
        def lhsT_wo(k, ot):
            return woTb[k][:, ot * 128:(ot + 1) * 128]
        def rhs_wo(k, ot):
            return vb[k][:, 0:TCH]
        def evac_wo(ot, pb, tgt):
            te = P.sv.bump(1)
            @P.on("vector")
            def _(e, ot=ot, pb=pb, tgt=tgt):
                e.wait_ge(P.st.h, tgt)
                e.tensor_tensor(sot[ot][:, :], pb[:, 0:TCH], rsb[ot][:, :], ALU.mult).then_inc(P.sv.h, 1)
            P.store(s_s[ot, :, off:off + TCH], sot[ot][:, :], wait=[(P.sv, te)])
            return (P.sv, te)
        P.barrier_engine("tensor", [(P.ss, None)])
        emit_proj(P, DT, DT, lhsT_wo, rhs_wo, evac_wo, [PM0, PM1], TCH)
    return P.finalize() if own else None


def build_l2b(P=None):
    own = P is None
    if own:
        P = Prog("l2b")
    x_in = P.dram("x_in", [DT, 128, NT], F32, "ExternalInput")
    xn_s = P.dram("xn_s", [DT, 128, NT], BF16, "ExternalInput")
    s_s = P.dram("s_s", [DT, 128, NT], F32, "ExternalInput")
    wg = P.dram("l2_wg", [DT, 128, 3072], I32, "ExternalInput")
    wu = P.dram("l2_wu", [DT, 128, 3072], I32, "ExternalInput")
    wd = P.dram("l2_wd", [24, 128, D], I32, "ExternalInput")
    x_out = P.dram("x_out", [DT, 128, NT], F32, "ExternalOutput")

    QS = 1.0 / 127.0
    wscr = P.sbuf("wscr", [128, 3072], I32)
    wgb = [P.sbuf(f"wgb{k}", [128, 3072], BF16) for k in range(DT)]
    wub = [P.sbuf(f"wub{k}", [128, 3072], BF16) for k in range(DT)]
    wdb = [P.sbuf(f"wdb{k}", [128, D], BF16) for k in range(24)]
    def wload(dst, src_ap, ncol):
        tw = P.load(wscr[:, 0:ncol], src_ap, wait=[(P.sv, None)])
        tv = P.sv.bump(1)
        @P.on("vector")
        def _(e, dst=dst, tw=tw, ncol=ncol):
            e.wait_ge(P.sd.h, tw)
            e.tensor_copy(dst[:, :], wscr[:, 0:ncol]).then_inc(P.sv.h, 1)
    for k in range(DT):
        wload(wgb[k], wg[k], 3072)
        wload(wub[k], wu[k], 3072)
    for k in range(24):
        wload(wdb[k], wd[k], D)

    xnb = [P.sbuf(f"xnb{k}", [128, TCH], BF16) for k in range(DT)]
    hb = [P.sbuf(f"hb{k}", [128, TCH], BF16) for k in range(24)]
    ub = P.sbuf("ub", [128, TCH], F32)
    xts = P.sbuf("xts", [128, TCH], F32)
    sts = P.sbuf("sts", [128, TCH], F32)
    xo = P.sbuf("xo", [128, TCH], F32)
    PM = [P.psum(f"PM{i}", [128, TCH]) for i in range(4)]

    for ci in range(NCH):
        off = ci * TCH
        tld = None
        for k in range(DT):
            tld = P.load(xnb[k][:, :], xn_s[k, :, off:off + TCH],
                         wait=[(P.st, None), (P.sv, None)] if k == 0 else ())
        P.barrier_engine("tensor", [(P.sd, tld), (P.sv, None)])
        P.barrier_engine("scalar", [(P.st, None)])
        gfree = [None, None]
        ufree = [None, None]
        for ot in range(24):
            pg = PM[ot % 2]
            pu = PM[2 + ot % 2]
            gf, uf = gfree[ot % 2], ufree[ot % 2]
            tgt = P.st.bump(1)
            @P.on("tensor")
            def _(e, ot=ot, pg=pg, pu=pu, gf=gf, uf=uf):
                if gf is not None:
                    e.wait_ge(gf[0].h, gf[1])
                if uf is not None:
                    e.wait_ge(uf[0].h, uf[1])
                for k in range(DT):
                    e.matmul(pg[:, 0:TCH], wgb[k][:, ot * 128:(ot + 1) * 128], xnb[k][:, 0:TCH], start=(k == 0), stop=(k == DT - 1))
                for k in range(DT):
                    r = e.matmul(pu[:, 0:TCH], wub[k][:, ot * 128:(ot + 1) * 128], xnb[k][:, 0:TCH], start=(k == 0), stop=(k == DT - 1))
                r.then_inc(P.st.h, 1)
            ts_ = P.ss.bump(1)
            @P.on("scalar")
            def _(e, ot=ot, pg=pg, tgt=tgt):
                e.wait_ge(P.st.h, tgt)
                e.activation(hb[ot][:, :], pg[:, 0:TCH], AF.Silu, scale=QS).then_inc(P.ss.h, 1)
            tv = P.sv.bump(1)
            @P.on("vector")
            def _(e, ot=ot, pu=pu, tgt=tgt, ts_=ts_):
                e.wait_ge(P.st.h, tgt)
                e.tensor_scalar_mul(ub[:, :], pu[:, 0:TCH], QS)
                e.wait_ge(P.ss.h, ts_)
                e.tensor_tensor(hb[ot][:, :], hb[ot][:, :], ub[:, :], ALU.mult).then_inc(P.sv.h, 1)
            gfree[ot % 2] = (P.ss, ts_)
            ufree[ot % 2] = (P.sv, tv)
        P.barrier_engine("tensor", [(P.sv, None)])
        dfree = [None, None]
        for ot in range(DT):
            pb = PM[ot % 2]
            df = dfree[ot % 2]
            tgt = P.st.bump(1)
            @P.on("tensor")
            def _(e, ot=ot, pb=pb, df=df):
                if df is not None:
                    e.wait_ge(df[0].h, df[1])
                for k in range(24):
                    r = e.matmul(pb[:, 0:TCH], wdb[k][:, ot * 128:(ot + 1) * 128], hb[k][:, 0:TCH], start=(k == 0), stop=(k == 23))
                r.then_inc(P.st.h, 1)
            tx = P.load(xts[:, :], x_in[ot, :, off:off + TCH], wait=[(P.sv, None)])
            ts2 = P.load(sts[:, :], s_s[ot, :, off:off + TCH])
            tsi = P.ss.bump(1)
            sgn = P.sg.n
            @P.on("scalar")
            def _(e, ot=ot, pb=pb, tgt=tgt, sgn=sgn):
                e.wait_ge(P.st.h, tgt)
                e.wait_ge(P.sg.h, sgn)
                e.activation(xo[:, :], pb[:, 0:TCH], AF.Silu, scale=QS).then_inc(P.ss.h, 1)
            tv = P.sv.bump(1)
            @P.on("vector")
            def _(e, ot=ot, tsi=tsi, ts2=ts2):
                e.wait_ge(P.ss.h, tsi)
                e.wait_ge(P.sd.h, ts2)
                e.tensor_tensor(xo[:, :], xo[:, :], sts[:, :], ALU.mult)
                e.tensor_tensor(xo[:, :], xo[:, :], xts[:, :], ALU.add).then_inc(P.sv.h, 1)
            P.store(x_out[ot, :, off:off + TCH], xo[:, :], wait=[(P.sv, tv)])
            dfree[ot % 2] = (P.ss, tsi)
    return P.finalize() if own else None



def _fm(x):
    xt = np.ascontiguousarray(x.transpose(2, 0, 1).reshape(D, NT))
    return np.ascontiguousarray(xt.reshape(DT, 128, NT))


def _unfm(xf):
    xd = np.asarray(xf).reshape(D, BL, S)
    return np.ascontiguousarray(xd.transpose(1, 2, 0))


def _wT_tiles(w):
    wt = np.ascontiguousarray(np.transpose(np.asarray(w)))
    return np.ascontiguousarray(wt.reshape(wt.shape[0] // 128, 128, wt.shape[1]))


def _colsT(*vs):
    mats = [np.ascontiguousarray(np.asarray(v).reshape(DT, 128).T) for v in vs]
    return np.ascontiguousarray(np.concatenate(mats, axis=1))


_CACHE = {}


def _run(build_fn, name, in_maps):
    if name not in _CACHE:
        _CACHE[name] = build_fn()
    nc = _CACHE[name]
    try:
        res = run_bass_kernel_spmd(nc, in_maps, core_ids=list(range(NCORES)))
        _ = [np.asarray(next(iter(r.values()))) for r in res.results]
    except Exception:
        res = run_bass_kernel_spmd(nc, in_maps, core_ids=list(range(NCORES)))
    return res.results


CONSTS = {
    "c_ones": np.ones((128, 1), np.float32),
    "c_ones1": np.ones((1, 128), np.float32),
    "c_misc": np.array([[EPS, 0, 0, 0, 0, 0, 0, 0]], np.float32),
}


def prep_l2(ins):
    l2w = {
        "l2_wvrT": np.ascontiguousarray(
            np.concatenate([np.transpose(np.asarray(ins["L2_attn_Wv"])),
                            np.transpose(np.asarray(ins["L2_attn_Wr"]))], axis=1
                           ).reshape(DT, 128, 2048)),
        "l2_woT": _wT_tiles(ins["L2_attn_Wo"]),
        "l2_mixc": _colsT(ins["L2_attn_time_mix_v"], ins["L2_attn_time_mix_r"], ins["L2_attn_time_decay"]),
    }
    l2f = {
        "l2_wg": _wT_tiles(ins["L2_ffn_w_gate"]),
        "l2_wu": _wT_tiles(ins["L2_ffn_w_up"]),
        "l2_wd": _wT_tiles(ins["L2_ffn_w_down"]),
    }
    return l2w, l2f


def run_l2(xcore, l2w, l2f):
    in_maps = [{"x_in": xcore[c], **l2w, **CONSTS} for c in range(NCORES)]
    ra = _run(build_l2a, "l2a", in_maps)
    in_maps = [{"x_in": xcore[c], "xn_s": ra[c]["xn_s"], "s_s": ra[c]["s_s"], **l2f} for c in range(NCORES)]
    rb = _run(build_l2b, "l2b", in_maps)
    return [rb[c]["x_out"] for c in range(NCORES)]


def kernel(**inputs):
    ins = {k: np.asarray(v) for k, v in inputs.items()}
    x0 = ins["L0_attn_x"]
    xcore = [_fm(np.ascontiguousarray(x0[c * BL:(c + 1) * BL])) for c in range(NCORES)]
    if FUSED:
        xcore = kernel_fused(ins, xcore)
        out = np.empty((B, S, D), np.float32)
        for c in range(NCORES):
            out[c * BL:(c + 1) * BL] = _unfm(xcore[c])
        return out
    l0a, l0f = prep_l0(ins)
    xcore = run_l0(xcore, l0a, l0f)
    l1a, l1s, l1f = prep_l1(ins)
    xcore = run_l1(xcore, l1a, l1s, l1f)
    l2w, l2f = prep_l2(ins)
    xcore = run_l2(xcore, l2w, l2f)
    l3a, l3b1, l3b2 = prep_l3(ins)
    xcore = run_l3(xcore, l3a, l3b1, l3b2)
    out = np.empty((B, S, D), np.float32)
    for c in range(NCORES):
        out[c * BL:(c + 1) * BL] = _unfm(xcore[c])
    return out



def build_l3a1(P=None):
    own = P is None
    if own:
        P = Prog("l3a1")
    x_in = P.dram("x_in", [DT, 128, NT], F32, "ExternalInput")
    waT = P.dram("l3_waT", [DT, 128, 3072], F32, "ExternalInput")
    ba = P.dram("l3_ba", [1, 3072], F32, "ExternalInput")
    rope = P.dram("c_rope", [128, S], F32, "ExternalInput")
    ones = P.dram("c_ones", [128, 1], F32, "ExternalInput")
    ones1 = P.dram("c_ones1", [1, 128], F32, "ExternalInput")
    onesr = P.dram("c_onesr", [1, 512], F32, "ExternalInput")
    misc = P.dram("c_misc", [1, 8], F32, "ExternalInput")
    xn_s = P.dram("xn_s", [DT, 128, NT], BF16, "ExternalOutput")
    K_s = P.dram("K_s", [DT, 128, NT], BF16, "ExternalOutput")
    Q_s = P.dram("Q_s", [DT, 128, NT], BF16, "ExternalOutput")
    VT_s = P.dram("VT_s", [BL, 16, 128, 16 * 65], BF16, "ExternalOutput")

    ones_f = P.sbuf("ones_f", [128, 1], F32)
    ones1_f = P.sbuf("ones1_f", [1, 128], F32)
    onesr_f = P.sbuf("onesr_f", [1, 512], F32)
    P.c_eps = P.sbuf("c_eps", [1, 8], F32)
    ropeb = P.sbuf("ropeb", [128, S], BF16)
    bar = P.sbuf("bar", [1, 3072], F32)
    P.load(ones_f[:, :], ones[:, :])
    P.load(ones1_f[:, :], ones1[:, :])
    P.load(onesr_f[:, :], onesr[:, :])
    P.load(P.c_eps[0:1, :], misc[0:1, :])
    P.load(bar[0:1, :], ba[0:1, :])
    wscr = P.sbuf("wscr", [128, 3072], F32)
    waTb = [P.sbuf(f"waTb{k}", [128, 3072], BF16) for k in range(DT)]
    def wload(dst, src_ap, ncol):
        tw = P.load(wscr[:, 0:ncol], src_ap, wait=[(P.sv, None)])
        tvv = P.sv.bump(1)
        @P.on("vector")
        def _(e, dst=dst, tw=tw, ncol=ncol):
            e.wait_ge(P.sd.h, tw)
            e.tensor_copy(dst[:, 0:ncol], wscr[:, 0:ncol]).then_inc(P.sv.h, 1)
    wload(ropeb, rope[:, :], 2048)
    for k in range(DT):
        wload(waTb[k], waT[k], 3072)

    xt = [P.sbuf(f"xt{k}", [128, TCH], F32) for k in range(DT)]
    sq = [P.sbuf(f"sq{k}", [128, TCH], F32) for k in range(DT)]
    xnb = [P.sbuf(f"xnb{k}", [128, TCH], BF16) for k in range(DT)]
    rows = (P.sbuf("r_mean", [1, TCH], F32), P.sbuf("r_var", [1, TCH], F32), P.sbuf("r_tmp", [1, TCH], F32))
    qkt = [P.sbuf(f"qkt{k}", [128, TCH], BF16) for k in range(2)]
    VTe = [P.sbuf(f"VTec{i}", [128, 16 * 65], BF16) for i in range(4)]
    PST = P.psum("PST", [128, TCH])
    PBC = P.psum("PBC", [128, TCH])
    PM0 = P.psum("PM0", [128, TCH])
    PM1 = P.psum("PM1", [128, TCH])

    for b in range(BL):
        for ciq in range(S // TCH):
            off = b * S + ciq * TCH
            qoff = ciq * TCH
            tld = None
            for k in range(DT):
                tld = P.load(xt[k][:, :], x_in[k, :, off:off + TCH],
                             wait=[(P.st, None), (P.sv, None)] if k == 0 else ())
            P.barrier_engine("tensor", [(P.sd, tld), (P.sv, None)])
            P.barrier_engine("scalar", [(P.sd, tld)])
            P.barrier_engine("vector", [(P.sd, tld), (P.sg, None), (P.st, None)])
            t_xn = emit_ln_chunk(P, xt, xnb, sq, rows, ones_f, ones1_f, PST, PBC, TCH)
            for k in range(DT):
                P.store(xn_s[k, :, off:off + TCH], xnb[k][:, :], wait=[(P.sv, t_xn)] if k == 0 else ())
            P.barrier_engine("tensor", [(P.sv, t_xn)])
            for (dstD, fbase) in ((Q_s, 0), (K_s, 1024)):
                for ot in range(DT):
                    pb = [PM0, PM1][ot % 2]
                    buf = qkt[ot % 2]
                    svn = P.sv.n
                    sgn = P.sg.n
                    tgt = P.st.bump(1)
                    @P.on("tensor")
                    def _(e, pb=pb, ot=ot, fbase=fbase, svn=svn):
                        e.wait_ge(P.sv.h, svn)
                        e.matmul(pb[:, 0:TCH], bar[0:1, fbase + ot * 128:fbase + (ot + 1) * 128], onesr_f[0:1, 0:TCH], start=True, stop=False)
                        for k in range(DT):
                            r = e.matmul(pb[:, 0:TCH], waTb[k][:, fbase + ot * 128:fbase + (ot + 1) * 128], xnb[k][:, 0:TCH], start=False, stop=(k == DT - 1))
                        r.then_inc(P.st.h, 1)
                    tvx = P.sv.bump(1)
                    @P.on("vector")
                    def _(e, pb=pb, buf=buf, qoff=qoff, tgt=tgt, sgn=sgn):
                        e.wait_ge(P.st.h, tgt)
                        e.wait_ge(P.sg.h, sgn)
                        e.tensor_copy(buf[:, :], pb[:, 0:TCH])
                        e.tensor_tensor(buf[:, :], buf[:, :], ropeb[:, qoff:qoff + TCH], ALU.mult).then_inc(P.sv.h, 1)
                    P.store(dstD[ot, :, off:off + TCH], buf[:, :], wait=[(P.sv, tvx)])
            for sub in range(TCH // 128):
                kt = (qoff + sub * 128) // 128
                vbuf = VTe[sub]
                sgn = P.sg.n
                tvm = P.sv.bump(1)
                @P.on("vector")
                def _(e, vbuf=vbuf, sgn=sgn):
                    e.wait_ge(P.sg.h, sgn)
                    e.memset(vbuf.rearrange("p (h c) -> p h c", c=65)[:, :, 64:65], 1.0).then_inc(P.sv.h, 1)
                for fc in range(2):
                    pb = [PM0, PM1][fc % 2]
                    svn = P.sv.n
                    tgt = P.st.bump(1)
                    @P.on("tensor")
                    def _(e, pb=pb, sub=sub, fc=fc, svn=svn):
                        e.wait_ge(P.sv.h, svn)
                        e.matmul(pb[:, 0:TCH], ones1_f[0:1, :], bar[0:1, 2048 + fc * 512:2048 + (fc + 1) * 512], start=True, stop=False)
                        for k in range(DT):
                            r = e.matmul(pb[:, 0:TCH], xnb[k][:, sub * 128:(sub + 1) * 128], waTb[k][:, 2048 + fc * 512:2048 + (fc + 1) * 512], start=False, stop=(k == DT - 1))
                        r.then_inc(P.st.h, 1)
                    tvv = P.sv.bump(1)
                    @P.on("vector")
                    def _(e, pb=pb, vbuf=vbuf, fc=fc, tgt=tgt):
                        e.wait_ge(P.st.h, tgt)
                        ve = vbuf.rearrange("p (h c) -> p h c", c=65)
                        e.tensor_copy(ve[:, fc * 8:(fc + 1) * 8, 0:64], pb[:, 0:TCH].rearrange("p (h c) -> p h c", c=64)).then_inc(P.sv.h, 1)
                P.store(VT_s[b, kt, :, :], vbuf[:, :], wait=[(P.sv, None)])
    return P.finalize() if own else None


def build_l3a2(P=None):
    own = P is None
    if own:
        P = Prog("l3a2")
    K_sD = P.dram("K_s", [DT, 128, NT], BF16, "ExternalInput")
    Q_sD = P.dram("Q_s", [DT, 128, NT], BF16, "ExternalInput")
    VT_sD = P.dram("VT_s", [BL, 16, 128, 16 * 65], BF16, "ExternalInput")
    wpT = P.dram("l3_wpTh", [16, 64, D], F32, "ExternalInput")
    bp = P.dram("l3_bp", [1, D], F32, "ExternalInput")
    maskt = P.dram("c_mask", [128, 1536], F32, "ExternalInput")
    ones1 = P.dram("c_ones1", [1, 128], F32, "ExternalInput")
    onesr = P.dram("c_onesr", [1, 512], F32, "ExternalInput")
    ccols = P.dram("c_cols", [128, 4], F32, "ExternalInput")
    s_s = P.dram("s_s", [DT, 128, NT], F32, "ExternalOutput")

    ones1_f = P.sbuf("ones1_f", [1, 128], F32)
    onesr_f = P.sbuf("onesr_f", [1, 512], F32)
    ccol_f = P.sbuf("ccol_f", [128, 4], F32)
    maskb = P.sbuf("maskb", [128, 1536], BF16)
    bpr = P.sbuf("bpr", [1, D], F32)
    P.load(ones1_f[:, :], ones1[:, :])
    P.load(onesr_f[:, :], onesr[:, :])
    P.load(ccol_f[:, :], ccols[:, :])
    P.load(bpr[0:1, :], bp[0:1, :])
    wscr = P.sbuf("wscr", [128, 1536], F32)
    wpTb = [P.sbuf(f"wpTb{k}", [64, D], BF16) for k in range(16)]
    def wload(dst, src_ap, ncol, npart=128):
        tw = P.load(wscr[0:npart, 0:ncol], src_ap, wait=[(P.sv, None)])
        tvv = P.sv.bump(1)
        @P.on("vector")
        def _(e, dst=dst, tw=tw, ncol=ncol, npart=npart):
            e.wait_ge(P.sd.h, tw)
            e.tensor_copy(dst[0:npart, 0:ncol], wscr[0:npart, 0:ncol]).then_inc(P.sv.h, 1)
    wload(maskb, maskt[:, :], 1536)
    for k in range(16):
        wload(wpTb[k], wpT[k], D, npart=64)

    Kb = [P.sbuf(f"Kb{k}", [128, S], BF16) for k in range(DT)]
    Qb = [P.sbuf(f"Qb{k}", [128, S], BF16) for k in range(DT)]
    VTe = [P.sbuf(f"VTe{k}", [128, 16 * 65], BF16) for k in range(16)]
    eS = [P.sbuf(f"eS{i}", [128, TCH], BF16) for i in range(2)]
    Yh = [P.sbuf(f"Yh{h}", [64, TCH], BF16) for h in range(16)]
    rrow = P.sbuf("rrow", [1, TCH], F32)
    sot = P.sbuf("sot", [128, TCH], F32)
    PBC = P.psum("PBC", [128, TCH])
    PM0 = P.psum("PM0", [128, TCH])
    PM1 = P.psum("PM1", [128, TCH])
    PAV = P.psum("PAV", [128, TCH])

    for b in range(BL):
        tld = None
        stn = P.st.n
        for k in range(DT):
            tld = P.load(Kb[k][:, :], K_sD[k, :, b * S:(b + 1) * S], wait=[(P.st, stn), (P.sv, None)] if k == 0 else ())
            tld = P.load(Qb[k][:, :], Q_sD[k, :, b * S:(b + 1) * S])
        for kt in range(16):
            tld = P.load(VTe[kt][:, :], VT_sD[b, kt, :, :])
        P.barrier_engine("tensor", [(P.sd, tld)])
        for ciq in range(S // TCH):
            qoff = ciq * TCH
            off = b * S + qoff
            nkt = qoff // 128 + 4
            for h in range(16):
                hh = h % 2
                ht = h // 2
                esfree = [None, None]
                pingfree = [None, None]
                for kt in range(nkt):
                    pb = [PM0, PM1][kt % 2]
                    ebuf = eS[kt % 2]
                    pf = pingfree[kt % 2]
                    ef = esfree[kt % 2]
                    tgt = P.st.bump(1)
                    @P.on("tensor")
                    def _(e, pb=pb, kt=kt, hh=hh, ht=ht, qoff=qoff, pf=pf):
                        if pf is not None:
                            e.wait_ge(pf[0].h, pf[1])
                        e.matmul(pb[:, 0:TCH], Kb[ht][hh * 64:hh * 64 + 64, kt * 128:(kt + 1) * 128], Qb[ht][hh * 64:hh * 64 + 64, qoff:qoff + TCH], start=True, stop=True).then_inc(P.st.h, 1)
                    tse = P.ss.bump(1)
                    @P.on("scalar")
                    def _(e, pb=pb, ebuf=ebuf, tgt=tgt, ef=ef):
                        e.wait_ge(P.st.h, tgt)
                        if ef is not None:
                            e.wait_ge(ef[0].h, ef[1])
                        e.activation(ebuf[:, :], pb[:, 0:TCH], AF.Exp, scale=0.125, bias=ccol_f[:, 0:1]).then_inc(P.ss.h, 1)
                    pingfree[kt % 2] = (P.ss, tse)
                    t_ready = (P.ss, tse)
                    offd = qoff - kt * 128
                    if offd < 512:
                        tvm = P.sv.bump(1)
                        @P.on("vector")
                        def _(e, ebuf=ebuf, offd=offd, tse=tse):
                            e.wait_ge(P.ss.h, tse)
                            e.tensor_tensor(ebuf[:, :], ebuf[:, :], maskb[:, 512 + offd:1024 + offd], ALU.mult).then_inc(P.sv.h, 1)
                        t_ready = (P.sv, tvm)
                    tav = P.st.bump(1)
                    @P.on("tensor")
                    def _(e, kt=kt, h=h, ebuf=ebuf, t_ready=t_ready, nkt=nkt):
                        e.wait_ge(t_ready[0].h, t_ready[1])
                        e.matmul(PAV[0:65, 0:TCH], VTe[kt][:, 65 * h:65 * h + 65], ebuf[:, 0:TCH], start=(kt == 0), stop=(kt == nkt - 1)).then_inc(P.st.h, 1)
                    esfree[kt % 2] = (P.st, tav)
                t_fin = P.st.n
                tvr = P.sv.bump(1)
                @P.on("vector")
                def _(e, t_fin=t_fin):
                    e.wait_ge(P.st.h, t_fin)
                    e.reciprocal(rrow[0:1, :], PAV[64:65, 0:TCH]).then_inc(P.sv.h, 1)
                tbc = P.st.bump(1)
                @P.on("tensor")
                def _(e, tvr=tvr):
                    e.wait_ge(P.sv.h, tvr)
                    e.matmul(PBC[:, 0:TCH], ones1_f[0:1, :], rrow[0:1, 0:TCH], start=True, stop=True).then_inc(P.st.h, 1)
                tsy = P.ss.bump(1)
                @P.on("scalar")
                def _(e, h=h, tbc=tbc):
                    e.wait_ge(P.st.h, tbc)
                    e.activation(Yh[h][:, :], PAV[0:64, 0:TCH], AF.Copy).then_inc(P.ss.h, 1)
                tvy = P.sv.bump(1)
                @P.on("vector")
                def _(e, h=h, tsy=tsy):
                    e.wait_ge(P.ss.h, tsy)
                    e.tensor_tensor(Yh[h][:, :], Yh[h][:, :], PBC[0:64, 0:TCH], ALU.mult).then_inc(P.sv.h, 1)
                P.barrier_engine("tensor", [(P.ss, tsy), (P.sv, tvy)])
            dfree = [None, None]
            P.barrier_engine("tensor", [(P.sv, None)])
            for ot in range(DT):
                pb = [PM0, PM1][ot % 2]
                df = dfree[ot % 2]
                tgt = P.st.bump(1)
                @P.on("tensor")
                def _(e, pb=pb, ot=ot, df=df):
                    if df is not None:
                        e.wait_ge(df[0].h, df[1])
                    e.matmul(pb[:, 0:TCH], bpr[0:1, ot * 128:(ot + 1) * 128], onesr_f[0:1, 0:TCH], start=True, stop=False)
                    for h in range(16):
                        r = e.matmul(pb[:, 0:TCH], wpTb[h][:, ot * 128:(ot + 1) * 128], Yh[h][:, 0:TCH], start=False, stop=(h == 15))
                    r.then_inc(P.st.h, 1)
                tvs = P.sv.bump(1)
                @P.on("vector")
                def _(e, pb=pb, tgt=tgt):
                    e.wait_ge(P.st.h, tgt)
                    e.tensor_copy(sot[:, :], pb[:, 0:TCH]).then_inc(P.sv.h, 1)
                tst = P.store(s_s[ot, :, off:off + TCH], sot[:, :], wait=[(P.sv, tvs)])
                dfree[ot % 2] = (P.sv, tvs)
                P.barrier_engine("vector", [(P.sg, tst)])
    return P.finalize() if own else None


def build_l3b1(P=None):
    own = P is None
    if own:
        P = Prog("l3b1")
    xn_s = P.dram("xn_s", [DT, 128, NT], BF16, "ExternalInput")
    rwT = P.dram("l3_rwT", [DT, 128, 8], F32, "ExternalInput")
    egD = P.dram("l3_eg", [8, DT, 128, 2048], F32, "ExternalInput")
    euD = P.dram("l3_eu", [8, DT, 128, 2048], F32, "ExternalInput")
    edD = P.dram("l3_ed", [8, 16, 128, 1024], F32, "ExternalInput")
    ones1 = P.dram("c_ones1", [1, 128], F32, "ExternalInput")
    iden = P.dram("c_iden", [128, 128], F32, "ExternalInput")
    moe_s = P.dram("moe_s", [DT, 128, NT], BF16, "ExternalOutput")
    xs_s = P.dram("xs_s", [DT, 128, NT], BF16, "Internal")
    rows_s = P.dram("rows_s", [9, NT], F32, "Internal")

    ones1_f = P.sbuf("ones1_f", [1, 128], F32)
    iden_f = P.sbuf("iden_f", [128, 128], F32)
    P.load(ones1_f[:, :], ones1[:, :])
    P.load(iden_f[:, :], iden[:, :])
    rwb = P.sbuf("rwb", [128, DT * 8], BF16)
    rww = P.sbuf("rww", [128, DT * 8], F32)
    t_rw = None
    for k in range(DT):
        t_rw = P.load(rww[:, 8 * k:8 * k + 8], rwT[k])
    tvr = P.sv.bump(1)
    @P.on("vector")
    def _(e, t_rw=t_rw):
        e.wait_ge(P.sd.h, t_rw)
        e.tensor_copy(rwb[:, :], rww[:, :]).then_inc(P.sv.h, 1)

    xnb = [P.sbuf(f"xnb{k}", [128, TCH], BF16) for k in range(DT)]
    rowstage = P.sbuf("rowstage", [1, 9 * TCH], F32)
    rsl = P.sbuf("rsl", [1, TCH], F32)
    selsc = P.sbuf("selsc", [128, 8], F32)
    topc = P.sbuf("topc", [128, 1], F32)
    scc = P.sbuf("scc", [128, 1], F32)
    mbb = P.sbuf("mbb", [128, TCH], BF16)
    mbi = P.sbuf("mbi", [128, TCH], mybir.dt.uint8)
    sg = P.sbuf("sg_", [128, TCH], BF16)
    h1 = P.sbuf("h1_", [128, TCH], BF16)
    hb = [P.sbuf(f"hb{o}", [128, TCH], BF16) for o in range(16)]
    moet = [P.sbuf(f"moet{k}", [128, TCH], BF16) for k in range(DT)]
    wscr = P.sbuf("wscr", [128, 1024], F32)
    egb = [P.sbuf(f"egb{k}", [128, 2048], BF16) for k in range(DT)]
    eub = [P.sbuf(f"eub{k}", [128, 2048], BF16) for k in range(DT)]
    edb = [P.sbuf(f"edb{k}", [128, 1024], BF16) for k in range(16)]
    PRT = P.psum("PRT", [128, TCH])
    PSC = P.psum("PSC", [128, TCH])
    PG = [P.psum(f"PG{i}", [128, TCH]) for i in range(2)]
    PU = [P.psum(f"PU{i}", [128, TCH]) for i in range(2)]
    PD = [P.psum(f"PD{i}", [128, TCH]) for i in range(2)]

    tvz = P.sv.bump(1)
    @P.on("vector")
    def _(e):
        for k in range(DT):
            r = e.memset(moet[k][:, :], 0.0)
        r.then_inc(P.sv.h, 1)
    for ci in range(NCH):
        for k in range(DT):
            P.store(moe_s[k, :, ci * TCH:(ci + 1) * TCH], moet[k][:, :],
                    wait=[(P.sv, tvz)] if (ci == 0 and k == 0) else ())

    for ci in range(NCH):
        off = ci * TCH
        tld = None
        for k in range(DT):
            tld = P.load(xnb[k][:, :], xn_s[k, :, off:off + TCH],
                         wait=[(P.st, None)] if k == 0 else ())
        P.barrier_engine("tensor", [(P.sd, tld), (P.sv, None)])
        for sub in range(TCH // 128):
            tgt = P.st.bump(1)
            @P.on("tensor")
            def _(e, sub=sub):
                for k in range(DT):
                    r = e.matmul(PRT[:, 0:8], xnb[k][:, sub * 128:(sub + 1) * 128], rwb[:, 8 * k:8 * k + 8], start=(k == 0), stop=(k == DT - 1))
                r.then_inc(P.st.h, 1)
            tsel = P.sv.bump(1)
            @P.on("vector")
            def _(e, tgt=tgt):
                e.wait_ge(P.st.h, tgt)
                e.tensor_reduce(topc[:, 0:1], PRT[:, 0:8], AX.X, ALU.max)
                e.tensor_tensor(selsc[:, :], PRT[:, 0:8], topc[:, 0:1].broadcast_to([128, 8]), ALU.is_ge).then_inc(P.sv.h, 1)
            tss = P.ss.bump(1)
            @P.on("scalar")
            def _(e, tsel=tsel):
                e.wait_ge(P.sv.h, tsel)
                e.activation(scc[:, 0:1], topc[:, 0:1], AF.Sigmoid).then_inc(P.ss.h, 1)
            tsc = P.sv.bump(1)
            @P.on("vector")
            def _(e, tss=tss):
                e.wait_ge(P.ss.h, tss)
                e.tensor_single_scalar(selsc[:, :], selsc[:, :], scc[:, 0:1], ALU.mult).then_inc(P.sv.h, 1)
            for j in range(9):
                src = selsc[:, j:j + 1] if j < 8 else scc[:, 0:1]
                pcol = 128 + (j % 2) * 128
                svn2 = P.sv.n
                ttr = P.st.bump(1)
                @P.on("tensor")
                def _(e, src=src, pcol=pcol, svn2=svn2):
                    e.wait_ge(P.sv.h, svn2)
                    e.transpose(PRT[0:1, pcol:pcol + 128], src, iden_f[:, :]).then_inc(P.st.h, 1)
                tcp = P.sv.bump(1)
                @P.on("vector")
                def _(e, j=j, sub=sub, pcol=pcol, ttr=ttr):
                    e.wait_ge(P.st.h, ttr)
                    e.tensor_copy(rowstage[0:1, j * TCH + sub * 128:j * TCH + sub * 128 + 128], PRT[0:1, pcol:pcol + 128]).then_inc(P.sv.h, 1)
        svn = P.sv.n
        for j in range(9):
            P.store(rows_s[j:j + 1, off:off + TCH], rowstage[0:1, j * TCH:(j + 1) * TCH],
                    wait=[(P.sv, svn)] if j == 0 else ())
        tbc = P.st.bump(1)
        @P.on("tensor")
        def _(e, svn=svn):
            e.wait_ge(P.sv.h, svn)
            e.matmul(PSC[:, 0:TCH], ones1_f[0:1, :], rowstage[0:1, 8 * TCH:9 * TCH], start=True, stop=True).then_inc(P.st.h, 1)
        t_xs = None
        for k in range(DT):
            t_xs = P.sv.bump(1)
            @P.on("vector")
            def _(e, k=k, tbc=tbc):
                if k == 0:
                    e.wait_ge(P.st.h, tbc)
                e.tensor_tensor(xnb[k][:, :], xnb[k][:, :], PSC[:, 0:TCH], ALU.mult).then_inc(P.sv.h, 1)
            P.store(xs_s[k, :, off:off + TCH], xnb[k][:, :], wait=[(P.sv, t_xs)])

    for ex in range(8):
        def wload2(dst, src_ap, base, ncol):
            tw = P.load(wscr[:, 0:ncol], src_ap, wait=[(P.sv, None), (P.st, None)])
            tvv = P.sv.bump(1)
            @P.on("vector")
            def _(e, dst=dst, tw=tw, base=base, ncol=ncol):
                e.wait_ge(P.sd.h, tw)
                e.tensor_copy(dst[:, base:base + ncol], wscr[:, 0:ncol]).then_inc(P.sv.h, 1)
        for k in range(DT):
            wload2(egb[k], egD[ex, k, :, 0:1024], 0, 1024)
            wload2(egb[k], egD[ex, k, :, 1024:2048], 1024, 1024)
            wload2(eub[k], euD[ex, k, :, 0:1024], 0, 1024)
            wload2(eub[k], euD[ex, k, :, 1024:2048], 1024, 1024)
        for k in range(16):
            wload2(edb[k], edD[ex, k], 0, 1024)
        for ci in range(NCH):
            off = ci * TCH
            tld = None
            for k in range(DT):
                tld = P.load(xnb[k][:, :], xs_s[k, :, off:off + TCH],
                             wait=[(P.st, None), (P.sv, None)] if k == 0 else ())
            trl = P.load(rsl[0:1, :], rows_s[ex:ex + 1, off:off + TCH])
            tbc = P.st.bump(1)
            svn = P.sv.n
            @P.on("tensor")
            def _(e, svn=svn, trl=trl, tld=tld):
                e.wait_ge(P.sv.h, svn)
                e.wait_ge(P.sd.h, trl)
                e.matmul(PSC[:, 0:TCH], ones1_f[0:1, :], rsl[0:1, 0:TCH], start=True, stop=True).then_inc(P.st.h, 1)
            tmb = P.ss.bump(1)
            @P.on("scalar")
            def _(e, tbc=tbc):
                e.wait_ge(P.st.h, tbc)
                e.activation(mbb[:, :], PSC[:, 0:TCH], AF.Copy).then_inc(P.ss.h, 1)
            tmi = P.sv.bump(1)
            @P.on("vector")
            def _(e, tmb=tmb):
                e.wait_ge(P.ss.h, tmb)
                e.tensor_single_scalar(mbi[:, :], mbb[:, :], 0.0, ALU.is_gt).then_inc(P.sv.h, 1)
            gfree = [None, None]
            ufree = [None, None]
            for ot in range(16):
                pg = PG[ot % 2]
                pu = PU[ot % 2]
                gf, uf = gfree[ot % 2], ufree[ot % 2]
                tgt = P.st.bump(1)
                @P.on("tensor")
                def _(e, ot=ot, pg=pg, pu=pu, gf=gf, uf=uf):
                    if gf is not None:
                        e.wait_ge(gf[0].h, gf[1])
                    if uf is not None:
                        e.wait_ge(uf[0].h, uf[1])
                    for k in range(DT):
                        e.matmul(pg[:, 0:TCH], egb[k][:, ot * 128:(ot + 1) * 128], xnb[k][:, 0:TCH], start=(k == 0), stop=(k == DT - 1))
                    for k in range(DT):
                        r = e.matmul(pu[:, 0:TCH], eub[k][:, ot * 128:(ot + 1) * 128], xnb[k][:, 0:TCH], start=(k == 0), stop=(k == DT - 1))
                    r.then_inc(P.st.h, 1)
                tsg = P.ss.bump(1)
                @P.on("scalar")
                def _(e, pg=pg, tgt=tgt):
                    e.wait_ge(P.st.h, tgt)
                    e.activation(sg[:, :], pg[:, 0:TCH], AF.Silu).then_inc(P.ss.h, 1)
                tvh = P.sv.bump(1)
                @P.on("vector")
                def _(e, ot=ot, pu=pu, tsg=tsg, tgt=tgt, tmb=tmb):
                    e.wait_ge(P.ss.h, tsg)
                    e.wait_ge(P.st.h, tgt)
                    if ot == 0:
                        e.wait_ge(P.ss.h, tmb)
                    e.tensor_tensor(h1[:, :], pu[:, 0:TCH], sg[:, :], ALU.mult)
                    e.tensor_tensor(hb[ot][:, :], h1[:, :], mbb[:, :], ALU.mult).then_inc(P.sv.h, 1)
                gfree[ot % 2] = (P.ss, tsg)
                ufree[ot % 2] = (P.sv, tvh)
            P.barrier_engine("tensor", [(P.sv, None)])
            sgn_pre = P.sg.n
            tmo = None
            for k in range(DT):
                tmo = P.load(moet[k][:, :], moe_s[k, :, off:off + TCH],
                             wait=[(P.sg, sgn_pre), (P.sv, None)] if k == 0 else ())
            dfree = [None, None]
            t_sel = None
            for ot in range(DT):
                pd = PD[ot % 2]
                df = dfree[ot % 2]
                tgt = P.st.bump(1)
                @P.on("tensor")
                def _(e, ot=ot, pd=pd, df=df):
                    if df is not None:
                        e.wait_ge(df[0].h, df[1])
                    for k in range(16):
                        r = e.matmul(pd[:, 0:TCH], edb[k][:, ot * 128:(ot + 1) * 128], hb[k][:, 0:TCH], start=(k == 0), stop=(k == 15))
                    r.then_inc(P.st.h, 1)
                t_sel = P.sv.bump(1)
                @P.on("vector")
                def _(e, ot=ot, pd=pd, tgt=tgt, tmo=tmo):
                    e.wait_ge(P.st.h, tgt)
                    if ot == 0:
                        e.wait_ge(P.sd.h, tmo)
                    e.copy_predicated(moet[ot][:, :], mbi[:, :], pd[:, 0:TCH]).then_inc(P.sv.h, 1)
                dfree[ot % 2] = (P.sv, t_sel)
            for k in range(DT):
                P.store(moe_s[k, :, off:off + TCH], moet[k][:, :],
                        wait=[(P.sv, t_sel)] if k == 0 else ())
    return P.finalize() if own else None


def build_l3b2(P=None):
    own = P is None
    if own:
        P = Prog("l3b2")
    x_in = P.dram("x_in", [DT, 128, NT], F32, "ExternalInput")
    xn_s = P.dram("xn_s", [DT, 128, NT], BF16, "ExternalInput")
    s_s = P.dram("s_s", [DT, 128, NT], F32, "ExternalInput")
    moe_s = P.dram("moe_s", [DT, 128, NT], BF16, "ExternalInput")
    sgD = P.dram("l3_sg", [DT, 128, 2048], F32, "ExternalInput")
    suD = P.dram("l3_su", [DT, 128, 2048], F32, "ExternalInput")
    sdD = P.dram("l3_sd", [16, 128, 1024], F32, "ExternalInput")
    x_out = P.dram("x_out", [DT, 128, NT], F32, "ExternalOutput")

    wscr = P.sbuf("wscr", [128, 2048], F32)
    sgb = [P.sbuf(f"sgb{k}", [128, 2048], BF16) for k in range(DT)]
    sub = [P.sbuf(f"sub{k}", [128, 2048], BF16) for k in range(DT)]
    sdb = [P.sbuf(f"sdb{k}", [128, 1024], BF16) for k in range(16)]
    def wload(dst, src_ap, ncol):
        tw = P.load(wscr[:, 0:ncol], src_ap, wait=[(P.sv, None)])
        tvv = P.sv.bump(1)
        @P.on("vector")
        def _(e, dst=dst, tw=tw, ncol=ncol):
            e.wait_ge(P.sd.h, tw)
            e.tensor_copy(dst[:, 0:ncol], wscr[:, 0:ncol]).then_inc(P.sv.h, 1)
    for k in range(DT):
        wload(sgb[k], sgD[k], 2048)
        wload(sub[k], suD[k], 2048)
    for k in range(16):
        wload(sdb[k], sdD[k], 1024)

    xnb = [P.sbuf(f"xnb{k}", [128, TCH], BF16) for k in range(DT)]
    hb = [P.sbuf(f"hb{o}", [128, TCH], BF16) for o in range(16)]
    sgs = P.sbuf("sgs", [128, TCH], BF16)
    moet = P.sbuf("moet", [128, TCH], BF16)
    xts = P.sbuf("xts", [128, TCH], F32)
    sts = P.sbuf("sts", [128, TCH], F32)
    xo = P.sbuf("xo", [128, TCH], F32)
    PG = [P.psum(f"PG{i}", [128, TCH]) for i in range(2)]
    PU = [P.psum(f"PU{i}", [128, TCH]) for i in range(2)]
    PD = [P.psum(f"PD{i}", [128, TCH]) for i in range(2)]

    for ci in range(NCH):
        off = ci * TCH
        tld = None
        for k in range(DT):
            tld = P.load(xnb[k][:, :], xn_s[k, :, off:off + TCH],
                         wait=[(P.st, None), (P.sv, None)] if k == 0 else ())
        P.barrier_engine("tensor", [(P.sd, tld), (P.sv, None)])
        P.barrier_engine("scalar", [(P.st, None)])
        gfree = [None, None]
        ufree = [None, None]
        for ot in range(16):
            pg = PG[ot % 2]
            pu = PU[ot % 2]
            gf, uf = gfree[ot % 2], ufree[ot % 2]
            tgt = P.st.bump(1)
            @P.on("tensor")
            def _(e, ot=ot, pg=pg, pu=pu, gf=gf, uf=uf):
                if gf is not None:
                    e.wait_ge(gf[0].h, gf[1])
                if uf is not None:
                    e.wait_ge(uf[0].h, uf[1])
                for k in range(DT):
                    e.matmul(pg[:, 0:TCH], sgb[k][:, ot * 128:(ot + 1) * 128], xnb[k][:, 0:TCH], start=(k == 0), stop=(k == DT - 1))
                for k in range(DT):
                    r = e.matmul(pu[:, 0:TCH], sub[k][:, ot * 128:(ot + 1) * 128], xnb[k][:, 0:TCH], start=(k == 0), stop=(k == DT - 1))
                r.then_inc(P.st.h, 1)
            tsg = P.ss.bump(1)
            @P.on("scalar")
            def _(e, pg=pg, tgt=tgt):
                e.wait_ge(P.st.h, tgt)
                e.activation(sgs[:, :], pg[:, 0:TCH], AF.Silu).then_inc(P.ss.h, 1)
            tvh = P.sv.bump(1)
            @P.on("vector")
            def _(e, ot=ot, pu=pu, tsg=tsg, tgt=tgt):
                e.wait_ge(P.ss.h, tsg)
                e.wait_ge(P.st.h, tgt)
                e.tensor_tensor(hb[ot][:, :], pu[:, 0:TCH], sgs[:, :], ALU.mult).then_inc(P.sv.h, 1)
            gfree[ot % 2] = (P.ss, tsg)
            ufree[ot % 2] = (P.sv, tvh)
        P.barrier_engine("tensor", [(P.sv, None)])
        dfree = [None, None]
        for ot in range(DT):
            pd = PD[ot % 2]
            df = dfree[ot % 2]
            tgt = P.st.bump(1)
            @P.on("tensor")
            def _(e, ot=ot, pd=pd, df=df):
                if df is not None:
                    e.wait_ge(df[0].h, df[1])
                for k in range(16):
                    r = e.matmul(pd[:, 0:TCH], sdb[k][:, ot * 128:(ot + 1) * 128], hb[k][:, 0:TCH], start=(k == 0), stop=(k == 15))
                r.then_inc(P.st.h, 1)
            tmo = P.load(moet[:, :], moe_s[ot, :, off:off + TCH], wait=[(P.sv, None)])
            tx = P.load(xts[:, :], x_in[ot, :, off:off + TCH])
            ts2 = P.load(sts[:, :], s_s[ot, :, off:off + TCH])
            tvg = P.sv.bump(1)
            sgn = P.sg.n
            @P.on("vector")
            def _(e, pd=pd, tgt=tgt, ts2=ts2, sgn=sgn):
                e.wait_ge(P.st.h, tgt)
                e.wait_ge(P.sd.h, ts2)
                e.wait_ge(P.sg.h, sgn)
                e.tensor_tensor(xo[:, :], pd[:, 0:TCH], moet[:, :], ALU.add).then_inc(P.sv.h, 1)
            tsi = P.ss.bump(1)
            @P.on("scalar")
            def _(e, tvg=tvg):
                e.wait_ge(P.sv.h, tvg)
                e.activation(xo[:, :], xo[:, :], AF.Silu).then_inc(P.ss.h, 1)
            tvc = P.sv.bump(1)
            @P.on("vector")
            def _(e, tsi=tsi):
                e.wait_ge(P.ss.h, tsi)
                e.tensor_tensor(xo[:, :], xo[:, :], sts[:, :], ALU.mult)
                e.tensor_tensor(xo[:, :], xo[:, :], xts[:, :], ALU.add).then_inc(P.sv.h, 1)
            P.store(x_out[ot, :, off:off + TCH], xo[:, :], wait=[(P.sv, tvc)])
            dfree[ot % 2] = (P.sv, tvg)
    return P.finalize() if own else None


def _rope_table():
    d = 64
    half = np.arange(0, d // 2, dtype=np.float64)
    theta = 1.0 / (10000.0 ** (2.0 * half / d))
    seq = np.arange(1, S + 1, dtype=np.float64)
    ang = seq[:, None] * theta[None, :]
    ang = np.concatenate([ang, ang], axis=1)
    f = (np.cos(ang) + np.sin(ang)).astype(np.float32)
    return np.ascontiguousarray(np.tile(f.T, (2, 1)))


def _mask_table():
    p = np.arange(128)[:, None]
    i = np.arange(1536)[None, :]
    return (p <= i - 512).astype(np.float32)


CONSTS.update({
    "c_onesr": np.ones((1, 512), np.float32),
    "c_rope": _rope_table(),
    "c_mask": _mask_table(),
    "c_cols": np.concatenate([np.full((128, 1), -40.0, np.float32), np.ones((128, 1), np.float32), np.zeros((128, 2), np.float32)], axis=1),
    "c_iden": np.eye(128, dtype=np.float32),
})


def prep_l3(ins):
    l3a = {
        "l3_waT": _wT_tiles(ins["L3_attn_W_attn"]),
        "l3_ba": np.asarray(ins["L3_attn_b_attn"]).reshape(1, 3072),
        "l3_wpTh": np.ascontiguousarray(np.transpose(np.asarray(ins["L3_attn_W_proj"])).reshape(16, 64, D)),
        "l3_bp": np.asarray(ins["L3_attn_b_proj"]).reshape(1, D),
    }
    l3b1 = {
        "l3_rwT": _wT_tiles(ins["L3_ffn_router_w"]),
        "l3_eg": np.ascontiguousarray(np.asarray(ins["L3_ffn_expert_gate"]).reshape(8, DT, 128, 2048)),
        "l3_eu": np.ascontiguousarray(np.asarray(ins["L3_ffn_expert_up"]).reshape(8, DT, 128, 2048)),
        "l3_ed": np.ascontiguousarray(np.asarray(ins["L3_ffn_expert_down"]).reshape(8, 16, 128, 1024)),
    }
    l3b2 = {
        "l3_sg": np.ascontiguousarray(np.asarray(ins["L3_ffn_shared_gate"]).reshape(DT, 128, 2048)),
        "l3_su": np.ascontiguousarray(np.asarray(ins["L3_ffn_shared_up"]).reshape(DT, 128, 2048)),
        "l3_sd": np.ascontiguousarray(np.asarray(ins["L3_ffn_shared_down"]).reshape(16, 128, 1024)),
    }
    return l3a, l3b1, l3b2


def run_l3(xcore, l3a, l3b1, l3b2):
    in_maps = [{"x_in": xcore[c], **l3a, **CONSTS} for c in range(NCORES)]
    ra = _run(build_l3a1, "l3a1", in_maps)
    la2 = {k: l3a[k] for k in ("l3_wpTh", "l3_bp")}
    in_maps = [{"K_s": ra[c]["K_s"], "Q_s": ra[c]["Q_s"], "VT_s": ra[c]["VT_s"], **la2, **CONSTS} for c in range(NCORES)]
    ra2 = _run(build_l3a2, "l3a2", in_maps)
    in_maps = [{"xn_s": ra[c]["xn_s"], **l3b1, **CONSTS} for c in range(NCORES)]
    rb = _run(build_l3b1, "l3b1", in_maps)
    in_maps = [{"x_in": xcore[c], "xn_s": ra[c]["xn_s"], "s_s": ra2[c]["s_s"],
                "moe_s": rb[c]["moe_s"], **l3b2} for c in range(NCORES)]
    rc = _run(build_l3b2, "l3b2", in_maps)
    return [rc[c]["x_out"] for c in range(NCORES)]



L1CH = 128
L1SEG = 512
DCONV = 4


def build_l1a1(P=None):
    own = P is None
    if own:
        P = Prog("l1a1")
    xn_s = P.dram("xn_s", [DT, 128, NT], BF16, "ExternalInput")
    winT = P.dram("l1_winT", [DT, 128, 4384], F32, "ExternalInput")
    convw = P.dram("l1_convw", [18, 128, DCONV + 1], F32, "ExternalInput")
    iden = P.dram("c_iden", [128, 128], F32, "ExternalInput")
    z_s = P.dram("z_s", [BL, S // L1CH, 128, 2048], BF16, "ExternalOutput")
    cdt_s = P.dram("cdt_s", [BL, S // L1CH, 128, 2336], BF16, "ExternalOutput")
    xbc_s = P.dram("xbc_s", [19, 128, NT], BF16, "Internal")

    iden_f = P.sbuf("iden_f", [128, 128], F32)
    iden_b = P.sbuf("iden_b", [128, 128], BF16)
    P.load(iden_f[:, :], iden[:, :])
    wscr = P.sbuf("wscr", [128, 2192], F32)
    wb = [P.sbuf(f"wb{k}", [128, 4384], BF16) for k in range(DT)]
    cwb = P.sbuf("cwb", [128, 18 * (DCONV + 1)], F32)
    t_cw = None
    for k in range(18):
        t_cw = P.load(cwb[:, k * 5:(k + 1) * 5], convw[k])
    tvi = P.sv.bump(1)
    @P.on("vector")
    def _(e, t_cw=t_cw):
        e.wait_ge(P.sd.h, t_cw)
        e.tensor_copy(iden_b[:, :], iden_f[:, :]).then_inc(P.sv.h, 1)
    def wload(dst, base, src_ap, ncol):
        tw = P.load(wscr[:, 0:ncol], src_ap, wait=[(P.sv, None)])
        tvv = P.sv.bump(1)
        @P.on("vector")
        def _(e, dst=dst, base=base, tw=tw, ncol=ncol):
            e.wait_ge(P.sd.h, tw)
            e.tensor_copy(dst[:, base:base + ncol], wscr[:, 0:ncol]).then_inc(P.sv.h, 1)
    for k in range(DT):
        wload(wb[k], 0, winT[k][:, 0:2192], 2192)
        wload(wb[k], 2192, winT[k][:, 2192:4384], 2192)

    xnb = [P.sbuf(f"xnb{k}", [128, TCH], BF16) for k in range(DT)]
    obuf = [P.sbuf(f"obuf{i}", [128, TCH], BF16) for i in range(2)]
    PM0 = P.psum("PM0", [128, TCH])
    PM1 = P.psum("PM1", [128, TCH])

    for ci in range(NCH):
        off = ci * TCH
        b = off // S
        tld = None
        for k in range(DT):
            tld = P.load(xnb[k][:, :], xn_s[k, :, off:off + TCH],
                         wait=[(P.st, None), (P.sv, None)] if k == 0 else ())
        P.barrier_engine("tensor", [(P.sd, tld), (P.sv, None), (P.ss, None)])
        bfree = [None, None]
        for ot in range(35):
            nrow = 128 if ot < 34 else 32
            pb = [PM0, PM1][ot % 2]
            ob = obuf[ot % 2]
            bf = bfree[ot % 2]
            tgt = P.st.bump(1)
            @P.on("tensor")
            def _(e, ot=ot, pb=pb, nrow=nrow, bf=bf):
                if bf is not None:
                    e.wait_ge(bf[0].h, bf[1])
                for k in range(DT):
                    r = e.matmul(pb[0:nrow, 0:TCH], wb[k][:, ot * 128:ot * 128 + nrow], xnb[k][:, 0:TCH], start=(k == 0), stop=(k == DT - 1))
                r.then_inc(P.st.h, 1)
            sgn = P.sg.n
            tse = P.ss.bump(1)
            @P.on("scalar")
            def _(e, pb=pb, ob=ob, tgt=tgt, sgn=sgn, nrow=nrow, ot=ot):
                e.wait_ge(P.st.h, tgt)
                e.wait_ge(P.sg.h, sgn)
                if ot < 16:
                    e.activation(ob[:, :], pb[:, 0:TCH], AF.Silu).then_inc(P.ss.h, 1)
                else:
                    e.activation(ob[0:nrow, :], pb[0:nrow, 0:TCH], AF.Copy).then_inc(P.ss.h, 1)
            if ot < 16:
                for sc in range(TCH // L1CH):
                    cc = (off % S + sc * L1CH) // L1CH
                    P.store(z_s[b, cc, :, ot * 128:(ot + 1) * 128], ob[:, sc * L1CH:(sc + 1) * L1CH],
                            wait=[(P.ss, tse)] if sc == 0 else ())
            else:
                P.store(xbc_s[ot - 16, 0:nrow, off:off + TCH], ob[0:nrow, :], wait=[(P.ss, tse)])
            bfree[ot % 2] = (P.ss, tse)
    xh = P.sbuf("xh", [128, L1CH + 3], BF16)
    cva = P.sbuf("cva", [128, L1CH], F32)
    cvb = P.sbuf("cvb", [128, L1CH], BF16)
    ttb = P.sbuf("ttb", [128, L1CH], BF16)
    PMT = P.psum("PMT", [128, L1CH], BF16)
    P.barrier_engine("sync", [(P.sg, None)])
    for b in range(BL):
        for cc in range(S // L1CH):
            gcol = b * S + cc * L1CH
            for ft in range(19):
                nrow = 128 if ft < 18 else 32
                svn = P.sv.n
                ssn = P.ss.n
                stn = P.st.n
                if ft < 18:
                    if cc == 0:
                        tvh = P.sv.bump(1)
                        @P.on("vector")
                        def _(e, ssn=ssn, stn=stn):
                            e.wait_ge(P.ss.h, ssn)
                            e.wait_ge(P.st.h, stn)
                            e.memset(xh[:, 0:3], 0.0).then_inc(P.sv.h, 1)
                        tld = P.load(xh[:, 3:3 + L1CH], xbc_s[ft, :, gcol:gcol + L1CH], wait=[(P.sv, tvh)])
                    else:
                        tld = P.load(xh[:, 0:3 + L1CH], xbc_s[ft, :, gcol - 3:gcol + L1CH],
                                     wait=[(P.sv, svn), (P.ss, ssn), (P.st, stn)])
                    tcv = P.sv.bump(1)
                    @P.on("vector")
                    def _(e, ft=ft, tld=tld):
                        e.wait_ge(P.sd.h, tld)
                        e.tensor_single_scalar(cva[:, :], xh[:, 3:3 + L1CH], cwb[:, 5 * ft + 3:5 * ft + 4], ALU.mult)
                        for kk in range(3):
                            e.scalar_tensor_tensor(cva[:, :], xh[:, kk:kk + L1CH], cwb[:, 5 * ft + kk:5 * ft + kk + 1], cva[:, :], ALU.mult, ALU.add)
                        e.tensor_single_scalar(cvb[:, :], cva[:, :], cwb[:, 5 * ft + 4:5 * ft + 5], ALU.add).then_inc(P.sv.h, 1)
                    src = cvb
                else:
                    tld = P.load(xh[0:32, 3:3 + L1CH], xbc_s[18, 0:32, gcol:gcol + L1CH],
                                 wait=[(P.sv, svn), (P.ss, ssn), (P.st, stn)])
                    tcv = P.sv.bump(1)
                    @P.on("vector")
                    def _(e, tld=tld):
                        e.wait_ge(P.sd.h, tld)
                        e.tensor_copy(cvb[0:32, :], xh[0:32, 3:3 + L1CH]).then_inc(P.sv.h, 1)
                    src = cvb
                ssn3 = P.ss.n
                ttr = P.st.bump(1)
                @P.on("tensor")
                def _(e, src=src, tcv=tcv, nrow=nrow, ssn3=ssn3):
                    e.wait_ge(P.sv.h, tcv)
                    e.wait_ge(P.ss.h, ssn3)
                    e.transpose(PMT[:, 0:nrow], src[0:nrow, 0:L1CH], iden_b[0:nrow, 0:nrow]).then_inc(P.st.h, 1)
                sgn = P.sg.n
                tev = P.ss.bump(1)
                @P.on("scalar")
                def _(e, ttr=ttr, sgn=sgn, nrow=nrow):
                    e.wait_ge(P.st.h, ttr)
                    e.wait_ge(P.sg.h, sgn)
                    e.activation(ttb[:, 0:nrow], PMT[:, 0:nrow], AF.Copy).then_inc(P.ss.h, 1)
                P.store(cdt_s[b, cc, :, ft * 128:ft * 128 + nrow], ttb[:, 0:nrow], wait=[(P.ss, tev)])
    return P.finalize() if own else None


def build_l1scan(P=None, seg=None):
    own = P is None
    if own:
        P = Prog("l1scan")
    wrT = P.dram("l1_wrT", [16, 128, 2336], F32, "ExternalInput")
    woutT = P.dram("l1_woutT", [16, 128, 1024], F32, "ExternalInput")
    cdtD = P.dram("cdt_c", [BL, 4, 128, 2336], BF16, "ExternalInput")
    zD = P.dram("z_c", [BL, 4, 128, 2048], BF16, "ExternalInput")
    S0i = P.dram("S0_i", [BL, 128, 2048], BF16, "ExternalInput")
    ypi = P.dram("yprev_i", [128, 32], BF16, "ExternalInput")
    hcolsD = P.dram("l1_hcols", [32, 4], F32, "ExternalInput")
    nwD = P.dram("l1_nw", [128, 16], F32, "ExternalInput")
    DexpD = P.dram("l1_Dexp", [128, 16], F32, "ExternalInput")
    iden = P.dram("c_iden", [128, 128], F32, "ExternalInput")
    inj = P.dram("c_inj", [128, 512], F32, "ExternalInput")
    eye2D = P.dram("c_eye2", [128, 2], F32, "ExternalInput")
    ones = P.dram("c_ones", [128, 1], F32, "ExternalInput")
    ones1 = P.dram("c_ones1", [1, 128], F32, "ExternalInput")
    ccols = P.dram("c_cols", [128, 4], F32, "ExternalInput")
    misc = P.dram("c_misc", [1, 8], F32, "ExternalInput")
    S0o = P.dram("S0_o", [BL, 128, 2048], BF16, "ExternalOutput")
    dbgD = P.dram("dbg_o", [128, 1024], F32, "ExternalOutput") if L1DBG else None
    ypo = P.dram("yprev_o", [128, 32], BF16, "ExternalOutput")
    o_s = P.dram("o_s", [DT, 128, BL * L1SEG], F32, "ExternalOutput")

    Wm = [P.sbuf(f"Wm{k}", [128, 2336], BF16) for k in range(16)]
    wout = [P.sbuf(f"wout{k}", [128, 1024], BF16) for k in range(16)]
    wscr = P.sbuf("wscr", [128, 2336], F32)
    iden_f = P.sbuf("iden_f", [128, 128], F32)
    iden_b = P.sbuf("iden_b", [128, 128], BF16)
    injb = P.sbuf("injb", [128, 512], BF16)
    eye2b = P.sbuf("eye2b", [128, 2], BF16)
    eye2f = P.sbuf("eye2f", [128, 2], F32)
    ones_f = P.sbuf("ones_f", [128, 1], F32)
    ones1_f = P.sbuf("ones1_f", [1, 128], F32)
    ccol_f = P.sbuf("ccol_f", [128, 4], F32)
    ceps = P.sbuf("ceps", [1, 8], F32)
    hcols = P.sbuf("hcols", [32, 8], F32)
    nw = P.sbuf("nw", [128, 16], F32)
    Dexp = P.sbuf("Dexp", [128, 16], F32)
    tmpD = P.sbuf("tmpD", [128, 32], F32)
    cdt = [P.sbuf(f"cdt{b}", [128, 2336], BF16) for b in range(BL)]
    zc = [P.sbuf(f"zc{b}", [128, 2048], BF16) for b in range(BL)]
    X_s = [P.sbuf(f"X_s{b}", [128, 2048], BF16) for b in range(BL)]
    S0 = [P.sbuf(f"S0{b}", [128, 2048], BF16) for b in range(BL)]
    XU = P.sbuf("XU", [128, 2048], BF16)
    B_m = [P.sbuf(f"B_m{b}", [128, 128], BF16) for b in range(BL)]
    BTsb = P.sbuf("BTsb", [128, 128], BF16)
    U_r = P.sbuf("U_r", [128, 64], F32)
    U_v = P.sbuf("U_v", [128, 64], F32)
    yfm = P.sbuf("yfm", [128, 32], BF16)
    srs = P.sbuf("srs", [128, 512], BF16)
    srs2 = P.sbuf("srs2", [128, 512], BF16)
    fm2 = P.sbuf("fm2", [128, 38], BF16)
    dtc = P.sbuf("dtc", [32, 2], F32)
    dAc = P.sbuf("dAc", [32, 2], F32)
    negeA = P.sbuf("negeA", [32, 1], F32)
    drow = P.sbuf("drow", [1, 64], F32)
    dtrow = P.sbuf("dtrow", [1, 64], F32)
    ur_sb = P.sbuf("ur_sb", [128, 64], BF16)
    Cp = P.sbuf("Cp", [128, 64], BF16)
    uv_sb = P.sbuf("uv_sb", [128, 64], BF16)
    ysb = P.sbuf("ysb", [128, 32], F32)
    g32 = P.sbuf("g32", [128, 32], F32)
    sqy = P.sbuf("sqy", [128, 32], F32)
    msr = P.sbuf("msr", [1, 2], F32)
    YN = P.sbuf("YN", [128, 4096], BF16)
    obuf = P.sbuf("obuf", [128, 256], F32)
    zsb = P.sbuf("zsb", [128, 128], BF16)
    dbgb = P.sbuf("dbgb", [128, 1024], F32) if L1DBG else None
    PRa = P.psum("PRa", [128, 512])
    PRb = P.psum("PRb", [128, 512])
    PT = P.psum("PT", [128, 128], BF16)
    PBC = P.psum("PBC", [128, 512])
    YP = P.psum("YP", [128, 128])
    DD = P.psum("DD", [128, 64])

    P.load(iden_f[:, :], iden[:, :])
    P.load(ones_f[:, :], ones[:, :])
    P.load(ones1_f[:, :], ones1[:, :])
    P.load(ccol_f[:, :], ccols[:, :])
    P.load(ceps[0:1, :], misc[0:1, :])
    P.load(hcols[:, 0:4], hcolsD[:, :])
    P.load(nw[:, :], nwD[:, :])
    P.load(Dexp[:, :], DexpD[:, :])
    P.load(yfm[:, :], ypi[:, :])
    for b in range(BL):
        P.load(S0[b][:, :], S0i[b])
    P.load(eye2f[:, :], eye2D[:, :])
    tw0 = P.load(wscr[:, 0:512], inj[:, :])
    t_ini = P.sv.bump(1)
    @P.on("vector")
    def _(e, tw0=tw0):
        e.wait_ge(P.sd.h, tw0)
        e.tensor_copy(iden_b[:, :], iden_f[:, :])
        e.tensor_copy(eye2b[:, :], eye2f[:, :])
        for b in range(BL):
            e.memset(B_m[b][:, :], 0.0)
            e.memset(X_s[b][:, :], 0.0)
        e.memset(PRa[:, :], 0.0)
        e.memset(PRb[:, :], 0.0)
        e.memset(zsb[:, :], 0.0)
        e.memset(PBC[:, :], 0.0)
        e.memset(YP[:, :], 0.0)
        e.memset(DD[:, :], 0.0)
        e.tensor_copy(injb[:, :], wscr[:, 0:512]).then_inc(P.sv.h, 1)
    t_iniS = P.ss.bump(1)
    @P.on("scalar")
    def _(e, tw0=tw0):
        e.wait_ge(P.sd.h, tw0)
        e.activation(negeA[:, 0:1], hcols[:, 1:2], AF.Exp)
        e.activation(negeA[:, 0:1], negeA[:, 0:1], AF.Copy, scale=-1.0).then_inc(P.ss.h, 1)
    t_zpt = P.st.bump(1)
    @P.on("tensor")
    def _(e, t_ini=t_ini):
        e.wait_ge(P.sv.h, t_ini)
        e.transpose(PT[:, 0:128], zsb[:, 0:128], iden_b[:, :]).then_inc(P.st.h, 1)
    def wload(dst, src_ap, ncol):
        tw = P.load(wscr[:, 0:ncol], src_ap, wait=[(P.sv, None)])
        tvv = P.sv.bump(1)
        @P.on("vector")
        def _(e, dst=dst, tw=tw, ncol=ncol):
            e.wait_ge(P.sd.h, tw)
            e.tensor_copy(dst[:, 0:ncol], wscr[:, 0:ncol]).then_inc(P.sv.h, 1)
    for k in range(16):
        wload(Wm[k], wrT[k], 2336)
    for k in range(16):
        wload(wout[k], woutT[k], 1024)

    NCC = L1SEG // L1CH
    t_prev = {"silu": 0, "transp": 0, "fmevac": 0, "yfm": t_ini, "xdma": 0,
              "udma": 0, "ymm": 0, "extr": 0, "udec": t_ini, "ucons": 0,
              "bcopy": 0, "dtch": t_iniS, "pbcv": 0, "pbct": 0, "dots": 0,
              "ynyn": 0, "sq": 0}
    for cc in range(NCC):
        svn = P.sv.n
        stn = P.st.n
        sgn0 = P.sg.n
        tld = None
        for b in range(BL):
            ccg = cc if seg is None else seg * 4 + cc
            tld = P.load(cdt[b][:, :], cdtD[b, ccg], wait=[(P.st, stn), (P.sv, svn)] if b == 0 else ())
            tld = P.load(zc[b][:, :], zD[b, ccg])
        tum = P.sv.bump(1)
        @P.on("vector")
        def _(e, svn=svn, stn=stn, sgn0=sgn0, tld=tld):
            e.wait_ge(P.st.h, stn)
            e.wait_ge(P.sg.h, sgn0)
            e.wait_ge(P.sd.h, tld)
            e.memset(U_r[:, :], 0.0)
            e.memset(U_v[:, :], 1.0).then_inc(P.sv.h, 1)
        t_prev["udec"] = tum
        P.barrier_engine("tensor", [(P.sd, tld), (P.sv, None)])
        for t in range(L1CH):
            waits = [(P.sd, None), (P.sg, None), (P.st, None), (P.sv, None), (P.ss, None)]
            P.barrier_engine("tensor", [w for w in waits if w[0] is not P.st])
            P.barrier_engine("vector", [w for w in waits if w[0] is not P.sv])
            P.barrier_engine("scalar", [w for w in waits if w[0] is not P.ss])
            P.barrier_engine("sync", [(P.st, None), (P.sv, None), (P.ss, None)])
            P.barrier_engine("gpsimd", [(P.st, None), (P.sv, None), (P.ss, None)])
            tmv = P.st.bump(1)
            @P.on("tensor")
            def _(e, t=t, tp_yfm=t_prev["yfm"], tp_silu=t_prev["silu"]):
                e.wait_ge(P.sv.h, tp_yfm)
                e.wait_ge(P.ss.h, tp_silu)
                for j in range(5):
                    bank = PRa if j < 3 else PRb
                    base = 32 * (j % 3)
                    ncol = 512 if j < 4 else 288
                    tp = (0, base)
                    for b in range(BL):
                        e.matmul(bank[base:base + 2, 0:ncol], injb[:, 4 * t + 2 * b:4 * t + 2 * b + 2],
                                 cdt[b][:, 512 * j:512 * j + ncol], start=(b == 0), stop=False, tile_position=tp)
                    for k in range(16):
                        r = e.matmul(bank[base:base + 2, 0:ncol], yfm[:, 2 * k:2 * k + 2],
                                     Wm[k][:, 512 * j:512 * j + ncol], start=False, stop=(k == 15), tile_position=tp)
                r.then_inc(P.st.h, 1)
            tsl = P.ss.bump(1)
            @P.on("scalar")
            def _(e, tmv=tmv, tp_tr=t_prev["transp"], tp_x=t_prev["xdma"]):
                e.wait_ge(P.st.h, tmv)
                e.wait_ge(P.st.h, tp_tr)
                e.wait_ge(P.sd.h, tp_x)
                e.activation(srs[0:66, :], PRa[0:66, 0:512], AF.Silu)
                e.activation(srs2[0:66, :], PRb[0:66, 0:512], AF.Silu).then_inc(P.ss.h, 1)
            t_prev["silu"] = tsl
            txd = None
            for b in range(BL):
                for j in range(4):
                    src = srs if j < 3 else srs2
                    row = 32 * (j % 3) + b
                    txd = P.load(X_s[b][t:t + 1, 512 * j:512 * (j + 1)], src[row:row + 1, 0:512],
                                 wait=[(P.ss, tsl)] if (b == 0 and j == 0) else ())
            t_prev["xdma"] = txd
            ttr = P.st.bump(1)
            @P.on("tensor")
            def _(e, tsl=tsl, tp_fm=t_prev["fmevac"]):
                e.wait_ge(P.ss.h, tsl)
                e.wait_ge(P.sv.h, tp_fm)
                for J in range(19):
                    j = J // 4
                    co = 128 * (J % 4)
                    src = srs if j < 3 else srs2
                    base = 32 * (j % 3) if j < 3 else 32 * (j - 3)
                    ncol = 128 if J < 18 else 32
                    r = e.transpose(PT[0:ncol, 2 * J:2 * J + 2], src[base:base + 2, co:co + ncol], eye2b[base:base + 2, 0:2])
                r.then_inc(P.st.h, 1)
            t_prev["transp"] = ttr
            tfm = P.sv.bump(1)
            @P.on("vector")
            def _(e, ttr=ttr, tp_dt=t_prev["dtch"], tp_dots=t_prev["dots"]):
                e.wait_ge(P.st.h, ttr)
                e.wait_ge(P.ss.h, tp_dt)
                e.wait_ge(P.st.h, tp_dots)
                pt_v = PT[:, 0:38].rearrange("p (j two) -> p j two", two=2)
                fm_v = fm2.rearrange("p (b j) -> p j b", j=19)
                e.tensor_copy(fm_v, pt_v).then_inc(P.sv.h, 1)
            t_prev["fmevac"] = tfm
            tbc_ = P.sv.bump(1)
            @P.on("vector")
            def _(e, t=t, tfm=tfm):
                e.wait_ge(P.sv.h, tfm)
                for b in range(BL):
                    r = e.tensor_copy(B_m[b][:, t:t + 1], fm2[:, 19 * b + 16:19 * b + 17])
                r.then_inc(P.sv.h, 1)
            t_prev["bcopy"] = tbc_
            tdt1 = P.ss.bump(1)
            @P.on("scalar")
            def _(e, tfm=tfm):
                e.wait_ge(P.sv.h, tfm)
                fmdt = fm2[0:32, :].rearrange("p (b j) -> p b j", j=19)[:, :, 18]
                e.add(dtc[:, :], fmdt, hcols[:, 0:1])
                e.activation(dAc[:, :], dtc[:, :], AF.Abs)
                e.activation(dAc[:, :], dAc[:, :], AF.Exp, scale=-1.0)
                e.activation(dAc[:, :], dAc[:, :], AF.Ln, bias=ccol_f[0:32, 1:2])
                e.activation(dtc[:, :], dtc[:, :], AF.Relu).then_inc(P.ss.h, 1)
            tva = P.sv.bump(1)
            @P.on("vector")
            def _(e, tdt1=tdt1):
                e.wait_ge(P.ss.h, tdt1)
                e.tensor_tensor(dtc[:, :], dtc[:, :], dAc[:, :], ALU.add).then_inc(P.sv.h, 1)
            tdt = P.ss.bump(1)
            @P.on("scalar")
            def _(e, tva=tva):
                e.wait_ge(P.sv.h, tva)
                e.activation(dAc[:, :], dtc[:, :], AF.Exp, scale=negeA[:, 0:1]).then_inc(P.ss.h, 1)
            t_prev["dtch"] = tdt
            ttr2 = P.st.bump(1)
            @P.on("tensor")
            def _(e, tdt=tdt, tp_pv=t_prev["pbcv"]):
                e.wait_ge(P.ss.h, tdt)
                e.wait_ge(P.sv.h, tp_pv)
                e.transpose(PBC[0:1, 0:32], dAc[:, 0:1], iden_f[0:32, 0:32])
                e.transpose(PBC[0:1, 32:64], dAc[:, 1:2], iden_f[0:32, 0:32])
                e.transpose(PBC[0:1, 136:168], dtc[:, 0:1], iden_f[0:32, 0:32])
                r = e.transpose(PBC[0:1, 168:200], dtc[:, 1:2], iden_f[0:32, 0:32])
                r.then_inc(P.st.h, 1)
            tev = P.sv.bump(1)
            @P.on("vector")
            def _(e, ttr2=ttr2):
                e.wait_ge(P.st.h, ttr2)
                e.tensor_copy(drow[0:1, :], PBC[0:1, 0:64])
                e.tensor_copy(dtrow[0:1, :], PBC[0:1, 136:200]).then_inc(P.sv.h, 1)
            tbcm = P.st.bump(1)
            @P.on("tensor")
            def _(e, tev=tev):
                e.wait_ge(P.sv.h, tev)
                e.matmul(PBC[:, 64:128], ones1_f[0:1, :], drow[0:1, 0:64], start=True, stop=True).then_inc(P.st.h, 1)
            tud = P.sv.bump(1)
            @P.on("vector")
            def _(e, tbcm=tbcm, tp_udma=t_prev["udma"]):
                e.wait_ge(P.st.h, tbcm)
                e.wait_ge(P.sg.h, tp_udma)
                e.tensor_tensor(U_r[:, :], U_r[:, :], PBC[:, 64:128], ALU.mult)
                e.tensor_tensor(U_v[:, :], U_v[:, :], PBC[:, 64:128], ALU.mult).then_inc(P.sv.h, 1)
            t_prev["udec"] = tud
            tudma = P.store(U_r[t:t + 1, :], dtrow[0:1, :], wait=[(P.sv, tud)])
            t_prev["udma"] = tudma
            tcp_ = P.sv.bump(1)
            @P.on("vector")
            def _(e, tbc_=tbc_):
                e.wait_ge(P.sv.h, tbc_)
                fmc2 = fm2.rearrange("p (b j) -> p b j", j=19)[:, :, 17:18].broadcast_to([128, BL, 32])
                e.tensor_copy(Cp.rearrange("p (b h) -> p b h", h=32), fmc2).then_inc(P.sv.h, 1)
            tdot = P.st.bump(1)
            @P.on("tensor")
            def _(e, tcp_=tcp_, tp_u=t_prev["ucons"]):
                e.wait_ge(P.sv.h, tcp_)
                e.wait_ge(P.sv.h, tp_u)
                for b in range(BL):
                    r = e.matmul(DD[:, 32 * b:32 * b + 32], B_m[b][:, :],
                                 Cp[:, 32 * b:32 * b + 32], start=True, stop=True)
                r.then_inc(P.st.h, 1)
            t_prev["dots"] = tdot
            tu = P.sv.bump(1)
            @P.on("vector")
            def _(e, tdot=tdot, tud=tud, tudma=tudma):
                e.wait_ge(P.st.h, tdot)
                e.wait_ge(P.sg.h, tudma)
                e.wait_ge(P.sv.h, tud)
                e.tensor_tensor(ur_sb[:, :], DD[:, 0:64], U_r[:, :], ALU.mult)
                fmc = fm2.rearrange("p (b j) -> p b j", j=19)[:, :, 17:18].broadcast_to([128, BL, 32])
                e.tensor_tensor(uv_sb.rearrange("p (b h) -> p b h", h=32), U_v.rearrange("p (b h) -> p b h", h=32), fmc, ALU.mult).then_inc(P.sv.h, 1)
            t_prev["ucons"] = tu
            tym = P.st.bump(1)
            @P.on("tensor")
            def _(e, tu=tu, tp_ex=t_prev["extr"], txd=t_prev["xdma"]):
                e.wait_ge(P.sv.h, tu)
                e.wait_ge(P.sv.h, tp_ex)
                e.wait_ge(P.sd.h, txd)
                for b in range(BL):
                    for j in range(16):
                        pc = (b * 16 + j) * 2
                        e.matmul(YP[:, pc:pc + 2], S0[b][:, 128 * j:128 * (j + 1)], uv_sb[:, b * 32 + 2 * j:b * 32 + 2 * j + 2], start=True, stop=False)
                        r = e.matmul(YP[:, pc:pc + 2], X_s[b][:, 128 * j:128 * (j + 1)], ur_sb[:, b * 32 + 2 * j:b * 32 + 2 * j + 2], start=False, stop=True)
                r.then_inc(P.st.h, 1)
            t_prev["ymm"] = tym
            tex0 = P.sv.bump(1)
            @P.on("vector")
            def _(e, tym=tym):
                e.wait_ge(P.st.h, tym)
                yp_u = YP[:, 0:64].rearrange("p (c two) -> p c two", two=2)
                e.tensor_copy(ysb[0:64, :], yp_u[0:64, :, 0])
                e.tensor_copy(ysb[64:128, :], yp_u[64:128, :, 1])
                xv = fm2.rearrange("p (b j) -> p b j", j=19)[:, :, 0:16]
                e.tensor_tensor(tmpD.rearrange("p (b j) -> p b j", j=16), xv,
                                Dexp[:, None, :].broadcast_to([128, 2, 16]), ALU.mult).then_inc(P.sv.h, 1)
            tex = P.sv.bump(1)
            @P.on("vector")
            def _(e, tex0=tex0):
                e.wait_ge(P.sv.h, tex0)
                e.tensor_tensor(ysb[:, :], ysb[:, :], tmpD[:, :], ALU.add).then_inc(P.sv.h, 1)
            t_prev["extr"] = tex
            tg = P.sv.bump(1)
            @P.on("vector")
            def _(e, t=t, tex=tex):
                e.wait_ge(P.sv.h, tex)
                for b in range(BL):
                    r = e.tensor_tensor(g32[:, 16 * b:16 * (b + 1)], ysb[:, 16 * b:16 * (b + 1)],
                                        zc[b].rearrange("p (k t2) -> p k t2", t2=128)[:, :, t], ALU.mult)
                r.then_inc(P.sv.h, 1)
            tsq = P.ss.bump(1)
            @P.on("scalar")
            def _(e, tg=tg):
                e.wait_ge(P.sv.h, tg)
                e.activation(sqy[:, :], g32[:, :], AF.Square).then_inc(P.ss.h, 1)
            t_prev["sq"] = tsq
            trm = P.st.bump(1)
            @P.on("tensor")
            def _(e, tsq=tsq):
                e.wait_ge(P.ss.h, tsq)
                e.matmul(PBC[0:1, 328:360], ones_f[:, 0:1], sqy[:, :], start=True, stop=True).then_inc(P.st.h, 1)
            tms0 = P.sv.bump(1)
            @P.on("vector")
            def _(e, trm=trm, tg=tg):
                e.wait_ge(P.st.h, trm)
                e.wait_ge(P.sv.h, tg)
                e.tensor_reduce(msr[0:1, 0:2], PBC[0:1, 328:360].rearrange("p (b j) -> p b j", j=16), AX.X, ALU.add).then_inc(P.sv.h, 1)
            tms = P.sv.bump(1)
            @P.on("vector")
            def _(e, tms0=tms0):
                e.wait_ge(P.sv.h, tms0)
                e.tensor_scalar_mul(msr[0:1, 0:2], msr[0:1, 0:2], 1.0 / 2048.0).then_inc(P.sv.h, 1)
            trs = P.ss.bump(1)
            @P.on("scalar")
            def _(e, tms=tms):
                e.wait_ge(P.sv.h, tms)
                e.activation(msr[0:1, 0:2], msr[0:1, 0:2], AF.Ln, bias=ceps[0:1, 0:1])
                e.activation(msr[0:1, 0:2], msr[0:1, 0:2], AF.Exp, scale=-0.5).then_inc(P.ss.h, 1)
            tbc2 = P.st.bump(1)
            @P.on("tensor")
            def _(e, trs=trs):
                e.wait_ge(P.ss.h, trs)
                e.matmul(PBC[:, 360:362], ones1_f[0:1, :], msr[0:1, 0:2], start=True, stop=True).then_inc(P.st.h, 1)
            t_prev["pbct"] = tbc2
            tyn0 = P.sv.bump(1)
            @P.on("vector")
            def _(e, tbc2=tbc2):
                e.wait_ge(P.st.h, tbc2)
                g3 = g32.rearrange("p (b j) -> p b j", j=16)
                e.tensor_tensor(g3, g3, PBC[:, 360:362][:, :, None].broadcast_to([128, 2, 16]), ALU.mult).then_inc(P.sv.h, 1)
            tyn1 = P.sv.bump(1)
            @P.on("vector")
            def _(e, tyn0=tyn0):
                e.wait_ge(P.sv.h, tyn0)
                g3 = g32.rearrange("p (b j) -> p b j", j=16)
                e.tensor_tensor(yfm.rearrange("p (j b) -> p b j", b=2), g3,
                                nw[:, None, :].broadcast_to([128, 2, 16]), ALU.mult).then_inc(P.sv.h, 1)
            tyn = P.sv.bump(1)
            @P.on("vector")
            def _(e, t=t, tyn1=tyn1):
                e.wait_ge(P.sv.h, tyn1)
                yn_v = YN.rearrange("p (j b t2) -> p j b t2", b=2, t2=128)
                e.tensor_copy(yn_v[:, :, :, t], yfm.rearrange("p (j b) -> p j b", b=2)).then_inc(P.sv.h, 1)
            t_prev["yfm"] = tyn
            t_prev["pbcv"] = tyn
            if L1DBG and cc == 0 and t < 4:
                tdbg = P.sv.bump(1)
                @P.on("vector")
                def _(e, t=t, tyn=tyn):
                    e.wait_ge(P.sv.h, tyn)
                    base = t * 256
                    e.tensor_copy(dbgb[:, base:base + 38], fm2[:, :])
                    e.tensor_copy(dbgb[0:32, base + 40:base + 42], dtc[:, :])
                    e.tensor_copy(dbgb[0:32, base + 42:base + 44], dAc[:, :])
                    e.tensor_copy(dbgb[:, base + 44:base + 45], B_m[0][:, t:t + 1])
                    e.tensor_copy(dbgb[:, base + 45:base + 46], Cp[:, 0:1])
                    e.tensor_copy(dbgb[:, base + 108:base + 140], ysb[:, :])
                    e.tensor_copy(dbgb[:, base + 140:base + 172], DD[:, 0:32])
                    e.tensor_copy(dbgb[:, base + 172:base + 236], ur_sb[:, :])
                    e.tensor_copy(dbgb[0:1, base + 236:base + 238], msr[0:1, :]).then_inc(P.sv.h, 1)
        svn = P.sv.n
        tfold_v = []
        for b in range(BL):
            tvx = P.sv.bump(1)
            sgn_f = P.sg.n
            sdn_f = P.sd.n
            @P.on("vector")
            def _(e, b=b, sgn_f=sgn_f, sdn_f=sdn_f):
                e.wait_ge(P.sg.h, sgn_f)
                e.wait_ge(P.sd.h, sdn_f)
                ur_v = U_r.rearrange("p (b h) -> p b h", h=32)[:, b, :][:, :, None].broadcast_to([128, 32, 64])
                uv_e = U_v.rearrange("p (b h) -> p b h", h=32)[:, b, :][:, :, None].broadcast_to([128, 32, 64])
                e.tensor_tensor(XU.rearrange("p (h d) -> p h d", d=64), S0[b].rearrange("p (h d) -> p h d", d=64), uv_e, ALU.mult)
                e.tensor_tensor(X_s[b].rearrange("p (h d) -> p h d", d=64), X_s[b].rearrange("p (h d) -> p h d", d=64), ur_v, ALU.mult).then_inc(P.sv.h, 1)
            ttb = P.st.bump(1)
            @P.on("tensor")
            def _(e, b=b, tvx=tvx):
                e.wait_ge(P.sv.h, tvx)
                e.transpose(PT[:, 0:128], B_m[b][:, :], iden_b[:, :]).then_inc(P.st.h, 1)
            tbe = P.sv.bump(1)
            @P.on("vector")
            def _(e, ttb=ttb):
                e.wait_ge(P.st.h, ttb)
                e.tensor_copy(BTsb[:, :], PT[:, 0:128]).then_inc(P.sv.h, 1)
            tlast = tbe
            for c4 in range(4):
                pk = [PRa, PRb][c4 % 2]
                tmm = P.st.bump(1)
                @P.on("tensor")
                def _(e, b=b, c4=c4, pk=pk, tbe=tbe, tlast=tlast):
                    e.wait_ge(P.sv.h, tlast)
                    e.matmul(pk[:, 0:512], BTsb[:, :], X_s[b][:, 512 * c4:512 * (c4 + 1)], start=True, stop=True).then_inc(P.st.h, 1)
                tse = P.sv.bump(1)
                @P.on("vector")
                def _(e, b=b, c4=c4, pk=pk, tmm=tmm):
                    e.wait_ge(P.st.h, tmm)
                    e.tensor_tensor(S0[b][:, 512 * c4:512 * (c4 + 1)], XU[:, 512 * c4:512 * (c4 + 1)], pk[:, 0:512], ALU.add).then_inc(P.sv.h, 1)
                tlast = tse
            tfold_v.append(tlast)
        P.barrier_engine("tensor", [(P.sv, None)])
        ofree = [None, None]
        for ot in range(DT):
            pk = [PRa, PRb][ot % 2]
            of = ofree[ot % 2]
            tgt = P.st.bump(1)
            @P.on("tensor")
            def _(e, ot=ot, pk=pk, of=of):
                if of is not None:
                    e.wait_ge(of[0].h, of[1])
                for j in range(16):
                    r = e.matmul(pk[:, 0:256], wout[j][:, 128 * ot:128 * (ot + 1)], YN[:, 256 * j:256 * (j + 1)], start=(j == 0), stop=(j == 15))
                r.then_inc(P.st.h, 1)
            sgn = P.sg.n
            tse = P.ss.bump(1)
            @P.on("scalar")
            def _(e, pk=pk, tgt=tgt, sgn=sgn):
                e.wait_ge(P.st.h, tgt)
                e.wait_ge(P.sg.h, sgn)
                e.activation(obuf[:, :], pk[:, 0:256], AF.Copy).then_inc(P.ss.h, 1)
            for b in range(BL):
                ob = (b * L1SEG if seg is None else b * S + seg * L1SEG) + cc * L1CH
                P.store(o_s[ot, :, ob:ob + L1CH],
                        obuf[:, b * 128:(b + 1) * 128], wait=[(P.ss, tse)] if b == 0 else ())
            ofree[ot % 2] = (P.ss, tse)
        P.barrier_engine("vector", [(P.st, None)])
    svn = P.sv.n
    for b in range(BL):
        P.store(S0o[b], S0[b][:, :], wait=[(P.sv, svn)] if b == 0 else ())
    P.store(ypo[:, :], yfm[:, :])
    if L1DBG:
        P.store(dbgD[:, :], dbgb[:, :])
    return P.finalize() if own else None


def build_l1ln(P=None):
    own = P is None
    if own:
        P = Prog("l1ln")
    x_in = P.dram("x_in", [DT, 128, NT], F32, "ExternalInput")
    ones = P.dram("c_ones", [128, 1], F32, "ExternalInput")
    ones1 = P.dram("c_ones1", [1, 128], F32, "ExternalInput")
    misc = P.dram("c_misc", [1, 8], F32, "ExternalInput")
    iden = P.dram("c_iden", [128, 128], F32, "ExternalInput")
    xn_s = P.dram("xn_s", [DT, 128, NT], BF16, "ExternalOutput")
    xnT_s = P.dram("xnT_s", [NT // 128, 128, D], BF16, "ExternalOutput")

    ones_f = P.sbuf("ones_f", [128, 1], F32)
    ones1_f = P.sbuf("ones1_f", [1, 128], F32)
    iden_b = P.sbuf("iden_b", [128, 128], BF16)
    iden_f = P.sbuf("iden_f", [128, 128], F32)
    P.c_eps = P.sbuf("c_eps", [1, 8], F32)
    P.load(ones_f[:, :], ones[:, :])
    P.load(ones1_f[:, :], ones1[:, :])
    P.load(P.c_eps[0:1, :], misc[0:1, :])
    ti = P.load(iden_f[:, :], iden[:, :])
    tvi = P.sv.bump(1)
    @P.on("vector")
    def _(e, ti=ti):
        e.wait_ge(P.sd.h, ti)
        e.tensor_copy(iden_b[:, :], iden_f[:, :]).then_inc(P.sv.h, 1)

    xt = [P.sbuf(f"xt{k}", [128, TCH], F32) for k in range(DT)]
    sq = [P.sbuf(f"sq{k}", [128, TCH], F32) for k in range(DT)]
    xnb = [P.sbuf(f"xnb{k}", [128, TCH], BF16) for k in range(DT)]
    rows = (P.sbuf("r_mean", [1, TCH], F32), P.sbuf("r_var", [1, TCH], F32), P.sbuf("r_tmp", [1, TCH], F32))
    tby = P.sbuf("tby", [128, TCH], BF16)
    PST = P.psum("PST", [128, TCH])
    PBC = P.psum("PBC", [128, TCH])
    PMT = P.psum("PMT", [128, 128], BF16)

    for ci in range(NCH):
        off = ci * TCH
        tld = None
        for k in range(DT):
            tld = P.load(xt[k][:, :], x_in[k, :, off:off + TCH],
                         wait=[(P.st, None), (P.sv, None)] if k == 0 else ())
        P.barrier_engine("tensor", [(P.sd, tld), (P.sv, None)])
        P.barrier_engine("scalar", [(P.sd, tld)])
        P.barrier_engine("vector", [(P.sd, tld), (P.sg, None), (P.st, None)])
        t_xn = emit_ln_chunk(P, xt, xnb, sq, rows, ones_f, ones1_f, PST, PBC, TCH)
        for k in range(DT):
            P.store(xn_s[k, :, off:off + TCH], xnb[k][:, :], wait=[(P.sv, t_xn)] if k == 0 else ())
        for sub in range(TCH // 128):
            tt = off // 128 + sub
            for k in range(DT):
                svn = P.sv.n
                ttr = P.st.bump(1)
                @P.on("tensor")
                def _(e, k=k, sub=sub, t_xn=t_xn, svn=svn):
                    e.wait_ge(P.sv.h, t_xn)
                    e.wait_ge(P.sv.h, svn)
                    e.transpose(PMT[:, :], xnb[k][:, sub * 128:(sub + 1) * 128], iden_b[:, :]).then_inc(P.st.h, 1)
                sgn = P.sg.n
                tev = P.sv.bump(1)
                @P.on("vector")
                def _(e, ttr=ttr, sgn=sgn):
                    e.wait_ge(P.st.h, ttr)
                    e.wait_ge(P.sg.h, sgn)
                    e.tensor_copy(tby[:, 0:128], PMT[:, :]).then_inc(P.sv.h, 1)
                P.store(xnT_s[tt, :, k * 128:(k + 1) * 128], tby[:, 0:128], wait=[(P.sv, tev)])
    return P.finalize() if own else None


def emit_fp4q(P, e, out_ap, in_ap, scr, n, scb, rsb):
    ng = n // 16
    vin = in_ap.rearrange("p (g c) -> p g c", c=16)
    ab, c1, c2 = scr["ab"], scr["c1"], scr["c2"]
    e.tensor_reduce(scb[:, 0:ng], vin, AX.X, ALU.max, apply_absolute_value=True)
    e.tensor_scalar(scb[:, 0:ng], scb[:, 0:ng], 1.0 / 6.0, 1e-12, ALU.mult, ALU.max)
    e.reciprocal(rsb[:, 0:ng], scb[:, 0:ng])
    e.tensor_single_scalar(c1[:, 0:n], in_ap, -1.0, ALU.mult)
    e.tensor_tensor(ab[:, 0:n], in_ap, c1[:, 0:n], ALU.max)
    e.tensor_tensor(ab[:, 0:n].rearrange("p (g c) -> p g c", c=16), ab[:, 0:n].rearrange("p (g c) -> p g c", c=16),
                    rsb[:, 0:ng, None].broadcast_to([128, ng, 16]), ALU.mult)
    e.tensor_scalar(c1[:, 0:n], ab[:, 0:n], -2.0, 0.0, ALU.add, ALU.max)
    e.tensor_scalar(c2[:, 0:n], ab[:, 0:n], -4.0, 0.0, ALU.add, ALU.max)
    e.scalar_tensor_tensor(c1[:, 0:n], c2[:, 0:n], 0.5, c1[:, 0:n], ALU.mult, ALU.add)
    e.scalar_tensor_tensor(c2[:, 0:n], ab[:, 0:n], 2.0, c1[:, 0:n], ALU.mult, ALU.subtract)
    e.tensor_single_scalar(ab[:, 0:n], c2[:, 0:n], 0.5, ALU.is_gt)
    for k in range(1, 7):
        e.scalar_tensor_tensor(ab[:, 0:n], c2[:, 0:n], k + 0.5, ab[:, 0:n], ALU.is_gt, ALU.add)
    e.tensor_scalar(c2[:, 0:n], ab[:, 0:n], -4.0, 0.0, ALU.add, ALU.max)
    e.tensor_scalar(c1[:, 0:n], ab[:, 0:n], -6.0, 0.0, ALU.add, ALU.max)
    e.scalar_tensor_tensor(c2[:, 0:n], c1[:, 0:n], 3.0, c2[:, 0:n], ALU.mult, ALU.add)
    e.tensor_tensor(c2[:, 0:n], c2[:, 0:n], ab[:, 0:n], ALU.add)
    e.tensor_single_scalar(c1[:, 0:n], in_ap, 0.0, ALU.is_gt)
    e.tensor_scalar(c1[:, 0:n], c1[:, 0:n], 1.0, -0.5, ALU.mult, ALU.add)
    e.tensor_tensor(c2[:, 0:n], c2[:, 0:n], c1[:, 0:n], ALU.mult)
    r = e.tensor_tensor(out_ap.rearrange("p (g c) -> p g c", c=16), c2[:, 0:n].rearrange("p (g c) -> p g c", c=16),
                        scb[:, 0:ng, None].broadcast_to([128, ng, 16]), ALU.mult)
    return r


def build_l1f(P=None):
    own = P is None
    if own:
        P = Prog("l1f")
    x_in = P.dram("x_in", [DT, 128, NT], F32, "ExternalInput")
    xnT_s = P.dram("xnT_s", [NT // 128, 128, D], BF16, "ExternalInput")
    s_s = P.dram("s_s", [DT, 128, NT], F32, "ExternalInput")
    wgD = P.dram("l1_wg", [22, 128, 1024], F32, "ExternalInput")
    wuD = P.dram("l1_wu", [22, 128, 1024], F32, "ExternalInput")
    wdD = P.dram("l1_wd", [8, 128, 2816], F32, "ExternalInput")
    iden = P.dram("c_iden", [128, 128], F32, "ExternalInput")
    x_out = P.dram("x_out", [DT, 128, NT], F32, "ExternalOutput")

    iden_f = P.sbuf("iden_f", [128, 128], F32)
    iden_b = P.sbuf("iden_b", [128, 128], BF16)
    ti = P.load(iden_f[:, :], iden[:, :])
    tvi = P.sv.bump(1)
    @P.on("vector")
    def _(e, ti=ti):
        e.wait_ge(P.sd.h, ti)
        e.tensor_copy(iden_b[:, :], iden_f[:, :]).then_inc(P.sv.h, 1)
    wscr = P.sbuf("wscr", [128, 1408], F32)
    wstg = P.sbuf("wstg", [128, 2816], BF16)
    qscr = {k: P.sbuf(f"q_{k}", [128, 2816], BF16) for k in ("ab", "c1", "c2")}
    qsc = P.sbuf("qsc", [128, 2816 // 16], F32)
    qrs = P.sbuf("qrs", [128, 2816 // 16], F32)
    wgT = [P.sbuf(f"wgT{j}", [128, 2816], BF16) for j in range(DT)]
    wuT = [P.sbuf(f"wuT{j}", [128, 2816], BF16) for j in range(DT)]
    wdT = [P.sbuf(f"wdT{j}", [128, 1024], BF16) for j in range(22)]
    PMT = P.psum("PMT", [128, 128], BF16)
    PG = [P.psum(f"PG{i}", [128, TCH]) for i in range(2)]
    PU = [P.psum(f"PU{i}", [128, TCH]) for i in range(2)]
    PD = [P.psum(f"PD{i}", [128, TCH]) for i in range(2)]

    def quant_transpose(srcD, k, ncol, dstlist, dst_colbase):
        nh = ncol // 2
        for half in range(2):
            tw = P.load(wscr[:, 0:nh], srcD[k][:, half * nh:(half + 1) * nh], wait=[(P.sv, None), (P.st, None)])
            tq0 = P.sv.bump(1)
            @P.on("vector")
            def _(e, tw=tw, nh=nh, half=half):
                e.wait_ge(P.sd.h, tw)
                e.tensor_copy(wstg[:, half * nh:(half + 1) * nh], wscr[:, 0:nh]).then_inc(P.sv.h, 1)
        tq = P.sv.bump(1)
        @P.on("vector")
        def _(e, ncol=ncol):
            emit_fp4q(P, e, wstg[:, 0:ncol], wstg[:, 0:ncol], qscr, ncol, qsc, qrs).then_inc(P.sv.h, 1)
        for j in range(ncol // 128):
            svn3 = P.sv.n
            ttr = P.st.bump(1)
            @P.on("tensor")
            def _(e, j=j, tq=tq, svn3=svn3):
                e.wait_ge(P.sv.h, max(tq, svn3))
                e.transpose(PMT[:, :], wstg[:, j * 128:(j + 1) * 128], iden_b[:, :]).then_inc(P.st.h, 1)
            tev = P.sv.bump(1)
            @P.on("vector")
            def _(e, j=j, ttr=ttr, dst_colbase=dst_colbase):
                e.wait_ge(P.st.h, ttr)
                e.tensor_copy(dstlist[j][:, dst_colbase:dst_colbase + 128], PMT[:, :]).then_inc(P.sv.h, 1)
    for k in range(22):
        quant_transpose(wgD, k, 1024, wgT, k * 128)
        quant_transpose(wuD, k, 1024, wuT, k * 128)
    for k in range(DT):
        quant_transpose(wdD, k, 2816, wdT, k * 128)

    xqT = P.sbuf("xqT", [128, 1024], BF16)
    xqf = [P.sbuf(f"xqf{k}", [128, TCH], BF16) for k in range(DT)]
    gT = qscr["ab"]
    hT = P.sbuf("hT", [128, 2816], BF16)
    hqf = [P.sbuf(f"hqf{k}", [128, TCH], BF16) for k in range(22)]
    xts = P.sbuf("xts", [128, TCH], F32)
    sts = P.sbuf("sts", [128, TCH], F32)
    xo = P.sbuf("xo", [128, TCH], F32)

    for ci in range(NCH):
        off = ci * TCH
        for sub in range(TCH // 128):
            tt = off // 128 + sub
            svn = P.sv.n
            stn = P.st.n
            tld = P.load(xqT[:, :], xnT_s[tt], wait=[(P.sv, svn), (P.st, stn)])
            tq = P.sv.bump(1)
            @P.on("vector")
            def _(e, tld=tld):
                e.wait_ge(P.sd.h, tld)
                emit_fp4q(P, e, xqT[:, :], xqT[:, :], qscr, 1024, qsc, qrs).then_inc(P.sv.h, 1)
            for j in range(DT):
                svn3 = P.sv.n
                ttr = P.st.bump(1)
                @P.on("tensor")
                def _(e, j=j, tq=tq, svn3=svn3):
                    e.wait_ge(P.sv.h, max(tq, svn3))
                    e.transpose(PMT[:, :], xqT[:, j * 128:(j + 1) * 128], iden_b[:, :]).then_inc(P.st.h, 1)
                tev = P.sv.bump(1)
                @P.on("vector")
                def _(e, j=j, sub=sub, ttr=ttr):
                    e.wait_ge(P.st.h, ttr)
                    e.tensor_copy(xqf[j][:, sub * 128:(sub + 1) * 128], PMT[:, :]).then_inc(P.sv.h, 1)
        for sub in range(TCH // 128):
            for hc in range(6):
                ncol = 512 if hc < 5 else 256
                pg = PG[hc % 2]
                pu = PU[hc % 2]
                svn = P.sv.n
                ssn = P.ss.n
                tgt = P.st.bump(1)
                @P.on("tensor")
                def _(e, sub=sub, hc=hc, pg=pg, pu=pu, ncol=ncol, svn=svn, ssn=ssn):
                    e.wait_ge(P.sv.h, svn)
                    e.wait_ge(P.ss.h, ssn)
                    for k in range(DT):
                        e.matmul(pg[:, 0:ncol], xqf[k][:, sub * 128:(sub + 1) * 128], wgT[k][:, hc * 512:hc * 512 + ncol], start=(k == 0), stop=(k == DT - 1))
                    for k in range(DT):
                        r = e.matmul(pu[:, 0:ncol], xqf[k][:, sub * 128:(sub + 1) * 128], wuT[k][:, hc * 512:hc * 512 + ncol], start=(k == 0), stop=(k == DT - 1))
                    r.then_inc(P.st.h, 1)
                tsg = P.ss.bump(1)
                @P.on("scalar")
                def _(e, hc=hc, pg=pg, tgt=tgt, ncol=ncol):
                    e.wait_ge(P.st.h, tgt)
                    e.activation(gT[:, hc * 512:hc * 512 + ncol], pg[:, 0:ncol], AF.Silu).then_inc(P.ss.h, 1)
                tvh = P.sv.bump(1)
                @P.on("vector")
                def _(e, hc=hc, pu=pu, tsg=tsg, tgt=tgt, ncol=ncol):
                    e.wait_ge(P.ss.h, tsg)
                    e.wait_ge(P.st.h, tgt)
                    e.tensor_tensor(hT[:, hc * 512:hc * 512 + ncol], pu[:, 0:ncol], gT[:, hc * 512:hc * 512 + ncol], ALU.mult).then_inc(P.sv.h, 1)
            tqh = P.sv.bump(1)
            @P.on("vector")
            def _(e):
                emit_fp4q(P, e, hT[:, :], hT[:, :], qscr, 2816, qsc, qrs).then_inc(P.sv.h, 1)
            for j in range(22):
                svn3 = P.sv.n
                ttr = P.st.bump(1)
                @P.on("tensor")
                def _(e, j=j, tqh=tqh, svn3=svn3):
                    e.wait_ge(P.sv.h, max(tqh, svn3))
                    e.transpose(PMT[:, :], hT[:, j * 128:(j + 1) * 128], iden_b[:, :]).then_inc(P.st.h, 1)
                tev = P.sv.bump(1)
                @P.on("vector")
                def _(e, j=j, sub=sub, ttr=ttr):
                    e.wait_ge(P.st.h, ttr)
                    e.tensor_copy(hqf[j][:, sub * 128:(sub + 1) * 128], PMT[:, :]).then_inc(P.sv.h, 1)
        P.barrier_engine("tensor", [(P.sv, None)])
        dfree = [None, None]
        for ot in range(DT):
            pd = PD[ot % 2]
            df = dfree[ot % 2]
            tgt = P.st.bump(1)
            @P.on("tensor")
            def _(e, ot=ot, pd=pd, df=df):
                if df is not None:
                    e.wait_ge(df[0].h, df[1])
                for k in range(22):
                    r = e.matmul(pd[:, 0:TCH], wdT[k][:, ot * 128:(ot + 1) * 128], hqf[k][:, 0:TCH], start=(k == 0), stop=(k == 21))
                r.then_inc(P.st.h, 1)
            tx = P.load(xts[:, :], x_in[ot, :, off:off + TCH], wait=[(P.sv, None)])
            ts2 = P.load(sts[:, :], s_s[ot, :, off:off + TCH])
            sgn = P.sg.n
            tsi = P.ss.bump(1)
            @P.on("scalar")
            def _(e, pd=pd, tgt=tgt, sgn=sgn):
                e.wait_ge(P.st.h, tgt)
                e.wait_ge(P.sg.h, sgn)
                e.activation(xo[:, :], pd[:, 0:TCH], AF.Silu).then_inc(P.ss.h, 1)
            tvc = P.sv.bump(1)
            @P.on("vector")
            def _(e, tsi=tsi, ts2=ts2):
                e.wait_ge(P.ss.h, tsi)
                e.wait_ge(P.sd.h, ts2)
                e.tensor_tensor(xo[:, :], xo[:, :], sts[:, :], ALU.mult)
                e.tensor_tensor(xo[:, :], xo[:, :], xts[:, :], ALU.add).then_inc(P.sv.h, 1)
            P.store(x_out[ot, :, off:off + TCH], xo[:, :], wait=[(P.sv, tvc)])
            dfree[ot % 2] = (P.ss, tsi)
    return P.finalize() if own else None


def prep_l1(ins):
    W_in = np.asarray(ins["L1_attn_W_in"])
    conv_w = np.asarray(ins["L1_attn_conv_w"]).reshape(2304, 4)
    conv_b = np.asarray(ins["L1_attn_conv_b"]).reshape(2304, 1)
    cw = np.concatenate([conv_w, conv_b], axis=1).reshape(18, 128, 5)
    Wr = np.concatenate([np.asarray(ins["L1_attn_W_rxbc"]), np.asarray(ins["L1_attn_W_rdt"])], axis=0)
    Dp = np.asarray(ins["L1_attn_D_param"])
    p = np.arange(128)[:, None]
    j = np.arange(16)[None, :]
    Dexp = Dp[(2 * j + (p >= 64)).astype(np.int64)].astype(np.float32)
    l1a = {
        "l1_winT": _wT_tiles(W_in),
        "l1_convw": np.ascontiguousarray(cw),
    }
    l1s = {
        "l1_wrT": _wT_tiles(Wr),
        "l1_woutT": _wT_tiles(ins["L1_attn_W_out"]),
        "l1_hcols": np.ascontiguousarray(np.stack(
            [np.asarray(ins["L1_attn_dt_bias"]), np.asarray(ins["L1_attn_A_log"]),
             Dp, np.zeros(32, np.float32)], axis=1).astype(np.float32)),
        "l1_nw": np.ascontiguousarray(np.asarray(ins["L1_attn_norm_w"]).reshape(16, 128).T),
        "l1_Dexp": np.ascontiguousarray(Dexp),
    }
    l1f = {
        "l1_wg": np.ascontiguousarray(np.asarray(ins["L1_ffn_w_gate"]).reshape(22, 128, 1024)),
        "l1_wu": np.ascontiguousarray(np.asarray(ins["L1_ffn_w_up"]).reshape(22, 128, 1024)),
        "l1_wd": np.ascontiguousarray(np.asarray(ins["L1_ffn_w_down"]).reshape(8, 128, 2816)),
    }
    return l1a, l1s, l1f


def _inj_const():
    inj = np.zeros((128, 512), np.float32)
    t = np.arange(128)
    inj[t, 4 * t] = 1.0
    inj[t, 4 * t + 3] = 1.0
    return inj


CONSTS["c_inj"] = _inj_const()
_e2 = np.zeros((128, 2), np.float32)
_e2[np.arange(128), np.arange(128) % 32 % 2] = (np.arange(128) % 32 < 2).astype(np.float32)
_e2 = np.zeros((128, 2), np.float32)
for _p in range(128):
    if _p % 32 < 2:
        _e2[_p, _p % 32] = 1.0
CONSTS["c_eye2"] = _e2


def run_l1(xcore, l1a, l1s, l1f):
    in_maps = [{"x_in": xcore[c], **CONSTS} for c in range(NCORES)]
    rln = _run(build_l1ln, "l1ln", in_maps)
    in_maps = [{"xn_s": rln[c]["xn_s"], **l1a, **CONSTS} for c in range(NCORES)]
    ra = _run(build_l1a1, "l1a1", in_maps)
    S0 = [np.zeros((BL, 128, 2048), np.float32).astype(_bf16) for _ in range(NCORES)]
    yp = [np.zeros((128, 32), np.float32).astype(_bf16) for _ in range(NCORES)]
    o_parts = []
    for seg in range(4):
        in_maps = []
        for c in range(NCORES):
            in_maps.append({
                "cdt_c": np.ascontiguousarray(np.asarray(ra[c]["cdt_s"])[:, seg * 4:(seg + 1) * 4]),
                "z_c": np.ascontiguousarray(np.asarray(ra[c]["z_s"])[:, seg * 4:(seg + 1) * 4]),
                "S0_i": S0[c], "yprev_i": yp[c], **l1s, **CONSTS})
        rs = _run(build_l1scan, "l1scan", in_maps)
        S0 = [np.asarray(rs[c]["S0_o"]) for c in range(NCORES)]
        yp = [np.asarray(rs[c]["yprev_o"]) for c in range(NCORES)]
        o_parts.append([np.asarray(rs[c]["o_s"]) for c in range(NCORES)])
    s_s = []
    for c in range(NCORES):
        sc = np.zeros((DT, 128, NT), np.float32)
        for seg in range(4):
            for b in range(BL):
                sc[:, :, b * S + seg * L1SEG:(b * S + (seg + 1) * L1SEG)] = \
                    o_parts[seg][c][:, :, b * L1SEG:(b + 1) * L1SEG]
        s_s.append(sc)
    in_maps = [{"x_in": xcore[c], "xnT_s": rln[c]["xnT_s"], "s_s": s_s[c], **l1f, **CONSTS}
               for c in range(NCORES)]
    rf = _run(build_l1f, "l1f", in_maps)
    return [rf[c]["x_out"] for c in range(NCORES)]



def build_l0a1(P=None):
    own = P is None
    if own:
        P = Prog("l0a1")
    x_in = P.dram("x_in", [DT, 128, NT], F32, "ExternalInput")
    ones = P.dram("c_ones", [128, 1], F32, "ExternalInput")
    ones1 = P.dram("c_ones1", [1, 128], F32, "ExternalInput")
    misc = P.dram("c_misc", [1, 8], F32, "ExternalInput")
    xn_s = P.dram("xn_s", [DT, 128, NT], BF16, "ExternalOutput")
    xnf_s = P.dram("xnf_s", [DT, 128, NT], F32, "ExternalOutput")
    ones_f = P.sbuf("ones_f", [128, 1], F32)
    ones1_f = P.sbuf("ones1_f", [1, 128], F32)
    P.c_eps = P.sbuf("c_eps", [1, 8], F32)
    P.load(ones_f[:, :], ones[:, :])
    P.load(ones1_f[:, :], ones1[:, :])
    P.load(P.c_eps[0:1, :], misc[0:1, :])
    xt = [P.sbuf(f"xt{k}", [128, TCH], F32) for k in range(DT)]
    sq = [P.sbuf(f"sq{k}", [128, TCH], F32) for k in range(DT)]
    xnb = [P.sbuf(f"xnb{k}", [128, TCH], BF16) for k in range(DT)]
    rows = (P.sbuf("r_mean", [1, TCH], F32), P.sbuf("r_var", [1, TCH], F32), P.sbuf("r_tmp", [1, TCH], F32))
    PST = P.psum("PST", [128, TCH])
    PBC = P.psum("PBC", [128, TCH])
    for ci in range(NCH):
        off = ci * TCH
        tld = None
        for k in range(DT):
            tld = P.load(xt[k][:, :], x_in[k, :, off:off + TCH],
                         wait=[(P.st, None), (P.sv, None), (P.sg, None)] if k == 0 else ())
        P.barrier_engine("tensor", [(P.sd, tld), (P.sv, None)])
        P.barrier_engine("scalar", [(P.sd, tld)])
        P.barrier_engine("vector", [(P.sd, tld), (P.sg, None), (P.st, None)])
        t_xn = emit_ln_chunk(P, xt, xnb, sq, rows, ones_f, ones1_f, PST, PBC, TCH, keep_f32=True)
        for k in range(DT):
            P.store(xn_s[k, :, off:off + TCH], xnb[k][:, :], wait=[(P.sv, t_xn)] if k == 0 else ())
            P.store(xnf_s[k, :, off:off + TCH], sq[k][:, :])
    return P.finalize() if own else None


def build_l0a2(P=None):
    own = P is None
    if own:
        P = Prog("l0a2")
    xn_s = P.dram("xn_s", [DT, 128, NT], BF16, "ExternalInput")
    wqkvT = P.dram("l0_wqkvT", [DT, 128, 3072], F32, "ExternalInput")
    lncol = P.dram("l0_ln1", [128, 2 * DT], F32, "ExternalInput")
    ones = P.dram("c_ones", [128, 1], F32, "ExternalInput")
    ones1 = P.dram("c_ones1", [1, 128], F32, "ExternalInput")
    misc = P.dram("c_misc", [1, 8], F32, "ExternalInput")
    Q_s = P.dram("Q_s", [DT, 128, NT], BF16, "ExternalOutput")
    K_s = P.dram("K_s", [DT, 128, NT], BF16, "ExternalOutput")
    KT_s = P.dram("KT_s", [NT // 128, 128, D], BF16, "ExternalOutput")
    VT_s = P.dram("VT_s", [NT // 128, 128, D], BF16, "ExternalOutput")

    ones_f = P.sbuf("ones_f", [128, 1], F32)
    ones1_f = P.sbuf("ones1_f", [1, 128], F32)
    P.c_eps = P.sbuf("c_eps", [1, 8], F32)
    lnc = P.sbuf("lnc", [128, 2 * DT], F32)
    P.load(ones_f[:, :], ones[:, :])
    P.load(ones1_f[:, :], ones1[:, :])
    P.load(P.c_eps[0:1, :], misc[0:1, :])
    P.load(lnc[:, :], lncol[:, :])
    wscr = P.sbuf("wscr", [128, 1536], F32)
    wb = [P.sbuf(f"wb{k}", [128, 3072], BF16) for k in range(DT)]
    def wload(dst, base, src_ap, ncol):
        tw = P.load(wscr[:, 0:ncol], src_ap, wait=[(P.sv, None)])
        tvv = P.sv.bump(1)
        @P.on("vector")
        def _(e, dst=dst, base=base, tw=tw, ncol=ncol):
            e.wait_ge(P.sd.h, tw)
            e.tensor_copy(dst[:, base:base + ncol], wscr[:, 0:ncol]).then_inc(P.sv.h, 1)
    for k in range(DT):
        wload(wb[k], 0, wqkvT[k][:, 0:1536], 1536)
        wload(wb[k], 1536, wqkvT[k][:, 1536:3072], 1536)

    xt = [P.sbuf(f"xt{k}", [128, TCH], F32) for k in range(DT)]
    xld = [P.sbuf(f"xld{k}", [128, TCH], BF16) for k in range(DT)]
    sq = [P.sbuf(f"sq{k}", [128, TCH], F32) for k in range(DT)]
    xnb = [P.sbuf(f"xnb{k}", [128, TCH], BF16) for k in range(DT)]
    rows = (P.sbuf("r_mean", [1, TCH], F32), P.sbuf("r_var", [1, TCH], F32), P.sbuf("r_tmp", [1, TCH], F32))
    qkt = [P.sbuf(f"qkt{i}", [128, TCH], BF16) for i in range(2)]
    tmt = [P.sbuf(f"tmt{i}", [128, TCH], BF16) for i in range(2)]
    PST = P.psum("PST", [128, TCH])
    PBC = P.psum("PBC", [128, TCH])
    PM0 = P.psum("PM0", [128, TCH])
    PM1 = P.psum("PM1", [128, TCH])

    for ci in range(NCH):
        off = ci * TCH
        tld = None
        for k in range(DT):
            tld = P.load(xld[k][:, :], xn_s[k, :, off:off + TCH],
                         wait=[(P.st, None), (P.sv, None)] if k == 0 else ())
        P.barrier_engine("vector", [(P.sd, tld), (P.sg, None), (P.st, None), (P.ss, None)])
        tcv = P.sv.bump(1)
        @P.on("vector")
        def _(e):
            for k in range(DT):
                r = e.tensor_copy(xt[k][:, :], xld[k][:, :])
            r.then_inc(P.sv.h, 1)
        P.barrier_engine("tensor", [(P.sv, tcv)])
        P.barrier_engine("scalar", [(P.sv, tcv)])
        P.barrier_engine("vector", [(P.sv, tcv)])
        t_xn = emit_ln_chunk(P, xt, xnb, sq, rows, ones_f, ones1_f, PST, PBC, TCH,
                             gb=(lnc[:, 0:DT], lnc[:, DT:2 * DT]))
        P.barrier_engine("tensor", [(P.sv, t_xn)])
        for (dstD, fbase) in ((Q_s, 0), (K_s, 1024)):
            for ot in range(DT):
                pb = [PM0, PM1][ot % 2]
                buf = qkt[ot % 2]
                svn = P.sv.n
                sgn = P.sg.n
                tgt = P.st.bump(1)
                @P.on("tensor")
                def _(e, pb=pb, ot=ot, fbase=fbase, svn=svn):
                    e.wait_ge(P.sv.h, svn)
                    for k in range(DT):
                        r = e.matmul(pb[:, 0:TCH], wb[k][:, fbase + ot * 128:fbase + (ot + 1) * 128], xnb[k][:, 0:TCH], start=(k == 0), stop=(k == DT - 1))
                    r.then_inc(P.st.h, 1)
                tvx = P.sv.bump(1)
                @P.on("vector")
                def _(e, pb=pb, buf=buf, tgt=tgt, sgn=sgn):
                    e.wait_ge(P.st.h, tgt)
                    e.wait_ge(P.sg.h, sgn)
                    e.tensor_copy(buf[:, :], pb[:, 0:TCH]).then_inc(P.sv.h, 1)
                P.store(dstD[ot, :, off:off + TCH], buf[:, :], wait=[(P.sv, tvx)])
        for (dstD, fbase) in ((KT_s, 1024), (VT_s, 2048)):
            for sub in range(TCH // 128):
                tt = off // 128 + sub
                for fc in range(2):
                    pb = [PM0, PM1][fc % 2]
                    buf = tmt[fc % 2]
                    svn = P.sv.n
                    sgn = P.sg.n
                    tgt = P.st.bump(1)
                    @P.on("tensor")
                    def _(e, pb=pb, sub=sub, fc=fc, fbase=fbase, svn=svn):
                        e.wait_ge(P.sv.h, svn)
                        for k in range(DT):
                            r = e.matmul(pb[:, 0:TCH], xnb[k][:, sub * 128:(sub + 1) * 128], wb[k][:, fbase + fc * 512:fbase + (fc + 1) * 512], start=(k == 0), stop=(k == DT - 1))
                        r.then_inc(P.st.h, 1)
                    tvv = P.sv.bump(1)
                    @P.on("vector")
                    def _(e, pb=pb, buf=buf, tgt=tgt, sgn=sgn):
                        e.wait_ge(P.st.h, tgt)
                        e.wait_ge(P.sg.h, sgn)
                        e.tensor_copy(buf[:, 0:TCH], pb[:, 0:TCH]).then_inc(P.sv.h, 1)
                    P.store(dstD[tt, :, fc * 512:(fc + 1) * 512], buf[:, 0:TCH], wait=[(P.sv, tvv)])
    return P.finalize() if own else None


def build_l0a3(P=None):
    own = P is None
    if own:
        P = Prog("l0a3")
    Q_sD = P.dram("Q_s", [DT, 128, NT], BF16, "ExternalInput")
    K_sD = P.dram("K_s", [DT, 128, NT], BF16, "ExternalInput")
    KT_sD = P.dram("KT_s", [NT // 128, 128, D], BF16, "ExternalInput")
    VT_sD = P.dram("VT_s", [NT // 128, 128, D], BF16, "ExternalInput")
    maskt = P.dram("c_mask", [128, 1536], F32, "ExternalInput")
    y_s = P.dram("y_s", [DT, 128, NT], BF16, "ExternalOutput")

    maskb = P.sbuf("maskb", [128, 128], BF16)
    mscr = P.sbuf("mscr", [128, 128], F32)
    tmk = P.load(mscr[:, :], maskt[:, 512:640])
    tvm = P.sv.bump(1)
    @P.on("vector")
    def _(e, tmk=tmk):
        e.wait_ge(P.sd.h, tmk)
        e.tensor_copy(maskb[:, :], mscr[:, :])
        e.memset(MTs[:, :], 0.0)
        e.memset(MTf[:, :], 0.0).then_inc(P.sv.h, 1)
    Qb = [P.sbuf(f"Qb{k}", [128, S], BF16) for k in range(DT)]
    Kb = [P.sbuf(f"Kb{k}", [128, S], BF16) for k in range(DT)]
    KTb = [P.sbuf(f"KTb{c}", [128, D], BF16) for c in range(16)]
    VTb = [P.sbuf(f"VTb{c}", [128, D], BF16) for c in range(16)]
    sm = [P.sbuf(f"sm{i}", [128, 128], BF16) for i in range(2)]
    MTs = P.sbuf("MTs", [128, 64], BF16)
    MTf = P.sbuf("MTf", [128, 64], F32)
    yh = P.sbuf("yh", [64, S], BF16)
    PS = P.psum("PS", [128, 128])
    PY = P.psum("PY", [128, 128])
    PM = P.psum("PM", [128, 64])

    for b in range(BL):
        tld = None
        svn = P.sv.n
        stn = P.st.n
        for k in range(DT):
            tld = P.load(Qb[k][:, :], Q_sD[k, :, b * S:(b + 1) * S],
                         wait=[(P.st, stn), (P.sv, svn), (P.ss, None), (P.sg, None)] if k == 0 else ())
            tld = P.load(Kb[k][:, :], K_sD[k, :, b * S:(b + 1) * S])
        for c in range(16):
            tld = P.load(KTb[c][:, :], KT_sD[b * 16 + c])
            tld = P.load(VTb[c][:, :], VT_sD[b * 16 + c])
        P.barrier_engine("tensor", [(P.sd, tld), (P.sv, None)])
        for h in range(16):
            ht, hh = h // 2, (h % 2) * 64
            for c in range(16):
                smb = sm[c % 2]
                svn2 = P.sv.n
                ssn2 = P.ss.n
                tS = P.st.bump(1)
                @P.on("tensor")
                def _(e, c=c, ht=ht, hh=hh, svn2=svn2):
                    e.wait_ge(P.sv.h, svn2)
                    e.matmul(PS[:, 0:128], Kb[ht][hh:hh + 64, c * 128:(c + 1) * 128], Qb[ht][hh:hh + 64, c * 128:(c + 1) * 128], start=True, stop=True).then_inc(P.st.h, 1)
                tmsk = P.sv.bump(1)
                @P.on("vector")
                def _(e, c=c, smb=smb, tS=tS):
                    e.wait_ge(P.st.h, tS)
                    e.tensor_tensor(smb[:, :], PS[:, 0:128], maskb[:, :], ALU.mult).then_inc(P.sv.h, 1)
                svn3 = P.sv.n
                tY = P.st.bump(1)
                @P.on("tensor")
                def _(e, c=c, h=h, ht=ht, hh=hh, smb=smb, tmsk=tmsk, ssn2=ssn2, svn3=svn3):
                    e.wait_ge(P.sv.h, tmsk)
                    e.wait_ge(P.ss.h, ssn2)
                    e.wait_ge(P.sv.h, svn3)
                    e.matmul(PY[0:64, 0:128], MTs[hh:hh + 64, :], Qb[ht][hh:hh + 64, c * 128:(c + 1) * 128], start=True, stop=False)
                    e.matmul(PY[0:64, 0:128], VTb[c][:, 64 * h:64 * h + 64], smb[:, :], start=False, stop=True)
                    r = e.matmul(PM[0:64, :], KTb[c][:, 64 * h:64 * h + 64], VTb[c][:, 64 * h:64 * h + 64], start=True, stop=True)
                    r.then_inc(P.st.h, 1)
                tev = P.ss.bump(1)
                @P.on("scalar")
                def _(e, c=c, tY=tY):
                    e.wait_ge(P.st.h, tY)
                    e.activation(yh[:, c * 128:(c + 1) * 128], PY[0:64, 0:128], AF.Copy).then_inc(P.ss.h, 1)
                tvM0 = P.sv.bump(1)
                @P.on("vector")
                def _(e, hh=hh, tY=tY):
                    e.wait_ge(P.st.h, tY)
                    e.tensor_tensor(MTf[hh:hh + 64, :], MTf[hh:hh + 64, :], PM[0:64, :], ALU.add).then_inc(P.sv.h, 1)
                tvM = P.sv.bump(1)
                @P.on("vector")
                def _(e, hh=hh, tvM0=tvM0):
                    e.wait_ge(P.sv.h, tvM0)
                    e.tensor_copy(MTs[hh:hh + 64, :], MTf[hh:hh + 64, :]).then_inc(P.sv.h, 1)
            sgn = P.sg.n
            ssn3 = P.ss.n
            for kt in range(16):
                P.store(y_s[h // 2, (h % 2) * 64:(h % 2) * 64 + 64, b * S + kt * 128:b * S + (kt + 1) * 128],
                        yh[:, kt * 128:(kt + 1) * 128], wait=[(P.ss, ssn3)] if kt == 0 else ())
            tvz = P.sv.bump(1)
            @P.on("vector")
            def _(e, ssn3=ssn3):
                e.wait_ge(P.ss.h, ssn3)
                e.memset(MTs[:, :], 0.0)
                e.memset(MTf[:, :], 0.0).then_inc(P.sv.h, 1)
            P.barrier_engine("scalar", [(P.sv, tvz), (P.sg, None)])
            P.barrier_engine("tensor", [(P.sv, tvz)])
    return P.finalize() if own else None


def build_l0a4(P=None):
    own = P is None
    if own:
        P = Prog("l0a4")
    xn_s = P.dram("xnf_s", [DT, 128, NT], F32, "ExternalInput")
    y_sD = P.dram("y_s", [DT, 128, NT], BF16, "ExternalInput")
    woT = P.dram("l0_woT", [DT, 128, D], F32, "ExternalInput")
    lncol = P.dram("l0_ln2", [128, 2 * DT], F32, "ExternalInput")
    ones = P.dram("c_ones", [128, 1], F32, "ExternalInput")
    ones1 = P.dram("c_ones1", [1, 128], F32, "ExternalInput")
    misc = P.dram("c_misc", [1, 8], F32, "ExternalInput")
    s1_s = P.dram("s1_s", [DT, 128, NT], F32, "ExternalOutput")
    h_s = P.dram("h_s", [DT, 128, NT], BF16, "ExternalOutput")

    ones_f = P.sbuf("ones_f", [128, 1], F32)
    ones1_f = P.sbuf("ones1_f", [1, 128], F32)
    P.c_eps = P.sbuf("c_eps", [1, 8], F32)
    lnc = P.sbuf("lnc", [128, 2 * DT], F32)
    P.load(ones_f[:, :], ones[:, :])
    P.load(ones1_f[:, :], ones1[:, :])
    P.load(P.c_eps[0:1, :], misc[0:1, :])
    P.load(lnc[:, :], lncol[:, :])
    wscr = P.sbuf("wscr", [128, 1024], F32)
    wob = [P.sbuf(f"wob{k}", [128, D], BF16) for k in range(DT)]
    def wload(dst, src_ap, ncol):
        tw = P.load(wscr[:, 0:ncol], src_ap, wait=[(P.sv, None)])
        tvv = P.sv.bump(1)
        @P.on("vector")
        def _(e, dst=dst, tw=tw, ncol=ncol):
            e.wait_ge(P.sd.h, tw)
            e.tensor_copy(dst[:, 0:ncol], wscr[:, 0:ncol]).then_inc(P.sv.h, 1)
    for k in range(DT):
        wload(wob[k], woT[k], D)

    yb = [P.sbuf(f"yb{k}", [128, TCH], BF16) for k in range(DT)]
    xnb = [P.sbuf(f"xnl{k}", [128, TCH], F32) for k in range(DT)]
    s1t = [P.sbuf(f"s1t{k}", [128, TCH], F32) for k in range(DT)]
    sq = [P.sbuf(f"sq{k}", [128, TCH], F32) for k in range(DT)]
    hb = [P.sbuf(f"hbt{k}", [128, TCH], BF16) for k in range(DT)]
    rows = (P.sbuf("r_mean", [1, TCH], F32), P.sbuf("r_var", [1, TCH], F32), P.sbuf("r_tmp", [1, TCH], F32))
    PST = P.psum("PST", [128, TCH])
    PBC = P.psum("PBC", [128, TCH])
    PM0 = P.psum("PM0", [128, TCH])
    PM1 = P.psum("PM1", [128, TCH])

    for ci in range(NCH):
        off = ci * TCH
        tld = None
        for k in range(DT):
            tld = P.load(yb[k][:, :], y_sD[k, :, off:off + TCH],
                         wait=[(P.st, None), (P.sv, None)] if k == 0 else ())
            tld = P.load(xnb[k][:, :], xn_s[k, :, off:off + TCH])
        P.barrier_engine("tensor", [(P.sd, tld), (P.sv, None)])
        dfree = [None, None]
        t_s1 = None
        for ot in range(DT):
            pb = [PM0, PM1][ot % 2]
            df = dfree[ot % 2]
            tgt = P.st.bump(1)
            @P.on("tensor")
            def _(e, ot=ot, pb=pb, df=df):
                if df is not None:
                    e.wait_ge(df[0].h, df[1])
                for k in range(DT):
                    r = e.matmul(pb[:, 0:TCH], wob[k][:, ot * 128:(ot + 1) * 128], yb[k][:, 0:TCH], start=(k == 0), stop=(k == DT - 1))
                r.then_inc(P.st.h, 1)
            t_s1 = P.sv.bump(1)
            @P.on("vector")
            def _(e, ot=ot, pb=pb, tgt=tgt):
                e.wait_ge(P.st.h, tgt)
                e.tensor_tensor(s1t[ot][:, :], pb[:, 0:TCH], xnb[ot][:, :], ALU.add).then_inc(P.sv.h, 1)
            P.store(s1_s[ot, :, off:off + TCH], s1t[ot][:, :], wait=[(P.sv, t_s1)])
            dfree[ot % 2] = (P.sv, t_s1)
        P.barrier_engine("tensor", [(P.sv, t_s1)])
        P.barrier_engine("scalar", [(P.sv, t_s1)])
        P.barrier_engine("vector", [(P.sv, t_s1)])
        t_h = emit_ln_chunk(P, s1t, hb, sq, rows, ones_f, ones1_f, PST, PBC, TCH,
                            gb=(lnc[:, 0:DT], lnc[:, DT:2 * DT]))
        for k in range(DT):
            P.store(h_s[k, :, off:off + TCH], hb[k][:, :], wait=[(P.sv, t_h)] if k == 0 else ())
    return P.finalize() if own else None


def build_l0a5(P=None):
    own = P is None
    if own:
        P = Prog("l0a5")
    h_sD = P.dram("h_s", [DT, 128, NT], BF16, "ExternalInput")
    s1_sD = P.dram("s1_s", [DT, 128, NT], F32, "ExternalInput")
    w1D = P.dram("l0_cw1T", [3, DT, 128, 4096], F32, "ExternalInput")
    b1D = P.dram("l0_cb1", [3, 1, 4096], F32, "ExternalInput")
    w2D = P.dram("l0_cw2T", [3, 32, 128, D], F32, "ExternalInput")
    b2D = P.dram("l0_cb2", [3, 1, D], F32, "ExternalInput")
    onesr = P.dram("c_onesr", [1, 512], F32, "ExternalInput")
    sA_s = P.dram("sA_s", [DT, 128, NT], F32, "ExternalOutput")
    hx_s = P.dram("hx_s", [DT, 128, NT], BF16, "Internal")

    onesr_f = P.sbuf("onesr_f", [1, 512], F32)
    P.load(onesr_f[:, :], onesr[:, :])
    wscr = P.sbuf("wscr", [128, 2048], F32)
    w1b = [P.sbuf(f"w1b{k}", [128, 4096], BF16) for k in range(DT)]
    w2b = [P.sbuf(f"w2b{k}", [128, D], BF16) for k in range(32)]
    b1r = P.sbuf("b1r", [1, 4096], F32)
    b2r = P.sbuf("b2r", [1, D], F32)
    hin = [P.sbuf(f"hin{k}", [128, TCH], BF16) for k in range(DT)]
    hmid = [P.sbuf(f"hmid{k}", [128, TCH], BF16) for k in range(32)]
    s1t = P.sbuf("s1t", [128, TCH], F32)
    hot = P.sbuf("hot", [128, TCH], BF16)
    sot = P.sbuf("sot", [128, TCH], F32)
    PM0 = P.psum("PM0", [128, TCH])
    PM1 = P.psum("PM1", [128, TCH])
    PM2 = P.psum("PM2", [128, TCH])
    PM3 = P.psum("PM3", [128, TCH])

    def wload(dst, src_ap, ncol, base=0):
        tw = P.load(wscr[:, 0:ncol], src_ap, wait=[(P.sv, None), (P.st, None)])
        tvv = P.sv.bump(1)
        @P.on("vector")
        def _(e, dst=dst, tw=tw, ncol=ncol, base=base):
            e.wait_ge(P.sd.h, tw)
            e.tensor_copy(dst[:, base:base + ncol], wscr[:, 0:ncol]).then_inc(P.sv.h, 1)

    for lev in range(3):
        for k in range(DT):
            wload(w1b[k], w1D[lev, k][:, 0:2048], 2048, 0)
            wload(w1b[k], w1D[lev, k][:, 2048:4096], 2048, 2048)
        for k in range(32):
            wload(w2b[k], w2D[lev, k], D)
        tb = P.load(b1r[0:1, :], b1D[lev], wait=[(P.st, None)])
        tb = P.load(b2r[0:1, :], b2D[lev])
        P.barrier_engine("tensor", [(P.sd, tb), (P.sv, None)])
        inD = h_sD if lev == 0 else hx_s
        for ci in range(NCH):
            off = ci * TCH
            tld = None
            for k in range(DT):
                tld = P.load(hin[k][:, :], inD[k, :, off:off + TCH],
                             wait=[(P.st, None), (P.sv, None)] if k == 0 else ())
            P.barrier_engine("tensor", [(P.sd, tld), (P.sv, None), (P.ss, None)])
            gfree = [None, None]
            for ot in range(32):
                pb = [PM0, PM1][ot % 2]
                gf = gfree[ot % 2]
                tgt = P.st.bump(1)
                @P.on("tensor")
                def _(e, ot=ot, pb=pb, gf=gf):
                    if gf is not None:
                        e.wait_ge(gf[0].h, gf[1])
                    e.matmul(pb[:, 0:TCH], b1r[0:1, ot * 128:(ot + 1) * 128], onesr_f[0:1, 0:TCH], start=True, stop=False)
                    for k in range(DT):
                        r = e.matmul(pb[:, 0:TCH], w1b[k][:, ot * 128:(ot + 1) * 128], hin[k][:, 0:TCH], start=False, stop=(k == DT - 1))
                    r.then_inc(P.st.h, 1)
                tsg = P.ss.bump(1)
                @P.on("scalar")
                def _(e, ot=ot, pb=pb, tgt=tgt):
                    e.wait_ge(P.st.h, tgt)
                    e.activation(hmid[ot][:, :], pb[:, 0:TCH], AF.Gelu_apprx_tanh).then_inc(P.ss.h, 1)
                gfree[ot % 2] = (P.ss, tsg)
            P.barrier_engine("tensor", [(P.ss, None)])
            dfree = [None, None]
            for ot in range(DT):
                pb = [PM2, PM3][ot % 2]
                df = dfree[ot % 2]
                tgt = P.st.bump(1)
                @P.on("tensor")
                def _(e, ot=ot, pb=pb, df=df):
                    if df is not None:
                        e.wait_ge(df[0].h, df[1])
                    e.matmul(pb[:, 0:TCH], b2r[0:1, ot * 128:(ot + 1) * 128], onesr_f[0:1, 0:TCH], start=True, stop=False)
                    for k in range(32):
                        r = e.matmul(pb[:, 0:TCH], w2b[k][:, ot * 128:(ot + 1) * 128], hmid[k][:, 0:TCH], start=False, stop=(k == 31))
                    r.then_inc(P.st.h, 1)
                if lev < 2:
                    sgn = P.sg.n
                    tcv = P.sv.bump(1)
                    @P.on("vector")
                    def _(e, ot=ot, pb=pb, tgt=tgt, sgn=sgn):
                        e.wait_ge(P.st.h, tgt)
                        e.wait_ge(P.sg.h, sgn)
                        e.tensor_copy(hot[:, :], pb[:, 0:TCH]).then_inc(P.sv.h, 1)
                    P.store(hx_s[ot, :, off:off + TCH], hot[:, :], wait=[(P.sv, tcv)])
                    dfree[ot % 2] = (P.sv, tcv)
                else:
                    ts1 = P.load(s1t[:, :], s1_sD[ot, :, off:off + TCH], wait=[(P.sv, None)])
                    sgn = P.sg.n
                    tcv = P.sv.bump(1)
                    @P.on("vector")
                    def _(e, ot=ot, pb=pb, tgt=tgt, ts1=ts1, sgn=sgn):
                        e.wait_ge(P.st.h, tgt)
                        e.wait_ge(P.sd.h, ts1)
                        e.wait_ge(P.sg.h, sgn)
                        e.tensor_tensor(sot[:, :], pb[:, 0:TCH], s1t[:, :], ALU.add).then_inc(P.sv.h, 1)
                    P.store(sA_s[ot, :, off:off + TCH], sot[:, :], wait=[(P.sv, tcv)])
                    dfree[ot % 2] = (P.sv, tcv)
    return P.finalize() if own else None


def build_l0f1(P=None):
    own = P is None
    if own:
        P = Prog("l0f1")
    xc_in = P.dram("xc_in", [DT, 128, NT], F32, "ExternalInput")
    wqkvT = P.dram("f_wqkvT", [DT, 128, 1536], F32, "ExternalInput")
    qs2D = P.dram("f_qs2", [128, 2], F32, "ExternalInput")
    rTD = P.dram("c_rT", [128, 128], F32, "ExternalInput")
    xc1D = P.dram("c_xc1", [128, S], F32, "ExternalInput")
    xs1D = P.dram("c_xs1", [128, S], F32, "ExternalInput")
    xc2D = P.dram("c_xc2", [128, S], F32, "ExternalInput")
    xs2D = P.dram("c_xs2", [128, S], F32, "ExternalInput")
    ones = P.dram("c_ones", [128, 1], F32, "ExternalInput")
    ones1 = P.dram("c_ones1", [1, 128], F32, "ExternalInput")
    Q_s = P.dram("Q_s", [4, 128, NT], BF16, "ExternalOutput")
    K_s = P.dram("K_s", [4, 128, NT], BF16, "ExternalOutput")
    VT_s = P.dram("VT_s", [NT // 128, 128, 8 * 65], BF16, "ExternalOutput")

    ones_f = P.sbuf("ones_f", [128, 1], F32)
    ones1_f = P.sbuf("ones1_f", [1, 128], F32)
    P.load(ones_f[:, :], ones[:, :])
    P.load(ones1_f[:, :], ones1[:, :])
    qs2 = P.sbuf("qs2", [128, 2], F32)
    P.load(qs2[:, :], qs2D[:, :])
    wscr = P.sbuf("wscr", [128, 2048], F32)
    wb = [P.sbuf(f"wb{k}", [128, 1536], BF16) for k in range(DT)]
    rqT = P.sbuf("rqT", [128, 128], BF16)
    rkT = P.sbuf("rkT", [128, 128], BF16)
    tabs = {k: P.sbuf(f"tab{k}", [128, S], BF16) for k in ("c1", "s1", "c2", "s2")}
    def wload_f(dst, src_ap, ncol, fn=None):
        tw = P.load(wscr[:, 0:ncol], src_ap, wait=[(P.sv, None)])
        tvv = P.sv.bump(1)
        @P.on("vector")
        def _(e, dst=dst, tw=tw, ncol=ncol, fn=fn):
            e.wait_ge(P.sd.h, tw)
            if fn is None:
                e.tensor_copy(dst[:, 0:ncol], wscr[:, 0:ncol]).then_inc(P.sv.h, 1)
            else:
                fn(e).then_inc(P.sv.h, 1)
    for k in range(DT):
        wload_f(wb[k], wqkvT[k], 1536)
    wload_f(rqT, rTD[:, :], 128, lambda e: e.tensor_single_scalar(rqT[:, :], wscr[:, 0:128], qs2[:, 0:1], ALU.mult))
    wload_f(rkT, rTD[:, :], 128, lambda e: e.tensor_single_scalar(rkT[:, :], wscr[:, 0:128], qs2[:, 1:2], ALU.mult))
    for half in range(4):
        nm, src = [("c1", xc1D), ("s1", xs1D), ("c2", xc2D), ("s2", xs2D)][half]
        for hc in range(2):
            sl = slice(hc * 1024, (hc + 1) * 1024)
            if nm in ("c1",):
                wload_f(tabs[nm], src[:, sl], 1024, (lambda e, nm=nm, sl=sl: e.tensor_single_scalar(tabs[nm][:, sl], wscr[:, 0:1024], qs2[:, 0:1], ALU.mult)))
            elif nm in ("c2",):
                wload_f(tabs[nm], src[:, sl], 1024, (lambda e, nm=nm, sl=sl: e.tensor_single_scalar(tabs[nm][:, sl], wscr[:, 0:1024], qs2[:, 1:2], ALU.mult)))
            else:
                wload_f(tabs[nm], src[:, sl], 1024, (lambda e, nm=nm, sl=sl: e.tensor_copy(tabs[nm][:, sl], wscr[:, 0:1024])))

    xt = [P.sbuf(f"xt{k}", [128, TCH + 1], F32) for k in range(DT)]
    hsh = [P.sbuf(f"hsh{k}", [128, TCH], BF16) for k in range(DT)]
    qn = [P.sbuf(f"qn{i}", [128, TCH], BF16) for i in range(4)]
    kn = [P.sbuf(f"kn{i}", [128, TCH], BF16) for i in range(4)]
    sqx = P.sbuf("sqx", [128, TCH], F32)
    nr1 = P.sbuf("nr1", [1, TCH], F32)
    nr2 = P.sbuf("nr2", [1, TCH], F32)
    t1b = P.sbuf("t1b", [128, TCH], BF16)
    t2b = P.sbuf("t2b", [128, TCH], BF16)
    vb = P.sbuf("vb", [128, 528], BF16)
    PM0 = P.psum("PM0", [128, TCH])
    PM1 = P.psum("PM1", [128, TCH])
    PRR = P.psum("PRR", [128, TCH])
    PB1 = P.psum("PB1", [128, TCH])

    for ci in range(NCH):
        off = ci * TCH
        bstart = (off % S == 0)
        tld = None
        for k in range(DT):
            if k < 4:
                tld = P.load(xt[k][:, 1:1 + TCH], xc_in[k, :, off:off + TCH],
                             wait=[(P.st, None), (P.sv, None), (P.ss, None)] if k == 0 else ())
            else:
                if bstart:
                    tld = P.load(xt[k][:, 1:1 + TCH], xc_in[k, :, off:off + TCH])
                else:
                    tld = P.load(xt[k][:, 0:1 + TCH], xc_in[k, :, off - 1:off + TCH])
        tsh = P.sv.bump(1)
        @P.on("vector")
        def _(e, bstart=bstart, tld=tld):
            e.wait_ge(P.sd.h, tld)
            r = None
            for k in range(DT):
                if k < 4:
                    r = e.tensor_copy(hsh[k][:, :], xt[k][:, 1:1 + TCH])
                else:
                    if bstart:
                        e.memset(hsh[k][:, 0:1], 0.0)
                        r = e.tensor_copy(hsh[k][:, 1:TCH], xt[k][:, 1:TCH])
                    else:
                        r = e.tensor_copy(hsh[k][:, :], xt[k][:, 0:TCH])
            r.then_inc(P.sv.h, 1)
        P.barrier_engine("tensor", [(P.sv, tsh)])
        for ot in range(8):
            pb = [PM0, PM1][ot % 2]
            dst = qn[ot] if ot < 4 else kn[ot - 4]
            svn = P.sv.n
            tgt = P.st.bump(1)
            @P.on("tensor")
            def _(e, ot=ot, pb=pb, svn=svn):
                e.wait_ge(P.sv.h, svn)
                for k in range(DT):
                    r = e.matmul(pb[:, 0:TCH], wb[k][:, ot * 128:(ot + 1) * 128], hsh[k][:, 0:TCH], start=(k == 0), stop=(k == DT - 1))
                r.then_inc(P.st.h, 1)
            sgn = P.sg.n
            tvx = P.sv.bump(1)
            @P.on("vector")
            def _(e, pb=pb, dst=dst, tgt=tgt, sgn=sgn):
                e.wait_ge(P.st.h, tgt)
                e.wait_ge(P.sg.h, sgn)
                e.tensor_copy(dst[:, :], pb[:, 0:TCH]).then_inc(P.sv.h, 1)
        for (tiles, rT, ctab, stab, outD) in ((qn, rqT, "c1", "s1", Q_s), (kn, rkT, "c2", "s2", K_s)):
            for tt in range(4):
                svn = P.sv.n
                tsq = P.ss.bump(1)
                @P.on("scalar")
                def _(e, tiles=tiles, tt=tt, svn=svn):
                    e.wait_ge(P.sv.h, svn)
                    e.activation(sqx[:, :], tiles[tt][:, :], AF.Square).then_inc(P.ss.h, 1)
                tnm = P.st.bump(1)
                @P.on("tensor")
                def _(e, tsq=tsq):
                    e.wait_ge(P.ss.h, tsq)
                    e.matmul(PRR[0:1, 0:TCH], ones_f[0:64, 0:1], sqx[0:64, :], start=True, stop=True)
                    e.matmul(PB1[0:1, 0:TCH], ones_f[64:128, 0:1], sqx[64:128, :], start=True, stop=True).then_inc(P.st.h, 1)
                trw = P.ss.bump(1)
                @P.on("scalar")
                def _(e, tnm=tnm):
                    e.wait_ge(P.st.h, tnm)
                    e.activation(nr1[0:1, :], PRR[0:1, 0:TCH], AF.Sqrt)
                    e.activation(nr2[0:1, :], PB1[0:1, 0:TCH], AF.Sqrt).then_inc(P.ss.h, 1)
                trc = P.sv.bump(1)
                @P.on("vector")
                def _(e, trw=trw):
                    e.wait_ge(P.ss.h, trw)
                    e.tensor_scalar_max(nr1[0:1, :], nr1[0:1, :], 1e-12)
                    e.tensor_scalar_max(nr2[0:1, :], nr2[0:1, :], 1e-12)
                    e.reciprocal(nr1[0:1, :], nr1[0:1, :])
                    e.reciprocal(nr2[0:1, :], nr2[0:1, :]).then_inc(P.sv.h, 1)
                tbc = P.st.bump(1)
                @P.on("tensor")
                def _(e, trc=trc):
                    e.wait_ge(P.sv.h, trc)
                    e.matmul(PRR[:, 0:TCH], ones1_f[0:1, :], nr1[0:1, 0:TCH], start=True, stop=True)
                    e.matmul(PB1[:, 0:TCH], ones1_f[0:1, :], nr2[0:1, 0:TCH], start=True, stop=True).then_inc(P.st.h, 1)
                tap = P.sv.bump(1)
                @P.on("vector")
                def _(e, tiles=tiles, tt=tt, tbc=tbc):
                    e.wait_ge(P.st.h, tbc)
                    e.tensor_tensor(tiles[tt][0:64, :], tiles[tt][0:64, :], PRR[0:64, 0:TCH], ALU.mult)
                    e.tensor_tensor(tiles[tt][64:128, :], tiles[tt][64:128, :], PB1[64:128, 0:TCH], ALU.mult).then_inc(P.sv.h, 1)
                trm = P.st.bump(1)
                @P.on("tensor")
                def _(e, tiles=tiles, tt=tt, rT=rT, tap=tap):
                    e.wait_ge(P.sv.h, tap)
                    e.matmul(PM0[:, 0:TCH], rT[:, :], tiles[tt][:, 0:TCH], start=True, stop=True).then_inc(P.st.h, 1)
                tcm = P.sv.bump(1)
                @P.on("vector")
                def _(e, tiles=tiles, tt=tt, ctab=ctab, stab=stab, trm=trm, off=off):
                    e.wait_ge(P.st.h, trm)
                    toff = off % S
                    e.tensor_tensor(t1b[:, :], tiles[tt][:, :], tabs[ctab][:, toff:toff + TCH], ALU.mult)
                    e.tensor_tensor(t2b[:, :], PM0[:, 0:TCH], tabs[stab][:, toff:toff + TCH], ALU.mult).then_inc(P.sv.h, 1)
                tcm2 = P.sv.bump(1)
                @P.on("vector")
                def _(e, tiles=tiles, tt=tt, tcm=tcm):
                    e.wait_ge(P.sv.h, tcm)
                    e.tensor_tensor(tiles[tt][:, :], t1b[:, :], t2b[:, :], ALU.add).then_inc(P.sv.h, 1)
                P.store(outD[tt, :, off:off + TCH], tiles[tt][:, :], wait=[(P.sv, tcm2)])
        for sub in range(TCH // 128):
            tt2 = off // 128 + sub
            svn = P.sv.n
            sgn = P.sg.n
            tgt = P.st.bump(1)
            @P.on("tensor")
            def _(e, sub=sub, svn=svn):
                e.wait_ge(P.sv.h, svn)
                for k in range(DT):
                    r = e.matmul(PM1[:, 0:512], hsh[k][:, sub * 128:(sub + 1) * 128], wb[k][:, 1024:1536], start=(k == 0), stop=(k == DT - 1))
                r.then_inc(P.st.h, 1)
            tvv = P.sv.bump(1)
            @P.on("vector")
            def _(e, tgt=tgt, sgn=sgn):
                e.wait_ge(P.st.h, tgt)
                e.wait_ge(P.sg.h, sgn)
                vv = vb[:, 0:520].rearrange("p (h c) -> p h c", c=65)
                e.memset(vv[:, :, 64:65], 1.0)
                e.tensor_copy(vv[:, :, 0:64], PM1[:, 0:512].rearrange("p (h c) -> p h c", c=64)).then_inc(P.sv.h, 1)
            P.store(VT_s[tt2, :, :], vb[:, 0:520], wait=[(P.sv, tvv)])
    return P.finalize() if own else None


def build_l0f2(P=None):
    own = P is None
    if own:
        P = Prog("l0f2")
    xc_in = P.dram("xc_in", [DT, 128, NT], F32, "ExternalInput")
    Q_sD = P.dram("Q_s", [4, 128, NT], BF16, "ExternalInput")
    K_sD = P.dram("K_s", [4, 128, NT], BF16, "ExternalInput")
    VT_sD = P.dram("VT_s", [NT // 128, 128, 8 * 65], BF16, "ExternalInput")
    woTh = P.dram("f_woTh", [8, 64, D], F32, "ExternalInput")
    mask2 = P.dram("c_mask2", [128, 1536], F32, "ExternalInput")
    ones1 = P.dram("c_ones1", [1, 128], F32, "ExternalInput")
    ccols = P.dram("c_cols", [128, 4], F32, "ExternalInput")
    xmid_s = P.dram("xmid_s", [DT, 128, NT], F32, "ExternalOutput")

    ones1_f = P.sbuf("ones1_f", [1, 128], F32)
    ccol_f = P.sbuf("ccol_f", [128, 4], F32)
    maskb = P.sbuf("maskb", [128, 1536], BF16)
    P.load(ones1_f[:, :], ones1[:, :])
    P.load(ccol_f[:, :], ccols[:, :])
    wscr = P.sbuf("wscr", [128, 1536], F32)
    wob = [P.sbuf(f"wob{h}", [64, D], BF16) for h in range(8)]
    def wload(dst, src_ap, ncol, npart=128):
        tw = P.load(wscr[0:npart, 0:ncol], src_ap, wait=[(P.sv, None)])
        tvv = P.sv.bump(1)
        @P.on("vector")
        def _(e, dst=dst, tw=tw, ncol=ncol, npart=npart):
            e.wait_ge(P.sd.h, tw)
            e.tensor_copy(dst[0:npart, 0:ncol], wscr[0:npart, 0:ncol]).then_inc(P.sv.h, 1)
    wload(maskb, mask2[:, :], 1536)
    for h in range(8):
        wload(wob[h], woTh[h], D, npart=64)

    Qb = [P.sbuf(f"Qb{k}", [128, S], BF16) for k in range(4)]
    Kb = [P.sbuf(f"Kb{k}", [128, S], BF16) for k in range(4)]
    VTe = [P.sbuf(f"VTe{c}", [128, 8 * 65], BF16) for c in range(16)]
    eS = [P.sbuf(f"eS{i}", [128, TCH], BF16) for i in range(2)]
    Yh = [P.sbuf(f"Yh{h}", [64, TCH], BF16) for h in range(8)]
    rrow = P.sbuf("rrow", [1, TCH], F32)
    zrow = P.sbuf("zrow", [1, TCH], F32)
    xts = P.sbuf("xts", [128, TCH], F32)
    sot = P.sbuf("sot", [128, TCH], F32)
    PBC = P.psum("PBC", [128, TCH])
    PM0 = P.psum("PM0", [128, TCH])
    PM1 = P.psum("PM1", [128, TCH])
    PAV = P.psum("PAV", [128, TCH])
    tvz0 = P.sv.bump(1)
    @P.on("vector")
    def _(e):
        e.memset(zrow[0:1, :], 0.0).then_inc(P.sv.h, 1)

    for b in range(BL):
        tld = None
        stn = P.st.n
        for k in range(4):
            tld = P.load(Qb[k][:, :], Q_sD[k, :, b * S:(b + 1) * S], wait=[(P.st, stn), (P.sv, None)] if k == 0 else ())
            tld = P.load(Kb[k][:, :], K_sD[k, :, b * S:(b + 1) * S])
        for c in range(16):
            tld = P.load(VTe[c][:, :], VT_sD[b * 16 + c])
        P.barrier_engine("tensor", [(P.sd, tld)])
        for w in range(4):
            qoff = w * 512
            ktg0 = max(0, (w - 1) * 4)
            nkt = w * 4 + 4 - ktg0
            for h in range(8):
                ht, hh = h // 2, (h % 2) * 64
                esfree = [None, None]
                pingfree = [None, None]
                for ki in range(nkt):
                    ktg = ktg0 + ki
                    pb = [PM0, PM1][ki % 2]
                    ebuf = eS[ki % 2]
                    pf = pingfree[ki % 2]
                    ef = esfree[ki % 2]
                    tgt = P.st.bump(1)
                    @P.on("tensor")
                    def _(e, pb=pb, ktg=ktg, ht=ht, hh=hh, qoff=qoff, pf=pf):
                        if pf is not None:
                            e.wait_ge(pf[0].h, pf[1])
                        e.matmul(pb[:, 0:TCH], Kb[ht][hh:hh + 64, ktg * 128:(ktg + 1) * 128], Qb[ht][hh:hh + 64, qoff:qoff + TCH], start=True, stop=True).then_inc(P.st.h, 1)
                    tse = P.ss.bump(1)
                    @P.on("scalar")
                    def _(e, pb=pb, ebuf=ebuf, tgt=tgt, ef=ef):
                        e.wait_ge(P.st.h, tgt)
                        if ef is not None:
                            e.wait_ge(ef[0].h, ef[1])
                        e.activation(ebuf[:, :], pb[:, 0:TCH], AF.Exp, scale=8.0, bias=ccol_f[:, 2:3]).then_inc(P.ss.h, 1)
                    pingfree[ki % 2] = (P.ss, tse)
                    offd = qoff - ktg * 128
                    tvm = P.sv.bump(1)
                    @P.on("vector")
                    def _(e, ebuf=ebuf, offd=offd, tse=tse):
                        e.wait_ge(P.ss.h, tse)
                        e.tensor_tensor(ebuf[:, :], ebuf[:, :], maskb[:, 512 + offd:1024 + offd], ALU.mult).then_inc(P.sv.h, 1)
                    tav = P.st.bump(1)
                    @P.on("tensor")
                    def _(e, ktg=ktg, h=h, ebuf=ebuf, tvm=tvm, ki=ki, nkt=nkt, b=b):
                        e.wait_ge(P.sv.h, tvm)
                        e.matmul(PAV[0:65, 0:TCH], VTe[ktg][:, 65 * h:65 * h + 65], ebuf[:, 0:TCH], start=(ki == 0), stop=(ki == nkt - 1)).then_inc(P.st.h, 1)
                    esfree[ki % 2] = (P.st, tav)
                t_fin = P.st.n
                tvr = P.sv.bump(1)
                @P.on("vector")
                def _(e, t_fin=t_fin):
                    e.wait_ge(P.st.h, t_fin)
                    e.reciprocal(rrow[0:1, :], PAV[64:65, 0:TCH]).then_inc(P.sv.h, 1)
                tbc = P.st.bump(1)
                @P.on("tensor")
                def _(e, tvr=tvr):
                    e.wait_ge(P.sv.h, tvr)
                    e.matmul(PBC[:, 0:TCH], ones1_f[0:1, :], rrow[0:1, 0:TCH], start=True, stop=True).then_inc(P.st.h, 1)
                tsy = P.ss.bump(1)
                @P.on("scalar")
                def _(e, h=h, tbc=tbc):
                    e.wait_ge(P.st.h, tbc)
                    e.activation(Yh[h][:, :], PAV[0:64, 0:TCH], AF.Copy).then_inc(P.ss.h, 1)
                tvy = P.sv.bump(1)
                @P.on("vector")
                def _(e, h=h, tsy=tsy):
                    e.wait_ge(P.ss.h, tsy)
                    e.tensor_tensor(Yh[h][:, :], Yh[h][:, :], PBC[0:64, 0:TCH], ALU.mult).then_inc(P.sv.h, 1)
                P.barrier_engine("tensor", [(P.ss, tsy), (P.sv, tvy)])
            off = b * S + qoff
            dfree = [None, None]
            P.barrier_engine("tensor", [(P.sv, None)])
            for ot in range(DT):
                pb = [PM0, PM1][ot % 2]
                df = dfree[ot % 2]
                tgt = P.st.bump(1)
                @P.on("tensor")
                def _(e, pb=pb, ot=ot, df=df):
                    if df is not None:
                        e.wait_ge(df[0].h, df[1])
                    e.matmul(pb[:, 0:TCH], ones1_f[0:1, :], zrow[0:1, 0:TCH], start=True, stop=False)
                    for h in range(8):
                        r = e.matmul(pb[:, 0:TCH], wob[h][:, ot * 128:(ot + 1) * 128], Yh[h][:, 0:TCH], start=False, stop=(h == 7))
                    r.then_inc(P.st.h, 1)
                tx = P.load(xts[:, :], xc_in[ot, :, off:off + TCH], wait=[(P.sv, None)])
                sgn = P.sg.n
                tvs = P.sv.bump(1)
                @P.on("vector")
                def _(e, pb=pb, tgt=tgt, tx=tx, sgn=sgn):
                    e.wait_ge(P.st.h, tgt)
                    e.wait_ge(P.sd.h, tx)
                    e.wait_ge(P.sg.h, sgn)
                    e.tensor_tensor(sot[:, :], pb[:, 0:TCH], xts[:, :], ALU.add).then_inc(P.sv.h, 1)
                P.store(xmid_s[ot, :, off:off + TCH], sot[:, :], wait=[(P.sv, tvs)])
                dfree[ot % 2] = (P.sv, tvs)
    return P.finalize() if own else None


def build_l0f3a(P=None):
    own = P is None
    if own:
        P = Prog("l0f3a")
    xm_in = P.dram("xm_in", [DT, 128, NT], F32, "ExternalInput")
    w1T = P.dram("f_w1T", [DT, 128, 5460], F32, "ExternalInput")
    b1D = P.dram("f_b1", [1, 5460], F32, "ExternalInput")
    lncol = P.dram("f_lng", [128, 2 * DT], F32, "ExternalInput")
    ones = P.dram("c_ones", [128, 1], F32, "ExternalInput")
    ones1 = P.dram("c_ones1", [1, 128], F32, "ExternalInput")
    onesr = P.dram("c_onesr", [1, 512], F32, "ExternalInput")
    misc = P.dram("c_misc", [1, 8], F32, "ExternalInput")
    hbD = P.dram("hb_s", [22, 128, NT], BF16, "ExternalOutput")

    ones_f = P.sbuf("ones_f", [128, 1], F32)
    ones1_f = P.sbuf("ones1_f", [1, 128], F32)
    onesr_f = P.sbuf("onesr_f", [1, 512], F32)
    P.c_eps = P.sbuf("c_eps", [1, 8], F32)
    lnc = P.sbuf("lnc", [128, 2 * DT], F32)
    b1r = P.sbuf("b1r", [1, 5460], F32)
    P.load(ones_f[:, :], ones[:, :])
    P.load(ones1_f[:, :], ones1[:, :])
    P.load(onesr_f[:, :], onesr[:, :])
    P.load(P.c_eps[0:1, :], misc[0:1, :])
    P.load(lnc[:, :], lncol[:, :])
    P.load(b1r[0:1, :], b1D[0:1, :])
    wscr = P.sbuf("wscr", [128, 2730], F32)
    w1b = [P.sbuf(f"w1b{k}", [128, 5460], BF16) for k in range(DT)]
    def wload(dst, base, src_ap, ncol):
        tw = P.load(wscr[:, 0:ncol], src_ap, wait=[(P.sv, None)])
        tvv = P.sv.bump(1)
        @P.on("vector")
        def _(e, dst=dst, base=base, tw=tw, ncol=ncol):
            e.wait_ge(P.sd.h, tw)
            e.tensor_copy(dst[:, base:base + ncol], wscr[:, 0:ncol]).then_inc(P.sv.h, 1)
    for k in range(DT):
        wload(w1b[k], 0, w1T[k][:, 0:2730], 2730)
        wload(w1b[k], 2730, w1T[k][:, 2730:5460], 2730)
    AOFF = [128 * i for i in range(21)] + [2688]
    ASZ = [128] * 21 + [42]

    xt = [P.sbuf(f"xt{k}", [128, TCH + 1], F32) for k in range(DT)]
    hsh = [P.sbuf(f"hsh{k}", [128, TCH], F32) for k in range(DT)]
    sq = [P.sbuf(f"sq{k}", [128, TCH], F32) for k in range(DT)]
    hnb = [P.sbuf(f"hnb{k}", [128, TCH], BF16) for k in range(DT)]
    rows = (P.sbuf("r_mean", [1, TCH], F32), P.sbuf("r_var", [1, TCH], F32), P.sbuf("r_tmp", [1, TCH], F32))
    ga = [P.sbuf(f"ga{i}", [128, TCH], BF16) for i in range(2)]
    hbt = [P.sbuf(f"hbt{i}", [128, TCH], BF16) for i in range(2)]
    PST = P.psum("PST", [128, TCH])
    PBC = P.psum("PBC", [128, TCH])
    PM0 = P.psum("PM0", [128, TCH])
    PM1 = P.psum("PM1", [128, TCH])

    for ci in range(NCH):
        off = ci * TCH
        bstart = (off % S == 0)
        tld = None
        for k in range(DT):
            if k < 4:
                tld = P.load(xt[k][:, 1:1 + TCH], xm_in[k, :, off:off + TCH],
                             wait=[(P.st, None), (P.sv, None), (P.ss, None)] if k == 0 else ())
            else:
                if bstart:
                    tld = P.load(xt[k][:, 1:1 + TCH], xm_in[k, :, off:off + TCH])
                else:
                    tld = P.load(xt[k][:, 0:1 + TCH], xm_in[k, :, off - 1:off + TCH])
        tsh = P.sv.bump(1)
        @P.on("vector")
        def _(e, bstart=bstart, tld=tld):
            e.wait_ge(P.sd.h, tld)
            r = None
            for k in range(DT):
                if k < 4:
                    r = e.tensor_copy(hsh[k][:, :], xt[k][:, 1:1 + TCH])
                else:
                    if bstart:
                        e.memset(hsh[k][:, 0:1], 0.0)
                        r = e.tensor_copy(hsh[k][:, 1:TCH], xt[k][:, 1:TCH])
                    else:
                        r = e.tensor_copy(hsh[k][:, :], xt[k][:, 0:TCH])
            r.then_inc(P.sv.h, 1)
        P.barrier_engine("tensor", [(P.sv, tsh)])
        P.barrier_engine("scalar", [(P.sv, tsh)])
        P.barrier_engine("vector", [(P.sv, tsh)])
        t_h = emit_ln_chunk(P, hsh, hnb, sq, rows, ones_f, ones1_f, PST, PBC, TCH,
                            gb=(lnc[:, 0:DT], lnc[:, DT:2 * DT]))
        P.barrier_engine("tensor", [(P.sv, t_h)])
        gfree = [None, None]
        prev_thb = None
        for i in range(22):
            aoff, asz = AOFF[i], ASZ[i]
            pa = [PM0, PM1][i % 2]
            gf = gfree[i % 2]
            tgt = P.st.bump(1)
            @P.on("tensor")
            def _(e, aoff=aoff, asz=asz, pa=pa, gf=gf):
                if gf is not None:
                    e.wait_ge(gf[0].h, gf[1])
                e.matmul(pa[0:asz, 0:TCH], b1r[0:1, aoff:aoff + asz], onesr_f[0:1, 0:TCH], start=True, stop=False)
                r = None
                for k in range(DT):
                    r = e.matmul(pa[0:asz, 0:TCH], w1b[k][:, aoff:aoff + asz], hnb[k][:, 0:TCH], start=False, stop=(k == DT - 1))
                r.then_inc(P.st.h, 1)
            tga = P.ss.bump(1)
            @P.on("scalar")
            def _(e, pa=pa, asz=asz, tgt=tgt, prev_thb=prev_thb):
                e.wait_ge(P.st.h, tgt)
                if prev_thb is not None:
                    e.wait_ge(P.sv.h, prev_thb)
                e.activation(ga[0][0:asz, :], pa[0:asz, 0:TCH], AF.Copy).then_inc(P.ss.h, 1)
            tgt2 = P.st.bump(1)
            @P.on("tensor")
            def _(e, aoff=aoff, asz=asz, pa=pa, tga=tga):
                e.wait_ge(P.ss.h, tga)
                e.matmul(pa[0:asz, 0:TCH], b1r[0:1, 2730 + aoff:2730 + aoff + asz], onesr_f[0:1, 0:TCH], start=True, stop=False)
                r = None
                for k in range(DT):
                    r = e.matmul(pa[0:asz, 0:TCH], w1b[k][:, 2730 + aoff:2730 + aoff + asz], hnb[k][:, 0:TCH], start=False, stop=(k == DT - 1))
                r.then_inc(P.st.h, 1)
            tgg = P.ss.bump(1)
            @P.on("scalar")
            def _(e, pa=pa, asz=asz, tgt2=tgt2):
                e.wait_ge(P.st.h, tgt2)
                e.activation(ga[1][0:asz, :], pa[0:asz, 0:TCH], AF.Gelu).then_inc(P.ss.h, 1)
            sgn = P.sg.n
            thb = P.sv.bump(1)
            @P.on("vector")
            def _(e, i=i, asz=asz, tgg=tgg, sgn=sgn):
                e.wait_ge(P.ss.h, tgg)
                e.wait_ge(P.sg.h, sgn)
                e.tensor_tensor(hbt[i % 2][0:asz, :], ga[0][0:asz, :], ga[1][0:asz, :], ALU.mult).then_inc(P.sv.h, 1)
            P.store(hbD[i, 0:asz, off:off + TCH], hbt[i % 2][0:asz, :], wait=[(P.sv, thb)])
            gfree[i % 2] = (P.sv, thb)
            prev_thb = thb
    return P.finalize() if own else None


def build_l0f3b(final, P=None):
    own = P is None
    if own:
        P = Prog("l0f3bf" if final else "l0f3b")
    xm_in = P.dram("xm_in", [DT, 128, NT], F32, "ExternalInput")
    hbD = P.dram("hb_s", [22, 128, NT], BF16, "ExternalInput")
    w2T = P.dram("f_w2T", [22, 128, D], F32, "ExternalInput")
    b2D = P.dram("f_b2", [1, D], F32, "ExternalInput")
    ones1 = P.dram("c_ones1", [1, 128], F32, "ExternalInput")
    onesr = P.dram("c_onesr", [1, 512], F32, "ExternalInput")
    if final:
        xorig = P.dram("x_in", [DT, 128, NT], F32, "ExternalInput")
        sAD = P.dram("sA_s", [DT, 128, NT], F32, "ExternalInput")
    out_s = P.dram("x_out", [DT, 128, NT], F32, "ExternalOutput")

    ones1_f = P.sbuf("ones1_f", [1, 128], F32)
    onesr_f = P.sbuf("onesr_f", [1, 512], F32)
    b2r = P.sbuf("b2r", [1, D], F32)
    P.load(ones1_f[:, :], ones1[:, :])
    P.load(onesr_f[:, :], onesr[:, :])
    P.load(b2r[0:1, :], b2D[0:1, :])
    wscr = P.sbuf("wscr", [128, 2048], F32)
    w2b = [P.sbuf(f"w2b{k}", [128, D], BF16) for k in range(22)]
    def wload(dst, src_ap, ncol):
        tw = P.load(wscr[:, 0:ncol], src_ap, wait=[(P.sv, None)])
        tvv = P.sv.bump(1)
        @P.on("vector")
        def _(e, dst=dst, tw=tw, ncol=ncol):
            e.wait_ge(P.sd.h, tw)
            e.tensor_copy(dst[:, 0:ncol], wscr[:, 0:ncol]).then_inc(P.sv.h, 1)
    for k in range(22):
        wload(w2b[k], w2T[k], D)

    hb2 = [P.sbuf(f"hb2{i}", [128, TCH], BF16) for i in range(22)]
    xmt = P.sbuf("xmt", [128, TCH], F32)
    xts = P.sbuf("xts", [128, TCH], F32)
    sAt = P.sbuf("sAt", [128, TCH], F32)
    xo = P.sbuf("xo", [128, TCH], F32)
    PM0 = P.psum("PM0", [128, TCH])
    PM1 = P.psum("PM1", [128, TCH])
    tvz = P.sv.bump(1)
    @P.on("vector")
    def _(e):
        e.memset(hb2[21][:, :], 0.0).then_inc(P.sv.h, 1)

    for ci in range(NCH):
        off = ci * TCH
        tld = None
        for i in range(22):
            asz = 128 if i < 21 else 42
            tld = P.load(hb2[i][0:asz, :], hbD[i, 0:asz, off:off + TCH],
                         wait=[(P.st, None), (P.sv, None), (P.ss, None)] if i == 0 else ())
        P.barrier_engine("tensor", [(P.sd, tld)])
        dfree = [None, None]
        for ot in range(DT):
            pb = [PM0, PM1][ot % 2]
            df = dfree[ot % 2]
            tgt = P.st.bump(1)
            @P.on("tensor")
            def _(e, ot=ot, pb=pb, df=df):
                if df is not None:
                    e.wait_ge(df[0].h, df[1])
                e.matmul(pb[:, 0:TCH], b2r[0:1, ot * 128:(ot + 1) * 128], onesr_f[0:1, 0:TCH], start=True, stop=False)
                r = None
                for k in range(22):
                    r = e.matmul(pb[:, 0:TCH], w2b[k][:, ot * 128:(ot + 1) * 128], hb2[k][:, 0:TCH], start=False, stop=(k == 21))
                r.then_inc(P.st.h, 1)
            tx = P.load(xmt[:, :], xm_in[ot, :, off:off + TCH], wait=[(P.sv, None)])
            sgn = P.sg.n
            if not final:
                tvs = P.sv.bump(1)
                @P.on("vector")
                def _(e, pb=pb, tgt=tgt, tx=tx, sgn=sgn):
                    e.wait_ge(P.st.h, tgt)
                    e.wait_ge(P.sd.h, tx)
                    e.wait_ge(P.sg.h, sgn)
                    e.tensor_tensor(xo[:, :], pb[:, 0:TCH], xmt[:, :], ALU.add).then_inc(P.sv.h, 1)
                P.store(out_s[ot, :, off:off + TCH], xo[:, :], wait=[(P.sv, tvs)])
                dfree[ot % 2] = (P.sv, tvs)
            else:
                tx2 = P.load(xts[:, :], xorig[ot, :, off:off + TCH])
                tsa = P.load(sAt[:, :], sAD[ot, :, off:off + TCH])
                tvs = P.sv.bump(1)
                @P.on("vector")
                def _(e, pb=pb, tgt=tgt, tx=tx, sgn=sgn):
                    e.wait_ge(P.st.h, tgt)
                    e.wait_ge(P.sd.h, tx)
                    e.wait_ge(P.sg.h, sgn)
                    e.tensor_tensor(xo[:, :], pb[:, 0:TCH], xmt[:, :], ALU.add).then_inc(P.sv.h, 1)
                tsi = P.ss.bump(1)
                @P.on("scalar")
                def _(e, tvs=tvs):
                    e.wait_ge(P.sv.h, tvs)
                    e.activation(xo[:, :], xo[:, :], AF.Silu).then_inc(P.ss.h, 1)
                tvc = P.sv.bump(1)
                @P.on("vector")
                def _(e, tsi=tsi, tsa=tsa):
                    e.wait_ge(P.ss.h, tsi)
                    e.wait_ge(P.sd.h, tsa)
                    e.tensor_tensor(xo[:, :], xo[:, :], sAt[:, :], ALU.mult)
                    e.tensor_tensor(xo[:, :], xo[:, :], xts[:, :], ALU.add).then_inc(P.sv.h, 1)
                P.store(out_s[ot, :, off:off + TCH], xo[:, :], wait=[(P.sv, tvc)])
                dfree[ot % 2] = (P.sv, tvc)
    return P.finalize() if own else None



def _xpos_tables():
    half = np.arange(0, 32, dtype=np.float64)
    inv_freq = 1.0 / 10000.0 ** (2.0 * half / 64.0)
    t = np.arange(S, dtype=np.float64)
    fr = t[:, None] * inv_freq[None, :]
    base = (2.0 * half + 0.4 * 64.0) / (1.4 * 64.0)
    power = (t - S // 2) / 256.0
    sc = base[None, :] ** power[:, None]
    fr64 = np.concatenate([fr, fr], axis=1)
    sc64 = np.concatenate([sc, sc], axis=1)
    c, sn = np.cos(fr64), np.sin(fr64)
    xc1 = (c * sc64).T; xs1 = (sn * sc64).T
    xc2 = (c / sc64).T; xs2 = (sn / sc64).T
    tile2 = lambda a: np.ascontiguousarray(np.tile(a, (2, 1)).astype(np.float32))
    return tile2(xc1), tile2(xs1), tile2(xc2), tile2(xs2)


def _rot_const():
    r = np.zeros((128, 128), np.float32)
    for blk in range(2):
        o = blk * 64
        for c in range(32):
            r[o + c + 32, o + c] = -1.0
            r[o + c, o + c + 32] = 1.0
    return r


def _mask2_table():
    p = np.arange(128)[:, None]
    c = np.arange(1536)[None, :]
    return ((p <= c - 512) & (p >= c - 1024)).astype(np.float32)


_xc1, _xs1, _xc2, _xs2 = _xpos_tables()
CONSTS.update({
    "c_rT": _rot_const(),
    "c_xc1": _xc1, "c_xs1": _xs1, "c_xc2": _xc2, "c_xs2": _xs2,
    "c_mask2": _mask2_table(),
})
CONSTS["c_cols"] = np.ascontiguousarray(np.tile(
    np.array([[-40.0, 1.0, -64.0, 0.0]], np.float32), (128, 1)))


def prep_l0(ins):
    l0a = {
        "l0_wqkvT": np.ascontiguousarray(np.concatenate(
            [np.asarray(ins["L0_attn_wq"]), np.asarray(ins["L0_attn_wk"]),
             np.asarray(ins["L0_attn_wv"])], axis=1).reshape(DT, 128, 3072)),
        "l0_ln1": _colsT(ins["L0_attn_ln1_g"], ins["L0_attn_ln1_b"]),
        "l0_ln2": _colsT(ins["L0_attn_ln2_g"], ins["L0_attn_ln2_b"]),
        "l0_woT": np.ascontiguousarray(np.asarray(ins["L0_attn_wo"]).reshape(DT, 128, D)),
        "l0_cw1T": np.ascontiguousarray(np.asarray(ins["L0_attn_cms_w1"]).reshape(3, DT, 128, 4096)),
        "l0_cb1": np.ascontiguousarray(np.asarray(ins["L0_attn_cms_b1"]).reshape(3, 1, 4096)),
        "l0_cw2T": np.ascontiguousarray(np.asarray(ins["L0_attn_cms_w2"]).reshape(3, 32, 128, D)),
        "l0_cb2": np.ascontiguousarray(np.asarray(ins["L0_attn_cms_b2"]).reshape(3, 1, D)),
    }
    w2p = np.zeros((2, 2816, D), np.float32)
    w2p[:, 0:2730] = np.asarray(ins["L0_ffn_w2"])
    l0f = []
    for l in range(2):
        l0f.append({
            "f_wqkvT": np.ascontiguousarray(np.asarray(ins["L0_ffn_wqkv"])[l].reshape(DT, 128, 1536)),
            "f_qs2": np.ascontiguousarray(np.stack(
                [np.tile(np.asarray(ins["L0_ffn_q_scale"])[l], 2),
                 np.tile(np.asarray(ins["L0_ffn_k_scale"])[l], 2)], axis=1).astype(np.float32)),
            "f_woTh": np.ascontiguousarray(np.asarray(ins["L0_ffn_wo"])[l].reshape(8, 64, D)),
            "f_lng": _colsT(np.asarray(ins["L0_ffn_ln_g"])[l], np.asarray(ins["L0_ffn_ln_b"])[l]),
            "f_w1T": np.ascontiguousarray(np.asarray(ins["L0_ffn_w1"])[l].reshape(DT, 128, 5460)),
            "f_b1": np.ascontiguousarray(np.asarray(ins["L0_ffn_b1"])[l].reshape(1, 5460)),
            "f_w2T": np.ascontiguousarray(w2p[l].reshape(22, 128, D)),
            "f_b2": np.ascontiguousarray(np.asarray(ins["L0_ffn_b2"])[l].reshape(1, D)),
        })
    return l0a, l0f


def run_l0(xcore, l0a, l0f):
    cm = CONSTS
    R = range(NCORES)
    in_maps = [{"x_in": xcore[c], **cm} for c in R]
    r1 = _run(build_l0a1, "l0a1", in_maps)
    in_maps = [{"xn_s": r1[c]["xn_s"], "l0_wqkvT": l0a["l0_wqkvT"], "l0_ln1": l0a["l0_ln1"], **cm} for c in R]
    r2 = _run(build_l0a2, "l0a2", in_maps)
    in_maps = [{"Q_s": r2[c]["Q_s"], "K_s": r2[c]["K_s"], "KT_s": r2[c]["KT_s"],
                "VT_s": r2[c]["VT_s"], **cm} for c in R]
    r3 = _run(build_l0a3, "l0a3", in_maps)
    in_maps = [{"xnf_s": r1[c]["xnf_s"], "y_s": r3[c]["y_s"], "l0_woT": l0a["l0_woT"],
                "l0_ln2": l0a["l0_ln2"], **cm} for c in R]
    r4 = _run(build_l0a4, "l0a4", in_maps)
    in_maps = [{"h_s": r4[c]["h_s"], "s1_s": r4[c]["s1_s"], "l0_cw1T": l0a["l0_cw1T"],
                "l0_cb1": l0a["l0_cb1"], "l0_cw2T": l0a["l0_cw2T"], "l0_cb2": l0a["l0_cb2"], **cm} for c in R]
    r5 = _run(build_l0a5, "l0a5", in_maps)
    xcur = [r1[c]["xnf_s"] for c in R]
    for l in range(2):
        in_maps = [{"xc_in": xcur[c], **l0f[l], **cm} for c in R]
        rf1 = _run(build_l0f1, "l0f1", in_maps)
        in_maps = [{"xc_in": xcur[c], "Q_s": rf1[c]["Q_s"], "K_s": rf1[c]["K_s"],
                    "VT_s": rf1[c]["VT_s"], "f_woTh": l0f[l]["f_woTh"], **cm} for c in R]
        rf2 = _run(build_l0f2, "l0f2", in_maps)
        in_maps = [{"xm_in": rf2[c]["xmid_s"], **l0f[l], **cm} for c in R]
        rf3a = _run(build_l0f3a, "l0f3a", in_maps)
        if l == 0:
            in_maps = [{"xm_in": rf2[c]["xmid_s"], "hb_s": rf3a[c]["hb_s"],
                        **l0f[l], **cm} for c in R]
            rf3 = _run(lambda: build_l0f3b(False), "l0f3b", in_maps)
            xcur = [rf3[c]["x_out"] for c in R]
        else:
            in_maps = [{"xm_in": rf2[c]["xmid_s"], "hb_s": rf3a[c]["hb_s"],
                        "x_in": xcore[c], "sA_s": r5[c]["sA_s"], **l0f[l], **cm} for c in R]
            rf3 = _run(lambda: build_l0f3b(True), "l0f3bf", in_maps)
            xcur = [rf3[c]["x_out"] for c in R]
    return xcur



def build_mega():
    P = Prog("mega")
    T = {}

    def phase(ns, fn, bind=None, is_out=False, **kw):
        P.begin_phase(ns, bind, force_internal=not is_out)
        fn(P=P, **kw)
        T[ns] = P.end_phase()

    phase("l0a1", build_l0a1)
    phase("l0a2", build_l0a2, {"xn_s": T["l0a1"]["xn_s"]})
    phase("l0a3", build_l0a3, {k: T["l0a2"][k] for k in ("Q_s", "K_s", "KT_s", "VT_s")})
    phase("l0a4", build_l0a4, {"xnf_s": T["l0a1"]["xnf_s"], "y_s": T["l0a3"]["y_s"]})
    phase("l0a5", build_l0a5, {"h_s": T["l0a4"]["h_s"], "s1_s": T["l0a4"]["s1_s"]})
    xcur = T["l0a1"]["xnf_s"]
    for l, sfx in ((0, "a"), (1, "b")):
        phase(f"f1{sfx}", build_l0f1, {"xc_in": xcur})
        phase(f"f2{sfx}", build_l0f2,
              {"xc_in": xcur, **{k: T[f"f1{sfx}"][k] for k in ("Q_s", "K_s", "VT_s")}})
        phase(f"f3a{sfx}", build_l0f3a, {"xm_in": T[f"f2{sfx}"]["xmid_s"]})
        bind = {"xm_in": T[f"f2{sfx}"]["xmid_s"], "hb_s": T[f"f3a{sfx}"]["hb_s"]}
        if l == 1:
            bind.update({"x_in": T["l0a1"]["x_in"], "sA_s": T["l0a5"]["sA_s"]})
        phase(f"f3b{sfx}", build_l0f3b, bind, final=(l == 1))
        xcur = T[f"f3b{sfx}"]["x_out"]
    X1 = xcur
    phase("l1ln", build_l1ln, {"x_in": X1})
    phase("l1a1", build_l1a1, {"xn_s": T["l1ln"]["xn_s"]})
    l1ss = P.nc.dram_tensor("l1_ss", [DT, 128, NT], F32, kind="Internal").ap()
    prev = None
    for seg in range(4):
        bind = {"cdt_c": T["l1a1"]["cdt_s"], "z_c": T["l1a1"]["z_s"], "o_s": l1ss}
        if seg > 0:
            bind.update({"S0_i": prev["S0_o"], "yprev_i": prev["yprev_o"]})
            bind.update({k: T["l1s0"][k] for k in
                         ("l1_wrT", "l1_woutT", "l1_hcols", "l1_nw", "l1_Dexp")})
        phase(f"l1s{seg}", build_l1scan, bind, seg=seg)
        prev = T[f"l1s{seg}"]
    phase("l1f", build_l1f, {"x_in": X1, "xnT_s": T["l1ln"]["xnT_s"], "s_s": l1ss})
    X2 = T["l1f"]["x_out"]
    phase("l2a", build_l2a, {"x_in": X2})
    phase("l2b", build_l2b, {"x_in": X2, "xn_s": T["l2a"]["xn_s"], "s_s": T["l2a"]["s_s"]})
    X3 = T["l2b"]["x_out"]
    phase("l3a1", build_l3a1, {"x_in": X3})
    phase("l3a2", build_l3a2, {k: T["l3a1"][k] for k in ("K_s", "Q_s", "VT_s")})
    phase("l3b1", build_l3b1, {"xn_s": T["l3a1"]["xn_s"]})
    phase("l3b2", build_l3b2,
          {"x_in": X3, "xn_s": T["l3a1"]["xn_s"], "s_s": T["l3a2"]["s_s"],
           "moe_s": T["l3b1"]["moe_s"]}, is_out=True)
    return P.finalize()


def fused_in_map(xc, l0a, l0f, l1a, l1s, l1fw, l2w, l2f, l3a, l3b1w, l3b2w):
    m = dict(CONSTS)
    def pref(ns, d):
        for k, v in d.items():
            m[f"{ns}.{k}"] = v
    m["l0a1.x_in"] = xc
    pref("l0a2", {k: l0a[k] for k in ("l0_wqkvT", "l0_ln1")})
    pref("l0a4", {k: l0a[k] for k in ("l0_woT", "l0_ln2")})
    pref("l0a5", {k: l0a[k] for k in ("l0_cw1T", "l0_cb1", "l0_cw2T", "l0_cb2")})
    for l, sfx in ((0, "a"), (1, "b")):
        pref(f"f1{sfx}", {k: l0f[l][k] for k in ("f_wqkvT", "f_qs2")})
        pref(f"f2{sfx}", {"f_woTh": l0f[l]["f_woTh"]})
        pref(f"f3a{sfx}", {k: l0f[l][k] for k in ("f_w1T", "f_b1", "f_lng")})
        pref(f"f3b{sfx}", {k: l0f[l][k] for k in ("f_w2T", "f_b2")})
    pref("l1a1", l1a)
    pref("l1s0", l1s)
    m["l1s0.S0_i"] = np.zeros((BL, 128, 2048), np.float32).astype(_bf16)
    m["l1s0.yprev_i"] = np.zeros((128, 32), np.float32).astype(_bf16)
    pref("l1f", l1fw)
    pref("l2a", l2w)
    pref("l2b", l2f)
    pref("l3a1", {k: l3a[k] for k in ("l3_waT", "l3_ba")})
    pref("l3a2", {k: l3a[k] for k in ("l3_wpTh", "l3_bp")})
    pref("l3b1", l3b1w)
    pref("l3b2", l3b2w)
    return m


FUSED = True


def kernel_fused(ins, xcore):
    l0a, l0f = prep_l0(ins)
    l1a, l1s, l1fw = prep_l1(ins)
    l2w, l2f = prep_l2(ins)
    l3a, l3b1w, l3b2w = prep_l3(ins)
    in_maps = [fused_in_map(xcore[c], l0a, l0f, l1a, l1s, l1fw, l2w, l2f,
                            l3a, l3b1w, l3b2w) for c in range(NCORES)]
    res = _run(build_mega, "mega", in_maps)
    return [res[c]["l3b2.x_out"] for c in range(NCORES)]
